# version 05
# baseline (speedup 1.0000x reference)
import math
import numpy as np
import ml_dtypes
import concourse.bass as bass
import concourse.mybir as mybir
from contextlib import ExitStack

F32 = mybir.dt.float32
BF16 = mybir.dt.bfloat16
U8 = mybir.dt.uint8
AF = mybir.ActivationFunctionType
ALU = mybir.AluOpType
AX = mybir.AxisListType

ENGINES = ("sync", "vector", "scalar", "tensor", "gpsimd")
USE_GP_QUANT = False

LN_EPS = 1e-5
CONST_SCALARS = {
    "eps_ln": 1e-5,
    "neg_ln16": -float(np.log(16.0)),
    "eps8": 1e-8,
    "eps12": 1e-12,
    "eps6": 1e-6,
    "neg30": -1e30,
    "zero": 0.0,
}


class Sem:
    def __init__(self, handle, name):
        self.h = handle
        self.name = name
        self.n = 0


class Prog:

    def __init__(self, nc):
        self.nc = nc
        self.es = ExitStack()
        self.q = {e: [] for e in ENGINES}
        self.sems = {}
        self.waited = {e: {} for e in ENGINES}
        self.carry = {e: {} for e in ENGINES}

    def sem(self, name):
        if name not in self.sems:
            h = self.es.enter_context(self.nc.semaphore(name))
            self.sems[name] = Sem(h, name)
        return self.sems[name]

    _uid = 0

    def sbuf(self, name, shape, dt=F32, es=None):
        Prog._uid += 1
        return (es or self.es).enter_context(self.nc.sbuf_tensor(f"{name}_{Prog._uid}", shape, dt))

    def psum(self, name, shape, dt=F32):
        return self.es.enter_context(self.nc.psum_tensor(name, shape, dt))

    def step(self, eng, fn, wait=(), inc=None, inc_k=1):
        w = []
        hw = self.waited[eng]
        carry = self.carry[eng]
        for s, v in wait:
            if s is None:
                continue
            assert v <= s.n, f"[{eng}] wait on {s.name} for {v} but only {s.n} planned"
            if v <= 0:
                continue
            if inc is None and carry.get(s, 0) < v and not s.name.startswith(("rx", "rw", "rs", "ld", "st")):
                carry[s] = v
            if not s.name.startswith(("rx", "rw", "rs", "ld", "st")):
                if hw.get(s.name, 0) >= v:
                    continue
                hw[s.name] = v
            w.append((s, v))
        ret = None
        if inc is not None:
            have = {s: v for s, v in w}
            for s, v in carry.items():
                if have.get(s, 0) < v:
                    w.append((s, v))
            carry.clear()
            inc.n += inc_k
            ret = inc.n
        self.q[eng].append((tuple(w), fn, inc, inc_k))
        return ret

    def n_queued(self):
        return sum(len(v) for v in self.q.values())

    def finalize_block(self):
        nc = self.nc
        with nc.Block() as blk:
            for e in ENGINES:
                items = self.q[e]
                if not items:
                    continue

                def make(items):
                    def run(proxy):
                        for w, fn, inc, inc_k in items:
                            for s, v in w:
                                proxy.wait_ge(s.h, v)
                            ins = fn(proxy) if fn is not None else None
                            if inc is not None:
                                assert ins is not None, "inc with no instruction"
                                ins.then_inc(inc.h, inc_k)
                    return run

                getattr(blk, e)(make(items))
        self.q = {e: [] for e in ENGINES}


def cdiv(a, b):
    return (a + b - 1) // b


class PsPool:

    def __init__(self, tensors):
        self.t = tensors
        self.busy = [[] for _ in tensors]
        self.rr = 0

    def acquire(self):
        i = self.rr % len(self.t)
        self.rr += 1
        w = self.busy[i]
        self.busy[i] = []
        return i, list(w)

    def release(self, i, *wspecs):
        self.busy[i] = [w for w in wspecs if w is not None]


class DmaRing:

    def __init__(self, p, name, K, queue="sync"):
        self.p = p
        self.K = K
        self.queue = queue
        self.sems = [p.sem(f"{name}{j}") for j in range(K)]
        self.acks = [[] for _ in range(K)]
        self.i = 0

    def load(self, fn, wait=()):
        j = self.i % self.K
        self.i += 1
        w = list(wait)
        if self.acks[j]:
            w.extend(self.acks[j])
        v = self.p.step(self.queue, fn, wait=w, inc=self.sems[j], inc_k=16)
        self.acks[j] = []
        return j, (self.sems[j], v)

    def ack(self, j, *wspecs):
        self.acks[j] = [w for w in wspecs if w is not None]




def build_const_host():
    f32_cols = {}
    bf_cols = {}
    cf_parts = []
    cb_parts = []
    colf = [0]
    colb = [0]

    def add_f(name, arr):
        arr = np.asarray(arr, dtype=np.float32)
        a = np.zeros((128, arr.shape[1]), dtype=np.float32)
        a[: arr.shape[0]] = arr
        f32_cols[name] = (colf[0], arr.shape[1])
        colf[0] += arr.shape[1]
        cf_parts.append(a)

    def add_b(name, arr):
        arr = np.asarray(arr, dtype=np.float32)
        a = np.zeros((128, arr.shape[1]), dtype=np.float32)
        a[: arr.shape[0]] = arr
        bf_cols[name] = (colb[0], arr.shape[1])
        colb[0] += arr.shape[1]
        cb_parts.append(a.astype(ml_dtypes.bfloat16))

    for nm, val in CONST_SCALARS.items():
        add_f("s_" + nm, np.full((128, 1), val, dtype=np.float32))
    add_b("ident", np.eye(128, dtype=np.float32))
    ar = np.arange(128)
    add_b("neg_low", np.where(ar[:, None] > ar[None, :], -1e30, 0.0).astype(np.float32))
    se = np.zeros((128, 1), np.float32); se[:64] = 1.0
    so = np.zeros((128, 1), np.float32); so[64:] = 1.0
    add_b("sel_even", se)
    add_b("sel_odd", so)
    add_b("ones_col128", np.ones((128, 1), np.float32))
    add_b("mask_t2_le_t1", (np.arange(128)[:, None] <= np.arange(128)[None, :]).astype(np.float32))
    add_b("ones_row_b", np.ones((1, 128), dtype=np.float32))
    add_f("ones_row", np.ones((1, 128), dtype=np.float32))
    cf = np.concatenate(cf_parts, axis=1)
    cb = np.concatenate(cb_parts, axis=1)
    return cf, cb, f32_cols, bf_cols


class Ctx:

    def __init__(self, p: Prog, T):
        self.p = p
        self.T = T
        self.NT = T // 128
        nc = p.nc
        cf, cb, f32_cols, bf_cols = build_const_host()
        self.const_f32_host = cf
        self.const_bf_host = cb
        self.cF = nc.dram_tensor("CONSTF", list(cf.shape), F32, kind="ExternalInput").ap()
        self.cB = nc.dram_tensor("CONSTB", list(cb.shape), BF16, kind="ExternalInput").ap()
        self.cf_sb = p.sbuf("cf_sb", list(cf.shape), F32)
        self.cb_sb = p.sbuf("cb_sb", list(cb.shape), BF16)
        self.f32_cols = f32_cols
        self.bf_cols = bf_cols
        self.x_res = p.sbuf("x_res", [128, self.NT * 1024], F32)
        self.xn_fm = None
        self.xn_stage = [p.sbuf(f"xn_stage{i}", [128, 1024], BF16) for i in range(2)]
        self.sq_scr = [p.sbuf("sq_scr0", [128, 1024], F32)]
        self.row_scr = p.sbuf("row_scr", [128, 256], F32)
        self.psF = [p.psum(f"psF{i}", [128, 512], F32) for i in range(6)]
        self.psT = [p.psum(f"psT{i}", [128, 512], BF16) for i in range(2)]
        self.pool = PsPool(self.psF)
        self.poolT = PsPool(self.psT)
        self.ld = p.sem("ld")
        self.stage_guard = [[], []]
        self.scr_last = [0]
        self._gb11 = p.sbuf("g_b11", [1, 1], BF16)
        self.rx = DmaRing(p, "rx", 6, "sync")
        self.rw = DmaRing(p, "rw", 10, "gpsimd")
        self.rs = DmaRing(p, "rs", 3, "sync")
        self.st = p.sem("st")
        self.te = p.sem("te")
        self.ve = p.sem("ve")
        self.se = p.sem("se")
        self.ge = p.sem("ge")

    def drain_rings(self):
        for ring, eng in ((self.rx, "sync"), (self.rs, "sync"), (self.rw, "gpsimd")):
            for s in ring.sems:
                if s.n > 0:
                    self.p.step(eng, None, wait=[(s, s.n)])
        if self.ld.n > 0:
            self.p.step("sync", None, wait=[(self.ld, self.ld.n)])
        if self.st.n > 0:
            self.p.step("sync", None, wait=[(self.st, self.st.n)])

    def fence(self):
        w = [(self.te, self.te.n), (self.ve, self.ve.n), (self.se, self.se.n),
             (self.ge, self.ge.n)]
        for ring in (self.rx, self.rw, self.rs):
            for s in ring.sems:
                if s.n > 0:
                    w.append((s, s.n))
        if self.ld.n > 0:
            w.append((self.ld, self.ld.n))
        if self.st.n > 0:
            w.append((self.st, self.st.n))
        return w

    def cfa(self, name):
        off, w = self.f32_cols[name]
        return self.cf_sb[:, off : off + w]

    def cba(self, name):
        off, w = self.bf_cols[name]
        return self.cb_sb[:, off : off + w]

    def load_consts(self):
        p = self.p
        p.step("sync", lambda e: e.dma_start(out=self.cf_sb[:], in_=self.cF), inc=self.ld, inc_k=16)
        p.step("sync", lambda e: e.dma_start(out=self.cb_sb[:], in_=self.cB), inc=self.ld, inc_k=16)
        self.const_ready = (self.ld, self.ld.n)
        for e in ENGINES:
            p.step(e, None, wait=[self.const_ready])






def ln_from_xres(ctx, ti, xwait, scr_last, dst_tile=None):
    p = ctx.p
    rs = ctx.row_scr
    col0 = (ti % 16) * 8
    c_sum = rs[:, col0 + 0 : col0 + 1]
    c_sumsq = rs[:, col0 + 1 : col0 + 2]
    c_mean = rs[:, col0 + 2 : col0 + 3]
    c_m2 = rs[:, col0 + 3 : col0 + 4]
    c_var = rs[:, col0 + 4 : col0 + 5]
    c_sd = rs[:, col0 + 5 : col0 + 6]
    c_rstd = rs[:, col0 + 6 : col0 + 7]
    xr = ctx.x_res[:, ti * 1024 : (ti + 1) * 1024]
    scr = ctx.sq_scr[0]
    sev = p.step(
        "scalar",
        lambda e, xr=xr, scr=scr, c_sumsq=c_sumsq: e.activation(
            scr[:], xr, AF.Square, accum_out=c_sumsq
        ),
        wait=[xwait, (ctx.se, scr_last[0])],
        inc=ctx.se,
    )
    scr_last[0] = sev
    vev = p.step(
        "vector",
        lambda e, xr=xr, c_sum=c_sum: e.reduce_sum(c_sum, xr, axis=AX.X),
        wait=[xwait],
        inc=ctx.ve,
    )
    vev = p.step(
        "vector",
        lambda e, c_mean=c_mean, c_sum=c_sum: e.tensor_scalar(
            c_mean, c_sum, 1.0 / 1024.0, None, ALU.mult
        ),
        wait=[(ctx.ve, vev)],
        inc=ctx.ve,
    )
    vev = p.step(
        "vector",
        lambda e, c_m2=c_m2, c_mean=c_mean: e.tensor_mul(c_m2, c_mean, c_mean),
        wait=[(ctx.ve, vev)],
        inc=ctx.ve,
    )
    vev = p.step(
        "vector",
        lambda e, c_var=c_var, c_sumsq=c_sumsq, c_m2=c_m2: e.scalar_tensor_tensor(
            c_var, c_sumsq, 1.0 / 1024.0, c_m2, ALU.mult, ALU.subtract
        ),
        wait=[(ctx.ve, vev), (ctx.se, sev)],
        inc=ctx.ve,
    )
    sev2 = p.step(
        "scalar",
        lambda e, c_sd=c_sd, c_var=c_var: e.activation(c_sd, c_var, AF.Sqrt, bias=ctx.cfa("s_eps_ln")[:, 0:1]),
        wait=[(ctx.ve, vev)],
        inc=ctx.se,
    )
    vev = p.step(
        "vector",
        lambda e, c_rstd=c_rstd, c_sd=c_sd: e.reciprocal(c_rstd, c_sd),
        wait=[(ctx.se, sev2)],
        inc=ctx.ve,
    )
    if dst_tile is None:
        stage = ctx.xn_stage[ti % 2]
        xt = stage[:, 0:1024]
    else:
        xt = dst_tile
    vev = p.step(
        "vector",
        lambda e, xr=xr, xt=xt, c_mean=c_mean, c_rstd=c_rstd: e.tensor_scalar(
            xt, xr, c_mean, c_rstd, ALU.subtract, ALU.mult
        ),
        wait=[(ctx.ve, vev)] + (ctx.stage_guard[ti % 2] if dst_tile is None else []),
        inc=ctx.ve,
    )
    if ctx.xn_fm is not None:
        ve_fm = tm_to_fm_src(ctx, xt, ti, ctx.xn_fm, (ctx.ve, vev))
    else:
        ve_fm = vev
    if dst_tile is None:
        ctx.stage_guard[ti % 2] = [(ctx.ve, ve_fm)]
    return [(ctx.se, sev), (ctx.ve, vev)]


def tm_to_fm_src(ctx, src_tile, ti, fm_buf, wait, nchunk=8):
    p = ctx.p
    T = ctx.T
    ident = ctx.cba("ident")
    fm3 = fm_buf.rearrange("p (c t) -> p c t", c=nchunk, t=T)
    vv = None
    for g in range(cdiv(nchunk, 4)):
        nin = min(4, nchunk - g * 4)
        bank, bw = ctx.poolT.acquire()
        ps = ctx.psT[bank]
        tw = bw + list([wait])
        tvs = None
        for c4 in range(nin):
            c = g * 4 + c4
            tvs = p.step(
                "tensor",
                lambda t, c=c, c4=c4, ps=ps: t.transpose(
                    ps[:, c4 * 128 : (c4 + 1) * 128],
                    src_tile[:, c * 128 : (c + 1) * 128],
                    ident,
                ),
                wait=tw,
                inc=ctx.te,
            )
            tw = ()
        dst = fm3[:, g * 4 : g * 4 + nin, ti * 128 : (ti + 1) * 128]
        src = ps.rearrange("p (c t) -> p c t", c=4, t=128)[:, 0:nin, :]
        vv = p.step(
            "vector",
            lambda v, dst=dst, src=src: v.tensor_copy(dst, src),
            wait=[(ctx.te, tvs)],
            inc=ctx.ve,
        )
        ctx.poolT.release(bank, (ctx.ve, vv))
    return vv


def ln_phase(ctx, x_src, prev_store, dst_fn=None):
    scr_last = ctx.scr_last
    for ti in range(ctx.NT):
        dst = dst_fn(ti) if dst_fn is not None else None
        if x_src is not None:
            xr = ctx.x_res[:, ti * 1024 : (ti + 1) * 1024]
            lw = [ctx.const_ready]
            if prev_store is not None:
                lw.append(prev_store)
            slot, wsp = ctx.rx.load(
                lambda e, ti=ti, xr=xr: e.dma_start(out=xr, in_=x_src[ti * 128 : (ti + 1) * 128, :]),
                wait=lw)
            consumed = ln_from_xres(ctx, ti, wsp, scr_last, dst_tile=dst)
            ctx.rx.ack(slot, *consumed)
        else:
            ln_from_xres(ctx, ti, (ctx.ve, ctx.ve.n), scr_last, dst_tile=dst)


def store_xres(ctx, x_dst):
    p = ctx.p
    NT = ctx.NT
    v = None
    for g in range(cdiv(NT, 4)):
        n = min(4, NT - g * 4)
        src = ctx.x_res.rearrange("p (ti d) -> p ti d", ti=NT, d=1024)[:, g * 4 : g * 4 + n, :]
        dst = x_dst[g * 512 : g * 512 + n * 128, :].rearrange("(ti p) d -> p ti d", p=128)
        v = p.step(
            "sync",
            lambda e, dst=dst, src=src: e.dma_start(out=dst, in_=src),
            wait=[(ctx.ve, ctx.ve.n), (ctx.se, ctx.se.n)],
            inc=ctx.st,
            inc_k=16,
        )
    return (ctx.st, v)




def load_w_cols(ctx, hbm, c0, w, dst_sb, wait=(), d_in=1024):
    nch = d_in // 128
    src = hbm[:, c0 : c0 + w].rearrange("(c p) w -> p c w", p=128)
    return ctx.rw.load(
        lambda e, src=src, dst=dst_sb, n=nch * w, nch=nch, w=w: e.dma_start(
            out=dst[:, 0:n].rearrange("p (c w) -> p c w", c=nch, w=w), in_=src
        ),
        wait=wait,
    )


def bcast_scalar(ctx, src11_bf, dst_f32, wait=()):
    p = ctx.p
    pi, pw = ctx.pool.acquire()
    ps = ctx.psF[pi]
    tv = p.step(
        "tensor",
        lambda t: t.matmul(ps[:, 0:1], ctx.cba("ones_row_b")[0:1, :], src11_bf, start=True, stop=True),
        wait=pw + list(wait),
        inc=ctx.te,
    )
    vv = p.step("vector", lambda v: v.tensor_copy(dst_f32, ps[:, 0:1]), wait=[(ctx.te, tv)], inc=ctx.ve)
    ctx.pool.release(pi, (ctx.ve, vv))
    return vv


def bcast_row(ctx, src_row_bf, n, dst, wait=(), dst_rows=128):
    p = ctx.p
    pi, pw = ctx.pool.acquire()
    ps = ctx.psF[pi]
    tv = p.step(
        "tensor",
        lambda t: t.matmul(ps[0:dst_rows, 0:n], ctx.cba("ones_row_b")[0:1, 0:dst_rows], src_row_bf, start=True, stop=True),
        wait=pw + list(wait),
        inc=ctx.te,
    )
    vv = p.step("vector", lambda v: v.tensor_copy(dst, ps[0:dst_rows, 0:n]), wait=[(ctx.te, tv)], inc=ctx.ve)
    ctx.pool.release(pi, (ctx.ve, vv))
    return vv




def causal_flash(ctx, sc, q_fm, k_fm, v_col, out_dst, hbase, hd, negc, rd_extra, qk_ready):
    p = ctx.p
    T, NT = ctx.T, ctx.NT
    probs = sc["probs"]
    probsT = sc["probsT"]
    arow = sc["arow"]
    probs_rel = sc.setdefault("probs_rel", [[] for _ in range(len(probs))])
    while len(probs_rel) < len(probs):
        probs_rel.append([])
    pvT_prev = sc.setdefault("pvT_prev", [])
    arow_guard = sc.setdefault("arow_guard", [[], []])
    ident = ctx.cba("ident")
    neg_low = ctx.cba("neg_low")

    def stage_a(tq):
        nblk = tq + 1
        nsl = cdiv(nblk, 4)
        aset = tq % 2
        ac0 = aset * 16
        dc = arow[:, ac0 : ac0 + nsl]
        sv_last = None
        exps = []
        for s in range(nsl):
            w = min(512, nblk * 128 - s * 512)
            slot = aset * 4 + s
            pi, pw = ctx.pool.acquire()
            ps = ctx.psF[pi]
            is_last = s == nsl - 1
            mm1 = p.step(
                "tensor",
                lambda t, ps=ps, w=w, s=s, tq=tq, il=is_last: t.matmul(
                    ps[:, 0:w],
                    q_fm[hbase : hbase + hd, tq * 128 : (tq + 1) * 128],
                    k_fm[hbase : hbase + hd, s * 512 : s * 512 + w],
                    start=True,
                    stop=not il,
                ),
                wait=pw + list(qk_ready),
                inc=None if is_last else ctx.te,
            )
            if is_last:
                jd = (tq % 4) * 128
                mm1 = p.step(
                    "tensor",
                    lambda t, ps=ps, jd=jd: t.matmul(
                        ps[:, jd : jd + 128], neg_low, ident, start=False, stop=True
                    ),
                    inc=ctx.te,
                )
            ew = [(ctx.te, mm1)] + probs_rel[slot]
            if s == 0:
                ew = ew + arow_guard[aset]
            sv_last = p.step(
                "scalar",
                lambda e, ps=ps, w=w, slot=slot, dcs=dc[:, s : s + 1]: e.activation(
                    probs[slot][:, 0:w], ps[:, 0:w], AF.Exp, bias=negc, accum_out=dcs
                ),
                wait=ew,
                inc=ctx.se,
            )
            ctx.pool.release(pi, (ctx.se, sv_last))
            exps.append(sv_last)
        return (tq, exps)

    def stage_b(st):
        tq, exps = st
        nblk = tq + 1
        nsl = cdiv(nblk, 4)
        aset = tq % 2
        ac0 = aset * 16
        dc = arow[:, ac0 : ac0 + nsl]
        tr_last = None
        for s in range(nsl):
            w = min(512, nblk * 128 - s * 512)
            slot = aset * 4 + s
            for j in range(w // 128):
                bank, bw = ctx.poolT.acquire()
                tw = bw + [(ctx.se, exps[s])]
                if s == 0 and j == 0:
                    tw = tw + pvT_prev
                tr_last = p.step(
                    "tensor",
                    lambda t, bank=bank, slot=slot, j=j: t.transpose(
                        ctx.psT[bank][:, 0:128], probs[slot][:, j * 128 : (j + 1) * 128], ident
                    ),
                    wait=tw,
                    inc=ctx.te,
                )
                vv = p.step(
                    "vector",
                    lambda v, bank=bank, c=s * 4 + j: v.tensor_copy(
                        probsT[:, c * 128 : (c + 1) * 128], ctx.psT[bank][:, 0:128]
                    ),
                    wait=[(ctx.te, tr_last)],
                    inc=ctx.ve,
                )
                ctx.poolT.release(bank, (ctx.ve, vv))
            probs_rel[slot] = [(ctx.te, tr_last)]
        c_d = arow[:, ac0 + 8 : ac0 + 9]
        c_rd = arow[:, ac0 + 9 : ac0 + 10]
        vv = p.step(
            "vector",
            lambda v, dc=dc, c_d=c_d: v.reduce_sum(c_d, dc, axis=AX.X),
            wait=[(ctx.se, exps[-1])],
            inc=ctx.ve,
        )
        vv = p.step("vector", lambda v, c_d=c_d, c_rd=c_rd: v.reciprocal(c_rd, c_d),
                    wait=[(ctx.ve, vv)], inc=ctx.ve)
        if rd_extra is not None:
            vv = p.step(
                "vector",
                lambda v, c_rd=c_rd: v.tensor_mul(c_rd, c_rd, rd_extra),
                wait=[(ctx.ve, vv)],
                inc=ctx.ve,
            )
        pi, pw = ctx.pool.acquire()
        ps = ctx.psF[pi]
        mmv = None
        for c in range(nblk):
            mw = []
            if c == 0:
                mw = pw + [(ctx.ve, ctx.ve.n)]
            mmv = p.step(
                "tensor",
                lambda t, ps=ps, c=c, nblk=nblk: t.matmul(
                    ps[:, 0:hd],
                    probsT[:, c * 128 : (c + 1) * 128],
                    v_col(c),
                    start=(c == 0),
                    stop=(c == nblk - 1),
                ),
                wait=mw,
                inc=ctx.te if c == nblk - 1 else None,
            )
        pvT_prev.clear()
        pvT_prev.append((ctx.te, mmv))
        vv2 = p.step(
            "vector",
            lambda v, ps=ps, tq=tq, c_rd=c_rd: v.tensor_scalar(
                out_dst(tq), ps[:, 0:hd], c_rd, None, ALU.mult
            ),
            wait=[(ctx.te, mmv), (ctx.ve, vv)],
            inc=ctx.ve,
        )
        ctx.pool.release(pi, (ctx.ve, vv2))
        arow_guard[aset] = [(ctx.ve, vv2)]

    pend = None
    for tq in range(NT):
        st = stage_a(tq)
        if pend is not None:
            stage_b(pend)
        pend = st
    stage_b(pend)


def head_norm_sqrtmax(ctx, sq_bf, hbase, hd, arow, tmp0, outcol, ready):
    p = ctx.p
    T = ctx.T
    nsl = cdiv(T, 512)
    ones = ctx.cba("ones_col128")
    vv = None
    for s in range(nsl):
        w = min(512, T - s * 512)
        pi, pw = ctx.pool.acquire()
        ps = ctx.psF[pi]
        tv = p.step(
            "tensor",
            lambda t, ps=ps, s=s, w=w: t.matmul(
                ps[0:1, 0:w],
                ones[hbase : hbase + hd, 0:1],
                sq_bf[hbase : hbase + hd, s * 512 : s * 512 + w],
                start=True,
                stop=True,
            ),
            wait=pw + list(ready),
            inc=ctx.te,
        )
        vv = p.step(
            "vector",
            lambda v, ps=ps, s=s, w=w: v.reduce_max(arow[0:1, tmp0 + s : tmp0 + s + 1], ps[0:1, 0:w], axis=AX.X),
            wait=[(ctx.te, tv)],
            inc=ctx.ve,
        )
        ctx.pool.release(pi, (ctx.ve, vv))
    vv = p.step(
        "vector",
        lambda v: v.reduce_max(arow[0:1, tmp0 + nsl : tmp0 + nsl + 1], arow[0:1, tmp0 : tmp0 + nsl], axis=AX.X),
        wait=[(ctx.ve, vv)],
        inc=ctx.ve,
    )
    sv = p.step(
        "scalar",
        lambda e: e.activation(arow[0:1, outcol : outcol + 1], arow[0:1, tmp0 + nsl : tmp0 + nsl + 1], AF.Sqrt),
        wait=[(ctx.ve, vv)],
        inc=ctx.se,
    )
    return sv




def l0_attn_batch(ctx, ws, sc, g_bc, gv):
    p = ctx.p
    T, NT = ctx.T, ctx.NT
    xn_fm = ctx.xn_fm
    attn_tm = sc["attn_tm"]
    qf = sc["q_pair"]
    kf = sc["k_pair"]
    vt = sc["v_pair"]
    sqb = sc["sq_pair"]
    arow = sc["arow"]
    negc = sc["negc"]
    b11 = sc["b11"]
    wbuf = sc["wq3"]
    xn_ready = sc["xn_ready"]
    for pair in range(sc.get("npairs", 8)):
        wslots = []
        for wi, wname in enumerate(("WQ", "WK", "WV")):
            dst = wbuf[:, wi * 1024 : (wi + 1) * 1024]
            slot, wsp = load_w_cols(ctx, ws[wname], pair * 128, 128, dst,
                                    wait=sc["wq3_guard"][wi])
            wslots.append((slot, wsp))
        wq = wbuf[:, 0:1024]
        wk = wbuf[:, 1024:2048]
        wv = wbuf[:, 2048:3072]
        q_done = k_done = None
        for which in range(2):
            wsl, wsp = wslots[which]
            wap = (wq, wk)[which]
            dstf = (qf, kf)[which]
            scale = (0.125, 1.0)[which]
            last = None
            for s in range(cdiv(T, 512)):
                w512 = min(512, T - s * 512)
                pi, pw = ctx.pool.acquire()
                ps = ctx.psF[pi]
                tv = None
                for c in range(8):
                    tv = p.step(
                        "tensor",
                        lambda t, ps=ps, c=c, s=s, wap=wap, w512=w512: t.matmul(
                            ps[:, 0:w512],
                            wap[:, c * 128 : (c + 1) * 128],
                            xn_fm[:, c * T + s * 512 : c * T + s * 512 + w512],
                            start=(c == 0),
                            stop=(c == 7),
                        ),
                        wait=(pw + [wsp] + xn_ready) if c == 0 else (),
                        inc=ctx.te if c == 7 else None,
                    )
                ev = p.step(
                    "scalar",
                    lambda e, ps=ps, s=s, dstf=dstf, scale=scale, w512=w512: e.activation(
                        dstf[:, s * 512 : s * 512 + w512], ps[:, 0:w512], AF.Copy, scale=scale
                    ),
                    wait=[(ctx.te, tv)],
                    inc=ctx.se,
                )
                ctx.pool.release(pi, (ctx.se, ev))
                last = ev
            if which == 0:
                q_done = last
            else:
                k_done = last
        ctx.rw.ack(wslots[0][0], (ctx.se, q_done))
        ctx.rw.ack(wslots[1][0], (ctx.se, k_done))
        sc["wq3_guard"][0] = [(ctx.se, q_done)]
        sc["wq3_guard"][1] = [(ctx.se, k_done)]
        v_done = None
        for ti in range(NT):
            pi, pw = ctx.pool.acquire()
            ps = ctx.psF[pi]
            tv = None
            for c in range(8):
                tv = p.step(
                    "tensor",
                    lambda t, ps=ps, c=c, ti=ti: t.matmul(
                        ps[:, 0:128],
                        xn_fm[:, c * T + ti * 128 : c * T + ti * 128 + 128],
                        wv[:, c * 128 : (c + 1) * 128],
                        start=(c == 0),
                        stop=(c == 7),
                    ),
                    wait=(pw + [wslots[2][1]] + xn_ready) if c == 0 else (),
                    inc=ctx.te if c == 7 else None,
                )
            v_done = p.step(
                "vector",
                lambda v, ps=ps, ti=ti: v.tensor_copy(vt[:, ti * 128 : (ti + 1) * 128], ps[:, 0:128]),
                wait=[(ctx.te, tv)],
                inc=ctx.ve,
            )
            ctx.pool.release(pi, (ctx.ve, v_done))
        ctx.rw.ack(wslots[2][0], (ctx.ve, v_done))
        sc["wq3_guard"][2] = [(ctx.ve, v_done)]
        if sc.get("attn_sub") == "qkv":
            sc["sq_guard"] = [(ctx.ve, ctx.ve.n)]
            continue
        nsub = sc.get("attn_sub") or ""
        sqv = p.step(
            "scalar",
            lambda e: e.activation(sqb[:], qf[:], AF.Square),
            wait=[(ctx.se, q_done)] + sc["sq_guard"],
            inc=ctx.se,
        )
        sqk = p.step(
            "scalar",
            lambda e, sq2=sc["sq_pair2"]: e.activation(sq2[:], kf[:], AF.Square),
            wait=[(ctx.se, k_done)],
            inc=ctx.se,
        )
        if nsub == "norms0":
            sc["sq_guard"] = [(ctx.se, ctx.se.n), (ctx.ve, ctx.ve.n)]
            continue
        for hh in range(2):
            hbase = hh * 64
            qm = head_norm_sqrtmax(ctx, sqb, hbase, 64, arow, 34, 40, [(ctx.se, sqv)])
            if nsub == "norms1":
                sc["sq_guard"] = [(ctx.se, ctx.se.n), (ctx.ve, ctx.ve.n)]
                continue
            km = head_norm_sqrtmax(ctx, sc["sq_pair2"], hbase, 64, arow, 34, 41, [(ctx.se, qm), (ctx.se, sqk)])
            vv = p.step(
                "vector",
                lambda v: v.tensor_mul(arow[0:1, 42:43], arow[0:1, 40:41], arow[0:1, 41:42]),
                wait=[(ctx.se, km)],
                inc=ctx.ve,
            )
            vv = p.step(
                "vector",
                lambda v: v.tensor_scalar(b11[0:1, 0:1], arow[0:1, 42:43], -1.0, None, ALU.mult),
                wait=[(ctx.ve, vv)],
                inc=ctx.ve,
            )
            bv = bcast_scalar(ctx, b11[0:1, 0:1], negc[:, hh : hh + 1],
                              wait=[(ctx.ve, vv), (ctx.se, ctx.se.n)])
            if sc.get("attn_sub") == "norms":
                continue
            habs = pair * 2 + hh
            sc_flash = sc
            vcol = lambda c, hbase=hbase: vt[:, c * 128 + hbase : c * 128 + hbase + 64]
            odst = lambda tq, habs=habs: attn_tm[:, tq * 1024 + habs * 64 : tq * 1024 + habs * 64 + 64]
            causal_flash(
                ctx, sc_flash, qf, kf, vcol, odst, hbase, 64,
                negc[:, hh : hh + 1], g_bc,
                qk_ready=[(ctx.se, q_done), (ctx.se, k_done), (ctx.ve, v_done), (ctx.ve, bv)],
            )
        sc["sq_guard"] = [(ctx.ve, ctx.ve.n)]
    return [(ctx.ve, ctx.ve.n)]


def fhn_g(ctx, frow, fhn_wait, g_bc):
    p = ctx.p
    I0 = float(0.1 + 0.9 / (1.0 + np.exp(-5.0)))
    TAU = 12.5
    c = lambda i: frow[0:1, i : i + 1]
    a_, b_, dt_ = c(0), c(1), c(2)
    v1, r1, n1, n2, d1, rd, w1 = c(3), c(4), c(5), c(6), c(7), c(8), c(9)
    t1, t2, s2, v2, g0, fac, g = c(10), c(11), c(12), c(13), c(14), c(15), c(5)
    vv = p.step("vector", lambda v: v.tensor_scalar(v1, dt_, I0, None, ALU.mult),
                wait=list(fhn_wait), inc=ctx.ve)
    seq = [
        lambda v: v.tensor_scalar(r1, dt_, 1.0 / TAU, None, ALU.mult),
        lambda v: v.tensor_add(n1, v1, a_),
        lambda v: v.tensor_mul(n2, r1, n1),
        lambda v: v.tensor_mul(d1, r1, b_),
        lambda v: v.tensor_scalar(d1, d1, 1.0, None, ALU.add),
        lambda v: v.reciprocal(rd, d1),
        lambda v: v.tensor_mul(w1, n2, rd),
        lambda v: v.tensor_mul(t1, v1, v1),
        lambda v: v.tensor_mul(t2, t1, v1),
        lambda v: v.scalar_tensor_tensor(t2, t2, -1.0 / 3.0, v1, ALU.mult, ALU.add),
        lambda v: v.tensor_sub(s2, t2, w1),
        lambda v: v.tensor_scalar(s2, s2, I0, None, ALU.add),
        lambda v: v.tensor_mul(s2, dt_, s2),
        lambda v: v.tensor_add(v2, v1, s2),
    ]
    for fn in seq:
        vv = p.step("vector", fn, wait=[(ctx.ve, vv)], inc=ctx.ve)
    sv = p.step("scalar", lambda e: e.activation(g0, v2, AF.Sigmoid),
                wait=[(ctx.ve, vv)], inc=ctx.se)
    vv = p.step("vector", lambda v: v.tensor_scalar(fac, g0, 0.5, 0.5, ALU.mult, ALU.add),
                wait=[(ctx.se, sv)], inc=ctx.ve)
    vv = p.step("vector", lambda v: v.tensor_scalar(d1, fac, 1e-8, None, ALU.add),
                wait=[(ctx.ve, vv)], inc=ctx.ve)
    vv = p.step("vector", lambda v: v.reciprocal(rd, d1), wait=[(ctx.ve, vv)], inc=ctx.ve)
    vv = p.step("vector", lambda v: v.tensor_mul(g, fac, rd), wait=[(ctx.ve, vv)], inc=ctx.ve)
    gb = ctx._gb11
    vv = p.step("vector", lambda v: v.tensor_copy(gb[0:1, 0:1], g), wait=[(ctx.ve, vv)], inc=ctx.ve)
    return bcast_scalar(ctx, gb[0:1, 0:1], g_bc, wait=[(ctx.ve, vv)])


def l0_ffn_gate_prep(ctx, ws, sc):
    p = ctx.p
    simt = sc["simt"]
    slot_t, w_t = ctx.rx.load(lambda e: e.dma_start(out=simt[:], in_=ws["SIMT"]),
                              wait=sc.get("simt_guard", []))
    sq = ctx.sq_scr[0]
    sv = p.step("scalar", lambda e: e.activation(sq[0:16, :], simt[:], AF.Square,
                                                 accum_out=sc["r16"][0:16, 0:1]),
                wait=[w_t, (ctx.se, sc["scr_last"][0])], inc=ctx.se)
    sc["scr_last"][0] = sv
    ctx.rx.ack(slot_t, (ctx.se, sv))
    sv2 = p.step("scalar", lambda e: e.activation(sc["r16"][0:16, 1:2], sc["r16"][0:16, 0:1], AF.Sqrt),
                 wait=[(ctx.se, sv)], inc=ctx.se)
    vv = p.step("vector", lambda v: v.tensor_scalar(sc["r16"][0:16, 1:2], sc["r16"][0:16, 1:2],
                                                    1e-12, None, ALU.max),
                wait=[(ctx.se, sv2)], inc=ctx.ve)
    vv = p.step("vector", lambda v: v.reciprocal(sc["r16"][0:16, 2:3], sc["r16"][0:16, 1:2]),
                wait=[(ctx.ve, vv)], inc=ctx.ve)
    vv = p.step("vector", lambda v: v.tensor_copy(sc["r16b"][0:16, 0:1], sc["r16"][0:16, 2:3]),
                wait=[(ctx.ve, vv)], inc=ctx.ve)
    bank, bw = ctx.poolT.acquire()
    tv = p.step("tensor", lambda t: t.transpose(ctx.psT[bank][0:1, 0:16], sc["r16b"][0:16, 0:1],
                                                ctx.cba("ident")[0:16, 0:16]),
                wait=bw + [(ctx.ve, vv)], inc=ctx.te)
    vv = p.step("vector", lambda v: v.tensor_copy(sc["row16b"][0:1, 0:16], ctx.psT[bank][0:1, 0:16]),
                wait=[(ctx.te, tv)], inc=ctx.ve)
    ctx.poolT.release(bank, (ctx.ve, vv))
    rcb = sc["rcb"]
    bv = bcast_row(ctx, sc["row16b"][0:1, 0:16], 16, rcb[:], wait=[(ctx.ve, vv)])
    simb = sc["simb"]
    slot_s, w_s = load_w_cols(ctx, ws["SIM"], 0, 16, simb, wait=sc.get("simt_guard", []))
    nv = None
    for cch in range(8):
        nv = p.step(
            "vector",
            lambda v, cch=cch: v.tensor_mul(
                sc["nsim"][:, cch * 16 : (cch + 1) * 16], simb[:, cch * 16 : (cch + 1) * 16], rcb[:]
            ),
            wait=[w_s, (ctx.ve, bv)],
            inc=ctx.ve,
        )
    ctx.rw.ack(slot_s, (ctx.ve, nv))
    slot_g, w_g = ctx.rx.load(lambda e: e.dma_start(out=sc["row16"][0:1, 0:16], in_=ws["GATES"]))
    sv = p.step("scalar", lambda e: e.activation(sc["row16"][0:1, 16:32], sc["row16"][0:1, 0:16], AF.Sigmoid),
                wait=[w_g], inc=ctx.se)
    ctx.rx.ack(slot_g, (ctx.se, sv))
    vv = p.step("vector", lambda v: v.tensor_copy(sc["row16b"][0:1, 16:32], sc["row16"][0:1, 16:32]),
                wait=[(ctx.se, sv)], inc=ctx.ve)
    bv = bcast_row(ctx, sc["row16b"][0:1, 16:32], 16, sc["sg_bc"][:], wait=[(ctx.ve, vv)])
    return [(ctx.ve, max(nv, bv))]


def l0_ffn_batch(ctx, ws, sc, gate_prep_ready):
    p = ctx.p
    T, NT = ctx.T, ctx.NT
    xn_fm = ctx.xn_fm
    xn_ready = sc["xn_ready"]
    rw_buf = sc["rw_buf"]
    gscr = sc["gscr"]
    grow = sc["grow"]
    gu8 = sc["gu8"]
    gate_guard = [[], []]
    for ti in range(NT):
        st0 = (ti % 2) * 80
        gr0 = (ti % 2) * 16
        lg = gscr[:, st0 : st0 + 16]
        m8 = gscr[:, st0 + 16 : st0 + 24]
        maskf = gscr[:, st0 + 24 : st0 + 40]
        e16 = gscr[:, st0 + 40 : st0 + 56]
        gated = gscr[:, st0 + 56 : st0 + 72]
        comb = gscr[:, st0 + 72 : st0 + 80]
        crn = grow[:, gr0 + 1 : gr0 + 2]
        ccnt = grow[:, gr0 + 2 : gr0 + 3]
        cia = grow[:, gr0 + 3 : gr0 + 4]
        cnegm = grow[:, gr0 + 4 : gr0 + 5]
        cssum = grow[:, gr0 + 5 : gr0 + 6]
        crws = grow[:, gr0 + 6 : gr0 + 7]
        combu = gu8[:, gr0 : gr0 + 16]
        col0 = (ti % 16) * 8
        c_var = ctx.row_scr[:, col0 + 4 : col0 + 5]
        c_rstd = ctx.row_scr[:, col0 + 6 : col0 + 7]
        vv = p.step("vector", lambda v, cr=crn, c_var=c_var, c_rstd=c_rstd: v.tensor_mul(cr, c_var, c_rstd),
                    wait=xn_ready + gate_guard[ti % 2], inc=ctx.ve)
        vv = p.step("vector", lambda v, cr=crn, c_rstd=c_rstd: v.tensor_mul(cr, cr, c_rstd),
                    wait=[(ctx.ve, vv)], inc=ctx.ve)
        vv = p.step("vector", lambda v, cr=crn: v.tensor_scalar(cr, cr, 1024.0, None, ALU.mult),
                    wait=[(ctx.ve, vv)], inc=ctx.ve)
        sv2 = p.step("scalar", lambda e, cr=crn: e.activation(cr, cr, AF.Sqrt),
                     wait=[(ctx.ve, vv)], inc=ctx.se)
        vv = p.step("vector", lambda v, cr=crn: v.tensor_scalar(cr, cr, 1e-12, None, ALU.max),
                    wait=[(ctx.se, sv2)], inc=ctx.ve)
        vv = p.step("vector", lambda v, cr=crn: v.reciprocal(cr, cr), wait=[(ctx.ve, vv)], inc=ctx.ve)
        pi, pw = ctx.pool.acquire()
        ps = ctx.psF[pi]
        tv = None
        for cch in range(8):
            tv = p.step(
                "tensor",
                lambda t, ps=ps, cch=cch, ti=ti: t.matmul(
                    ps[:, 0:16],
                    xn_fm[:, cch * T + ti * 128 : cch * T + ti * 128 + 128],
                    sc["nsim"][:, cch * 16 : (cch + 1) * 16],
                    start=(cch == 0),
                    stop=(cch == 7),
                ),
                wait=(pw + xn_ready + gate_prep_ready) if cch == 0 else (),
                inc=ctx.te if cch == 7 else None,
            )
        vv = p.step("vector",
                    lambda v, ps=ps, lg=lg, cr=crn: v.scalar_tensor_tensor(lg, ps[:, 0:16], cr, sc["sg_bc"][:], ALU.mult, ALU.subtract),
                    wait=[(ctx.te, tv), (ctx.ve, vv)], inc=ctx.ve)
        ctx.pool.release(pi, (ctx.ve, vv))
        vv = p.step("vector", lambda v, m8=m8, lg=lg: v.max(out=m8, in_=lg), wait=[(ctx.ve, vv)], inc=ctx.ve)
        vv = p.step("vector", lambda v, maskf=maskf, lg=lg: v.tensor_scalar(maskf, lg, 0.0, None, ALU.is_gt),
                    wait=[(ctx.ve, vv)], inc=ctx.ve)
        vv = p.step("vector", lambda v, ccnt=ccnt, maskf=maskf: v.reduce_sum(ccnt, maskf, axis=AX.X),
                    wait=[(ctx.ve, vv)], inc=ctx.ve)
        vv = p.step("vector", lambda v, cia=cia, ccnt=ccnt: v.tensor_scalar(cia, ccnt, 0.5, None, ALU.is_lt),
                    wait=[(ctx.ve, vv)], inc=ctx.ve)
        vv = p.step("vector", lambda v, e16=e16, lg=lg, m8=m8: v.tensor_scalar(e16, lg, m8[:, 1:2], None, ALU.is_ge),
                    wait=[(ctx.ve, vv)], inc=ctx.ve)
        vv = p.step("vector", lambda v, e16=e16, cia=cia, maskf=maskf: v.scalar_tensor_tensor(maskf, e16, cia, maskf, ALU.mult, ALU.add),
                    wait=[(ctx.ve, vv)], inc=ctx.ve)
        vv = p.step("vector", lambda v, combu=combu, maskf=maskf: v.tensor_copy(combu, maskf),
                    wait=[(ctx.ve, vv)], inc=ctx.ve)
        vv = p.step("vector", lambda v, gated=gated, lg=lg: v.tensor_scalar(gated, lg, 0.0, None, ALU.max),
                    wait=[(ctx.ve, vv)], inc=ctx.ve)
        nb = ctx.cfa("s_neg30")[:, 0:1].broadcast_to((128, 16))
        vv = p.step("vector", lambda v, lg=lg, combu=combu, gated=gated, nb=nb: v.select(lg, combu, gated, nb, add_drain=True),
                    wait=[(ctx.ve, vv)], inc=ctx.ve)
        vv = p.step("vector", lambda v, cnegm=cnegm, lg=lg: v.reduce_max(cnegm, lg, axis=AX.X, negate=True),
                    wait=[(ctx.ve, vv)], inc=ctx.ve)
        sv3 = p.step("scalar", lambda e, e16=e16, lg=lg, cnegm=cnegm, cssum=cssum: e.activation(e16, lg, AF.Exp, bias=cnegm, accum_out=cssum),
                     wait=[(ctx.ve, vv)], inc=ctx.se)
        vv = p.step("vector", lambda v, crws=crws, cssum=cssum: v.reciprocal(crws, cssum),
                    wait=[(ctx.se, sv3)], inc=ctx.ve)
        vv = p.step("vector", lambda v, ti=ti, e16=e16, crws=crws: v.tensor_scalar(rw_buf[:, ti * 16 : (ti + 1) * 16], e16, crws, None, ALU.mult),
                    wait=[(ctx.ve, vv)], inc=ctx.ve)
        gate_guard[ti % 2] = [(ctx.ve, vv), (ctx.se, sv3)]
    rw_ready = [(ctx.ve, ctx.ve.n)]
    q1, k1, v1 = sc["q1"], sc["k1"], sc["v1"]
    wp = sc["wp"]
    mixtmp = sc["mixtmp"]
    mt_guard = []
    wp_last = [None, None, None]
    wp_guard = sc.get("wp_guard", [])
    for wi in range(3):
        wname = ("QP", "KP", "VP")[wi]
        wslot, wsp = load_w_cols(ctx, ws[wname], 0, 1024, wp, wait=wp_guard)
        for ti in range(NT):
            pis = []
            for half in range(2):
                pi, pw = ctx.pool.acquire()
                ps = ctx.psF[pi]
                tv = None
                for cch in range(8):
                    tv = p.step(
                        "tensor",
                        lambda t, ps=ps, cch=cch, ti=ti, half=half: t.matmul(
                            ps[:],
                            xn_fm[:, cch * T + ti * 128 : cch * T + ti * 128 + 128],
                            wp[:, cch * 1024 + half * 512 : cch * 1024 + half * 512 + 512],
                            start=(cch == 0),
                            stop=(cch == 7),
                        ),
                        wait=(pw + [wsp] + xn_ready) if cch == 0 else (),
                        inc=ctx.te if cch == 7 else None,
                    )
                pis.append((pi, tv))
            vv = None
            for half in range(2):
                pi, tv = pis[half]
                ps = ctx.psF[pi]
                rw3 = rw_buf[:, ti * 16 : (ti + 1) * 16].rearrange("p (one e) -> p one e", one=1, e=16).broadcast_to((128, 32, 16))
                t3 = mixtmp[:, half * 512 : (half + 1) * 512].rearrange("p (d e) -> p d e", d=32, e=16)
                p3 = ps.rearrange("p (d e) -> p d e", d=32, e=16)
                vv = p.step("vector",
                            lambda v, t3=t3, p3=p3, rw3=rw3: v.tensor_tensor(t3, p3, rw3, ALU.mult),
                            wait=[(ctx.te, tv)] + mt_guard + rw_ready, inc=ctx.ve)
                ctx.pool.release(pi, (ctx.ve, vv))
            dst = (q1, k1, v1)[wi]
            scale = 0.125 if wi == 0 else 1.0
            m3 = mixtmp.rearrange("p (d e) -> p d e", d=64, e=16)
            mr = sc["mixred"]
            o3 = mr.rearrange("p (d one) -> p d one", d=64, one=1)
            vv = p.step("vector",
                        lambda v, o3=o3, m3=m3: v.tensor_reduce(o3, m3, AX.X, ALU.add),
                        wait=[(ctx.ve, vv)], inc=ctx.ve)
            vv = p.step("vector",
                        lambda v, dst=dst, ti=ti, mr=mr, scale=scale: v.tensor_scalar(
                            dst[:, ti * 64 : (ti + 1) * 64], mr[:], scale, None, ALU.mult),
                        wait=[(ctx.ve, vv)], inc=ctx.ve)
            mt_guard = [(ctx.ve, vv)]
            wp_last[wi] = vv
        ctx.rw.ack(wslot, (ctx.te, ctx.te.n))
        wp_guard = [(ctx.te, ctx.te.n)]
    sc["wp_guard"] = [(ctx.te, ctx.te.n)]
    q1f, k1f = sc["q1f"], sc["k1f"]
    prs = []
    for ti in range(NT):
        prs.append((q1[:, ti * 64 : (ti + 1) * 64], q1f[0:64, ti * 128 : (ti + 1) * 128]))
        prs.append((k1[:, ti * 64 : (ti + 1) * 64], k1f[0:64, ti * 128 : (ti + 1) * 128]))
    tr_done = pe_transpose_pairs(ctx, prs, in_w=64, wait=[(ctx.ve, ctx.ve.n)])
    arow = sc["arow"]
    sqv = p.step("scalar", lambda e: e.activation(sc["sq_pair"][0:64, :], q1f[0:64, :], AF.Square),
                 wait=[(ctx.ve, tr_done)] + sc["sq_guard"], inc=ctx.se)
    sqk = p.step("scalar", lambda e: e.activation(sc["sq_pair2"][0:64, :], k1f[0:64, :], AF.Square),
                 wait=[(ctx.ve, tr_done)], inc=ctx.se)
    qm = head_norm_sqrtmax(ctx, sc["sq_pair"], 0, 64, arow, 34, 40, [(ctx.se, sqv)])
    km = head_norm_sqrtmax(ctx, sc["sq_pair2"], 0, 64, arow, 34, 41, [(ctx.se, qm), (ctx.se, sqk)])
    vv = p.step("vector", lambda v: v.tensor_mul(arow[0:1, 42:43], arow[0:1, 40:41], arow[0:1, 41:42]),
                wait=[(ctx.se, km)], inc=ctx.ve)
    vv = p.step("vector", lambda v: v.tensor_scalar(sc["b11"][0:1, 0:1], arow[0:1, 42:43], -1.0, None, ALU.mult),
                wait=[(ctx.ve, vv)], inc=ctx.ve)
    bv = bcast_scalar(ctx, sc["b11"][0:1, 0:1], sc["negc"][:, 0:1], wait=[(ctx.ve, vv), (ctx.se, ctx.se.n)])
    sc["sq_guard"] = [(ctx.ve, ctx.ve.n)]
    attn1 = sc["attn1"]
    vcol = lambda cblk: v1[:, cblk * 64 : (cblk + 1) * 64]
    odst = lambda tq: attn1[:, tq * 64 : (tq + 1) * 64]
    causal_flash(ctx, sc, q1f, k1f, vcol, odst, 0, 64, sc["negc"][:, 0:1], None,
                 qk_ready=[(ctx.ve, tr_done), (ctx.ve, bv)])
    attn1_ready = [(ctx.ve, ctx.ve.n)]
    wo = sc["wo_ffn"]
    slot_o, w_o = load_w_cols(ctx, ws["OP"], 0, 1024, wo, wait=sc.get("woffn_guard", []))
    mixb = sc["mixb"]
    mixf = sc["mixf"]
    wo_last = None
    mb_guard = []
    for ti in range(NT):
        a3 = attn1[:, ti * 64 : (ti + 1) * 64].rearrange("p (d one) -> p d one", d=64, one=1).broadcast_to((128, 64, 16))
        rw3 = rw_buf[:, ti * 16 : (ti + 1) * 16].rearrange("p (one e) -> p one e", one=1, e=16).broadcast_to((128, 64, 16))
        o3 = mixb.rearrange("p (d e) -> p d e", d=64, e=16)
        vv = p.step("vector", lambda v, o3=o3, a3=a3, rw3=rw3: v.tensor_tensor(o3, a3, rw3, ALU.mult),
                    wait=attn1_ready + mb_guard, inc=ctx.ve)
        tr = pe_transpose_pairs(
            ctx,
            [(mixb[:, cc * 128 : (cc + 1) * 128], mixf[:, cc * 128 : (cc + 1) * 128]) for cc in range(8)],
            wait=[(ctx.ve, vv)],
        )
        adds = []
        for half in range(2):
            pi, pw = ctx.pool.acquire()
            ps = ctx.psF[pi]
            tv = None
            for cch in range(8):
                tv = p.step(
                    "tensor",
                    lambda t, ps=ps, cch=cch, half=half: t.matmul(
                        ps[:],
                        mixf[:, cch * 128 : (cch + 1) * 128],
                        wo[:, cch * 1024 + half * 512 : cch * 1024 + half * 512 + 512],
                        start=(cch == 0),
                        stop=(cch == 7),
                    ),
                    wait=(pw + [w_o, (ctx.ve, tr)]) if cch == 0 else (),
                    inc=ctx.te if cch == 7 else None,
                )
            xr = ctx.x_res[:, ti * 1024 + half * 512 : ti * 1024 + half * 512 + 512]
            vv2 = p.step("vector", lambda v, xr=xr, ps=ps: v.tensor_add(xr, ps[:], xr),
                         wait=[(ctx.te, tv), (ctx.ve, ctx.ve.n)], inc=ctx.ve)
            ctx.pool.release(pi, (ctx.ve, vv2))
            adds.append(vv2)
            wo_last = vv2
        mb_guard = [(ctx.ve, adds[-1])]
    ctx.rw.ack(slot_o, (ctx.ve, wo_last))
    sc["woffn_guard"] = [(ctx.te, ctx.te.n)]
    return [(ctx.ve, ctx.ve.n)]


def pe_transpose_pairs(ctx, pairs, in_w=128, wait=()):
    p = ctx.p
    ident = ctx.cba("ident")
    vv = None
    first = True
    for src, dst in pairs:
        bank, bw = ctx.poolT.acquire()
        tw = bw + (list(wait) if first else [])
        first = False
        tv = p.step("tensor",
                    lambda t, bank=bank, src=src: t.transpose(ctx.psT[bank][0:in_w, 0:128], src, ident),
                    wait=tw, inc=ctx.te)
        vv = p.step("vector",
                    lambda v, bank=bank, dst=dst: v.tensor_copy(dst, ctx.psT[bank][0:in_w, 0:128]),
                    wait=[(ctx.te, tv)], inc=ctx.ve)
        ctx.poolT.release(bank, (ctx.ve, vv))
    return vv


def l0_wout(ctx, ws, sc, attn_ready):
    p = ctx.p
    T, NT = ctx.T, ctx.NT
    wo = sc["wout"]
    slot, w_o = load_w_cols(ctx, ws["WOUT"], 0, 1024, wo, wait=sc.get("wout_guard", []))
    afm = sc["mixf"]
    last = None
    guard = []
    for ti in range(NT):
        tr = pe_transpose_pairs(
            ctx,
            [(sc["attn_tm"][:, ti * 1024 + cc * 128 : ti * 1024 + (cc + 1) * 128],
              afm[:, cc * 128 : (cc + 1) * 128]) for cc in range(8)],
            wait=list(attn_ready) + guard,
        )
        adds = []
        for half in range(2):
            pi, pw = ctx.pool.acquire()
            ps = ctx.psF[pi]
            tv = None
            for cch in range(8):
                tv = p.step(
                    "tensor",
                    lambda t, ps=ps, cch=cch, half=half: t.matmul(
                        ps[:],
                        afm[:, cch * 128 : (cch + 1) * 128],
                        wo[:, cch * 1024 + half * 512 : cch * 1024 + half * 512 + 512],
                        start=(cch == 0),
                        stop=(cch == 7),
                    ),
                    wait=(pw + [w_o, (ctx.ve, tr)]) if cch == 0 else (),
                    inc=ctx.te if cch == 7 else None,
                )
            xr = ctx.x_res[:, ti * 1024 + half * 512 : ti * 1024 + half * 512 + 512]
            vv2 = p.step("vector", lambda v, xr=xr, ps=ps: v.tensor_add(xr, ps[:], xr),
                         wait=[(ctx.te, tv)], inc=ctx.ve)
            ctx.pool.release(pi, (ctx.ve, vv2))
            adds.append(vv2)
            last = vv2
        guard = [(ctx.te, ctx.te.n)]
    ctx.rw.ack(slot, (ctx.ve, last))
    sc["wout_guard"] = [(ctx.te, ctx.te.n)]
    return [(ctx.ve, ctx.ve.n)]


def layer0(ctx, ws, x_src, x_dst, prev_store=None, parts="all"):
    p = ctx.p
    NT = ctx.NT
    les = ExitStack()
    sc = {}
    ctx.xn_fm = p.sbuf("l0_xnfm", [128, 8 * ctx.T], BF16, es=les)
    sc["arow"] = p.sbuf("l0_arow", [128, 48], F32, es=les)
    sc["negc"] = p.sbuf("l0_negc", [128, 2], F32, es=les)
    sc["b11"] = p.sbuf("l0_b11", [1, 4], BF16, es=les)
    sc["frow"] = p.sbuf("l0_frow", [1, 16], F32, es=les)
    sc["g_bc"] = p.sbuf("l0_gbc", [128, 1], F32, es=les)
    sc["sq_guard"] = []
    sc["wq3_guard"] = [[], [], []]
    sc["scr_last"] = ctx.scr_last
    slot_f, w_f = ctx.rx.load(lambda e: e.dma_start(out=sc["frow"][0:1, 0:3], in_=ws["FHN"]))
    gv = fhn_g(ctx, sc["frow"], [w_f], sc["g_bc"][:, 0:1])
    ctx.rx.ack(slot_f, (ctx.ve, ctx.ve.n))
    ln_phase(ctx, x_src, prev_store)
    sc["xn_ready"] = [(ctx.ve, ctx.ve.n), (ctx.se, ctx.se.n)]
    fence0 = ctx.fence()
    pes = ExitStack()
    sc["attn_tm"] = p.sbuf("l0_attn_tm", [128, ctx.NT * 1024], BF16, es=pes)
    sc["q_pair"] = p.sbuf("l0_qp", [128, ctx.T], BF16, es=pes)
    sc["k_pair"] = p.sbuf("l0_kp", [128, ctx.T], BF16, es=pes)
    sc["v_pair"] = p.sbuf("l0_vp", [128, ctx.NT * 128], BF16, es=pes)
    sc["sq_pair"] = p.sbuf("l0_sq", [128, ctx.T], BF16, es=pes)
    sc["sq_pair2"] = p.sbuf("l0_sq2", [128, ctx.T], BF16, es=pes)
    sc["probs"] = [p.sbuf(f"l0_probs{i}", [128, 512], BF16, es=pes) for i in range(8)]
    sc["probsT"] = p.sbuf("l0_probsT", [128, ctx.T], BF16, es=pes)
    sc["wq3"] = p.sbuf("l0_wq3", [128, 3 * 1024], BF16, es=pes)
    sc["wout"] = p.sbuf("l0_wout", [128, 8192], BF16, es=pes)
    sc["mixf"] = p.sbuf("l0_afm", [128, 1024], BF16, es=pes)
    sc["probs_rel"] = [list(fence0) for _ in range(8)]
    sc["pvT_prev"] = list(fence0)
    sc["arow_guard"] = [list(fence0), list(fence0)]
    sc["wq3_guard"] = [list(fence0), list(fence0), list(fence0)]
    sc["wout_guard"] = list(fence0)
    sc["fence0"] = fence0
    if parts in ("all", "attn", "attn1p"):
        if "1p" in parts:
            sc["npairs"] = 1
        if "qkv" in parts:
            sc["attn_sub"] = "qkv"
        if "norms" in parts:
            sc["attn_sub"] = parts.split("_")[-1]
        attn_ready = l0_attn_batch(ctx, ws, sc, sc["g_bc"][:, 0:1], gv)
        if sc.get("attn_sub") is None:
            l0_wout(ctx, ws, sc, attn_ready)
    pes.close()
    fence1 = ctx.fence()
    pes2 = ExitStack()
    sc["simt"] = p.sbuf("l0_simt", [16, 1024], F32, es=pes2)
    sc["r16"] = p.sbuf("l0_r16", [16, 4], F32, es=pes2)
    sc["r16b"] = p.sbuf("l0_r16b", [16, 2], BF16, es=pes2)
    sc["row16"] = p.sbuf("l0_row16", [1, 32], F32, es=pes2)
    sc["row16b"] = p.sbuf("l0_row16b", [1, 32], BF16, es=pes2)
    sc["rcb"] = p.sbuf("l0_rcb", [128, 16], BF16, es=pes2)
    sc["simb"] = p.sbuf("l0_simb", [128, 128], BF16, es=pes2)
    sc["nsim"] = p.sbuf("l0_nsim", [128, 128], BF16, es=pes2)
    sc["sg_bc"] = p.sbuf("l0_sgbc", [128, 16], F32, es=pes2)
    sc["rw_buf"] = p.sbuf("l0_rw", [128, ctx.NT * 16], F32, es=pes2)
    sc["gscr"] = p.sbuf("l0_gscr", [128, 160], F32, es=pes2)
    sc["grow"] = p.sbuf("l0_grow", [128, 32], F32, es=pes2)
    sc["gu8"] = p.sbuf("l0_gu8", [128, 32], U8, es=pes2)
    sc["q1"] = p.sbuf("l0_q1", [128, ctx.NT * 64], BF16, es=pes2)
    sc["k1"] = p.sbuf("l0_k1", [128, ctx.NT * 64], BF16, es=pes2)
    sc["v1"] = p.sbuf("l0_v1", [128, ctx.NT * 64], BF16, es=pes2)
    sc["q1f"] = p.sbuf("l0_q1f", [128, ctx.T], BF16, es=pes2)
    sc["k1f"] = p.sbuf("l0_k1f", [128, ctx.T], BF16, es=pes2)
    sc["sq_pair"] = p.sbuf("l0_fsq", [128, ctx.T], BF16, es=pes2)
    sc["sq_pair2"] = p.sbuf("l0_fsq2", [128, ctx.T], BF16, es=pes2)
    sc["attn1"] = p.sbuf("l0_attn1", [128, ctx.NT * 64], BF16, es=pes2)
    sc["wp"] = p.sbuf("l0_wp", [128, 8192], BF16, es=pes2)
    sc["wo_ffn"] = p.sbuf("l0_woffn", [128, 8192], BF16, es=pes2)
    sc["mixtmp"] = p.sbuf("l0_mixtmp", [128, 1024], F32, es=pes2)
    sc["mixred"] = p.sbuf("l0_mixred", [128, 64], F32, es=pes2)
    sc["mixb"] = p.sbuf("l0_mixb", [128, 1024], BF16, es=pes2)
    sc["mixf"] = p.sbuf("l0_mixf", [128, 1024], BF16, es=pes2)
    sc["probs"] = [p.sbuf(f"l0_fprobs{i}", [128, 512], BF16, es=pes2) for i in range(8)]
    sc["probsT"] = p.sbuf("l0_fprobsT", [128, ctx.T], BF16, es=pes2)
    sc["probs_rel"] = [list(fence1) for _ in range(8)]
    sc["pvT_prev"] = list(fence1)
    sc["arow_guard"] = [list(fence1), list(fence1)]
    sc["sq_guard"] = list(fence1)
    sc["wp_guard"] = list(fence1)
    sc["woffn_guard"] = list(fence1)
    sc["simt_guard"] = list(fence1)
    if parts in ("all", "ffn"):
        gate_prep_ready = l0_ffn_gate_prep(ctx, ws, sc)
        l0_ffn_batch(ctx, ws, sc, gate_prep_ready)
    pes2.close()
    les.close()
    ctx.xn_fm = None
    if x_dst is None:
        return None
    return store_xres(ctx, x_dst)




def lse_logp_tiles(ctx, sc, xn_tm, logp_fm):
    p = ctx.p
    NT, T = ctx.NT, ctx.T
    lrow = sc["lrow"]
    escr = sc["escr"]
    eguard = []
    last = None
    for ti in range(NT):
        c0 = (ti % 8) * 4
        negm = lrow[:, c0 : c0 + 1]
        ssum = lrow[:, c0 + 1 : c0 + 2]
        lse = lrow[:, c0 + 2 : c0 + 3]
        xt = xn_tm[:, ti * 1024 : (ti + 1) * 1024]
        g = [(ctx.ve, sc["lrow_g"][ti % 8])] if sc["lrow_g"][ti % 8] else []
        vv = p.step("vector", lambda v, xt=xt, negm=negm: v.reduce_max(negm, xt, axis=AX.X, negate=True),
                    wait=sc["xn_ready"] + g, inc=ctx.ve)
        sv = p.step("scalar", lambda e, xt=xt, negm=negm, ssum=ssum: e.activation(
            escr[:], xt, AF.Exp, bias=negm, accum_out=ssum), wait=[(ctx.ve, vv)] + eguard, inc=ctx.se)
        eguard = [(ctx.se, sv)]
        sv2 = p.step("scalar", lambda e, ssum=ssum, lse=lse: e.activation(lse, ssum, AF.Ln),
                     wait=[(ctx.se, sv)], inc=ctx.se)
        vv = p.step("vector", lambda v, lse=lse, negm=negm: v.tensor_sub(lse, lse, negm),
                    wait=[(ctx.se, sv2)], inc=ctx.ve)
        lt = ctx.xn_stage[ti % 2][:, 0:1024]
        vv = p.step("vector", lambda v, lt=lt, xt=xt, lse=lse: v.tensor_scalar(lt, xt, lse, None, ALU.subtract),
                    wait=[(ctx.ve, vv)] + ctx.stage_guard[ti % 2], inc=ctx.ve)
        last = tm_to_fm_src(ctx, lt, ti, logp_fm, (ctx.ve, vv))
        ctx.stage_guard[ti % 2] = [(ctx.ve, last)]
        sc["lrow_g"][ti % 8] = last
    return [(ctx.ve, ctx.ve.n), (ctx.se, ctx.se.n)]


def l1_attn_batch(ctx, sc, logp_ready):
    p = ctx.p
    NT, T = ctx.NT, ctx.T
    logp_fm = sc["logp_fm"]
    xn_tm = sc["xn_tm"]
    probs = sc["probs"]
    probsT = sc["probsT"]
    arow = sc["arow"]
    probs_rel = sc["probs_rel"]
    pvT_prev = sc["pvT_prev"]
    arow_guard = sc["arow_guard"]
    ident = ctx.cba("ident")
    ptile = sc["ptile"]
    pt_guard = [list(sc["fence0"]), list(sc["fence0"])]
    for tq in range(NT):
        aset = tq % 2
        ac0 = aset * 16
        nsl = cdiv(NT * 128, 512)
        dc = arow[:, ac0 : ac0 + nsl]
        pt = ptile[aset]
        pev = None
        for c in range(8):
            pev = p.step("scalar",
                         lambda e, c=c, tq=tq, pt=pt: e.activation(
                             pt[:, c * 128 : (c + 1) * 128],
                             logp_fm[:, c * T + tq * 128 : c * T + tq * 128 + 128], AF.Exp),
                         wait=(logp_ready + pt_guard[aset]) if c == 0 else (), inc=ctx.se)
        sv_last = None
        tr_last = None
        for s in range(nsl):
            w = min(512, NT * 128 - s * 512)
            pi, pw = ctx.pool.acquire()
            ps = ctx.psF[pi]
            tv = None
            for c in range(8):
                tv = p.step(
                    "tensor",
                    lambda t, ps=ps, c=c, s=s, w=w, pt=pt, tq=tq: t.matmul(
                        ps[:, 0:w],
                        pt[:, c * 128 : (c + 1) * 128],
                        logp_fm[:, c * T + s * 512 : c * T + s * 512 + w],
                        start=(c == 0),
                        stop=(c == 7),
                    ),
                    wait=(pw + [(ctx.se, pev)]) if c == 0 else (),
                    inc=ctx.te if c == 7 else None,
                )
            ew = [(ctx.te, tv)] + probs_rel[s % 2]
            if s == 0:
                ew += arow_guard[aset]
            sv_last = p.step(
                "scalar",
                lambda e, ps=ps, w=w, slot=s % 2, dcs=dc[:, s : s + 1]: e.activation(
                    probs[slot][:, 0:w], ps[:, 0:w], AF.Exp, bias=0.0, accum_out=dcs
                ),
                wait=ew,
                inc=ctx.se,
            )
            ctx.pool.release(pi, (ctx.se, sv_last))
            for j in range(w // 128):
                bank, bw = ctx.poolT.acquire()
                tw = bw + [(ctx.se, sv_last)]
                if s == 0 and j == 0:
                    tw += pvT_prev
                tr_last = p.step(
                    "tensor",
                    lambda t, bank=bank, slot=s % 2, j=j: t.transpose(
                        ctx.psT[bank][:, 0:128], probs[slot][:, j * 128 : (j + 1) * 128], ident
                    ),
                    wait=tw,
                    inc=ctx.te,
                )
                vv = p.step(
                    "vector",
                    lambda v, bank=bank, c=s * 4 + j: v.tensor_copy(
                        probsT[:, c * 128 : (c + 1) * 128], ctx.psT[bank][:, 0:128]
                    ),
                    wait=[(ctx.te, tr_last)],
                    inc=ctx.ve,
                )
                ctx.poolT.release(bank, (ctx.ve, vv))
            probs_rel[s % 2] = [(ctx.te, tr_last)]
        c_d = arow[:, ac0 + 8 : ac0 + 9]
        c_rd = arow[:, ac0 + 9 : ac0 + 10]
        vv = p.step("vector", lambda v, dc=dc, c_d=c_d: v.reduce_sum(c_d, dc, axis=AX.X),
                    wait=[(ctx.se, sv_last)], inc=ctx.ve)
        vv = p.step("vector", lambda v, c_d=c_d, c_rd=c_rd: v.reciprocal(c_rd, c_d),
                    wait=[(ctx.ve, vv)], inc=ctx.ve)
        evs = []
        for sl in range(2):
            pi, pw = ctx.pool.acquire()
            ps = ctx.psF[pi]
            mmv = None
            for c in range(NT):
                mw = []
                if c == 0:
                    mw = pw + [(ctx.ve, ctx.ve.n)]
                mmv = p.step(
                    "tensor",
                    lambda t, ps=ps, c=c, sl=sl: t.matmul(
                        ps[:],
                        probsT[:, c * 128 : (c + 1) * 128],
                        xn_tm[:, c * 1024 + sl * 512 : c * 1024 + sl * 512 + 512],
                        start=(c == 0),
                        stop=(c == NT - 1),
                    ),
                    wait=mw,
                    inc=ctx.te if c == NT - 1 else None,
                )
            xr = ctx.x_res[:, tq * 1024 + sl * 512 : tq * 1024 + sl * 512 + 512]
            ev = p.step(
                "vector",
                lambda v, xr=xr, ps=ps, c_rd=c_rd: v.scalar_tensor_tensor(
                    xr, ps[:], c_rd, xr, ALU.mult, ALU.add
                ),
                wait=[(ctx.te, mmv), (ctx.ve, vv)],
                inc=ctx.ve,
            )
            ctx.pool.release(pi, (ctx.ve, ev))
            evs.append(ev)
        pvT_prev.clear()
        pvT_prev.extend([(ctx.te, ctx.te.n)])
        arow_guard[aset] = [(ctx.ve, evs[-1])]
        pt_guard[aset] = [(ctx.te, ctx.te.n)]
    return [(ctx.ve, ctx.ve.n)]


def quant_tile(ctx, sc, src_ap, W, dst_ap, wait, use_gp=USE_GP_QUANT):
    p = ctx.p
    qa, sgn, acc, t1, t2 = sc["q_qa"], sc["q_sgn"], sc["q_acc"], sc["q_t1"], sc["q_t2"]
    amax, scal, rscal = sc["q_amax"], sc["q_scal"], sc["q_rscal"]
    nb = W // 16
    g = sc["q_guard"]
    sv1 = p.step("scalar", lambda e: e.activation(qa[:, 0:W], src_ap, AF.Abs),
                 wait=list(wait) + g, inc=ctx.se)
    sv2 = p.step("scalar", lambda e: e.activation(sgn[:, 0:W], src_ap, AF.Sign),
                 wait=[(ctx.se, sv1)], inc=ctx.se)
    a3 = qa[:, 0:W].rearrange("p (b k) -> p b k", b=nb, k=16)
    am3 = amax[:, 0:nb].rearrange("p (b one) -> p b one", b=nb, one=1)
    vv = p.step("vector", lambda v: v.tensor_reduce(am3, a3, AX.X, ALU.max),
                wait=[(ctx.se, sv1)] + g, inc=ctx.ve)
    vv = p.step("vector", lambda v: v.tensor_scalar(scal[:, 0:nb], amax[:, 0:nb], 1.0 / 6.0, 1e-12, ALU.mult, ALU.max),
                wait=[(ctx.ve, vv)], inc=ctx.ve)
    vv = p.step("vector", lambda v: v.reciprocal(rscal[:, 0:nb], scal[:, 0:nb]),
                wait=[(ctx.ve, vv)], inc=ctx.ve)
    r3 = rscal[:, 0:nb].rearrange("p (b one) -> p b one", b=nb, one=1).broadcast_to((128, nb, 16))
    vv = p.step("vector", lambda v: v.tensor_tensor(a3, a3, r3, ALU.mult),
                wait=[(ctx.ve, vv)], inc=ctx.ve)
    eng = "gpsimd" if use_gp else "vector"
    esem = ctx.ge if use_gp else ctx.ve
    gv = p.step(eng, lambda v: v.tensor_scalar(acc[:, 0:W], qa[:, 0:W], 0.25, None, ALU.is_gt),
                wait=[(ctx.ve, vv)], inc=esem)
    for m in (0.75, 1.25, 1.75, 2.5, 3.5, 5.0):
        gv = p.step(eng, lambda v, m=m: v.scalar_tensor_tensor(acc[:, 0:W], qa[:, 0:W], m, acc[:, 0:W], ALU.is_gt, ALU.add),
                    wait=[(esem, gv)], inc=esem)
    gv = p.step(eng, lambda v: v.tensor_scalar(t1[:, 0:W], acc[:, 0:W], 4.0, 0.0, ALU.subtract, ALU.max),
                wait=[(esem, gv)], inc=esem)
    gv = p.step(eng, lambda v: v.tensor_scalar(t2[:, 0:W], acc[:, 0:W], 6.0, 0.0, ALU.subtract, ALU.max),
                wait=[(esem, gv)], inc=esem)
    gv = p.step(eng, lambda v: v.scalar_tensor_tensor(t1[:, 0:W], t1[:, 0:W], 0.5, t2[:, 0:W], ALU.mult, ALU.add),
                wait=[(esem, gv)], inc=esem)
    gv = p.step(eng, lambda v: v.scalar_tensor_tensor(acc[:, 0:W], acc[:, 0:W], 0.5, t1[:, 0:W], ALU.mult, ALU.add),
                wait=[(esem, gv)], inc=esem)
    vv = p.step("vector", lambda v: v.tensor_mul(acc[:, 0:W], acc[:, 0:W], sgn[:, 0:W]),
                wait=[(esem, gv), (ctx.se, sv2)], inc=ctx.ve)
    s3 = scal[:, 0:nb].rearrange("p (b one) -> p b one", b=nb, one=1).broadcast_to((128, nb, 16))
    ac3 = acc[:, 0:W].rearrange("p (b k) -> p b k", b=nb, k=16)
    d3 = dst_ap.rearrange("p (b k) -> p b k", b=nb, k=16)
    vv = p.step("vector", lambda v: v.tensor_tensor(d3, ac3, s3, ALU.mult),
                wait=[(ctx.ve, vv)], inc=ctx.ve)
    sc["q_guard"] = [(ctx.ve, vv)]
    return [(ctx.ve, vv)]


def alloc_quant_scratch(ctx, sc, es, W=1024):
    p = ctx.p
    sc["q_qa"] = p.sbuf("q_qa", [128, W], F32, es=es)
    sc["q_sgn"] = p.sbuf("q_sgn", [128, W], BF16, es=es)
    sc["q_acc"] = p.sbuf("q_acc", [128, W], F32, es=es)
    sc["q_t1"] = p.sbuf("q_t1", [128, W], F32, es=es)
    sc["q_t2"] = p.sbuf("q_t2", [128, W], BF16, es=es)
    sc["q_u2"] = p.sbuf("q_u2", [128, W], BF16, es=es)
    sc["q_amax"] = p.sbuf("q_amax", [128, W // 16], F32, es=es)
    sc["q_scal"] = p.sbuf("q_scal", [128, W // 16], F32, es=es)
    sc["q_rscal"] = p.sbuf("q_rscal", [128, W // 16], F32, es=es)
    sc["q_guard"] = []


def l1_wquant(ctx, ws, wq_scratch):
    p = ctx.p
    es = ExitStack()
    sc = {}
    alloc_quant_scratch(ctx, sc, es, W=1024)
    fence = ctx.fence()
    sc["q_guard"] = list(fence)
    wld = p.sbuf("l1_wld", [128, 2816], F32, es=es)
    wqb = p.sbuf("l1_wqb", [128, 2816], BF16, es=es)
    wtb = p.sbuf("l1_wtb", [128, 22 * 128], BF16, es=es)
    WQG, WQU, WQD = wq_scratch
    st_guard = list(fence)
    for wi, (wname, dst, nrow, ncol) in enumerate(
        (("L1WG", WQG, 2816, 1024), ("L1WU", WQU, 2816, 1024), ("L1WD", WQD, 1024, 2816))
    ):
        hbm = ws[wname]
        for rt in range(nrow // 128):
            slot, wsp = ctx.rx.load(
                lambda e, hbm=hbm, rt=rt, ncol=ncol: e.dma_start(
                    out=wld[:, 0:ncol], in_=hbm[rt * 128 : (rt + 1) * 128, :]
                ),
                wait=sc["q_guard"],
            )
            qdone = []
            for s0 in range(0, ncol, 1024):
                w = min(1024, ncol - s0)
                qdone = quant_tile(ctx, sc, wld[:, s0 : s0 + w], w, wqb[:, s0 : s0 + w], [wsp])
            ctx.rx.ack(slot, *qdone)
            nch = ncol // 128
            for g in range(cdiv(nch, 4)):
                nin = min(4, nch - g * 4)
                bank, bw = ctx.poolT.acquire()
                tw = bw + qdone + st_guard
                tvs = None
                for c4 in range(nin):
                    c = g * 4 + c4
                    tvs = p.step(
                        "tensor",
                        lambda t, c=c, c4=c4, bank=bank: t.transpose(
                            ctx.psT[bank][:, c4 * 128 : (c4 + 1) * 128],
                            wqb[:, c * 128 : (c + 1) * 128],
                            ctx.cba("ident"),
                        ),
                        wait=tw,
                        inc=ctx.te,
                    )
                    tw = ()
                vv = p.step(
                    "vector",
                    lambda v, g=g, nin=nin, bank=bank: v.tensor_copy(
                        wtb[:, g * 512 : g * 512 + nin * 128], ctx.psT[bank][:, 0 : nin * 128]
                    ),
                    wait=[(ctx.te, tvs)],
                    inc=ctx.ve,
                )
                ctx.poolT.release(bank, (ctx.ve, vv))
            dstap = dst[:, rt * 128 : (rt + 1) * 128].rearrange("(c p) w -> p c w", p=128)
            srcap = wtb[:, 0 : nch * 128].rearrange("p (c w) -> p c w", c=nch, w=128)
            sslot, ssp = ctx.rs.load(
                lambda e, dstap=dstap, srcap=srcap: e.dma_start(out=dstap, in_=srcap),
                wait=[(ctx.ve, ctx.ve.n)],
            )
            ctx.rs.ack(sslot, ssp)
            st_guard = [ssp]
    es.close()
    return [ssp, (ctx.ve, ctx.ve.n)]


def l1_ffn_batch(ctx, sc, wq_scratch, wq_ready):
    p = ctx.p
    NT, T = ctx.NT, ctx.T
    WQG, WQU, WQD = wq_scratch
    fence = sc["ffn_fence"]
    xq_fm = sc["xq_fm"]
    xstage = sc["xstage"]
    xg = list(fence)
    for ti in range(NT):
        col0 = (ti % 16) * 8
        c_mean = ctx.row_scr[:, col0 + 2 : col0 + 3]
        c_rstd = ctx.row_scr[:, col0 + 6 : col0 + 7]
        xr = ctx.x_res[:, ti * 1024 : (ti + 1) * 1024]
        vv = p.step("vector",
                    lambda v, xr=xr, c_mean=c_mean, c_rstd=c_rstd: v.tensor_scalar(
                        xstage[:], xr, c_mean, c_rstd, ALU.subtract, ALU.mult),
                    wait=sc["ln_stats_ready"] + xg, inc=ctx.ve)
        qd = quant_tile(ctx, sc, xstage[:], 1024, sc["xqstage"][:], [(ctx.ve, vv)])
        fmv = tm_to_fm_src(ctx, sc["xqstage"], ti, xq_fm, qd[0])
        xg = [(ctx.ve, fmv)]
    xq_ready = [(ctx.ve, ctx.ve.n)]
    if "dbg_xq" in sc:
        p.step("sync", lambda e: e.dma_start(out=sc["dbg_xq"], in_=xq_fm[:]),
               wait=xq_ready, inc=ctx.st, inc_k=16)
    wg_sb, wu_sb, wd_sb = sc["wg_sb"], sc["wu_sb"], sc["wd_sb"]
    hqT = sc["hqT"]
    hscr = sc["hscr"]
    hqstage = sc["q_u2"]
    NSUP = cdiv(NT, 4)
    wgu_guard = [list(fence), list(fence)]
    wd_guard = list(fence)
    hqT_guard = list(fence)
    for sup in range(NSUP):
        ntile4 = min(4, NT - sup * 4)
        hq_done = []
        for fs in range(6):
            fw = min(512, 2816 - fs * 512)
            gslot, gw = ctx.rw.load(
                lambda e, fs=fs, fw=fw: e.dma_start(
                    out=wg_sb[:, 0 : 8 * fw].rearrange("p (c w) -> p c w", c=8, w=fw),
                    in_=WQG[:, fs * 512 : fs * 512 + fw].rearrange("(c p) w -> p c w", p=128),
                ),
                wait=wq_ready + wgu_guard[0],
            )
            uslot, uw = ctx.rw.load(
                lambda e, fs=fs, fw=fw: e.dma_start(
                    out=wu_sb[:, 0 : 8 * fw].rearrange("p (c w) -> p c w", c=8, w=fw),
                    in_=WQU[:, fs * 512 : fs * 512 + fw].rearrange("(c p) w -> p c w", p=128),
                ),
                wait=wq_ready + wgu_guard[1],
            )
            glast = ulast = None
            for ti4 in range(ntile4):
                ti = sup * 4 + ti4
                pig, pwg = ctx.pool.acquire()
                psg = ctx.psF[pig]
                tv = None
                for c in range(8):
                    tv = p.step(
                        "tensor",
                        lambda t, psg=psg, c=c, ti=ti, fw=fw: t.matmul(
                            psg[:, 0:fw],
                            xq_fm[:, c * T + ti * 128 : c * T + ti * 128 + 128],
                            wg_sb[:, c * fw : (c + 1) * fw],
                            start=(c == 0), stop=(c == 7),
                        ),
                        wait=(pwg + [gw] + xq_ready) if c == 0 else (),
                        inc=ctx.te if c == 7 else None,
                    )
                sv = p.step("scalar",
                            lambda e, psg=psg, fw=fw: e.activation(hscr[:, 0:fw], psg[:, 0:fw], AF.Silu),
                            wait=[(ctx.te, tv)] + sc["hscr_guard"] + sc["sscr_guard"], inc=ctx.se)
                glast = sv
                ctx.pool.release(pig, (ctx.se, sv))
                piu, pwu = ctx.pool.acquire()
                psu = ctx.psF[piu]
                tv = None
                for c in range(8):
                    tv = p.step(
                        "tensor",
                        lambda t, psu=psu, c=c, ti=ti, fw=fw: t.matmul(
                            psu[:, 0:fw],
                            xq_fm[:, c * T + ti * 128 : c * T + ti * 128 + 128],
                            wu_sb[:, c * fw : (c + 1) * fw],
                            start=(c == 0), stop=(c == 7),
                        ),
                        wait=(pwu + [uw] + xq_ready) if c == 0 else (),
                        inc=ctx.te if c == 7 else None,
                    )
                vv = p.step("vector",
                            lambda v, psu=psu, fw=fw: v.tensor_mul(hscr[:, 0:fw], hscr[:, 0:fw], psu[:, 0:fw]),
                            wait=[(ctx.te, tv), (ctx.se, sv)], inc=ctx.ve)
                ulast = vv
                ctx.pool.release(piu, (ctx.ve, vv))
                sc["sscr_guard"] = [(ctx.ve, vv)]
                qd = quant_tile(ctx, sc, hscr[:, 0:fw], fw, hqstage[:, 0:fw], [(ctx.ve, vv)])
                sc["hscr_guard"] = qd
                prs = []
                for j in range(fw // 128):
                    fc = fs * 4 + j
                    prs.append((hqstage[:, j * 128 : (j + 1) * 128],
                                hqT[:, fc * 512 + ti4 * 128 : fc * 512 + ti4 * 128 + 128]))
                tr = pe_transpose_pairs(ctx, prs, wait=qd + hqT_guard)
                hq_done.append((ctx.ve, tr))
                sc["hscr_guard"] = [(ctx.ve, tr)]
            wgu_guard[0] = [(ctx.te, ctx.te.n)]
            wgu_guard[1] = [(ctx.te, ctx.te.n)]
            ctx.rw.ack(gslot, (ctx.te, ctx.te.n))
            ctx.rw.ack(uslot, (ctx.te, ctx.te.n))
        if "dbg_hqT" in sc and sup == 0:
            p.step("sync", lambda e: e.dma_start(out=sc["dbg_hqT"], in_=hqT[:]),
                   wait=hq_done + [(ctx.ve, ctx.ve.n)], inc=ctx.st, inc_k=16)
        last_add = None
        for sl in range(2):
            pis = []
            for ti4 in range(ntile4):
                pi, pw = ctx.pool.acquire()
                pis.append((pi, pw))
            tvs = [None] * ntile4
            for half in range(2):
                dslot, dw = ctx.rw.load(
                    lambda e, sl=sl, half=half: e.dma_start(
                        out=wd_sb[:, 0 : 11 * 512].rearrange("p (c w) -> p c w", c=11, w=512),
                        in_=WQD[half * 1408 : half * 1408 + 1408, sl * 512 : (sl + 1) * 512]
                        .rearrange("(c p) w -> p c w", p=128),
                    ),
                    wait=wq_ready + wd_guard,
                )
                for ti4 in range(ntile4):
                    ti = sup * 4 + ti4
                    pi, pw = pis[ti4]
                    ps = ctx.psF[pi]
                    for fc2 in range(11):
                        fc = half * 11 + fc2
                        tvs[ti4] = p.step(
                            "tensor",
                            lambda t, ps=ps, fc=fc, fc2=fc2, ti4=ti4: t.matmul(
                                ps[:],
                                hqT[:, fc * 512 + ti4 * 128 : fc * 512 + ti4 * 128 + 128],
                                wd_sb[:, fc2 * 512 : (fc2 + 1) * 512],
                                start=(fc == 0), stop=(fc == 21),
                            ),
                            wait=(pw + [dw] + hq_done) if fc2 == 0 else (),
                            inc=ctx.te if fc2 == 10 else None,
                        )
                ctx.rw.ack(dslot, (ctx.te, ctx.te.n))
                wd_guard = [(ctx.te, ctx.te.n)]
            for ti4 in range(ntile4):
                ti = sup * 4 + ti4
                pi, pw = pis[ti4]
                ps = ctx.psF[pi]
                xr = ctx.x_res[:, ti * 1024 + sl * 512 : ti * 1024 + sl * 512 + 512]
                vv = p.step("vector", lambda v, xr=xr, ps=ps: v.tensor_add(xr, ps[:], xr),
                            wait=[(ctx.te, tvs[ti4])], inc=ctx.ve)
                ctx.pool.release(pi, (ctx.ve, vv))
                last_add = vv
        hqT_guard = [(ctx.te, ctx.te.n)]
    return [(ctx.ve, ctx.ve.n)]


def layer1(ctx, ws, x_src, x_dst, wq_scratch, wq_ready, prev_store=None, parts="all"):
    p = ctx.p
    NT, T = ctx.NT, ctx.T
    ctx.xn_fm = None
    les = ExitStack()
    sc = {}
    pes = ExitStack()
    sc["xn_tm"] = p.sbuf("l1_xntm", [128, NT * 1024], BF16, es=pes)
    sc["logp_fm"] = p.sbuf("l1_logpfm", [128, 8 * T], BF16, es=pes)
    sc["ptile"] = [p.sbuf(f"l1_pt{i}", [128, 1024], BF16, es=pes) for i in range(2)]
    sc["escr"] = p.sbuf("l1_escr", [128, 1024], BF16, es=pes)
    sc["lrow"] = p.sbuf("l1_lrow", [128, 64], F32, es=pes)
    sc["probs"] = [p.sbuf(f"l1_probs{i}", [128, 512], BF16, es=pes) for i in range(2)]
    sc["probsT"] = p.sbuf("l1_probsT", [128, T], BF16, es=pes)
    sc["arow"] = p.sbuf("l1_arow", [128, 48], F32, es=pes)
    fence0 = ctx.fence()
    sc["fence0"] = fence0
    sc["probs_rel"] = [list(fence0), list(fence0)]
    sc["pvT_prev"] = list(fence0)
    sc["arow_guard"] = [list(fence0), list(fence0)]
    sc["lrow_g"] = [[] for _ in range(8)]
    ln_phase(ctx, x_src, prev_store,
             dst_fn=lambda ti: sc["xn_tm"][:, ti * 1024 : (ti + 1) * 1024])
    sc["xn_ready"] = [(ctx.ve, ctx.ve.n), (ctx.se, ctx.se.n)]
    sc["ln_stats_ready"] = [(ctx.ve, ctx.ve.n)]
    if parts in ("all", "attn"):
        logp_ready = lse_logp_tiles(ctx, sc, sc["xn_tm"], sc["logp_fm"])
        l1_attn_batch(ctx, sc, logp_ready)
    pes.close()
    fence1 = ctx.fence()
    pes2 = ExitStack()
    sc["ffn_fence"] = fence1
    sc["xq_fm"] = p.sbuf("l1_xqfm", [128, 8 * T], BF16, es=pes2)
    alloc_quant_scratch(ctx, sc, pes2, W=1024)
    sc["q_guard"] = list(fence1)
    sc["xstage"] = p.sbuf("l1_xstage", [128, 1024], F32, es=pes2)
    sc["xqstage"] = sc["q_u2"]
    sc["wg_sb"] = p.sbuf("l1_wg", [128, 8 * 512], BF16, es=pes2)
    sc["wu_sb"] = p.sbuf("l1_wu", [128, 8 * 512], BF16, es=pes2)
    sc["wd_sb"] = p.sbuf("l1_wd", [128, 11 * 512], BF16, es=pes2)
    sc["hqT"] = p.sbuf("l1_hqT", [128, 22 * 512], BF16, es=pes2)
    sc["hscr"] = p.sbuf("l1_hscr", [128, 512], F32, es=pes2)
    sc["sscr_guard"] = list(fence1)
    sc["hscr_guard"] = list(fence1)
    if parts in ("all", "ffn"):
        l1_ffn_batch(ctx, sc, wq_scratch, wq_ready)
    pes2.close()
    les.close()
    if x_dst is None:
        return None
    return store_xres(ctx, x_dst)




def l2_attn_batch(ctx, ws, sc):
    p = ctx.p
    T, NT = ctx.T, ctx.NT
    xn_fm = ctx.xn_fm
    fence = sc["fence0"]
    wq_sb, wk_sb, wv_sb = sc["wq_sb"], sc["wk_sb"], sc["wv_sb"]
    wld = []
    for wi, (wname, dst) in enumerate((("L2WQ", wq_sb), ("L2WK", wk_sb), ("L2WV", wv_sb))):
        slot, wsp = load_w_cols(ctx, ws[wname], 0, 1024, dst, wait=sc["wbig_guard"][wi])
        wld.append((slot, wsp))
    fmw = sc["fmw"]
    slot_fq, w_fq = ctx.rw.load(lambda e: e.dma_start(out=fmw[:, 0:128], in_=ws["L2FMQ"]),
                                wait=sc["fmw_guard"])
    slot_fk, w_fk = ctx.rw.load(lambda e: e.dma_start(out=fmw[:, 128:256], in_=ws["L2FMK"]),
                                wait=sc["fmw_guard"])
    frow = sc["l2rows"]
    slot_b, w_b = ctx.rx.load(lambda e: e.dma_start(out=frow[:, 0:2],
                                                    in_=ws["L2FMB"]), wait=sc["fmw_guard"])
    LN16 = float(np.log(16.0))
    vv = p.step("vector", lambda v: v.tensor_scalar(frow[:, 2:3], frow[:, 0:1], -LN16, None, ALU.add),
                wait=[w_b], inc=ctx.ve)
    vv = p.step("vector", lambda v: v.tensor_scalar(frow[:, 3:4], frow[:, 0:1], -1.0, -LN16, ALU.mult, ALU.add),
                wait=[(ctx.ve, vv)], inc=ctx.ve)
    vv = p.step("vector", lambda v: v.tensor_copy(frow[:, 4:5], frow[:, 1:2]),
                wait=[(ctx.ve, vv)], inc=ctx.ve)
    vv = p.step("vector", lambda v: v.tensor_scalar(frow[:, 5:6], frow[:, 1:2], -1.0, None, ALU.mult),
                wait=[(ctx.ve, vv)], inc=ctx.ve)
    ctx.rx.ack(slot_b, (ctx.ve, vv))
    bias_ready = (ctx.ve, vv)
    rmsrow = sc["rmsrow"]
    slot_r, w_r = ctx.rw.load(lambda e: e.dma_start(out=rmsrow[0:1, :], in_=ws["L2RMSW"]),
                              wait=sc["fmw_guard"])
    rv = bcast_row(ctx, rmsrow[0:1, :], 128, sc["rmsw_bc"][:], wait=[w_r])
    ctx.rw.ack(slot_r, (ctx.ve, rv))
    S32 = sc["S32"]
    Sbf = sc["Sbf"]
    zv = p.step("vector", lambda v: v.memset(S32[:], 0.0), wait=list(fence) + sc["S_guard"], inc=ctx.ve)
    zv = p.step("vector", lambda v: v.memset(Sbf[:], 0.0), wait=[(ctx.ve, zv)], inc=ctx.ve)
    s_ready = [[(ctx.ve, zv)] for _ in range(8)]
    qkv_sb = sc["qkv_sb"]
    vtm = sc["vtm"]
    phq = sc["phq"]
    phk = sc["phk"]
    phkT = sc["phkT"]
    am = sc["am"]
    mask = ctx.cba("mask_t2_le_t1")
    wo_slot, wo_w = load_w_cols(ctx, ws["L2WO"], 0, 1024, sc["wo_sb"], wait=sc["wbig_guard"][3])
    sc["wo_w"] = wo_w
    orow = sc["l2orow"]
    og = [[], []]
    GRP = 2
    for g in range(cdiv(NT, GRP)):
        gn = min(GRP, NT - g * GRP)
        gw512 = gn * 128
        pgd = {}
        for wi in range(3):
            for h in range(8):
                pi, pw = ctx.pool.acquire()
                ps = ctx.psF[pi]
                tv = None
                for c in range(8):
                    tv = p.step(
                        "tensor",
                        lambda t, ps=ps, c=c, h=h, wi=wi, g=g, gw512=gw512: t.matmul(
                            ps[:, 0:gw512],
                            (wq_sb, wk_sb, wv_sb)[wi][:, c * 1024 + h * 128 : c * 1024 + h * 128 + 128],
                            xn_fm[:, c * T + g * GRP * 128 : c * T + g * GRP * 128 + gw512],
                            start=(c == 0), stop=(c == 7),
                        ),
                        wait=(pw + [wld[wi][1]] + sc["xn_ready"]) if c == 0 else (),
                        inc=ctx.te if c == 7 else None,
                    )
                dst = qkv_sb[:, (wi * 8 + h) * (GRP * 128) : (wi * 8 + h) * (GRP * 128) + gw512]
                vv2 = p.step("vector", lambda v, dst=dst, ps=ps, gw512=gw512: v.tensor_copy(dst, ps[:, 0:gw512]),
                             wait=[(ctx.te, tv)] + (sc["qkv_guard"] if (wi == 0 and h == 0) else []),
                             inc=ctx.ve)
                ctx.pool.release(pi, (ctx.ve, vv2))
                pgd[(wi, h)] = vv2
        prs = []
        for h in range(8):
            for c4 in range(gn):
                src = qkv_sb[:, (2 * 8 + h) * (GRP * 128) + c4 * 128 : (2 * 8 + h) * (GRP * 128) + c4 * 128 + 128]
                dst = vtm[:, c4 * 1024 + h * 128 : c4 * 1024 + h * 128 + 128]
                prs.append((src, dst))
        trv = pe_transpose_pairs(ctx, prs, wait=[(ctx.ve, ctx.ve.n)] + sc["vtm_guard"])
        sc["vtm_guard"] = []
        vready = (ctx.ve, trv)
        for c4 in range(gn):
            cg = g * GRP + c4
            for h in range(8):
                set_ = (c4 * 8 + h) % 2
                pi, pw = ctx.pool.acquire()
                ps = ctx.psF[pi]
                tv = p.step("tensor",
                            lambda t, ps=ps, h=h, c4=c4: t.matmul(
                                ps[:, 0:128], fmw[:, 0:128],
                                qkv_sb[:, (0 * 8 + h) * (GRP * 128) + c4 * 128 : (0 * 8 + h) * (GRP * 128) + c4 * 128 + 128],
                                start=True, stop=True),
                            wait=pw + [w_fq, (ctx.ve, pgd[(0, h)])], inc=ctx.te)
                sv = p.step("scalar",
                            lambda e, ps=ps, set_=set_: e.activation(phq[set_][:, 0:128], ps[:, 0:128], AF.Exp,
                                                                     bias=frow[:, 2:3]),
                            wait=[(ctx.te, tv), bias_ready] + sc["phq_guard"][set_], inc=ctx.se)
                sv = p.step("scalar",
                            lambda e, ps=ps, set_=set_: e.activation(phq[set_][:, 128:256], ps[:, 0:128], AF.Exp,
                                                                     bias=frow[:, 3:4], scale=-1.0),
                            wait=[(ctx.se, sv)], inc=ctx.se)
                ctx.pool.release(pi, (ctx.se, sv))
                phq_r = (ctx.se, sv)
                pi2, pw2 = ctx.pool.acquire()
                ps2 = ctx.psF[pi2]
                tv = p.step("tensor",
                            lambda t, ps2=ps2, h=h, c4=c4: t.matmul(
                                ps2[:, 0:128], fmw[:, 128:256],
                                qkv_sb[:, (1 * 8 + h) * (GRP * 128) + c4 * 128 : (1 * 8 + h) * (GRP * 128) + c4 * 128 + 128],
                                start=True, stop=True),
                            wait=pw2 + [w_fk, (ctx.ve, pgd[(1, h)])], inc=ctx.te)
                sv = p.step("scalar",
                            lambda e, ps2=ps2, set_=set_: e.activation(phk[set_][:, 0:128], ps2[:, 0:128], AF.Exp,
                                                                      bias=frow[:, 4:5]),
                            wait=[(ctx.te, tv)] + sc["phk_guard"][set_], inc=ctx.se)
                sv = p.step("scalar",
                            lambda e, ps2=ps2, set_=set_: e.activation(phk[set_][:, 128:256], ps2[:, 0:128], AF.Exp,
                                                                      bias=frow[:, 5:6], scale=-1.0),
                            wait=[(ctx.se, sv)], inc=ctx.se)
                ctx.pool.release(pi2, (ctx.se, sv))
                phk_r = (ctx.se, sv)
                prs = [(phk[set_][:, 0:128], phkT[set_][:, 0:128]),
                       (phk[set_][:, 128:256], phkT[set_][:, 128:256])]
                trk = pe_transpose_pairs(ctx, prs, wait=[phk_r] + sc["phkT_guard"][set_])
                pi3, pw3 = ctx.pool.acquire()
                ps3 = ctx.psF[pi3]
                tv = None
                for fc in range(2):
                    tv = p.step("tensor",
                                lambda t, ps3=ps3, fc=fc, set_=set_: t.matmul(
                                    ps3[:, 0:128], phk[set_][:, fc * 128 : (fc + 1) * 128],
                                    phq[set_][:, fc * 128 : (fc + 1) * 128],
                                    start=(fc == 0), stop=(fc == 1)),
                                wait=(pw3 + [phk_r, phq_r]) if fc == 0 else (), inc=ctx.te if fc else None)
                amv = p.step("vector",
                             lambda v, ps3=ps3, set_=set_: v.tensor_mul(am[set_][:], ps3[:, 0:128], mask),
                             wait=[(ctx.te, tv)] + sc["am_guard"][set_], inc=ctx.ve)
                ctx.pool.release(pi3, (ctx.ve, amv))
                pi4, pw4 = ctx.pool.acquire()
                ps4 = ctx.psF[pi4]
                tv = p.step("tensor",
                            lambda t, ps4=ps4, set_=set_, c4=c4, h=h: t.matmul(
                                ps4[:, 0:128], am[set_][:],
                                vtm[:, c4 * 1024 + h * 128 : c4 * 1024 + h * 128 + 128],
                                start=True, stop=False),
                            wait=pw4 + [(ctx.ve, amv), vready], inc=None)
                for fc in range(2):
                    tv = p.step("tensor",
                                lambda t, ps4=ps4, fc=fc, set_=set_, h=h: t.matmul(
                                    ps4[:, 0:128], phq[set_][:, fc * 128 : (fc + 1) * 128],
                                    Sbf[:, h * 264 + fc * 132 : h * 264 + fc * 132 + 128],
                                    start=False, stop=(fc == 1)),
                                wait=s_ready[h] if fc == 0 else (), inc=ctx.te if fc else None)
                p1_done = (ctx.te, tv)
                pi5, pw5 = ctx.pool.acquire()
                ps5 = ctx.psF[pi5]
                tv2 = None
                for fc in range(2):
                    tv2 = p.step("tensor",
                                 lambda t, ps5=ps5, fc=fc, set_=set_, c4=c4, h=h: t.matmul(
                                     ps5[:, fc * 132 : fc * 132 + 128],
                                     phkT[set_][:, fc * 128 : (fc + 1) * 128],
                                     vtm[:, c4 * 1024 + h * 128 : c4 * 1024 + h * 128 + 128],
                                     start=(fc == 0), stop=(fc == 1)),
                                 wait=(pw5 + [(ctx.ve, trk), vready] + s_ready[h]) if fc == 0 else (),
                                 inc=ctx.te if fc else None)
                sc["phkT_guard"][set_] = [(ctx.te, tv2)]
                s32h3 = S32[:, h * 264 : h * 264 + 264].rearrange("p (c w) -> p c w", c=2, w=132)[:, :, 0:128]
                ps53 = ps5[:, 0:264].rearrange("p (c w) -> p c w", c=2, w=132)[:, :, 0:128]
                sbf3 = Sbf[:, h * 264 : h * 264 + 264].rearrange("p (c w) -> p c w", c=2, w=132)[:, :, 0:128]
                sv1 = p.step("vector",
                             lambda v, ps53=ps53, s32h3=s32h3: v.tensor_add(s32h3, ps53, s32h3),
                             wait=[(ctx.te, tv2)], inc=ctx.ve)
                sv2 = p.step("vector",
                             lambda v, sbf3=sbf3, s32h3=s32h3: v.tensor_copy(sbf3, s32h3),
                             wait=[(ctx.ve, sv1)], inc=ctx.ve)
                ctx.pool.release(pi5, (ctx.ve, sv1))
                s_ready[h] = [(ctx.ve, sv2)]
                oc0 = (set_) * 8
                o_f = sc["o_f"]
                osq = sc["osq"]
                ev = p.step("vector",
                            lambda v, ps4=ps4, o_f=o_f, set_=set_: v.tensor_copy(o_f[set_][:], ps4[:, 0:128]),
                            wait=[p1_done] + sc["of_guard"][set_], inc=ctx.ve)
                ctx.pool.release(pi4, (ctx.ve, ev))
                sq = p.step("scalar",
                            lambda e, o_f=o_f, osq=osq, set_=set_, oc0=oc0: e.activation(
                                osq[set_][:], o_f[set_][:], AF.Square, accum_out=orow[:, oc0 : oc0 + 1]),
                            wait=[(ctx.ve, ev)] + sc["osq_guard"][set_], inc=ctx.se)
                sc["osq_guard"][set_] = [(ctx.se, sq)]
                vv = p.step("vector",
                            lambda v, oc0=oc0: v.tensor_scalar(orow[:, oc0 + 1 : oc0 + 2],
                                                               orow[:, oc0 : oc0 + 1],
                                                               1.0 / 128.0, 1e-5, ALU.mult, ALU.add),
                            wait=[(ctx.se, sq)], inc=ctx.ve)
                sv3 = p.step("scalar",
                             lambda e, oc0=oc0: e.activation(orow[:, oc0 + 2 : oc0 + 3],
                                                             orow[:, oc0 + 1 : oc0 + 2], AF.Sqrt),
                             wait=[(ctx.ve, vv)], inc=ctx.se)
                vv = p.step("vector",
                            lambda v, oc0=oc0: v.reciprocal(orow[:, oc0 + 3 : oc0 + 4],
                                                            orow[:, oc0 + 2 : oc0 + 3]),
                            wait=[(ctx.se, sv3)], inc=ctx.ve)
                ost = sc["ost"]
                vv = p.step("vector",
                            lambda v, o_f=o_f, set_=set_, oc0=oc0, ost=ost: v.scalar_tensor_tensor(
                                ost[set_][:], o_f[set_][:], orow[:, oc0 + 3 : oc0 + 4],
                                sc["rmsw_bc"][:], ALU.mult, ALU.mult),
                            wait=[(ctx.ve, vv), (ctx.ve, rv)] + sc["ost_guard"][set_], inc=ctx.ve)
                sc["of_guard"][set_] = [(ctx.ve, vv)]
                oset = 0
                ofm = sc["ofm"][0]
                tro = pe_transpose_pairs(ctx, [(ost[set_][:], ofm[:, h * 128 : (h + 1) * 128])],
                                         wait=[(ctx.ve, vv)] + (sc["ofm_guard"][oset] if h == 0 else []))
                sc["ost_guard"][set_] = [(ctx.te, ctx.te.n)]
                sc["am_guard"][set_] = [(ctx.te, ctx.te.n)]
                sc["phq_guard"][set_] = [(ctx.te, ctx.te.n)]
                sc["phk_guard"][set_] = [(ctx.te, ctx.te.n)]
            oset = 0
            ofm = sc["ofm"][0]
            ofm_ready = (ctx.ve, ctx.ve.n)
            for half in range(2):
                pi, pw = ctx.pool.acquire()
                ps = ctx.psF[pi]
                tv = None
                for h2 in range(8):
                    tv = p.step(
                        "tensor",
                        lambda t, ps=ps, h2=h2, half=half, ofm=ofm: t.matmul(
                            ps[:],
                            ofm[:, h2 * 128 : (h2 + 1) * 128],
                            sc["wo_sb"][:, h2 * 1024 + half * 512 : h2 * 1024 + half * 512 + 512],
                            start=(h2 == 0), stop=(h2 == 7)),
                        wait=(pw + [sc["wo_w"], ofm_ready]) if h2 == 0 else (),
                        inc=ctx.te if h2 == 7 else None)
                xr = ctx.x_res[:, cg * 1024 + half * 512 : cg * 1024 + half * 512 + 512]
                vv3 = p.step("vector", lambda v, xr=xr, ps=ps: v.tensor_add(xr, ps[:], xr),
                             wait=[(ctx.te, tv)], inc=ctx.ve)
                ctx.pool.release(pi, (ctx.ve, vv3))
            sc["ofm_guard"][oset] = [(ctx.te, ctx.te.n)]
        sc["qkv_guard"] = [(ctx.te, ctx.te.n)]
        sc["vtm_guard"] = [(ctx.te, ctx.te.n)]
    for wi in range(3):
        ctx.rw.ack(wld[wi][0], (ctx.te, ctx.te.n))
        sc["wbig_guard"][wi] = [(ctx.te, ctx.te.n)]
    ctx.rw.ack(wo_slot, (ctx.te, ctx.te.n))
    sc["wbig_guard"][3] = [(ctx.te, ctx.te.n)]
    ctx.rw.ack(slot_fq, (ctx.te, ctx.te.n))
    ctx.rw.ack(slot_fk, (ctx.te, ctx.te.n))
    sc["fmw_guard"] = [(ctx.te, ctx.te.n)]
    return [(ctx.ve, ctx.ve.n)]


def moe_expert_pass(ctx, sc, e_idx, w1_ap, w3_ap, w2_ap, inter, gate_col_fn, silu_on_w1=True):
    p = ctx.p
    T, NT = ctx.T, ctx.NT
    xn_fm = ctx.xn_fm
    w1b, w3b, w2b = sc["w1b"], sc["w3b"], sc["w2b"]
    s1, w1w = load_w_cols(ctx, w1_ap, 0, inter, w1b, wait=sc["wmoe_guard"][0])
    s3, w3w = load_w_cols(ctx, w3_ap, 0, inter, w3b, wait=sc["wmoe_guard"][1])
    s2, w2w = load_w_cols(ctx, w2_ap, 0, 1024, w2b, wait=sc["wmoe_guard"][2], d_in=inter)
    nfc = inter // 128
    hfm = sc["hfm"]
    hscr2 = sc["hscr2"]
    for sl in range(cdiv(NT, 4)):
        sn = min(4, NT - sl * 4)
        sw512 = sn * 128
        for fc in range(nfc):
            pi, pw = ctx.pool.acquire()
            ps = ctx.psF[pi]
            tv = None
            for c in range(8):
                tv = p.step(
                    "tensor",
                    lambda t, ps=ps, c=c, fc=fc, sl=sl, sw512=sw512: t.matmul(
                        ps[:, 0:sw512],
                        w1b[:, c * inter + fc * 128 : c * inter + fc * 128 + 128],
                        xn_fm[:, c * T + sl * 512 : c * T + sl * 512 + sw512],
                        start=(c == 0), stop=(c == 7)),
                    wait=(pw + [w1w] + sc["xn_ready"]) if c == 0 else (),
                    inc=ctx.te if c == 7 else None)
            sv = p.step("scalar",
                        lambda e, ps=ps, sw512=sw512: e.activation(hscr2[:, 0:sw512], ps[:, 0:sw512], AF.Silu),
                        wait=[(ctx.te, tv)] + sc["h2_guard"], inc=ctx.se)
            ctx.pool.release(pi, (ctx.se, sv))
            pi2, pw2 = ctx.pool.acquire()
            ps2 = ctx.psF[pi2]
            tv = None
            for c in range(8):
                tv = p.step(
                    "tensor",
                    lambda t, ps2=ps2, c=c, fc=fc, sl=sl, sw512=sw512: t.matmul(
                        ps2[:, 0:sw512],
                        w3b[:, c * inter + fc * 128 : c * inter + fc * 128 + 128],
                        xn_fm[:, c * T + sl * 512 : c * T + sl * 512 + sw512],
                        start=(c == 0), stop=(c == 7)),
                    wait=(pw2 + [w3w]) if c == 0 else (),
                    inc=ctx.te if c == 7 else None)
            hv = p.step("vector",
                        lambda v, ps2=ps2, fc=fc, sw512=sw512: v.tensor_mul(
                            hfm[:, fc * 512 : fc * 512 + sw512], hscr2[:, 0:sw512], ps2[:, 0:sw512]),
                        wait=[(ctx.te, tv), (ctx.se, sv)] + sc["hfm_guard"], inc=ctx.ve)
            ctx.pool.release(pi2, (ctx.ve, hv))
            sc["h2_guard"] = [(ctx.ve, hv)]
        h_ready = (ctx.ve, ctx.ve.n)
        for t4 in range(sn):
            ti = sl * 4 + t4
            pis = []
            for half in range(2):
                pi, pw = ctx.pool.acquire()
                ps = ctx.psF[pi]
                tv = None
                for fc in range(nfc):
                    tv = p.step(
                        "tensor",
                        lambda t, ps=ps, fc=fc, t4=t4, half=half: t.matmul(
                            ps[:],
                            hfm[:, fc * 512 + t4 * 128 : fc * 512 + t4 * 128 + 128],
                            w2b[:, fc * 1024 + half * 512 : fc * 1024 + half * 512 + 512],
                            start=(fc == 0), stop=(fc == nfc - 1)),
                        wait=(pw + [w2w, h_ready]) if fc == 0 else (),
                        inc=ctx.te if fc == nfc - 1 else None)
                pis.append((pi, tv))
            for half in range(2):
                pi, tv = pis[half]
                ps = ctx.psF[pi]
                xr = ctx.x_res[:, ti * 1024 + half * 512 : ti * 1024 + half * 512 + 512]
                gc = gate_col_fn(ti)
                if gc is None:
                    vv = p.step("vector", lambda v, xr=xr, ps=ps: v.tensor_add(xr, ps[:], xr),
                                wait=[(ctx.te, tv)], inc=ctx.ve)
                else:
                    vv = p.step("vector",
                                lambda v, xr=xr, ps=ps, gc=gc: v.scalar_tensor_tensor(
                                    xr, ps[:], gc, xr, ALU.mult, ALU.add),
                                wait=[(ctx.te, tv)], inc=ctx.ve)
                ctx.pool.release(pi, (ctx.ve, vv))
        sc["hfm_guard"] = [(ctx.te, ctx.te.n)]
    te_g = [(ctx.te, ctx.te.n)]
    ctx.rw.ack(s1, *te_g)
    ctx.rw.ack(s3, *te_g)
    ctx.rw.ack(s2, *te_g)
    sc["wmoe_guard"] = [te_g + [w1w], te_g + [w3w], te_g + [w2w]]


def moe_gate8(ctx, sc, gw_name, ws, bias_mode, wts_buf):
    p = ctx.p
    T, NT = ctx.T, ctx.NT
    xn_fm = ctx.xn_fm
    gwb = sc["gwb"]
    slot, gww = load_w_cols(ctx, ws[gw_name], 0, 8, gwb, wait=sc["gw_guard"])
    grow8 = sc["grow8"]
    bname = "L2GB" if bias_mode == "l2" else "L3GB"
    slot_b, gbw = ctx.rx.load(lambda e: e.dma_start(out=grow8[0:1, 0:8], in_=ws[bname]),
                              wait=sc["gw_guard"])
    vvb = p.step("vector", lambda v: v.tensor_copy(sc["grow8b"][0:1, 0:8], grow8[0:1, 0:8]),
                 wait=[gbw], inc=ctx.ve)
    bv = bcast_row(ctx, sc["grow8b"][0:1, 0:8], 8, sc["gb_bc"][:], wait=[(ctx.ve, vvb)])
    ctx.rx.ack(slot_b, (ctx.ve, bv))
    gsc = sc["gsc8"]
    g_guard = sc["g8_guard"]
    last = None
    for ti in range(NT):
        st0 = (ti % 2) * 32
        lg = gsc[:, st0 : st0 + 8]
        m8 = gsc[:, st0 + 8 : st0 + 16]
        e8 = gsc[:, st0 + 16 : st0 + 24]
        r0 = gsc[:, st0 + 24 : st0 + 25]
        r1 = gsc[:, st0 + 25 : st0 + 26]
        r2 = gsc[:, st0 + 26 : st0 + 27]
        pi, pw = ctx.pool.acquire()
        ps = ctx.psF[pi]
        tv = None
        for c in range(8):
            tv = p.step("tensor",
                        lambda t, ps=ps, c=c, ti=ti: t.matmul(
                            ps[:, 0:8], xn_fm[:, c * T + ti * 128 : c * T + ti * 128 + 128],
                            gwb[:, c * 8 : (c + 1) * 8], start=(c == 0), stop=(c == 7)),
                        wait=(pw + [gww] + sc["xn_ready"]) if c == 0 else (),
                        inc=ctx.te if c == 7 else None)
        if bias_mode == "l3":
            vv = p.step("vector", lambda v, ps=ps, lg=lg: v.tensor_add(lg, ps[:, 0:8], sc["gb_bc"][:]),
                        wait=[(ctx.te, tv), (ctx.ve, bv)] + g_guard[ti % 2], inc=ctx.ve)
        else:
            vv = p.step("vector", lambda v, ps=ps, lg=lg: v.tensor_copy(lg, ps[:, 0:8]),
                        wait=[(ctx.te, tv)] + g_guard[ti % 2], inc=ctx.ve)
        ctx.pool.release(pi, (ctx.ve, vv))
        vv = p.step("vector", lambda v, lg=lg, r0=r0: v.reduce_max(r0, lg, axis=AX.X, negate=True),
                    wait=[(ctx.ve, vv)], inc=ctx.ve)
        sv = p.step("scalar", lambda e, lg=lg, e8=e8, r0=r0, r1=r1: e.activation(
            e8, lg, AF.Exp, bias=r0, accum_out=r1), wait=[(ctx.ve, vv)] + sc["g8s_guard"][ti % 2],
            inc=ctx.se)
        sc["g8s_guard"][ti % 2] = [(ctx.se, sv)]
        vv = p.step("vector", lambda v, r1=r1, r2=r2: v.reciprocal(r2, r1),
                    wait=[(ctx.se, sv)], inc=ctx.ve)
        vv = p.step("vector", lambda v, e8=e8, r2=r2: v.tensor_scalar(e8, e8, r2, None, ALU.mult),
                    wait=[(ctx.ve, vv)], inc=ctx.ve)
        if bias_mode == "l2":
            vv = p.step("vector", lambda v, lg=lg, e8=e8: v.tensor_add(lg, e8, sc["gb_bc"][:]),
                        wait=[(ctx.ve, vv), (ctx.ve, bv)], inc=ctx.ve)
            vv = p.step("vector", lambda v, m8=m8, lg=lg: v.max(out=m8, in_=lg),
                        wait=[(ctx.ve, vv)], inc=ctx.ve)
            vv = p.step("vector",
                        lambda v, ti=ti, lg=lg, e8=e8, m8=m8: v.scalar_tensor_tensor(
                            wts_buf[:, ti * 8 : (ti + 1) * 8], lg, m8[:, 1:2], e8, ALU.is_ge, ALU.mult),
                        wait=[(ctx.ve, vv)] + sc["wts_guard"], inc=ctx.ve)
        else:
            vv = p.step("vector", lambda v, m8=m8, e8=e8: v.max(out=m8, in_=e8),
                        wait=[(ctx.ve, vv)], inc=ctx.ve)
            vv = p.step("vector",
                        lambda v, lg=lg, e8=e8, m8=m8: v.scalar_tensor_tensor(
                            lg, e8, m8[:, 1:2], e8, ALU.is_ge, ALU.mult),
                        wait=[(ctx.ve, vv)], inc=ctx.ve)
            vv = p.step("vector", lambda v, r0=r0, m8=m8: v.tensor_add(r0, m8[:, 0:1], m8[:, 1:2]),
                        wait=[(ctx.ve, vv)], inc=ctx.ve)
        ctx.pool.release if False else None
        if bias_mode == "l3":
            vv = p.step("vector", lambda v, r1=r1, r0=r0: v.reciprocal(r1, r0),
                        wait=[(ctx.ve, vv)], inc=ctx.ve)
            vv = p.step("vector",
                        lambda v, ti=ti, lg=lg, r1=r1: v.tensor_scalar(
                            wts_buf[:, ti * 8 : (ti + 1) * 8], lg, r1, None, ALU.mult),
                        wait=[(ctx.ve, vv)] + sc["wts_guard"], inc=ctx.ve)
        g_guard[ti % 2] = [(ctx.ve, vv)]
        last = vv
    ctx.rw.ack(slot, (ctx.te, ctx.te.n))
    sc["gw_guard"] = [(ctx.te, ctx.te.n)]
    return [(ctx.ve, last)]


def layer2(ctx, ws, x_src, x_dst, prev_store=None, parts="all"):
    p = ctx.p
    NT, T = ctx.NT, ctx.T
    les = ExitStack()
    sc = {}
    ctx.xn_fm = p.sbuf("l2_xnfm", [128, 8 * T], BF16, es=les)
    ln_phase(ctx, x_src, prev_store)
    sc["xn_ready"] = [(ctx.ve, ctx.ve.n), (ctx.se, ctx.se.n)]
    fence0 = ctx.fence()
    sc["fence0"] = fence0
    if parts in ("all", "attn"):
        pes = ExitStack()
        sc["wq_sb"] = p.sbuf("l2_wq", [128, 8192], BF16, es=pes)
        sc["wk_sb"] = p.sbuf("l2_wk", [128, 8192], BF16, es=pes)
        sc["wv_sb"] = p.sbuf("l2_wv", [128, 8192], BF16, es=pes)
        sc["wo_sb"] = p.sbuf("l2_wo", [128, 8192], BF16, es=pes)
        sc["fmw"] = p.sbuf("l2_fmw", [128, 256], BF16, es=pes)
        sc["l2rows"] = p.sbuf("l2_rows", [128, 8], F32, es=pes)
        sc["rmsrow"] = p.sbuf("l2_rmsr", [1, 128], BF16, es=pes)
        sc["rmsw_bc"] = p.sbuf("l2_rmsbc", [128, 128], F32, es=pes)
        sc["S32"] = p.sbuf("l2_S32", [128, 8 * 264], F32, es=pes)
        sc["Sbf"] = p.sbuf("l2_Sbf", [128, 8 * 264], BF16, es=pes)
        sc["qkv_sb"] = p.sbuf("l2_qkv", [128, 3 * 8 * 256], BF16, es=pes)
        sc["vtm"] = p.sbuf("l2_vtm", [128, 2 * 1024], BF16, es=pes)
        sc["phq"] = [p.sbuf(f"l2_phq{i}", [128, 256], BF16, es=pes) for i in range(2)]
        sc["phk"] = [p.sbuf(f"l2_phk{i}", [128, 256], BF16, es=pes) for i in range(2)]
        sc["phkT"] = [p.sbuf(f"l2_phkT{i}", [128, 256], BF16, es=pes) for i in range(2)]
        sc["am"] = [p.sbuf(f"l2_am{i}", [128, 128], BF16, es=pes) for i in range(2)]
        sc["o_f"] = [p.sbuf(f"l2_of{i}", [128, 128], F32, es=pes) for i in range(2)]
        sc["osq"] = [p.sbuf(f"l2_osq{i}", [128, 128], BF16, es=pes) for i in range(2)]
        sc["ost"] = [p.sbuf(f"l2_ost{i}", [128, 128], BF16, es=pes) for i in range(2)]
        sc["ofm"] = [p.sbuf("l2_ofm0", [128, 1024], BF16, es=pes)]
        sc["l2orow"] = p.sbuf("l2_orow", [128, 16], F32, es=pes)
        for k in ("wbig_guard",):
            sc[k] = [list(fence0) for _ in range(4)]
        for k in ("fmw_guard", "qkv_guard", "vtm_guard", "S_guard"):
            sc[k] = list(fence0)
        for k in ("phq_guard", "phk_guard", "phkT_guard", "am_guard", "of_guard",
                  "osq_guard", "ost_guard", "ofm_guard"):
            sc[k] = [list(fence0), list(fence0)]
        sc["ofm_guard"] = [list(fence0)]
        l2_attn_batch(ctx, ws, sc)
        pes.close()
    fence1 = ctx.fence()
    if parts in ("all", "ffn"):
        pes2 = ExitStack()
        sc["w1b"] = p.sbuf("l2_w1b", [128, 8 * 1024], BF16, es=pes2)
        sc["w3b"] = p.sbuf("l2_w3b", [128, 8 * 1024], BF16, es=pes2)
        sc["w2b"] = p.sbuf("l2_w2b", [128, 8 * 1024], BF16, es=pes2)
        sc["hfm"] = p.sbuf("l2_hfm", [128, 8 * 512], BF16, es=pes2)
        sc["hscr2"] = p.sbuf("l2_hscr", [128, 512], F32, es=pes2)
        sc["gwb"] = p.sbuf("l2_gwb", [128, 64], BF16, es=pes2)
        sc["grow8"] = p.sbuf("l2_gr8", [1, 16], F32, es=pes2)
        sc["grow8b"] = p.sbuf("l2_gr8b", [1, 16], BF16, es=pes2)
        sc["gb_bc"] = p.sbuf("l2_gbbc", [128, 8], F32, es=pes2)
        sc["gsc8"] = p.sbuf("l2_gsc8", [128, 64], F32, es=pes2)
        sc["wts"] = p.sbuf("l2_wts", [128, NT * 8], F32, es=pes2)
        sc["gw_guard"] = list(fence1)
        sc["g8_guard"] = [list(fence1), list(fence1)]
        sc["g8s_guard"] = [list(fence1), list(fence1)]
        sc["wts_guard"] = list(fence1)
        sc["wmoe_guard"] = [list(fence1) for _ in range(3)]
        sc["h2_guard"] = list(fence1)
        sc["hfm_guard"] = list(fence1)
        wts_ready = moe_gate8(ctx, sc, "L2GW", ws, "l2", sc["wts"][:])
        for e in range(8):
            gcf = lambda ti, e=e: sc["wts"][:, ti * 8 + e : ti * 8 + e + 1]
            moe_expert_pass(ctx, sc, e, ws["L2W1"][e], ws["L2W3"][e], ws["L2W2"][e], 1024, gcf)
        for hf in range(2):
            moe_expert_pass(ctx, sc, 8 + hf, ws["L2SW1"][:, hf * 1024 : (hf + 1) * 1024],
                            ws["L2SW3"][:, hf * 1024 : (hf + 1) * 1024],
                            ws["L2SW2"][hf * 1024 : (hf + 1) * 1024, :], 1024, lambda ti: None)
        pes2.close()
    les.close()
    ctx.xn_fm = None
    if x_dst is None:
        return None
    return store_xres(ctx, x_dst)




def l3_attn_batch(ctx, ws, sc):
    p = ctx.p
    T, NT = ctx.T, ctx.NT
    xn_fm = ctx.xn_fm
    fence = sc["fence0"]
    RRD2 = 0.5
    R2S = float(2.0 ** -0.5)
    sub = ExitStack()
    wqkv = p.sbuf("l3_wqkv", [128, 12288], BF16, es=sub)
    sq, qw = load_w_cols(ctx, ws["L3WQ"], 0, 256, wqkv[:, 0:2048], wait=sc["l3w_guard"][0])
    sk, kw = load_w_cols(ctx, ws["L3WK"], 0, 256, wqkv[:, 2048:4096], wait=sc["l3w_guard"][1])
    sv_, vw = load_w_cols(ctx, ws["L3WV"], 0, 1024, wqkv[:, 4096:12288], wait=sc["l3w_guard"][2])
    q_tm, k_tm, v_tm = sc["q_tm"], sc["k_tm"], sc["v_tm"]
    prj_done = None
    for ti in range(NT):
        for wi, (dst, wid, nc_, scale) in enumerate(
            ((q_tm, 0, 256, RRD2), (k_tm, 1, 256, RRD2), (v_tm, 2, 1024, 1.0))
        ):
            for half in range(cdiv(nc_, 512)):
                hw_ = min(512, nc_ - half * 512)
                pi, pw = ctx.pool.acquire()
                ps = ctx.psF[pi]
                tv = None
                for c in range(8):
                    tv = p.step(
                        "tensor",
                        lambda t, ps=ps, c=c, ti=ti, wid=wid, half=half, hw_=hw_, nc_=nc_: t.matmul(
                            ps[:, 0:hw_],
                            xn_fm[:, c * T + ti * 128 : c * T + ti * 128 + 128],
                            wqkv[:, (0, 2048, 4096)[wid] + c * nc_ + half * 512 :
                                 (0, 2048, 4096)[wid] + c * nc_ + half * 512 + hw_],
                            start=(c == 0), stop=(c == 7)),
                        wait=(pw + [(qw, kw, vw)[wi]] + sc["xn_ready"]) if c == 0 else (),
                        inc=ctx.te if c == 7 else None)
                if scale != 1.0:
                    ev = p.step("scalar",
                                lambda e, ps=ps, dst=dst, ti=ti, nc_=nc_, half=half, hw_=hw_, scale=scale: e.activation(
                                    dst[:, ti * nc_ + half * 512 : ti * nc_ + half * 512 + hw_],
                                    ps[:, 0:hw_], AF.Copy, scale=scale),
                                wait=[(ctx.te, tv)], inc=ctx.se)
                    ctx.pool.release(pi, (ctx.se, ev))
                else:
                    ev = p.step("vector",
                                lambda v, ps=ps, dst=dst, ti=ti, nc_=nc_, half=half, hw_=hw_: v.tensor_copy(
                                    dst[:, ti * nc_ + half * 512 : ti * nc_ + half * 512 + hw_], ps[:, 0:hw_]),
                                wait=[(ctx.te, tv)], inc=ctx.ve)
                    ctx.pool.release(pi, (ctx.ve, ev))
    prj_done = [(ctx.se, ctx.se.n), (ctx.ve, ctx.ve.n)]
    te_prj = [(ctx.te, ctx.te.n)]
    for s_ in (sq, sk, sv_):
        ctx.rw.ack(s_, *te_prj)
    sub.close()
    wo_sb = p.sbuf("l3_wo", [128, 8192], BF16, es=sc["pes"])
    sc["l3wo"] = wo_sb
    wo_slot, wo_w = load_w_cols(ctx, ws["L3WO"], 0, 1024, wo_sb,
                                wait=te_prj + [qw, kw, vw])
    qf = sc["qf"]
    kf = sc["kf"]
    qffm = sc["qffm"]
    kfm0 = sc["kfm0"]
    vaug = sc["vaug"]
    a2 = sc["a2"]
    am3 = sc["am3"]
    ost3 = sc["ost3"]
    ofm3 = sc["ofm3"]
    S32 = sc["S32_3"]
    Sbf = sc["Sbf_3"]
    zrow = sc["zrow"]
    mask = ctx.cba("mask_t2_le_t1")
    zv = p.step("vector", lambda v: v.memset(S32[:], 0.0), wait=list(fence) + sc["S3_guard"], inc=ctx.ve)
    zv = p.step("vector", lambda v: v.memset(Sbf[:], 0.0), wait=[(ctx.ve, zv)], inc=ctx.ve)
    for b_ in qf + kf:
        zv = p.step("vector", lambda v, b_=b_: v.memset(b_[:, 256:384], 0.0), wait=[(ctx.ve, zv)], inc=ctx.ve)
        zv = p.step("vector", lambda v, b_=b_: v.memset(b_[:, 16:17], 1.0), wait=[(ctx.ve, zv)], inc=ctx.ve)
    for b_ in vaug:
        zv = p.step("vector", lambda v, b_=b_: v.memset(b_[:, 64:65], 1.0), wait=[(ctx.ve, zv)], inc=ctx.ve)
    init_done = (ctx.ve, zv)
    s_ready = [[init_done] for _ in range(16)]
    qf_guard = [[init_done], [init_done]]
    kf_guard = [[init_done], [init_done]]
    va_guard = [[init_done], [init_done]]
    ofm_ready = []
    for cg in range(NT):
        for h in range(16):
            set_ = (cg * 16 + h) % 2
            qsl = q_tm[:, cg * 256 + h * 16 : cg * 256 + h * 16 + 16]
            ksl = k_tm[:, cg * 256 + h * 16 : cg * 256 + h * 16 + 16]
            vv = p.step("vector", lambda v, qsl=qsl, set_=set_: v.tensor_copy(qf[set_][:, 0:16], qsl),
                        wait=prj_done + qf_guard[set_], inc=ctx.ve)
            q3a = qsl.rearrange("p (d one) -> p d one", d=16, one=1).broadcast_to((128, 16, 16))
            q3b = qsl.rearrange("p (one d) -> p one d", one=1, d=16).broadcast_to((128, 16, 16))
            o3 = qf[set_][:, 17:273].rearrange("p (a b) -> p a b", a=16, b=16)
            vv = p.step("vector", lambda v, o3=o3, q3a=q3a, q3b=q3b: v.scalar_tensor_tensor(
                o3, q3a, R2S, q3b, ALU.mult, ALU.mult), wait=[(ctx.ve, vv)], inc=ctx.ve)
            qfv = vv
            vv = p.step("vector", lambda v, ksl=ksl, set_=set_: v.tensor_copy(kf[set_][:, 0:16], ksl),
                        wait=prj_done + kf_guard[set_], inc=ctx.ve)
            k3a = ksl.rearrange("p (d one) -> p d one", d=16, one=1).broadcast_to((128, 16, 16))
            k3b = ksl.rearrange("p (one d) -> p one d", one=1, d=16).broadcast_to((128, 16, 16))
            ko3 = kf[set_][:, 17:273].rearrange("p (a b) -> p a b", a=16, b=16)
            vv = p.step("vector", lambda v, ko3=ko3, k3a=k3a, k3b=k3b: v.scalar_tensor_tensor(
                ko3, k3a, R2S, k3b, ALU.mult, ALU.mult), wait=[(ctx.ve, vv)], inc=ctx.ve)
            kfv = vv
            vva = p.step("vector",
                         lambda v, set_=set_, cg=cg, h=h: v.tensor_copy(
                             vaug[set_][:, 0:64], v_tm[:, cg * 1024 + h * 64 : cg * 1024 + h * 64 + 64]),
                         wait=prj_done + va_guard[set_], inc=ctx.ve)
            trq = pe_transpose_pairs(ctx, [(qf[set_][:, i * 128 : (i + 1) * 128],
                                            qffm[set_][:, i * 128 : (i + 1) * 128]) for i in range(3)],
                                     wait=[(ctx.ve, qfv)] + sc["qffm_guard"][set_])
            trk = pe_transpose_pairs(ctx, [(kf[set_][:, 0:128], kfm0[set_][:])],
                                     wait=[(ctx.ve, kfv)] + sc["kfm0_guard"][set_])
            pi, pw = ctx.pool.acquire()
            ps = ctx.psF[pi]
            tv = p.step("tensor",
                        lambda t, ps=ps, set_=set_: t.matmul(
                            ps[:, 0:128], kfm0[set_][0:16, 0:128], qffm[set_][0:16, 0:128],
                            start=True, stop=True),
                        wait=pw + [(ctx.ve, trk), (ctx.ve, trq)], inc=ctx.te)
            sv = p.step("scalar",
                        lambda e, ps=ps: e.activation(a2[:], ps[:, 0:128], AF.Square, bias=1.0),
                        wait=[(ctx.te, tv)] + sc["a2_guard"], inc=ctx.se)
            ctx.pool.release(pi, (ctx.se, sv))
            vv = p.step("vector",
                        lambda v: v.tensor_scalar(a2[:], a2[:], 0.5, 0.5, ALU.mult, ALU.add),
                        wait=[(ctx.se, sv)], inc=ctx.ve)
            amv = p.step("vector",
                         lambda v, set_=set_: v.tensor_mul(am3[set_][:], a2[:], mask),
                         wait=[(ctx.ve, vv)] + sc["am3_guard"][set_], inc=ctx.ve)
            sc["a2_guard"] = [(ctx.ve, amv)]
            pi4, pw4 = ctx.pool.acquire()
            ps4 = ctx.psF[pi4]
            tv = p.step("tensor",
                        lambda t, ps4=ps4, set_=set_: t.matmul(
                            ps4[:, 0:65], am3[set_][:], vaug[set_][:, 0:65], start=True, stop=False),
                        wait=pw4 + [(ctx.ve, amv), (ctx.ve, vva)], inc=None)
            for fc in range(3):
                tv = p.step("tensor",
                            lambda t, ps4=ps4, fc=fc, set_=set_, h=h: t.matmul(
                                ps4[:, 0:65], qffm[set_][:, fc * 128 : (fc + 1) * 128],
                                Sbf[:, h * 204 + fc * 68 : h * 204 + fc * 68 + 65],
                                start=False, stop=(fc == 2)),
                            wait=s_ready[h] if fc == 0 else (), inc=ctx.te if fc == 2 else None)
            p1_done = (ctx.te, tv)
            pi5, pw5 = ctx.pool.acquire()
            ps5 = ctx.psF[pi5]
            tv2 = None
            for fc in range(3):
                tv2 = p.step("tensor",
                             lambda t, ps5=ps5, fc=fc, set_=set_: t.matmul(
                                 ps5[:, fc * 68 : fc * 68 + 65],
                                 kf[set_][:, fc * 128 : (fc + 1) * 128],
                                 vaug[set_][:, 0:65],
                                 start=(fc == 0), stop=(fc == 2)),
                             wait=(pw5 + [(ctx.ve, kfv), (ctx.ve, vva)] + s_ready[h]) if fc == 0 else (),
                             inc=ctx.te if fc == 2 else None)
            kf_guard[set_] = [(ctx.te, tv2)]
            qf_guard[set_] = [(ctx.te, ctx.te.n)]
            va_guard[set_] = [(ctx.te, tv2)]
            s33 = S32[:, h * 204 : h * 204 + 204].rearrange("p (c w) -> p c w", c=3, w=68)[:, :, 0:65]
            ps53 = ps5[:, 0:204].rearrange("p (c w) -> p c w", c=3, w=68)[:, :, 0:65]
            sb3 = Sbf[:, h * 204 : h * 204 + 204].rearrange("p (c w) -> p c w", c=3, w=68)[:, :, 0:65]
            sv1 = p.step("vector", lambda v, ps53=ps53, s33=s33: v.tensor_add(s33, ps53, s33),
                         wait=[(ctx.te, tv2)], inc=ctx.ve)
            sv2 = p.step("vector", lambda v, sb3=sb3, s33=s33: v.tensor_copy(sb3, s33),
                         wait=[(ctx.ve, sv1)], inc=ctx.ve)
            ctx.pool.release(pi5, (ctx.ve, sv1))
            s_ready[h] = [(ctx.ve, sv2)]
            zc0 = set_ * 4
            vv = p.step("vector",
                        lambda v, ps4=ps4, zc0=zc0: v.tensor_scalar(
                            zrow[:, zc0 : zc0 + 1], ps4[:, 64:65], 1e-12, None, ALU.add),
                        wait=[p1_done] + sc["zrow_guard"][set_], inc=ctx.ve)
        
            vv = p.step("vector",
                        lambda v, zc0=zc0: v.reciprocal(zrow[:, zc0 + 1 : zc0 + 2], zrow[:, zc0 : zc0 + 1]),
                        wait=[(ctx.ve, vv)], inc=ctx.ve)
            ev = p.step("vector",
                        lambda v, ps4=ps4, set_=set_, zc0=zc0: v.tensor_scalar(
                            ost3[set_][:], ps4[:, 0:64], zrow[:, zc0 + 1 : zc0 + 2], None, ALU.mult),
                        wait=[(ctx.ve, vv)] + sc["ost3_guard"][set_], inc=ctx.ve)
            ctx.pool.release(pi4, (ctx.ve, ev))
            sc["zrow_guard"][set_] = [(ctx.ve, ev)]
            tro = pe_transpose_pairs(
                ctx, [(ost3[set_][:], ofm3[(h % 2) * 64 : (h % 2) * 64 + 64,
                                           (h // 2) * 128 : (h // 2) * 128 + 128])],
                wait=[(ctx.ve, ev)] + (sc["ofm3_guard"] if h == 0 else []), in_w=64)
            sc["ost3_guard"][set_] = [(ctx.te, ctx.te.n)]
            sc["am3_guard"][set_] = [(ctx.te, ctx.te.n)]
            sc["qffm_guard"][set_] = [(ctx.te, ctx.te.n)]
            sc["kfm0_guard"][set_] = [(ctx.te, ctx.te.n)]
        ofm_ready = (ctx.ve, ctx.ve.n)
        for half in range(2):
            pi, pw = ctx.pool.acquire()
            ps = ctx.psF[pi]
            tv = None
            for j in range(8):
                tv = p.step(
                    "tensor",
                    lambda t, ps=ps, j=j, half=half: t.matmul(
                        ps[:],
                        ofm3[:, j * 128 : (j + 1) * 128],
                        sc["l3wo"][:, j * 1024 + half * 512 : j * 1024 + half * 512 + 512],
                        start=(j == 0), stop=(j == 7)),
                    wait=(pw + [wo_w, ofm_ready]) if j == 0 else (),
                    inc=ctx.te if j == 7 else None)
            xr = ctx.x_res[:, cg * 1024 + half * 512 : cg * 1024 + half * 512 + 512]
            vv3 = p.step("vector", lambda v, xr=xr, ps=ps: v.tensor_add(xr, ps[:], xr),
                         wait=[(ctx.te, tv)], inc=ctx.ve)
            ctx.pool.release(pi, (ctx.ve, vv3))
        sc["ofm3_guard"] = [(ctx.te, ctx.te.n)]
    te_g = [(ctx.te, ctx.te.n)]
    ctx.rw.ack(wo_slot, *te_g)
    sc["l3w_guard"] = [list(te_g) for _ in range(4)]
    return [(ctx.ve, ctx.ve.n)]


def l3_ffn_batch(ctx, ws, sc):
    p = ctx.p
    T, NT = ctx.NT * 0 + ctx.T, ctx.NT
    NT = ctx.NT
    xn_fm = ctx.xn_fm
    fence = sc["ffn_fence"]
    wts = sc["wts3"]
    wts_ready = moe_gate8(ctx, sc, "L3GW", ws, "l3", wts[:])
    b1sb = sc["b1sb"]
    sb1, b1w = ctx.rx.load(lambda e: e.dma_start(out=b1sb[:], in_=ws["L3B1"]), wait=list(fence))
    w1buf = sc["w1buf"]
    w2buf = sc["w2buf"]
    hbuf = sc["hbuf"]
    w1g = sc["w1g"]
    w2g = sc["w2g"]
    hg = sc["hg"]
    for e in range(8):
        for fh in range(4):
            s1, w1w = ctx.rw.load(
                lambda eng, e=e, fh=fh: eng.dma_start(
                    out=w1buf[:].rearrange("p (c w) -> p c w", c=8, w=1024),
                    in_=ws["L3W1"][e][:, fh * 1024 : (fh + 1) * 1024].rearrange("(c p) w -> p c w", p=128)),
                wait=w1g)
            s2, w2w = ctx.rw.load(
                lambda eng, e=e, fh=fh: eng.dma_start(
                    out=w2buf[:].rearrange("p (c w) -> p c w", c=8, w=1024),
                    in_=ws["L3W2"][e][fh * 1024 : (fh + 1) * 1024, :].rearrange("(c p) w -> p c w", p=128)),
                wait=w2g)
            for fc in range(8):
                fcg = fh * 8 + fc
                for ts4 in range(cdiv(NT, 4)):
                    w512 = min(512, NT * 128 - ts4 * 512)
                    pi, pw = ctx.pool.acquire()
                    ps = ctx.psF[pi]
                    tv = None
                    for c in range(8):
                        tv = p.step(
                            "tensor",
                            lambda t, ps=ps, c=c, fc=fc, ts4=ts4, w512=w512: t.matmul(
                                ps[:, 0:w512],
                                w1buf[:, c * 1024 + fc * 128 : c * 1024 + fc * 128 + 128],
                                xn_fm[:, c * T + ts4 * 512 : c * T + ts4 * 512 + w512],
                                start=(c == 0), stop=(c == 7)),
                            wait=(pw + [w1w] + sc["xn_ready"]) if c == 0 else (),
                            inc=ctx.te if c == 7 else None)
                    ev = p.step(
                        "scalar",
                        lambda sce, ps=ps, fc=fc, fcg=fcg, ts4=ts4, w512=w512, e=e: sce.activation(
                            hbuf[:, fc * (NT * 128) + ts4 * 512 : fc * (NT * 128) + ts4 * 512 + w512],
                            ps[:, 0:w512], AF.Gelu, bias=b1sb[:, e * 32 + fcg : e * 32 + fcg + 1]),
                        wait=[(ctx.te, tv), b1w] + (hg if (fc == 0 and ts4 == 0) else []),
                        inc=ctx.se)
                    ctx.pool.release(pi, (ctx.se, ev))
            h_ready = (ctx.se, ctx.se.n)
            for ti in range(NT):
                pis = []
                for half in range(2):
                    pi, pw = ctx.pool.acquire()
                    ps = ctx.psF[pi]
                    tv = None
                    for fc in range(8):
                        tv = p.step(
                            "tensor",
                            lambda t, ps=ps, fc=fc, ti=ti, half=half: t.matmul(
                                ps[:],
                                hbuf[:, fc * (NT * 128) + ti * 128 : fc * (NT * 128) + ti * 128 + 128],
                                w2buf[:, fc * 1024 + half * 512 : fc * 1024 + half * 512 + 512],
                                start=(fc == 0), stop=(fc == 7)),
                            wait=(pw + [w2w, h_ready]) if fc == 0 else (),
                            inc=ctx.te if fc == 7 else None)
                    pis.append((pi, tv))
                for half in range(2):
                    pi, tv = pis[half]
                    ps = ctx.psF[pi]
                    xr = ctx.x_res[:, ti * 1024 + half * 512 : ti * 1024 + half * 512 + 512]
                    gc = wts[:, ti * 8 + e : ti * 8 + e + 1]
                    vv = p.step("vector",
                                lambda v, xr=xr, ps=ps, gc=gc: v.scalar_tensor_tensor(
                                    xr, ps[:], gc, xr, ALU.mult, ALU.add),
                                wait=[(ctx.te, tv)] + wts_ready, inc=ctx.ve)
                    ctx.pool.release(pi, (ctx.ve, vv))
            w1g = [(ctx.te, ctx.te.n), w1w]
            w2g = [(ctx.te, ctx.te.n), w2w]
            hg = [(ctx.te, ctx.te.n)]
            ctx.rw.ack(s1, (ctx.te, ctx.te.n))
            ctx.rw.ack(s2, (ctx.te, ctx.te.n))
    ctx.rx.ack(sb1, (ctx.se, ctx.se.n))
    b2sb = sc["b2sb"]
    sb2, b2w = ctx.rw.load(lambda e: e.dma_start(out=b2sb[:], in_=ws["L3B2"]), wait=list(fence))
    wtsf = sc["wtsf"]
    last = None
    for ti in range(NT):
        vv = p.step("vector", lambda v, ti=ti: v.tensor_copy(wtsf[:, 0:8], wts[:, ti * 8 : (ti + 1) * 8]),
                    wait=wts_ready + sc["wtsf_guard"], inc=ctx.ve)
        trw = pe_transpose_pairs(ctx, [(wtsf[:, 0:8], sc["wtsfT"][0:8, 0:128])],
                                 wait=[(ctx.ve, vv)] + sc["wtsfT_guard"], in_w=8)
        for half in range(2):
            pi, pw = ctx.pool.acquire()
            ps = ctx.psF[pi]
            tv = p.step("tensor",
                        lambda t, ps=ps, half=half: t.matmul(
                            ps[:], sc["wtsfT"][0:8, 0:128], b2sb[0:8, half * 512 : half * 512 + 512],
                            start=True, stop=True),
                        wait=pw + [b2w, (ctx.ve, trw)], inc=ctx.te)
            xr = ctx.x_res[:, ti * 1024 + half * 512 : ti * 1024 + half * 512 + 512]
            vv2 = p.step("vector", lambda v, xr=xr, ps=ps: v.tensor_add(xr, ps[:], xr),
                         wait=[(ctx.te, tv)], inc=ctx.ve)
            ctx.pool.release(pi, (ctx.ve, vv2))
            last = vv2
        sc["wtsf_guard"] = [(ctx.te, ctx.te.n)]
        sc["wtsfT_guard"] = [(ctx.te, ctx.te.n)]
    ctx.rw.ack(sb2, (ctx.te, ctx.te.n))
    return [(ctx.ve, ctx.ve.n)]


def layer3(ctx, ws, x_src, x_dst, prev_store=None, parts="all"):
    p = ctx.p
    NT, T = ctx.NT, ctx.T
    les = ExitStack()
    sc = {}
    ctx.xn_fm = p.sbuf("l3_xnfm", [128, 8 * T], BF16, es=les)
    ln_phase(ctx, x_src, prev_store)
    sc["xn_ready"] = [(ctx.ve, ctx.ve.n), (ctx.se, ctx.se.n)]
    fence0 = ctx.fence()
    sc["fence0"] = fence0
    if parts in ("all", "attn"):
        pes = ExitStack()
        sc["q_tm"] = p.sbuf("l3_qtm", [128, NT * 256], BF16, es=pes)
        sc["k_tm"] = p.sbuf("l3_ktm", [128, NT * 256], BF16, es=pes)
        sc["v_tm"] = p.sbuf("l3_vtm", [128, NT * 1024], BF16, es=pes)
        sc["qf"] = [p.sbuf(f"l3_qf{i}", [128, 384], BF16, es=pes) for i in range(2)]
        sc["kf"] = [p.sbuf(f"l3_kf{i}", [128, 384], BF16, es=pes) for i in range(2)]
        sc["qffm"] = [p.sbuf(f"l3_qffm{i}", [128, 384], BF16, es=pes) for i in range(2)]
        sc["kfm0"] = [p.sbuf(f"l3_kfm0{i}", [128, 128], BF16, es=pes) for i in range(2)]
        sc["vaug"] = [p.sbuf(f"l3_vaug{i}", [128, 68], BF16, es=pes) for i in range(2)]
        sc["a2"] = p.sbuf("l3_a2", [128, 128], F32, es=pes)
        sc["am3"] = [p.sbuf(f"l3_am{i}", [128, 128], BF16, es=pes) for i in range(2)]
        sc["ost3"] = [p.sbuf(f"l3_ost{i}", [128, 64], BF16, es=pes) for i in range(2)]
        sc["ofm3"] = p.sbuf("l3_ofm", [128, 1024], BF16, es=pes)
        sc["S32_3"] = p.sbuf("l3_S32", [128, 16 * 204], F32, es=pes)
        sc["Sbf_3"] = p.sbuf("l3_Sbf", [128, 16 * 204], BF16, es=pes)
        sc["zrow"] = p.sbuf("l3_zrow", [128, 16], F32, es=pes)
        sc["pes"] = pes
        sc["l3w_guard"] = [list(fence0) for _ in range(4)]
        sc["S3_guard"] = list(fence0)
        sc["a2_guard"] = list(fence0)
        sc["ofm3_guard"] = list(fence0)
        for k in ("qffm_guard", "kfm0_guard", "am3_guard", "ost3_guard", "zrow_guard"):
            sc[k] = [list(fence0), list(fence0)]
        l3_attn_batch(ctx, ws, sc)
        pes.close()
    fence1 = ctx.fence()
    sc["ffn_fence"] = fence1
    if parts in ("all", "ffn"):
        pes2 = ExitStack()
        sc["wts3"] = p.sbuf("l3_wts", [128, NT * 8], F32, es=pes2)
        sc["gwb"] = p.sbuf("l3_gwb", [128, 64], BF16, es=pes2)
        sc["grow8"] = p.sbuf("l3_gr8", [1, 16], F32, es=pes2)
        sc["grow8b"] = p.sbuf("l3_gr8b", [1, 16], BF16, es=pes2)
        sc["gb_bc"] = p.sbuf("l3_gbbc", [128, 8], F32, es=pes2)
        sc["gsc8"] = p.sbuf("l3_gsc8", [128, 64], F32, es=pes2)
        sc["b1sb"] = p.sbuf("l3_b1", [128, 256], F32, es=pes2)
        sc["b2sb"] = p.sbuf("l3_b2", [8, 1024], BF16, es=pes2)
        sc["w1buf"] = p.sbuf("l3_w1b", [128, 8 * 1024], BF16, es=pes2)
        sc["w2buf"] = p.sbuf("l3_w2b", [128, 8 * 1024], BF16, es=pes2)
        sc["hbuf"] = p.sbuf("l3_hb", [128, 8 * NT * 128], BF16, es=pes2)
        sc["wtsf"] = p.sbuf("l3_wtsf", [128, 8], BF16, es=pes2)
        sc["wtsfT"] = p.sbuf("l3_wtsfT", [8, 128], BF16, es=pes2)
        sc["gw_guard"] = list(fence1)
        sc["g8_guard"] = [list(fence1), list(fence1)]
        sc["g8s_guard"] = [list(fence1), list(fence1)]
        sc["wts_guard"] = list(fence1)
        sc["wtsf_guard"] = list(fence1)
        sc["wtsfT_guard"] = list(fence1)
        sc["w1g"] = list(fence1)
        sc["w2g"] = list(fence1)
        sc["hg"] = list(fence1)
        l3_ffn_batch(ctx, ws, sc)
        pes2.close()
    les.close()
    ctx.xn_fm = None
    if x_dst is None:
        return None
    return store_xres(ctx, x_dst)
from concourse.bass_utils import run_bass_kernel_spmd

D4_T = 2048
D4_NB = 2
N_CORES = 8


def _win(nc, ws, host, name, arr):
    arr = np.ascontiguousarray(arr, dtype=np.float32)
    ws[name] = nc.dram_tensor(name, list(arr.shape), F32, kind="ExternalInput").ap()
    host[name] = arr


def _prep_host_weights(inputs):
    h = {}
    Wqkv = inputs["L0_attn_Wqkv"]
    h["WQ"] = Wqkv[0:1024].T
    h["WK"] = Wqkv[1024:2048].T
    h["WV"] = Wqkv[2048:3072].T
    h["WOUT"] = inputs["L0_attn_Wout"].T
    h["SIM"] = inputs["L0_ffn_sim_matrix"]
    h["SIMT"] = inputs["L0_ffn_sim_matrix"].T
    h["GATES"] = np.asarray(inputs["L0_ffn_gates"]).reshape(1, 16)
    h["FHN"] = np.array([[inputs["L0_attn_a"], inputs["L0_attn_b"], inputs["L0_attn_dt"]]],
                        dtype=np.float32)
    h["QP"] = np.asarray(inputs["L0_ffn_q_proj"]).transpose(1, 2, 0).reshape(1024, 1024)
    h["KP"] = np.asarray(inputs["L0_ffn_k_proj"]).transpose(1, 2, 0).reshape(1024, 1024)
    h["VP"] = np.asarray(inputs["L0_ffn_v_proj"]).transpose(1, 2, 0).reshape(1024, 1024)
    h["OP"] = np.asarray(inputs["L0_ffn_o_proj"]).transpose(1, 0, 2).reshape(1024, 1024)
    h["L1WG"] = inputs["L1_ffn_w_gate"]
    h["L1WU"] = inputs["L1_ffn_w_up"]
    h["L1WD"] = inputs["L1_ffn_w_down"]
    h["L2WQ"] = inputs["L2_attn_Wq"].T
    h["L2WK"] = inputs["L2_attn_Wk"].T
    h["L2WV"] = inputs["L2_attn_Wv"].T
    h["L2WO"] = inputs["L2_attn_Wo"].T
    h["L2FMQ"] = inputs["L2_attn_fmq_w"].T
    h["L2FMK"] = inputs["L2_attn_fmk_w"].T
    h["L2FMB"] = np.stack([np.asarray(inputs["L2_attn_fmq_b"]),
                           np.asarray(inputs["L2_attn_fmk_b"])], axis=1)
    h["L2RMSW"] = np.asarray(inputs["L2_attn_rms_w"]).reshape(1, 128)
    h["L2GW"] = inputs["L2_ffn_gate_w"].T
    h["L2GB"] = np.asarray(inputs["L2_ffn_gate_bias"]).reshape(1, 8)
    h["L2W1"] = np.asarray(inputs["L2_ffn_w1"]).transpose(0, 2, 1)
    h["L2W3"] = np.asarray(inputs["L2_ffn_w3"]).transpose(0, 2, 1)
    h["L2W2"] = np.asarray(inputs["L2_ffn_w2"]).transpose(0, 2, 1)
    h["L2SW1"] = inputs["L2_ffn_sw1"].T
    h["L2SW3"] = inputs["L2_ffn_sw3"].T
    h["L2SW2"] = inputs["L2_ffn_sw2"].T
    h["L3WQ"] = inputs["L3_attn_Wq"]
    h["L3WK"] = inputs["L3_attn_Wk"]
    h["L3WV"] = inputs["L3_attn_Wv"]
    h["L3WO"] = inputs["L3_attn_Wo"]
    h["L3GW"] = inputs["L3_ffn_Wg"]
    h["L3GB"] = np.asarray(inputs["L3_ffn_bg"]).reshape(1, 8)
    h["L3W1"] = inputs["L3_ffn_W1"]
    b1 = np.asarray(inputs["L3_ffn_b1"])
    h["L3B1"] = np.concatenate([b1[e].reshape(32, 128).T for e in range(8)], axis=1)
    h["L3W2"] = inputs["L3_ffn_W2"]
    h["L3B2"] = inputs["L3_ffn_b2"]
    return {k: np.ascontiguousarray(v, dtype=np.float32) for k, v in h.items()}


_L0_KEYS = ["WQ", "WK", "WV", "WOUT", "SIM", "SIMT", "GATES", "FHN", "QP", "KP", "VP", "OP"]
_L1_KEYS = ["L1WG", "L1WU", "L1WD"]
_L2_KEYS = ["L2WQ", "L2WK", "L2WV", "L2WO", "L2FMQ", "L2FMK", "L2FMB", "L2RMSW",
            "L2GW", "L2GB", "L2W1", "L2W3", "L2W2", "L2SW1", "L2SW3", "L2SW2"]
_L3_KEYS = ["L3WQ", "L3WK", "L3WV", "L3WO", "L3GW", "L3GB", "L3W1", "L3B1", "L3W2", "L3B2"]


def _build_layer_prog(layer_idx, host):
    nc = bass.Bass("TRN2", target_bir_lowering=False)
    p = Prog(nc)
    ctx = Ctx(p, D4_T)
    ws = {}
    hw = {}
    keys = (_L0_KEYS, _L1_KEYS, _L2_KEYS, _L3_KEYS)[layer_idx]
    for k in keys:
        _win(nc, ws, hw, k, host[k])
    X = nc.dram_tensor("X", [D4_NB, D4_T, 1024], F32, kind="ExternalInput").ap()
    XO = nc.dram_tensor("XO", [D4_NB, D4_T, 1024], F32, kind="ExternalOutput").ap()
    ctx.load_consts()
    wqs = None
    wq_ready = None
    if layer_idx == 1:
        WQG = nc.dram_tensor("WQG", [1024, 2816], BF16, kind="Internal").ap()
        WQU = nc.dram_tensor("WQU", [1024, 2816], BF16, kind="Internal").ap()
        WQD = nc.dram_tensor("WQD", [2816, 1024], BF16, kind="Internal").ap()
        wqs = (WQG, WQU, WQD)
        wq_ready = l1_wquant(ctx, ws, wqs)
    prev = None
    for b in range(D4_NB):
        if layer_idx == 0:
            prev = layer0(ctx, ws, X[b], XO[b], prev_store=prev)
        elif layer_idx == 1:
            prev = layer1(ctx, ws, X[b], XO[b], wqs, wq_ready, prev_store=prev)
        elif layer_idx == 2:
            prev = layer2(ctx, ws, X[b], XO[b], prev_store=prev)
        else:
            prev = layer3(ctx, ws, X[b], XO[b], prev_store=prev)
    p.step("sync", lambda e: None, wait=[prev])
    ctx.drain_rings()
    p.finalize_block()
    p.es.close()
    consts = {"CONSTF": ctx.const_f32_host, "CONSTB": ctx.const_bf_host}
    return nc, hw, consts


def kernel(**inputs):
    x = np.ascontiguousarray(np.asarray(inputs["L0_attn_x"], dtype=np.float32))
    host = _prep_host_weights(inputs)
    cur = x
    for li in range(4):
        nc, hw, consts = _build_layer_prog(li, host)
        in_maps = []
        for c in range(N_CORES):
            m = {"X": np.ascontiguousarray(cur[c * D4_NB : (c + 1) * D4_NB])}
            m.update(hw)
            m.update(consts)
            in_maps.append(m)
        res = run_bass_kernel_spmd(nc, in_maps, core_ids=list(range(N_CORES)))
        cur = np.concatenate([np.asarray(r["XO"]) for r in res.results], axis=0)
    return cur.astype(np.float32)


_ALL_KEYS = _L0_KEYS + _L1_KEYS + _L2_KEYS + _L3_KEYS


def _build_fused_prog(host, layers=(0, 1, 2, 3)):
    nc = bass.Bass("TRN2", target_bir_lowering=False)
    p = Prog(nc)
    ctx = Ctx(p, D4_T)
    ws = {}
    hw = {}
    for k in _ALL_KEYS:
        _win(nc, ws, hw, k, host[k])
    X = nc.dram_tensor("X", [D4_NB, D4_T, 1024], F32, kind="ExternalInput").ap()
    XO = nc.dram_tensor("XO", [D4_NB, D4_T, 1024], F32, kind="ExternalOutput").ap()
    ctx.load_consts()
    WQG = nc.dram_tensor("WQG", [1024, 2816], BF16, kind="Internal").ap()
    WQU = nc.dram_tensor("WQU", [1024, 2816], BF16, kind="Internal").ap()
    WQD = nc.dram_tensor("WQD", [2816, 1024], BF16, kind="Internal").ap()
    wqs = (WQG, WQU, WQD)
    wq_ready = l1_wquant(ctx, ws, wqs) if 1 in layers else None
    prev = None
    for b in range(D4_NB):
        first = True
        def src_of():
            nonlocal first
            if first:
                first = False
                return X[b]
            return None
        last = max(layers)
        for li in layers:
            xs = X[b] if li == min(layers) else None
            xd = XO[b] if li == last else None
            if li == 0:
                r = layer0(ctx, ws, xs, xd, prev_store=prev)
            elif li == 1:
                r = layer1(ctx, ws, xs, xd, wqs, wq_ready, prev_store=prev)
            elif li == 2:
                r = layer2(ctx, ws, xs, xd, prev_store=prev)
            else:
                r = layer3(ctx, ws, xs, xd, prev_store=prev)
        prev = r
    p.step("sync", lambda e: None, wait=[prev])
    ctx.drain_rings()
    p.finalize_block()
    p.es.close()
    consts = {"CONSTF": ctx.const_f32_host, "CONSTB": ctx.const_bf_host}
    return nc, hw, consts


def kernel_fused(**inputs, ):
    x = np.ascontiguousarray(np.asarray(inputs["L0_attn_x"], dtype=np.float32))
    host = _prep_host_weights(inputs)
    import os
    lys = tuple(int(c) for c in os.environ.get("K_LAYERS", "0123"))
    nc, hw, consts = _build_fused_prog(host, layers=lys)
    in_maps = []
    for c in range(N_CORES):
        m = {"X": np.ascontiguousarray(x[c * D4_NB : (c + 1) * D4_NB])}
        m.update(hw)
        m.update(consts)
        in_maps.append(m)
    res = run_bass_kernel_spmd(nc, in_maps, core_ids=list(range(N_CORES)))
    return np.concatenate([np.asarray(r["XO"]) for r in res.results], axis=0).astype(np.float32)


kernel = kernel_fused


# version 06
# speedup vs baseline: 1.1409x; 1.1409x over previous
import math
import numpy as np
import ml_dtypes
import concourse.bass as bass
import concourse.mybir as mybir
from contextlib import ExitStack

F32 = mybir.dt.float32
BF16 = mybir.dt.bfloat16
U8 = mybir.dt.uint8
AF = mybir.ActivationFunctionType
ALU = mybir.AluOpType
AX = mybir.AxisListType

ENGINES = ("sync", "vector", "scalar", "tensor", "gpsimd")
USE_GP_QUANT = False

LN_EPS = 1e-5
CONST_SCALARS = {
    "eps_ln": 1e-5,
    "neg_ln16": -float(np.log(16.0)),
    "eps8": 1e-8,
    "eps12": 1e-12,
    "eps6": 1e-6,
    "neg30": -1e30,
    "zero": 0.0,
}


class Sem:
    def __init__(self, handle, name):
        self.h = handle
        self.name = name
        self.n = 0


class Prog:

    def __init__(self, nc):
        self.nc = nc
        self.es = ExitStack()
        self.q = {e: [] for e in ENGINES}
        self.sems = {}
        self.waited = {e: {} for e in ENGINES}
        self.carry = {e: {} for e in ENGINES}

    def sem(self, name):
        if name not in self.sems:
            h = self.es.enter_context(self.nc.semaphore(name))
            self.sems[name] = Sem(h, name)
        return self.sems[name]

    _uid = 0

    def sbuf(self, name, shape, dt=F32, es=None):
        Prog._uid += 1
        return (es or self.es).enter_context(self.nc.sbuf_tensor(f"{name}_{Prog._uid}", shape, dt))

    def psum(self, name, shape, dt=F32):
        return self.es.enter_context(self.nc.psum_tensor(name, shape, dt))

    def step(self, eng, fn, wait=(), inc=None, inc_k=1):
        w = []
        hw = self.waited[eng]
        carry = self.carry[eng]
        for s, v in wait:
            if s is None:
                continue
            assert v <= s.n, f"[{eng}] wait on {s.name} for {v} but only {s.n} planned"
            if v <= 0:
                continue
            if inc is None and carry.get(s, 0) < v and not s.name.startswith(("rx", "rw", "rs", "ld", "st")):
                carry[s] = v
            if not s.name.startswith(("rx", "rw", "rs", "ld", "st")):
                if hw.get(s.name, 0) >= v:
                    continue
                hw[s.name] = v
            w.append((s, v))
        ret = None
        if inc is not None:
            have = {s: v for s, v in w}
            for s, v in carry.items():
                if have.get(s, 0) < v:
                    w.append((s, v))
            carry.clear()
            inc.n += inc_k
            ret = inc.n
        self.q[eng].append((tuple(w), fn, inc, inc_k))
        return ret

    def n_queued(self):
        return sum(len(v) for v in self.q.values())

    def finalize_block(self):
        nc = self.nc
        with nc.Block() as blk:
            for e in ENGINES:
                items = self.q[e]
                if not items:
                    continue

                def make(items):
                    def run(proxy):
                        for w, fn, inc, inc_k in items:
                            for s, v in w:
                                proxy.wait_ge(s.h, v)
                            ins = fn(proxy) if fn is not None else None
                            if inc is not None:
                                assert ins is not None, "inc with no instruction"
                                ins.then_inc(inc.h, inc_k)
                    return run

                getattr(blk, e)(make(items))
        self.q = {e: [] for e in ENGINES}


def cdiv(a, b):
    return (a + b - 1) // b


class PsPool:

    def __init__(self, tensors):
        self.t = tensors
        self.busy = [[] for _ in tensors]
        self.rr = 0

    def acquire(self):
        i = self.rr % len(self.t)
        self.rr += 1
        w = self.busy[i]
        self.busy[i] = []
        return i, list(w)

    def release(self, i, *wspecs):
        self.busy[i] = [w for w in wspecs if w is not None]


class DmaRing:

    def __init__(self, p, name, K, queue="sync"):
        self.p = p
        self.K = K
        self.queue = queue
        self.sems = [p.sem(f"{name}{j}") for j in range(K)]
        self.acks = [[] for _ in range(K)]
        self.i = 0

    def load(self, fn, wait=()):
        j = self.i % self.K
        self.i += 1
        w = list(wait)
        if self.acks[j]:
            w.extend(self.acks[j])
        v = self.p.step(self.queue, fn, wait=w, inc=self.sems[j], inc_k=16)
        self.acks[j] = []
        return j, (self.sems[j], v)

    def ack(self, j, *wspecs):
        self.acks[j] = [w for w in wspecs if w is not None]




def build_const_host():
    f32_cols = {}
    bf_cols = {}
    cf_parts = []
    cb_parts = []
    colf = [0]
    colb = [0]

    def add_f(name, arr):
        arr = np.asarray(arr, dtype=np.float32)
        a = np.zeros((128, arr.shape[1]), dtype=np.float32)
        a[: arr.shape[0]] = arr
        f32_cols[name] = (colf[0], arr.shape[1])
        colf[0] += arr.shape[1]
        cf_parts.append(a)

    def add_b(name, arr):
        arr = np.asarray(arr, dtype=np.float32)
        a = np.zeros((128, arr.shape[1]), dtype=np.float32)
        a[: arr.shape[0]] = arr
        bf_cols[name] = (colb[0], arr.shape[1])
        colb[0] += arr.shape[1]
        cb_parts.append(a.astype(ml_dtypes.bfloat16))

    for nm, val in CONST_SCALARS.items():
        add_f("s_" + nm, np.full((128, 1), val, dtype=np.float32))
    add_b("ident", np.eye(128, dtype=np.float32))
    ar = np.arange(128)
    add_b("neg_low", np.where(ar[:, None] > ar[None, :], -1e30, 0.0).astype(np.float32))
    se = np.zeros((128, 1), np.float32); se[:64] = 1.0
    so = np.zeros((128, 1), np.float32); so[64:] = 1.0
    add_b("sel_even", se)
    add_b("sel_odd", so)
    add_b("ones_col128", np.ones((128, 1), np.float32))
    add_b("mask_t2_le_t1", (np.arange(128)[:, None] <= np.arange(128)[None, :]).astype(np.float32))
    add_b("ones_row_b", np.ones((1, 128), dtype=np.float32))
    add_f("ones_row", np.ones((1, 128), dtype=np.float32))
    cf = np.concatenate(cf_parts, axis=1)
    cb = np.concatenate(cb_parts, axis=1)
    return cf, cb, f32_cols, bf_cols


class Ctx:

    def __init__(self, p: Prog, T):
        self.p = p
        self.T = T
        self.NT = T // 128
        nc = p.nc
        cf, cb, f32_cols, bf_cols = build_const_host()
        self.const_f32_host = cf
        self.const_bf_host = cb
        self.cF = nc.dram_tensor("CONSTF", list(cf.shape), F32, kind="ExternalInput").ap()
        self.cB = nc.dram_tensor("CONSTB", list(cb.shape), BF16, kind="ExternalInput").ap()
        self.cf_sb = p.sbuf("cf_sb", list(cf.shape), F32)
        self.cb_sb = p.sbuf("cb_sb", list(cb.shape), BF16)
        self.f32_cols = f32_cols
        self.bf_cols = bf_cols
        self.x_res = p.sbuf("x_res", [128, self.NT * 1024], F32)
        self.xn_fm = None
        self.xn_stage = [p.sbuf(f"xn_stage{i}", [128, 1024], BF16) for i in range(2)]
        self.sq_scr = [p.sbuf("sq_scr0", [128, 1024], F32)]
        self.row_scr = p.sbuf("row_scr", [128, 256], F32)
        self.psF = [p.psum(f"psF{i}", [128, 512], F32) for i in range(6)]
        self.psT = [p.psum(f"psT{i}", [128, 512], BF16) for i in range(2)]
        self.pool = PsPool(self.psF)
        self.poolT = PsPool(self.psT)
        self.ld = p.sem("ld")
        self.stage_guard = [[], []]
        self.scr_last = [0]
        self._gb11 = p.sbuf("g_b11", [1, 1], BF16)
        self.rx = DmaRing(p, "rx", 6, "sync")
        self.rw = DmaRing(p, "rw", 10, "gpsimd")
        self.rs = DmaRing(p, "rs", 3, "sync")
        self.st = p.sem("st")
        self.te = p.sem("te")
        self.ve = p.sem("ve")
        self.se = p.sem("se")
        self.ge = p.sem("ge")

    def drain_rings(self):
        for ring, eng in ((self.rx, "sync"), (self.rs, "sync"), (self.rw, "gpsimd")):
            for s in ring.sems:
                if s.n > 0:
                    self.p.step(eng, None, wait=[(s, s.n)])
        if self.ld.n > 0:
            self.p.step("sync", None, wait=[(self.ld, self.ld.n)])
        if self.st.n > 0:
            self.p.step("sync", None, wait=[(self.st, self.st.n)])

    def fence(self):
        w = [(self.te, self.te.n), (self.ve, self.ve.n), (self.se, self.se.n),
             (self.ge, self.ge.n)]
        for ring in (self.rx, self.rw, self.rs):
            for s in ring.sems:
                if s.n > 0:
                    w.append((s, s.n))
        if self.ld.n > 0:
            w.append((self.ld, self.ld.n))
        if self.st.n > 0:
            w.append((self.st, self.st.n))
        return w

    def cfa(self, name):
        off, w = self.f32_cols[name]
        return self.cf_sb[:, off : off + w]

    def cba(self, name):
        off, w = self.bf_cols[name]
        return self.cb_sb[:, off : off + w]

    def load_consts(self):
        p = self.p
        p.step("sync", lambda e: e.dma_start(out=self.cf_sb[:], in_=self.cF), inc=self.ld, inc_k=16)
        p.step("sync", lambda e: e.dma_start(out=self.cb_sb[:], in_=self.cB), inc=self.ld, inc_k=16)
        self.const_ready = (self.ld, self.ld.n)
        for e in ENGINES:
            p.step(e, None, wait=[self.const_ready])






def ln_from_xres(ctx, ti, xwait, scr_last, dst_tile=None):
    p = ctx.p
    rs = ctx.row_scr
    col0 = (ti % 16) * 8
    c_sum = rs[:, col0 + 0 : col0 + 1]
    c_sumsq = rs[:, col0 + 1 : col0 + 2]
    c_mean = rs[:, col0 + 2 : col0 + 3]
    c_m2 = rs[:, col0 + 3 : col0 + 4]
    c_var = rs[:, col0 + 4 : col0 + 5]
    c_sd = rs[:, col0 + 5 : col0 + 6]
    c_rstd = rs[:, col0 + 6 : col0 + 7]
    xr = ctx.x_res[:, ti * 1024 : (ti + 1) * 1024]
    scr = ctx.sq_scr[0]
    sev = p.step(
        "scalar",
        lambda e, xr=xr, scr=scr, c_sumsq=c_sumsq: e.activation(
            scr[:], xr, AF.Square, accum_out=c_sumsq
        ),
        wait=[xwait, (ctx.se, scr_last[0])],
        inc=ctx.se,
    )
    scr_last[0] = sev
    vev = p.step(
        "vector",
        lambda e, xr=xr, c_sum=c_sum: e.reduce_sum(c_sum, xr, axis=AX.X),
        wait=[xwait],
        inc=ctx.ve,
    )
    vev = p.step(
        "vector",
        lambda e, c_mean=c_mean, c_sum=c_sum: e.tensor_scalar(
            c_mean, c_sum, 1.0 / 1024.0, None, ALU.mult
        ),
        wait=[(ctx.ve, vev)],
        inc=ctx.ve,
    )
    vev = p.step(
        "vector",
        lambda e, c_m2=c_m2, c_mean=c_mean: e.tensor_mul(c_m2, c_mean, c_mean),
        wait=[(ctx.ve, vev)],
        inc=ctx.ve,
    )
    vev = p.step(
        "vector",
        lambda e, c_var=c_var, c_sumsq=c_sumsq, c_m2=c_m2: e.scalar_tensor_tensor(
            c_var, c_sumsq, 1.0 / 1024.0, c_m2, ALU.mult, ALU.subtract
        ),
        wait=[(ctx.ve, vev), (ctx.se, sev)],
        inc=ctx.ve,
    )
    sev2 = p.step(
        "scalar",
        lambda e, c_sd=c_sd, c_var=c_var: e.activation(c_sd, c_var, AF.Sqrt, bias=ctx.cfa("s_eps_ln")[:, 0:1]),
        wait=[(ctx.ve, vev)],
        inc=ctx.se,
    )
    vev = p.step(
        "vector",
        lambda e, c_rstd=c_rstd, c_sd=c_sd: e.reciprocal(c_rstd, c_sd),
        wait=[(ctx.se, sev2)],
        inc=ctx.ve,
    )
    if dst_tile is None:
        stage = ctx.xn_stage[ti % 2]
        xt = stage[:, 0:1024]
    else:
        xt = dst_tile
    vev = p.step(
        "vector",
        lambda e, xr=xr, xt=xt, c_mean=c_mean, c_rstd=c_rstd: e.tensor_scalar(
            xt, xr, c_mean, c_rstd, ALU.subtract, ALU.mult
        ),
        wait=[(ctx.ve, vev)] + (ctx.stage_guard[ti % 2] if dst_tile is None else []),
        inc=ctx.ve,
    )
    if ctx.xn_fm is not None:
        ve_fm = tm_to_fm_src(ctx, xt, ti, ctx.xn_fm, (ctx.ve, vev))
    else:
        ve_fm = vev
    if dst_tile is None:
        ctx.stage_guard[ti % 2] = [(ctx.ve, ve_fm)]
    return [(ctx.se, sev), (ctx.ve, vev)]


def tm_to_fm_src(ctx, src_tile, ti, fm_buf, wait, nchunk=8):
    p = ctx.p
    T = ctx.T
    ident = ctx.cba("ident")
    fm3 = fm_buf.rearrange("p (c t) -> p c t", c=nchunk, t=T)
    vv = None
    for g in range(cdiv(nchunk, 4)):
        nin = min(4, nchunk - g * 4)
        bank, bw = ctx.poolT.acquire()
        ps = ctx.psT[bank]
        tw = bw + list([wait])
        tvs = None
        for c4 in range(nin):
            c = g * 4 + c4
            tvs = p.step(
                "tensor",
                lambda t, c=c, c4=c4, ps=ps: t.transpose(
                    ps[:, c4 * 128 : (c4 + 1) * 128],
                    src_tile[:, c * 128 : (c + 1) * 128],
                    ident,
                ),
                wait=tw,
                inc=ctx.te,
            )
            tw = ()
        dst = fm3[:, g * 4 : g * 4 + nin, ti * 128 : (ti + 1) * 128]
        src = ps.rearrange("p (c t) -> p c t", c=4, t=128)[:, 0:nin, :]
        vv = p.step(
            "vector",
            lambda v, dst=dst, src=src: v.tensor_copy(dst, src),
            wait=[(ctx.te, tvs)],
            inc=ctx.ve,
        )
        ctx.poolT.release(bank, (ctx.ve, vv))
    return vv


def ln_phase(ctx, x_src, prev_store, dst_fn=None):
    scr_last = ctx.scr_last
    for ti in range(ctx.NT):
        dst = dst_fn(ti) if dst_fn is not None else None
        if x_src is not None:
            xr = ctx.x_res[:, ti * 1024 : (ti + 1) * 1024]
            lw = [ctx.const_ready]
            if prev_store is not None:
                lw.append(prev_store)
            slot, wsp = ctx.rx.load(
                lambda e, ti=ti, xr=xr: e.dma_start(out=xr, in_=x_src[ti * 128 : (ti + 1) * 128, :]),
                wait=lw)
            consumed = ln_from_xres(ctx, ti, wsp, scr_last, dst_tile=dst)
            ctx.rx.ack(slot, *consumed)
        else:
            ln_from_xres(ctx, ti, (ctx.ve, ctx.ve.n), scr_last, dst_tile=dst)


def store_xres(ctx, x_dst):
    p = ctx.p
    NT = ctx.NT
    v = None
    for g in range(cdiv(NT, 4)):
        n = min(4, NT - g * 4)
        src = ctx.x_res.rearrange("p (ti d) -> p ti d", ti=NT, d=1024)[:, g * 4 : g * 4 + n, :]
        dst = x_dst[g * 512 : g * 512 + n * 128, :].rearrange("(ti p) d -> p ti d", p=128)
        v = p.step(
            "sync",
            lambda e, dst=dst, src=src: e.dma_start(out=dst, in_=src),
            wait=[(ctx.ve, ctx.ve.n), (ctx.se, ctx.se.n)],
            inc=ctx.st,
            inc_k=16,
        )
    return (ctx.st, v)




def load_w_cols(ctx, hbm, c0, w, dst_sb, wait=(), d_in=1024):
    nch = d_in // 128
    src = hbm[:, c0 : c0 + w].rearrange("(c p) w -> p c w", p=128)
    return ctx.rw.load(
        lambda e, src=src, dst=dst_sb, n=nch * w, nch=nch, w=w: e.dma_start(
            out=dst[:, 0:n].rearrange("p (c w) -> p c w", c=nch, w=w), in_=src
        ),
        wait=wait,
    )


def bcast_scalar(ctx, src11_bf, dst_f32, wait=()):
    p = ctx.p
    pi, pw = ctx.pool.acquire()
    ps = ctx.psF[pi]
    tv = p.step(
        "tensor",
        lambda t: t.matmul(ps[:, 0:1], ctx.cba("ones_row_b")[0:1, :], src11_bf, start=True, stop=True),
        wait=pw + list(wait),
        inc=ctx.te,
    )
    vv = p.step("vector", lambda v: v.tensor_copy(dst_f32, ps[:, 0:1]), wait=[(ctx.te, tv)], inc=ctx.ve)
    ctx.pool.release(pi, (ctx.ve, vv))
    return vv


def bcast_row(ctx, src_row_bf, n, dst, wait=(), dst_rows=128):
    p = ctx.p
    pi, pw = ctx.pool.acquire()
    ps = ctx.psF[pi]
    tv = p.step(
        "tensor",
        lambda t: t.matmul(ps[0:dst_rows, 0:n], ctx.cba("ones_row_b")[0:1, 0:dst_rows], src_row_bf, start=True, stop=True),
        wait=pw + list(wait),
        inc=ctx.te,
    )
    vv = p.step("vector", lambda v: v.tensor_copy(dst, ps[0:dst_rows, 0:n]), wait=[(ctx.te, tv)], inc=ctx.ve)
    ctx.pool.release(pi, (ctx.ve, vv))
    return vv




def causal_flash(ctx, sc, q_fm, k_fm, v_col, out_dst, hbase, hd, negc, rd_extra, qk_ready):
    p = ctx.p
    T, NT = ctx.T, ctx.NT
    probs = sc["probs"]
    probsT = sc["probsT"]
    arow = sc["arow"]
    probs_rel = sc.setdefault("probs_rel", [[] for _ in range(len(probs))])
    while len(probs_rel) < len(probs):
        probs_rel.append([])
    pvT_prev = sc.setdefault("pvT_prev", [])
    arow_guard = sc.setdefault("arow_guard", [[], []])
    ident = ctx.cba("ident")
    neg_low = ctx.cba("neg_low")

    def stage_a(tq):
        nblk = tq + 1
        nsl = cdiv(nblk, 4)
        aset = tq % 2
        ac0 = aset * 16
        dc = arow[:, ac0 : ac0 + nsl]
        sv_last = None
        exps = []
        for s in range(nsl):
            w = min(512, nblk * 128 - s * 512)
            slot = aset * 4 + s
            pi, pw = ctx.pool.acquire()
            ps = ctx.psF[pi]
            is_last = s == nsl - 1
            mm1 = p.step(
                "tensor",
                lambda t, ps=ps, w=w, s=s, tq=tq, il=is_last: t.matmul(
                    ps[:, 0:w],
                    q_fm[hbase : hbase + hd, tq * 128 : (tq + 1) * 128],
                    k_fm[hbase : hbase + hd, s * 512 : s * 512 + w],
                    start=True,
                    stop=not il,
                ),
                wait=pw + list(qk_ready),
                inc=None if is_last else ctx.te,
            )
            if is_last:
                jd = (tq % 4) * 128
                mm1 = p.step(
                    "tensor",
                    lambda t, ps=ps, jd=jd: t.matmul(
                        ps[:, jd : jd + 128], neg_low, ident, start=False, stop=True
                    ),
                    inc=ctx.te,
                )
            ew = [(ctx.te, mm1)] + probs_rel[slot]
            if s == 0:
                ew = ew + arow_guard[aset]
            sv_last = p.step(
                "scalar",
                lambda e, ps=ps, w=w, slot=slot, dcs=dc[:, s : s + 1]: e.activation(
                    probs[slot][:, 0:w], ps[:, 0:w], AF.Exp, bias=negc, accum_out=dcs
                ),
                wait=ew,
                inc=ctx.se,
            )
            ctx.pool.release(pi, (ctx.se, sv_last))
            exps.append(sv_last)
        return (tq, exps)

    def stage_b(st):
        tq, exps = st
        nblk = tq + 1
        nsl = cdiv(nblk, 4)
        aset = tq % 2
        ac0 = aset * 16
        dc = arow[:, ac0 : ac0 + nsl]
        tr_last = None
        for s in range(nsl):
            w = min(512, nblk * 128 - s * 512)
            slot = aset * 4 + s
            for j in range(w // 128):
                bank, bw = ctx.poolT.acquire()
                tw = bw + [(ctx.se, exps[s])]
                if s == 0 and j == 0:
                    tw = tw + pvT_prev
                tr_last = p.step(
                    "tensor",
                    lambda t, bank=bank, slot=slot, j=j: t.transpose(
                        ctx.psT[bank][:, 0:128], probs[slot][:, j * 128 : (j + 1) * 128], ident
                    ),
                    wait=tw,
                    inc=ctx.te,
                )
                vv = p.step(
                    "vector",
                    lambda v, bank=bank, c=s * 4 + j: v.tensor_copy(
                        probsT[:, c * 128 : (c + 1) * 128], ctx.psT[bank][:, 0:128]
                    ),
                    wait=[(ctx.te, tr_last)],
                    inc=ctx.ve,
                )
                ctx.poolT.release(bank, (ctx.ve, vv))
            probs_rel[slot] = [(ctx.te, tr_last)]
        c_d = arow[:, ac0 + 8 : ac0 + 9]
        c_rd = arow[:, ac0 + 9 : ac0 + 10]
        vv = p.step(
            "vector",
            lambda v, dc=dc, c_d=c_d: v.reduce_sum(c_d, dc, axis=AX.X),
            wait=[(ctx.se, exps[-1])],
            inc=ctx.ve,
        )
        vv = p.step("vector", lambda v, c_d=c_d, c_rd=c_rd: v.reciprocal(c_rd, c_d),
                    wait=[(ctx.ve, vv)], inc=ctx.ve)
        if rd_extra is not None:
            vv = p.step(
                "vector",
                lambda v, c_rd=c_rd: v.tensor_mul(c_rd, c_rd, rd_extra),
                wait=[(ctx.ve, vv)],
                inc=ctx.ve,
            )
        pi, pw = ctx.pool.acquire()
        ps = ctx.psF[pi]
        mmv = None
        for c in range(nblk):
            mw = []
            if c == 0:
                mw = pw + [(ctx.ve, ctx.ve.n)]
            mmv = p.step(
                "tensor",
                lambda t, ps=ps, c=c, nblk=nblk: t.matmul(
                    ps[:, 0:hd],
                    probsT[:, c * 128 : (c + 1) * 128],
                    v_col(c),
                    start=(c == 0),
                    stop=(c == nblk - 1),
                ),
                wait=mw,
                inc=ctx.te if c == nblk - 1 else None,
            )
        pvT_prev.clear()
        pvT_prev.append((ctx.te, mmv))
        vv2 = p.step(
            "vector",
            lambda v, ps=ps, tq=tq, c_rd=c_rd: v.tensor_scalar(
                out_dst(tq), ps[:, 0:hd], c_rd, None, ALU.mult
            ),
            wait=[(ctx.te, mmv), (ctx.ve, vv)],
            inc=ctx.ve,
        )
        ctx.pool.release(pi, (ctx.ve, vv2))
        arow_guard[aset] = [(ctx.ve, vv2)]

    pend = None
    for tq in range(NT):
        st = stage_a(tq)
        if pend is not None:
            stage_b(pend)
        pend = st
    stage_b(pend)


def head_norm_sqrtmax(ctx, sq_bf, hbase, hd, arow, tmp0, outcol, ready):
    p = ctx.p
    T = ctx.T
    nsl = cdiv(T, 512)
    ones = ctx.cba("ones_col128")
    vv = None
    for s in range(nsl):
        w = min(512, T - s * 512)
        pi, pw = ctx.pool.acquire()
        ps = ctx.psF[pi]
        tv = p.step(
            "tensor",
            lambda t, ps=ps, s=s, w=w: t.matmul(
                ps[0:1, 0:w],
                ones[hbase : hbase + hd, 0:1],
                sq_bf[hbase : hbase + hd, s * 512 : s * 512 + w],
                start=True,
                stop=True,
            ),
            wait=pw + list(ready),
            inc=ctx.te,
        )
        vv = p.step(
            "vector",
            lambda v, ps=ps, s=s, w=w: v.reduce_max(arow[0:1, tmp0 + s : tmp0 + s + 1], ps[0:1, 0:w], axis=AX.X),
            wait=[(ctx.te, tv)],
            inc=ctx.ve,
        )
        ctx.pool.release(pi, (ctx.ve, vv))
    vv = p.step(
        "vector",
        lambda v: v.reduce_max(arow[0:1, tmp0 + nsl : tmp0 + nsl + 1], arow[0:1, tmp0 : tmp0 + nsl], axis=AX.X),
        wait=[(ctx.ve, vv)],
        inc=ctx.ve,
    )
    sv = p.step(
        "scalar",
        lambda e: e.activation(arow[0:1, outcol : outcol + 1], arow[0:1, tmp0 + nsl : tmp0 + nsl + 1], AF.Sqrt),
        wait=[(ctx.ve, vv)],
        inc=ctx.se,
    )
    return sv




def l0_attn_batch(ctx, ws, sc, g_bc, gv):
    p = ctx.p
    T, NT = ctx.T, ctx.NT
    xn_fm = ctx.xn_fm
    attn_tm = sc["attn_tm"]
    qf = sc["q_pair"]
    kf = sc["k_pair"]
    vt = sc["v_pair"]
    sqb = sc["sq_pair"]
    arow = sc["arow"]
    negc = sc["negc"]
    b11 = sc["b11"]
    wbuf = sc["wq3"]
    xn_ready = sc["xn_ready"]
    for pair in range(sc.get("npairs", 8)):
        wslots = []
        for wi, wname in enumerate(("WQ", "WK", "WV")):
            dst = wbuf[:, wi * 1024 : (wi + 1) * 1024]
            slot, wsp = load_w_cols(ctx, ws[wname], pair * 128, 128, dst,
                                    wait=sc["wq3_guard"][wi])
            wslots.append((slot, wsp))
        wq = wbuf[:, 0:1024]
        wk = wbuf[:, 1024:2048]
        wv = wbuf[:, 2048:3072]
        q_done = k_done = None
        for which in range(2):
            wsl, wsp = wslots[which]
            wap = (wq, wk)[which]
            dstf = (qf, kf)[which]
            scale = (0.125, 1.0)[which]
            last = None
            for s in range(cdiv(T, 512)):
                w512 = min(512, T - s * 512)
                pi, pw = ctx.pool.acquire()
                ps = ctx.psF[pi]
                tv = None
                for c in range(8):
                    tv = p.step(
                        "tensor",
                        lambda t, ps=ps, c=c, s=s, wap=wap, w512=w512: t.matmul(
                            ps[:, 0:w512],
                            wap[:, c * 128 : (c + 1) * 128],
                            xn_fm[:, c * T + s * 512 : c * T + s * 512 + w512],
                            start=(c == 0),
                            stop=(c == 7),
                        ),
                        wait=(pw + [wsp] + xn_ready) if c == 0 else (),
                        inc=ctx.te if c == 7 else None,
                    )
                ev = p.step(
                    "scalar",
                    lambda e, ps=ps, s=s, dstf=dstf, scale=scale, w512=w512: e.activation(
                        dstf[:, s * 512 : s * 512 + w512], ps[:, 0:w512], AF.Copy, scale=scale
                    ),
                    wait=[(ctx.te, tv)],
                    inc=ctx.se,
                )
                ctx.pool.release(pi, (ctx.se, ev))
                last = ev
            if which == 0:
                q_done = last
            else:
                k_done = last
        ctx.rw.ack(wslots[0][0], (ctx.se, q_done))
        ctx.rw.ack(wslots[1][0], (ctx.se, k_done))
        sc["wq3_guard"][0] = [(ctx.se, q_done)]
        sc["wq3_guard"][1] = [(ctx.se, k_done)]
        v_done = None
        for ti in range(NT):
            pi, pw = ctx.pool.acquire()
            ps = ctx.psF[pi]
            tv = None
            for c in range(8):
                tv = p.step(
                    "tensor",
                    lambda t, ps=ps, c=c, ti=ti: t.matmul(
                        ps[:, 0:128],
                        xn_fm[:, c * T + ti * 128 : c * T + ti * 128 + 128],
                        wv[:, c * 128 : (c + 1) * 128],
                        start=(c == 0),
                        stop=(c == 7),
                    ),
                    wait=(pw + [wslots[2][1]] + xn_ready) if c == 0 else (),
                    inc=ctx.te if c == 7 else None,
                )
            v_done = p.step(
                "vector",
                lambda v, ps=ps, ti=ti: v.tensor_copy(vt[:, ti * 128 : (ti + 1) * 128], ps[:, 0:128]),
                wait=[(ctx.te, tv)],
                inc=ctx.ve,
            )
            ctx.pool.release(pi, (ctx.ve, v_done))
        ctx.rw.ack(wslots[2][0], (ctx.ve, v_done))
        sc["wq3_guard"][2] = [(ctx.ve, v_done)]
        if sc.get("attn_sub") == "qkv":
            sc["sq_guard"] = [(ctx.ve, ctx.ve.n)]
            continue
        nsub = sc.get("attn_sub") or ""
        sqv = p.step(
            "scalar",
            lambda e: e.activation(sqb[:], qf[:], AF.Square),
            wait=[(ctx.se, q_done)] + sc["sq_guard"],
            inc=ctx.se,
        )
        sqk = p.step(
            "scalar",
            lambda e, sq2=sc["sq_pair2"]: e.activation(sq2[:], kf[:], AF.Square),
            wait=[(ctx.se, k_done)],
            inc=ctx.se,
        )
        if nsub == "norms0":
            sc["sq_guard"] = [(ctx.se, ctx.se.n), (ctx.ve, ctx.ve.n)]
            continue
        for hh in range(2):
            hbase = hh * 64
            qm = head_norm_sqrtmax(ctx, sqb, hbase, 64, arow, 34, 40, [(ctx.se, sqv)])
            if nsub == "norms1":
                sc["sq_guard"] = [(ctx.se, ctx.se.n), (ctx.ve, ctx.ve.n)]
                continue
            km = head_norm_sqrtmax(ctx, sc["sq_pair2"], hbase, 64, arow, 34, 41, [(ctx.se, qm), (ctx.se, sqk)])
            vv = p.step(
                "vector",
                lambda v: v.tensor_mul(arow[0:1, 42:43], arow[0:1, 40:41], arow[0:1, 41:42]),
                wait=[(ctx.se, km)],
                inc=ctx.ve,
            )
            vv = p.step(
                "vector",
                lambda v: v.tensor_scalar(b11[0:1, 0:1], arow[0:1, 42:43], -1.0, None, ALU.mult),
                wait=[(ctx.ve, vv)],
                inc=ctx.ve,
            )
            bv = bcast_scalar(ctx, b11[0:1, 0:1], negc[:, hh : hh + 1],
                              wait=[(ctx.ve, vv), (ctx.se, ctx.se.n)])
            if sc.get("attn_sub") == "norms":
                continue
            habs = pair * 2 + hh
            sc_flash = sc
            vcol = lambda c, hbase=hbase: vt[:, c * 128 + hbase : c * 128 + hbase + 64]
            odst = lambda tq, habs=habs: attn_tm[:, tq * 1024 + habs * 64 : tq * 1024 + habs * 64 + 64]
            causal_flash(
                ctx, sc_flash, qf, kf, vcol, odst, hbase, 64,
                negc[:, hh : hh + 1], g_bc,
                qk_ready=[(ctx.se, q_done), (ctx.se, k_done), (ctx.ve, v_done), (ctx.ve, bv)],
            )
        sc["sq_guard"] = [(ctx.ve, ctx.ve.n)]
    return [(ctx.ve, ctx.ve.n)]


def fhn_g(ctx, frow, fhn_wait, g_bc):
    p = ctx.p
    I0 = float(0.1 + 0.9 / (1.0 + np.exp(-5.0)))
    TAU = 12.5
    c = lambda i: frow[0:1, i : i + 1]
    a_, b_, dt_ = c(0), c(1), c(2)
    v1, r1, n1, n2, d1, rd, w1 = c(3), c(4), c(5), c(6), c(7), c(8), c(9)
    t1, t2, s2, v2, g0, fac, g = c(10), c(11), c(12), c(13), c(14), c(15), c(5)
    vv = p.step("vector", lambda v: v.tensor_scalar(v1, dt_, I0, None, ALU.mult),
                wait=list(fhn_wait), inc=ctx.ve)
    seq = [
        lambda v: v.tensor_scalar(r1, dt_, 1.0 / TAU, None, ALU.mult),
        lambda v: v.tensor_add(n1, v1, a_),
        lambda v: v.tensor_mul(n2, r1, n1),
        lambda v: v.tensor_mul(d1, r1, b_),
        lambda v: v.tensor_scalar(d1, d1, 1.0, None, ALU.add),
        lambda v: v.reciprocal(rd, d1),
        lambda v: v.tensor_mul(w1, n2, rd),
        lambda v: v.tensor_mul(t1, v1, v1),
        lambda v: v.tensor_mul(t2, t1, v1),
        lambda v: v.scalar_tensor_tensor(t2, t2, -1.0 / 3.0, v1, ALU.mult, ALU.add),
        lambda v: v.tensor_sub(s2, t2, w1),
        lambda v: v.tensor_scalar(s2, s2, I0, None, ALU.add),
        lambda v: v.tensor_mul(s2, dt_, s2),
        lambda v: v.tensor_add(v2, v1, s2),
    ]
    for fn in seq:
        vv = p.step("vector", fn, wait=[(ctx.ve, vv)], inc=ctx.ve)
    sv = p.step("scalar", lambda e: e.activation(g0, v2, AF.Sigmoid),
                wait=[(ctx.ve, vv)], inc=ctx.se)
    vv = p.step("vector", lambda v: v.tensor_scalar(fac, g0, 0.5, 0.5, ALU.mult, ALU.add),
                wait=[(ctx.se, sv)], inc=ctx.ve)
    vv = p.step("vector", lambda v: v.tensor_scalar(d1, fac, 1e-8, None, ALU.add),
                wait=[(ctx.ve, vv)], inc=ctx.ve)
    vv = p.step("vector", lambda v: v.reciprocal(rd, d1), wait=[(ctx.ve, vv)], inc=ctx.ve)
    vv = p.step("vector", lambda v: v.tensor_mul(g, fac, rd), wait=[(ctx.ve, vv)], inc=ctx.ve)
    gb = ctx._gb11
    vv = p.step("vector", lambda v: v.tensor_copy(gb[0:1, 0:1], g), wait=[(ctx.ve, vv)], inc=ctx.ve)
    return bcast_scalar(ctx, gb[0:1, 0:1], g_bc, wait=[(ctx.ve, vv)])


def l0_ffn_gate_prep(ctx, ws, sc):
    p = ctx.p
    simt = sc["simt"]
    slot_t, w_t = ctx.rx.load(lambda e: e.dma_start(out=simt[:], in_=ws["SIMT"]),
                              wait=sc.get("simt_guard", []))
    sq = ctx.sq_scr[0]
    sv = p.step("scalar", lambda e: e.activation(sq[0:16, :], simt[:], AF.Square,
                                                 accum_out=sc["r16"][0:16, 0:1]),
                wait=[w_t, (ctx.se, sc["scr_last"][0])], inc=ctx.se)
    sc["scr_last"][0] = sv
    ctx.rx.ack(slot_t, (ctx.se, sv))
    sv2 = p.step("scalar", lambda e: e.activation(sc["r16"][0:16, 1:2], sc["r16"][0:16, 0:1], AF.Sqrt),
                 wait=[(ctx.se, sv)], inc=ctx.se)
    vv = p.step("vector", lambda v: v.tensor_scalar(sc["r16"][0:16, 1:2], sc["r16"][0:16, 1:2],
                                                    1e-12, None, ALU.max),
                wait=[(ctx.se, sv2)], inc=ctx.ve)
    vv = p.step("vector", lambda v: v.reciprocal(sc["r16"][0:16, 2:3], sc["r16"][0:16, 1:2]),
                wait=[(ctx.ve, vv)], inc=ctx.ve)
    vv = p.step("vector", lambda v: v.tensor_copy(sc["r16b"][0:16, 0:1], sc["r16"][0:16, 2:3]),
                wait=[(ctx.ve, vv)], inc=ctx.ve)
    bank, bw = ctx.poolT.acquire()
    tv = p.step("tensor", lambda t: t.transpose(ctx.psT[bank][0:1, 0:16], sc["r16b"][0:16, 0:1],
                                                ctx.cba("ident")[0:16, 0:16]),
                wait=bw + [(ctx.ve, vv)], inc=ctx.te)
    vv = p.step("vector", lambda v: v.tensor_copy(sc["row16b"][0:1, 0:16], ctx.psT[bank][0:1, 0:16]),
                wait=[(ctx.te, tv)], inc=ctx.ve)
    ctx.poolT.release(bank, (ctx.ve, vv))
    rcb = sc["rcb"]
    bv = bcast_row(ctx, sc["row16b"][0:1, 0:16], 16, rcb[:], wait=[(ctx.ve, vv)])
    simb = sc["simb"]
    slot_s, w_s = load_w_cols(ctx, ws["SIM"], 0, 16, simb, wait=sc.get("simt_guard", []))
    nv = None
    for cch in range(8):
        nv = p.step(
            "vector",
            lambda v, cch=cch: v.tensor_mul(
                sc["nsim"][:, cch * 16 : (cch + 1) * 16], simb[:, cch * 16 : (cch + 1) * 16], rcb[:]
            ),
            wait=[w_s, (ctx.ve, bv)],
            inc=ctx.ve,
        )
    ctx.rw.ack(slot_s, (ctx.ve, nv))
    slot_g, w_g = ctx.rx.load(lambda e: e.dma_start(out=sc["row16"][0:1, 0:16], in_=ws["GATES"]))
    sv = p.step("scalar", lambda e: e.activation(sc["row16"][0:1, 16:32], sc["row16"][0:1, 0:16], AF.Sigmoid),
                wait=[w_g], inc=ctx.se)
    ctx.rx.ack(slot_g, (ctx.se, sv))
    vv = p.step("vector", lambda v: v.tensor_copy(sc["row16b"][0:1, 16:32], sc["row16"][0:1, 16:32]),
                wait=[(ctx.se, sv)], inc=ctx.ve)
    bv = bcast_row(ctx, sc["row16b"][0:1, 16:32], 16, sc["sg_bc"][:], wait=[(ctx.ve, vv)])
    return [(ctx.ve, max(nv, bv))]


def l0_ffn_batch(ctx, ws, sc, gate_prep_ready):
    p = ctx.p
    T, NT = ctx.T, ctx.NT
    xn_fm = ctx.xn_fm
    xn_ready = sc["xn_ready"]
    rw_buf = sc["rw_buf"]
    gscr = sc["gscr"]
    grow = sc["grow"]
    gu8 = sc["gu8"]
    gate_guard = [[], []]
    for ti in range(NT):
        st0 = (ti % 2) * 80
        gr0 = (ti % 2) * 16
        lg = gscr[:, st0 : st0 + 16]
        m8 = gscr[:, st0 + 16 : st0 + 24]
        maskf = gscr[:, st0 + 24 : st0 + 40]
        e16 = gscr[:, st0 + 40 : st0 + 56]
        gated = gscr[:, st0 + 56 : st0 + 72]
        comb = gscr[:, st0 + 72 : st0 + 80]
        crn = grow[:, gr0 + 1 : gr0 + 2]
        ccnt = grow[:, gr0 + 2 : gr0 + 3]
        cia = grow[:, gr0 + 3 : gr0 + 4]
        cnegm = grow[:, gr0 + 4 : gr0 + 5]
        cssum = grow[:, gr0 + 5 : gr0 + 6]
        crws = grow[:, gr0 + 6 : gr0 + 7]
        combu = gu8[:, gr0 : gr0 + 16]
        col0 = (ti % 16) * 8
        c_var = ctx.row_scr[:, col0 + 4 : col0 + 5]
        c_rstd = ctx.row_scr[:, col0 + 6 : col0 + 7]
        vv = p.step("vector", lambda v, cr=crn, c_var=c_var, c_rstd=c_rstd: v.tensor_mul(cr, c_var, c_rstd),
                    wait=xn_ready + gate_guard[ti % 2], inc=ctx.ve)
        vv = p.step("vector", lambda v, cr=crn, c_rstd=c_rstd: v.tensor_mul(cr, cr, c_rstd),
                    wait=[(ctx.ve, vv)], inc=ctx.ve)
        vv = p.step("vector", lambda v, cr=crn: v.tensor_scalar(cr, cr, 1024.0, None, ALU.mult),
                    wait=[(ctx.ve, vv)], inc=ctx.ve)
        sv2 = p.step("scalar", lambda e, cr=crn: e.activation(cr, cr, AF.Sqrt),
                     wait=[(ctx.ve, vv)], inc=ctx.se)
        vv = p.step("vector", lambda v, cr=crn: v.tensor_scalar(cr, cr, 1e-12, None, ALU.max),
                    wait=[(ctx.se, sv2)], inc=ctx.ve)
        vv = p.step("vector", lambda v, cr=crn: v.reciprocal(cr, cr), wait=[(ctx.ve, vv)], inc=ctx.ve)
        pi, pw = ctx.pool.acquire()
        ps = ctx.psF[pi]
        tv = None
        for cch in range(8):
            tv = p.step(
                "tensor",
                lambda t, ps=ps, cch=cch, ti=ti: t.matmul(
                    ps[:, 0:16],
                    xn_fm[:, cch * T + ti * 128 : cch * T + ti * 128 + 128],
                    sc["nsim"][:, cch * 16 : (cch + 1) * 16],
                    start=(cch == 0),
                    stop=(cch == 7),
                ),
                wait=(pw + xn_ready + gate_prep_ready) if cch == 0 else (),
                inc=ctx.te if cch == 7 else None,
            )
        vv = p.step("vector",
                    lambda v, ps=ps, lg=lg, cr=crn: v.scalar_tensor_tensor(lg, ps[:, 0:16], cr, sc["sg_bc"][:], ALU.mult, ALU.subtract),
                    wait=[(ctx.te, tv), (ctx.ve, vv)], inc=ctx.ve)
        ctx.pool.release(pi, (ctx.ve, vv))
        vv = p.step("vector", lambda v, m8=m8, lg=lg: v.max(out=m8, in_=lg), wait=[(ctx.ve, vv)], inc=ctx.ve)
        vv = p.step("vector", lambda v, maskf=maskf, lg=lg: v.tensor_scalar(maskf, lg, 0.0, None, ALU.is_gt),
                    wait=[(ctx.ve, vv)], inc=ctx.ve)
        vv = p.step("vector", lambda v, ccnt=ccnt, maskf=maskf: v.reduce_sum(ccnt, maskf, axis=AX.X),
                    wait=[(ctx.ve, vv)], inc=ctx.ve)
        vv = p.step("vector", lambda v, cia=cia, ccnt=ccnt: v.tensor_scalar(cia, ccnt, 0.5, None, ALU.is_lt),
                    wait=[(ctx.ve, vv)], inc=ctx.ve)
        vv = p.step("vector", lambda v, e16=e16, lg=lg, m8=m8: v.tensor_scalar(e16, lg, m8[:, 1:2], None, ALU.is_ge),
                    wait=[(ctx.ve, vv)], inc=ctx.ve)
        vv = p.step("vector", lambda v, e16=e16, cia=cia, maskf=maskf: v.scalar_tensor_tensor(maskf, e16, cia, maskf, ALU.mult, ALU.add),
                    wait=[(ctx.ve, vv)], inc=ctx.ve)
        vv = p.step("vector", lambda v, combu=combu, maskf=maskf: v.tensor_copy(combu, maskf),
                    wait=[(ctx.ve, vv)], inc=ctx.ve)
        vv = p.step("vector", lambda v, gated=gated, lg=lg: v.tensor_scalar(gated, lg, 0.0, None, ALU.max),
                    wait=[(ctx.ve, vv)], inc=ctx.ve)
        nb = ctx.cfa("s_neg30")[:, 0:1].broadcast_to((128, 16))
        vv = p.step("vector", lambda v, lg=lg, combu=combu, gated=gated, nb=nb: v.select(lg, combu, gated, nb, add_drain=True),
                    wait=[(ctx.ve, vv)], inc=ctx.ve)
        vv = p.step("vector", lambda v, cnegm=cnegm, lg=lg: v.reduce_max(cnegm, lg, axis=AX.X, negate=True),
                    wait=[(ctx.ve, vv)], inc=ctx.ve)
        sv3 = p.step("scalar", lambda e, e16=e16, lg=lg, cnegm=cnegm, cssum=cssum: e.activation(e16, lg, AF.Exp, bias=cnegm, accum_out=cssum),
                     wait=[(ctx.ve, vv)], inc=ctx.se)
        vv = p.step("vector", lambda v, crws=crws, cssum=cssum: v.reciprocal(crws, cssum),
                    wait=[(ctx.se, sv3)], inc=ctx.ve)
        vv = p.step("vector", lambda v, ti=ti, e16=e16, crws=crws: v.tensor_scalar(rw_buf[:, ti * 16 : (ti + 1) * 16], e16, crws, None, ALU.mult),
                    wait=[(ctx.ve, vv)], inc=ctx.ve)
        gate_guard[ti % 2] = [(ctx.ve, vv), (ctx.se, sv3)]
    rw_ready = [(ctx.ve, ctx.ve.n)]
    q1, k1, v1 = sc["q1"], sc["k1"], sc["v1"]
    wp = sc["wp"]
    mixtmp = sc["mixtmp"]
    mt_guard = []
    wp_last = [None, None, None]
    wp_guard = sc.get("wp_guard", [])
    for wi in range(3):
        wname = ("QP", "KP", "VP")[wi]
        wslot, wsp = load_w_cols(ctx, ws[wname], 0, 1024, wp, wait=wp_guard)
        for ti in range(NT):
            pis = []
            for half in range(2):
                pi, pw = ctx.pool.acquire()
                ps = ctx.psF[pi]
                tv = None
                for cch in range(8):
                    tv = p.step(
                        "tensor",
                        lambda t, ps=ps, cch=cch, ti=ti, half=half: t.matmul(
                            ps[:],
                            xn_fm[:, cch * T + ti * 128 : cch * T + ti * 128 + 128],
                            wp[:, cch * 1024 + half * 512 : cch * 1024 + half * 512 + 512],
                            start=(cch == 0),
                            stop=(cch == 7),
                        ),
                        wait=(pw + [wsp] + xn_ready) if cch == 0 else (),
                        inc=ctx.te if cch == 7 else None,
                    )
                pis.append((pi, tv))
            vv = None
            for half in range(2):
                pi, tv = pis[half]
                ps = ctx.psF[pi]
                rw3 = rw_buf[:, ti * 16 : (ti + 1) * 16].rearrange("p (one e) -> p one e", one=1, e=16).broadcast_to((128, 32, 16))
                t3 = mixtmp[:, half * 512 : (half + 1) * 512].rearrange("p (d e) -> p d e", d=32, e=16)
                p3 = ps.rearrange("p (d e) -> p d e", d=32, e=16)
                vv = p.step("vector",
                            lambda v, t3=t3, p3=p3, rw3=rw3: v.tensor_tensor(t3, p3, rw3, ALU.mult),
                            wait=[(ctx.te, tv)] + mt_guard + rw_ready, inc=ctx.ve)
                ctx.pool.release(pi, (ctx.ve, vv))
            dst = (q1, k1, v1)[wi]
            scale = 0.125 if wi == 0 else 1.0
            m3 = mixtmp.rearrange("p (d e) -> p d e", d=64, e=16)
            mr = sc["mixred"]
            o3 = mr.rearrange("p (d one) -> p d one", d=64, one=1)
            vv = p.step("vector",
                        lambda v, o3=o3, m3=m3: v.tensor_reduce(o3, m3, AX.X, ALU.add),
                        wait=[(ctx.ve, vv)], inc=ctx.ve)
            vv = p.step("vector",
                        lambda v, dst=dst, ti=ti, mr=mr, scale=scale: v.tensor_scalar(
                            dst[:, ti * 64 : (ti + 1) * 64], mr[:], scale, None, ALU.mult),
                        wait=[(ctx.ve, vv)], inc=ctx.ve)
            mt_guard = [(ctx.ve, vv)]
            wp_last[wi] = vv
        ctx.rw.ack(wslot, (ctx.te, ctx.te.n))
        wp_guard = [(ctx.te, ctx.te.n)]
    sc["wp_guard"] = [(ctx.te, ctx.te.n)]
    q1f, k1f = sc["q1f"], sc["k1f"]
    prs = []
    for ti in range(NT):
        prs.append((q1[:, ti * 64 : (ti + 1) * 64], q1f[0:64, ti * 128 : (ti + 1) * 128]))
        prs.append((k1[:, ti * 64 : (ti + 1) * 64], k1f[0:64, ti * 128 : (ti + 1) * 128]))
    tr_done = pe_transpose_pairs(ctx, prs, in_w=64, wait=[(ctx.ve, ctx.ve.n)])
    arow = sc["arow"]
    sqv = p.step("scalar", lambda e: e.activation(sc["sq_pair"][0:64, :], q1f[0:64, :], AF.Square),
                 wait=[(ctx.ve, tr_done)] + sc["sq_guard"], inc=ctx.se)
    sqk = p.step("scalar", lambda e: e.activation(sc["sq_pair2"][0:64, :], k1f[0:64, :], AF.Square),
                 wait=[(ctx.ve, tr_done)], inc=ctx.se)
    qm = head_norm_sqrtmax(ctx, sc["sq_pair"], 0, 64, arow, 34, 40, [(ctx.se, sqv)])
    km = head_norm_sqrtmax(ctx, sc["sq_pair2"], 0, 64, arow, 34, 41, [(ctx.se, qm), (ctx.se, sqk)])
    vv = p.step("vector", lambda v: v.tensor_mul(arow[0:1, 42:43], arow[0:1, 40:41], arow[0:1, 41:42]),
                wait=[(ctx.se, km)], inc=ctx.ve)
    vv = p.step("vector", lambda v: v.tensor_scalar(sc["b11"][0:1, 0:1], arow[0:1, 42:43], -1.0, None, ALU.mult),
                wait=[(ctx.ve, vv)], inc=ctx.ve)
    bv = bcast_scalar(ctx, sc["b11"][0:1, 0:1], sc["negc"][:, 0:1], wait=[(ctx.ve, vv), (ctx.se, ctx.se.n)])
    sc["sq_guard"] = [(ctx.ve, ctx.ve.n)]
    attn1 = sc["attn1"]
    vcol = lambda cblk: v1[:, cblk * 64 : (cblk + 1) * 64]
    odst = lambda tq: attn1[:, tq * 64 : (tq + 1) * 64]
    causal_flash(ctx, sc, q1f, k1f, vcol, odst, 0, 64, sc["negc"][:, 0:1], None,
                 qk_ready=[(ctx.ve, tr_done), (ctx.ve, bv)])
    attn1_ready = [(ctx.ve, ctx.ve.n)]
    wo = sc["wo_ffn"]
    slot_o, w_o = load_w_cols(ctx, ws["OP"], 0, 1024, wo, wait=sc.get("woffn_guard", []))
    mixb = sc["mixb"]
    mixf = sc["mixf"]
    wo_last = None
    mb_guard = []
    for ti in range(NT):
        a3 = attn1[:, ti * 64 : (ti + 1) * 64].rearrange("p (d one) -> p d one", d=64, one=1).broadcast_to((128, 64, 16))
        rw3 = rw_buf[:, ti * 16 : (ti + 1) * 16].rearrange("p (one e) -> p one e", one=1, e=16).broadcast_to((128, 64, 16))
        o3 = mixb.rearrange("p (d e) -> p d e", d=64, e=16)
        vv = p.step("vector", lambda v, o3=o3, a3=a3, rw3=rw3: v.tensor_tensor(o3, a3, rw3, ALU.mult),
                    wait=attn1_ready + mb_guard, inc=ctx.ve)
        tr = pe_transpose_pairs(
            ctx,
            [(mixb[:, cc * 128 : (cc + 1) * 128], mixf[:, cc * 128 : (cc + 1) * 128]) for cc in range(8)],
            wait=[(ctx.ve, vv)],
        )
        adds = []
        for half in range(2):
            pi, pw = ctx.pool.acquire()
            ps = ctx.psF[pi]
            tv = None
            for cch in range(8):
                tv = p.step(
                    "tensor",
                    lambda t, ps=ps, cch=cch, half=half: t.matmul(
                        ps[:],
                        mixf[:, cch * 128 : (cch + 1) * 128],
                        wo[:, cch * 1024 + half * 512 : cch * 1024 + half * 512 + 512],
                        start=(cch == 0),
                        stop=(cch == 7),
                    ),
                    wait=(pw + [w_o, (ctx.ve, tr)]) if cch == 0 else (),
                    inc=ctx.te if cch == 7 else None,
                )
            xr = ctx.x_res[:, ti * 1024 + half * 512 : ti * 1024 + half * 512 + 512]
            vv2 = p.step("vector", lambda v, xr=xr, ps=ps: v.tensor_add(xr, ps[:], xr),
                         wait=[(ctx.te, tv), (ctx.ve, ctx.ve.n)], inc=ctx.ve)
            ctx.pool.release(pi, (ctx.ve, vv2))
            adds.append(vv2)
            wo_last = vv2
        mb_guard = [(ctx.ve, adds[-1])]
    ctx.rw.ack(slot_o, (ctx.ve, wo_last))
    sc["woffn_guard"] = [(ctx.te, ctx.te.n)]
    return [(ctx.ve, ctx.ve.n)]


def pe_transpose_pairs(ctx, pairs, in_w=128, wait=()):
    p = ctx.p
    ident = ctx.cba("ident")
    vv = None
    first = True
    for src, dst in pairs:
        bank, bw = ctx.poolT.acquire()
        tw = bw + (list(wait) if first else [])
        first = False
        tv = p.step("tensor",
                    lambda t, bank=bank, src=src: t.transpose(ctx.psT[bank][0:in_w, 0:128], src, ident),
                    wait=tw, inc=ctx.te)
        vv = p.step("vector",
                    lambda v, bank=bank, dst=dst: v.tensor_copy(dst, ctx.psT[bank][0:in_w, 0:128]),
                    wait=[(ctx.te, tv)], inc=ctx.ve)
        ctx.poolT.release(bank, (ctx.ve, vv))
    return vv


def l0_wout(ctx, ws, sc, attn_ready):
    p = ctx.p
    T, NT = ctx.T, ctx.NT
    wo = sc["wout"]
    slot, w_o = load_w_cols(ctx, ws["WOUT"], 0, 1024, wo, wait=sc.get("wout_guard", []))
    afm = sc["mixf"]
    last = None
    guard = []
    for ti in range(NT):
        tr = pe_transpose_pairs(
            ctx,
            [(sc["attn_tm"][:, ti * 1024 + cc * 128 : ti * 1024 + (cc + 1) * 128],
              afm[:, cc * 128 : (cc + 1) * 128]) for cc in range(8)],
            wait=list(attn_ready) + guard,
        )
        adds = []
        for half in range(2):
            pi, pw = ctx.pool.acquire()
            ps = ctx.psF[pi]
            tv = None
            for cch in range(8):
                tv = p.step(
                    "tensor",
                    lambda t, ps=ps, cch=cch, half=half: t.matmul(
                        ps[:],
                        afm[:, cch * 128 : (cch + 1) * 128],
                        wo[:, cch * 1024 + half * 512 : cch * 1024 + half * 512 + 512],
                        start=(cch == 0),
                        stop=(cch == 7),
                    ),
                    wait=(pw + [w_o, (ctx.ve, tr)]) if cch == 0 else (),
                    inc=ctx.te if cch == 7 else None,
                )
            xr = ctx.x_res[:, ti * 1024 + half * 512 : ti * 1024 + half * 512 + 512]
            vv2 = p.step("vector", lambda v, xr=xr, ps=ps: v.tensor_add(xr, ps[:], xr),
                         wait=[(ctx.te, tv)], inc=ctx.ve)
            ctx.pool.release(pi, (ctx.ve, vv2))
            adds.append(vv2)
            last = vv2
        guard = [(ctx.te, ctx.te.n)]
    ctx.rw.ack(slot, (ctx.ve, last))
    sc["wout_guard"] = [(ctx.te, ctx.te.n)]
    return [(ctx.ve, ctx.ve.n)]


def layer0(ctx, ws, x_src, x_dst, prev_store=None, parts="all"):
    p = ctx.p
    NT = ctx.NT
    les = ExitStack()
    sc = {}
    ctx.xn_fm = p.sbuf("l0_xnfm", [128, 8 * ctx.T], BF16, es=les)
    sc["arow"] = p.sbuf("l0_arow", [128, 48], F32, es=les)
    sc["negc"] = p.sbuf("l0_negc", [128, 2], F32, es=les)
    sc["b11"] = p.sbuf("l0_b11", [1, 4], BF16, es=les)
    sc["frow"] = p.sbuf("l0_frow", [1, 16], F32, es=les)
    sc["g_bc"] = p.sbuf("l0_gbc", [128, 1], F32, es=les)
    sc["sq_guard"] = []
    sc["wq3_guard"] = [[], [], []]
    sc["scr_last"] = ctx.scr_last
    slot_f, w_f = ctx.rx.load(lambda e: e.dma_start(out=sc["frow"][0:1, 0:3], in_=ws["FHN"]))
    gv = fhn_g(ctx, sc["frow"], [w_f], sc["g_bc"][:, 0:1])
    ctx.rx.ack(slot_f, (ctx.ve, ctx.ve.n))
    ln_phase(ctx, x_src, prev_store)
    sc["xn_ready"] = [(ctx.ve, ctx.ve.n), (ctx.se, ctx.se.n)]
    fence0 = ctx.fence()
    pes = ExitStack()
    sc["attn_tm"] = p.sbuf("l0_attn_tm", [128, ctx.NT * 1024], BF16, es=pes)
    sc["q_pair"] = p.sbuf("l0_qp", [128, ctx.T], BF16, es=pes)
    sc["k_pair"] = p.sbuf("l0_kp", [128, ctx.T], BF16, es=pes)
    sc["v_pair"] = p.sbuf("l0_vp", [128, ctx.NT * 128], BF16, es=pes)
    sc["sq_pair"] = p.sbuf("l0_sq", [128, ctx.T], BF16, es=pes)
    sc["sq_pair2"] = p.sbuf("l0_sq2", [128, ctx.T], BF16, es=pes)
    sc["probs"] = [p.sbuf(f"l0_probs{i}", [128, 512], BF16, es=pes) for i in range(8)]
    sc["probsT"] = p.sbuf("l0_probsT", [128, ctx.T], BF16, es=pes)
    sc["wq3"] = p.sbuf("l0_wq3", [128, 3 * 1024], BF16, es=pes)
    sc["wout"] = p.sbuf("l0_wout", [128, 8192], BF16, es=pes)
    sc["mixf"] = p.sbuf("l0_afm", [128, 1024], BF16, es=pes)
    sc["probs_rel"] = [list(fence0) for _ in range(8)]
    sc["pvT_prev"] = list(fence0)
    sc["arow_guard"] = [list(fence0), list(fence0)]
    sc["wq3_guard"] = [list(fence0), list(fence0), list(fence0)]
    sc["wout_guard"] = list(fence0)
    sc["fence0"] = fence0
    if parts in ("all", "attn", "attn1p"):
        if "1p" in parts:
            sc["npairs"] = 1
        if "qkv" in parts:
            sc["attn_sub"] = "qkv"
        if "norms" in parts:
            sc["attn_sub"] = parts.split("_")[-1]
        attn_ready = l0_attn_batch(ctx, ws, sc, sc["g_bc"][:, 0:1], gv)
        if sc.get("attn_sub") is None:
            l0_wout(ctx, ws, sc, attn_ready)
    pes.close()
    fence1 = ctx.fence()
    pes2 = ExitStack()
    sc["simt"] = p.sbuf("l0_simt", [16, 1024], F32, es=pes2)
    sc["r16"] = p.sbuf("l0_r16", [16, 4], F32, es=pes2)
    sc["r16b"] = p.sbuf("l0_r16b", [16, 2], BF16, es=pes2)
    sc["row16"] = p.sbuf("l0_row16", [1, 32], F32, es=pes2)
    sc["row16b"] = p.sbuf("l0_row16b", [1, 32], BF16, es=pes2)
    sc["rcb"] = p.sbuf("l0_rcb", [128, 16], BF16, es=pes2)
    sc["simb"] = p.sbuf("l0_simb", [128, 128], BF16, es=pes2)
    sc["nsim"] = p.sbuf("l0_nsim", [128, 128], BF16, es=pes2)
    sc["sg_bc"] = p.sbuf("l0_sgbc", [128, 16], F32, es=pes2)
    sc["rw_buf"] = p.sbuf("l0_rw", [128, ctx.NT * 16], F32, es=pes2)
    sc["gscr"] = p.sbuf("l0_gscr", [128, 160], F32, es=pes2)
    sc["grow"] = p.sbuf("l0_grow", [128, 32], F32, es=pes2)
    sc["gu8"] = p.sbuf("l0_gu8", [128, 32], U8, es=pes2)
    sc["q1"] = p.sbuf("l0_q1", [128, ctx.NT * 64], BF16, es=pes2)
    sc["k1"] = p.sbuf("l0_k1", [128, ctx.NT * 64], BF16, es=pes2)
    sc["v1"] = p.sbuf("l0_v1", [128, ctx.NT * 64], BF16, es=pes2)
    sc["q1f"] = p.sbuf("l0_q1f", [128, ctx.T], BF16, es=pes2)
    sc["k1f"] = p.sbuf("l0_k1f", [128, ctx.T], BF16, es=pes2)
    sc["sq_pair"] = p.sbuf("l0_fsq", [128, ctx.T], BF16, es=pes2)
    sc["sq_pair2"] = p.sbuf("l0_fsq2", [128, ctx.T], BF16, es=pes2)
    sc["attn1"] = p.sbuf("l0_attn1", [128, ctx.NT * 64], BF16, es=pes2)
    sc["wp"] = p.sbuf("l0_wp", [128, 8192], BF16, es=pes2)
    sc["wo_ffn"] = p.sbuf("l0_woffn", [128, 8192], BF16, es=pes2)
    sc["mixtmp"] = p.sbuf("l0_mixtmp", [128, 1024], F32, es=pes2)
    sc["mixred"] = p.sbuf("l0_mixred", [128, 64], F32, es=pes2)
    sc["mixb"] = p.sbuf("l0_mixb", [128, 1024], BF16, es=pes2)
    sc["mixf"] = p.sbuf("l0_mixf", [128, 1024], BF16, es=pes2)
    sc["probs"] = [p.sbuf(f"l0_fprobs{i}", [128, 512], BF16, es=pes2) for i in range(8)]
    sc["probsT"] = p.sbuf("l0_fprobsT", [128, ctx.T], BF16, es=pes2)
    sc["probs_rel"] = [list(fence1) for _ in range(8)]
    sc["pvT_prev"] = list(fence1)
    sc["arow_guard"] = [list(fence1), list(fence1)]
    sc["sq_guard"] = list(fence1)
    sc["wp_guard"] = list(fence1)
    sc["woffn_guard"] = list(fence1)
    sc["simt_guard"] = list(fence1)
    if parts in ("all", "ffn"):
        gate_prep_ready = l0_ffn_gate_prep(ctx, ws, sc)
        l0_ffn_batch(ctx, ws, sc, gate_prep_ready)
    pes2.close()
    les.close()
    ctx.xn_fm = None
    if x_dst is None:
        return None
    return store_xres(ctx, x_dst)




def lse_logp_tiles(ctx, sc, xn_tm, logp_fm):
    p = ctx.p
    NT, T = ctx.NT, ctx.T
    lrow = sc["lrow"]
    escr = sc["escr"]
    eguard = []
    last = None
    for ti in range(NT):
        c0 = (ti % 8) * 4
        negm = lrow[:, c0 : c0 + 1]
        ssum = lrow[:, c0 + 1 : c0 + 2]
        lse = lrow[:, c0 + 2 : c0 + 3]
        xt = xn_tm[:, ti * 1024 : (ti + 1) * 1024]
        g = [(ctx.ve, sc["lrow_g"][ti % 8])] if sc["lrow_g"][ti % 8] else []
        vv = p.step("vector", lambda v, xt=xt, negm=negm: v.reduce_max(negm, xt, axis=AX.X, negate=True),
                    wait=sc["xn_ready"] + g, inc=ctx.ve)
        sv = p.step("scalar", lambda e, xt=xt, negm=negm, ssum=ssum: e.activation(
            escr[:], xt, AF.Exp, bias=negm, accum_out=ssum), wait=[(ctx.ve, vv)] + eguard, inc=ctx.se)
        eguard = [(ctx.se, sv)]
        sv2 = p.step("scalar", lambda e, ssum=ssum, lse=lse: e.activation(lse, ssum, AF.Ln),
                     wait=[(ctx.se, sv)], inc=ctx.se)
        vv = p.step("vector", lambda v, lse=lse, negm=negm: v.tensor_sub(lse, lse, negm),
                    wait=[(ctx.se, sv2)], inc=ctx.ve)
        lt = ctx.xn_stage[ti % 2][:, 0:1024]
        vv = p.step("vector", lambda v, lt=lt, xt=xt, lse=lse: v.tensor_scalar(lt, xt, lse, None, ALU.subtract),
                    wait=[(ctx.ve, vv)] + ctx.stage_guard[ti % 2], inc=ctx.ve)
        last = tm_to_fm_src(ctx, lt, ti, logp_fm, (ctx.ve, vv))
        ctx.stage_guard[ti % 2] = [(ctx.ve, last)]
        sc["lrow_g"][ti % 8] = last
    return [(ctx.ve, ctx.ve.n), (ctx.se, ctx.se.n)]


def l1_attn_batch(ctx, sc, logp_ready):
    p = ctx.p
    NT, T = ctx.NT, ctx.T
    logp_fm = sc["logp_fm"]
    xn_tm = sc["xn_tm"]
    probs = sc["probs"]
    probsT = sc["probsT"]
    arow = sc["arow"]
    probs_rel = sc["probs_rel"]
    pvT_prev = sc["pvT_prev"]
    arow_guard = sc["arow_guard"]
    ident = ctx.cba("ident")
    ptile = sc["ptile"]
    pt_guard = [list(sc["fence0"]), list(sc["fence0"])]
    for tq in range(NT):
        aset = tq % 2
        ac0 = aset * 16
        nsl = cdiv(NT * 128, 512)
        dc = arow[:, ac0 : ac0 + nsl]
        pt = ptile[aset]
        pev = None
        for c in range(8):
            pev = p.step("scalar",
                         lambda e, c=c, tq=tq, pt=pt: e.activation(
                             pt[:, c * 128 : (c + 1) * 128],
                             logp_fm[:, c * T + tq * 128 : c * T + tq * 128 + 128], AF.Exp),
                         wait=(logp_ready + pt_guard[aset]) if c == 0 else (), inc=ctx.se)
        sv_last = None
        tr_last = None
        for s in range(nsl):
            w = min(512, NT * 128 - s * 512)
            pi, pw = ctx.pool.acquire()
            ps = ctx.psF[pi]
            tv = None
            for c in range(8):
                tv = p.step(
                    "tensor",
                    lambda t, ps=ps, c=c, s=s, w=w, pt=pt, tq=tq: t.matmul(
                        ps[:, 0:w],
                        pt[:, c * 128 : (c + 1) * 128],
                        logp_fm[:, c * T + s * 512 : c * T + s * 512 + w],
                        start=(c == 0),
                        stop=(c == 7),
                    ),
                    wait=(pw + [(ctx.se, pev)]) if c == 0 else (),
                    inc=ctx.te if c == 7 else None,
                )
            ew = [(ctx.te, tv)] + probs_rel[s % 2]
            if s == 0:
                ew += arow_guard[aset]
            sv_last = p.step(
                "scalar",
                lambda e, ps=ps, w=w, slot=s % 2, dcs=dc[:, s : s + 1]: e.activation(
                    probs[slot][:, 0:w], ps[:, 0:w], AF.Exp, bias=0.0, accum_out=dcs
                ),
                wait=ew,
                inc=ctx.se,
            )
            ctx.pool.release(pi, (ctx.se, sv_last))
            for j in range(w // 128):
                bank, bw = ctx.poolT.acquire()
                tw = bw + [(ctx.se, sv_last)]
                if s == 0 and j == 0:
                    tw += pvT_prev
                tr_last = p.step(
                    "tensor",
                    lambda t, bank=bank, slot=s % 2, j=j: t.transpose(
                        ctx.psT[bank][:, 0:128], probs[slot][:, j * 128 : (j + 1) * 128], ident
                    ),
                    wait=tw,
                    inc=ctx.te,
                )
                vv = p.step(
                    "vector",
                    lambda v, bank=bank, c=s * 4 + j: v.tensor_copy(
                        probsT[:, c * 128 : (c + 1) * 128], ctx.psT[bank][:, 0:128]
                    ),
                    wait=[(ctx.te, tr_last)],
                    inc=ctx.ve,
                )
                ctx.poolT.release(bank, (ctx.ve, vv))
            probs_rel[s % 2] = [(ctx.te, tr_last)]
        c_d = arow[:, ac0 + 8 : ac0 + 9]
        c_rd = arow[:, ac0 + 9 : ac0 + 10]
        vv = p.step("vector", lambda v, dc=dc, c_d=c_d: v.reduce_sum(c_d, dc, axis=AX.X),
                    wait=[(ctx.se, sv_last)], inc=ctx.ve)
        vv = p.step("vector", lambda v, c_d=c_d, c_rd=c_rd: v.reciprocal(c_rd, c_d),
                    wait=[(ctx.ve, vv)], inc=ctx.ve)
        evs = []
        for sl in range(2):
            pi, pw = ctx.pool.acquire()
            ps = ctx.psF[pi]
            mmv = None
            for c in range(NT):
                mw = []
                if c == 0:
                    mw = pw + [(ctx.ve, ctx.ve.n)]
                mmv = p.step(
                    "tensor",
                    lambda t, ps=ps, c=c, sl=sl: t.matmul(
                        ps[:],
                        probsT[:, c * 128 : (c + 1) * 128],
                        xn_tm[:, c * 1024 + sl * 512 : c * 1024 + sl * 512 + 512],
                        start=(c == 0),
                        stop=(c == NT - 1),
                    ),
                    wait=mw,
                    inc=ctx.te if c == NT - 1 else None,
                )
            xr = ctx.x_res[:, tq * 1024 + sl * 512 : tq * 1024 + sl * 512 + 512]
            ev = p.step(
                "vector",
                lambda v, xr=xr, ps=ps, c_rd=c_rd: v.scalar_tensor_tensor(
                    xr, ps[:], c_rd, xr, ALU.mult, ALU.add
                ),
                wait=[(ctx.te, mmv), (ctx.ve, vv)],
                inc=ctx.ve,
            )
            ctx.pool.release(pi, (ctx.ve, ev))
            evs.append(ev)
        pvT_prev.clear()
        pvT_prev.extend([(ctx.te, ctx.te.n)])
        arow_guard[aset] = [(ctx.ve, evs[-1])]
        pt_guard[aset] = [(ctx.te, ctx.te.n)]
    return [(ctx.ve, ctx.ve.n)]


def quant_tile(ctx, sc, src_ap, W, dst_ap, wait, use_gp=USE_GP_QUANT):
    p = ctx.p
    qa, sgn, acc, t1, t2 = sc["q_qa"], sc["q_sgn"], sc["q_acc"], sc["q_t1"], sc["q_t2"]
    amax, scal, rscal = sc["q_amax"], sc["q_scal"], sc["q_rscal"]
    nb = W // 16
    g = sc["q_guard"]
    sv1 = p.step("scalar", lambda e: e.activation(qa[:, 0:W], src_ap, AF.Abs),
                 wait=list(wait) + g, inc=ctx.se)
    sv2 = p.step("scalar", lambda e: e.activation(sgn[:, 0:W], src_ap, AF.Sign),
                 wait=[(ctx.se, sv1)], inc=ctx.se)
    a3 = qa[:, 0:W].rearrange("p (b k) -> p b k", b=nb, k=16)
    am3 = amax[:, 0:nb].rearrange("p (b one) -> p b one", b=nb, one=1)
    vv = p.step("vector", lambda v: v.tensor_reduce(am3, a3, AX.X, ALU.max),
                wait=[(ctx.se, sv1)] + g, inc=ctx.ve)
    vv = p.step("vector", lambda v: v.tensor_scalar(scal[:, 0:nb], amax[:, 0:nb], 1.0 / 6.0, 1e-12, ALU.mult, ALU.max),
                wait=[(ctx.ve, vv)], inc=ctx.ve)
    vv = p.step("vector", lambda v: v.reciprocal(rscal[:, 0:nb], scal[:, 0:nb]),
                wait=[(ctx.ve, vv)], inc=ctx.ve)
    r3 = rscal[:, 0:nb].rearrange("p (b one) -> p b one", b=nb, one=1).broadcast_to((128, nb, 16))
    vv = p.step("vector", lambda v: v.tensor_tensor(a3, a3, r3, ALU.mult),
                wait=[(ctx.ve, vv)], inc=ctx.ve)
    eng = "gpsimd" if use_gp else "vector"
    esem = ctx.ge if use_gp else ctx.ve
    gv = p.step(eng, lambda v: v.tensor_scalar(acc[:, 0:W], qa[:, 0:W], 0.25, None, ALU.is_gt),
                wait=[(ctx.ve, vv)], inc=esem)
    for m in (0.75, 1.25, 1.75, 2.5, 3.5, 5.0):
        gv = p.step(eng, lambda v, m=m: v.scalar_tensor_tensor(acc[:, 0:W], qa[:, 0:W], m, acc[:, 0:W], ALU.is_gt, ALU.add),
                    wait=[(esem, gv)], inc=esem)
    gv = p.step(eng, lambda v: v.tensor_scalar(t1[:, 0:W], acc[:, 0:W], 4.0, 0.0, ALU.subtract, ALU.max),
                wait=[(esem, gv)], inc=esem)
    gv = p.step(eng, lambda v: v.tensor_scalar(t2[:, 0:W], acc[:, 0:W], 6.0, 0.0, ALU.subtract, ALU.max),
                wait=[(esem, gv)], inc=esem)
    gv = p.step(eng, lambda v: v.scalar_tensor_tensor(t1[:, 0:W], t1[:, 0:W], 0.5, t2[:, 0:W], ALU.mult, ALU.add),
                wait=[(esem, gv)], inc=esem)
    gv = p.step(eng, lambda v: v.scalar_tensor_tensor(acc[:, 0:W], acc[:, 0:W], 0.5, t1[:, 0:W], ALU.mult, ALU.add),
                wait=[(esem, gv)], inc=esem)
    vv = p.step("vector", lambda v: v.tensor_mul(acc[:, 0:W], acc[:, 0:W], sgn[:, 0:W]),
                wait=[(esem, gv), (ctx.se, sv2)], inc=ctx.ve)
    s3 = scal[:, 0:nb].rearrange("p (b one) -> p b one", b=nb, one=1).broadcast_to((128, nb, 16))
    ac3 = acc[:, 0:W].rearrange("p (b k) -> p b k", b=nb, k=16)
    d3 = dst_ap.rearrange("p (b k) -> p b k", b=nb, k=16)
    vv = p.step("vector", lambda v: v.tensor_tensor(d3, ac3, s3, ALU.mult),
                wait=[(ctx.ve, vv)], inc=ctx.ve)
    sc["q_guard"] = [(ctx.ve, vv)]
    return [(ctx.ve, vv)]


def alloc_quant_scratch(ctx, sc, es, W=1024):
    p = ctx.p
    sc["q_qa"] = p.sbuf("q_qa", [128, W], F32, es=es)
    sc["q_sgn"] = p.sbuf("q_sgn", [128, W], BF16, es=es)
    sc["q_acc"] = p.sbuf("q_acc", [128, W], F32, es=es)
    sc["q_t1"] = p.sbuf("q_t1", [128, W], F32, es=es)
    sc["q_t2"] = p.sbuf("q_t2", [128, W], BF16, es=es)
    sc["q_u2"] = p.sbuf("q_u2", [128, W], BF16, es=es)
    sc["q_amax"] = p.sbuf("q_amax", [128, W // 16], F32, es=es)
    sc["q_scal"] = p.sbuf("q_scal", [128, W // 16], F32, es=es)
    sc["q_rscal"] = p.sbuf("q_rscal", [128, W // 16], F32, es=es)
    sc["q_guard"] = []


def l1_wquant(ctx, ws, wq_scratch):
    p = ctx.p
    es = ExitStack()
    sc = {}
    alloc_quant_scratch(ctx, sc, es, W=1024)
    fence = ctx.fence()
    sc["q_guard"] = list(fence)
    wld = p.sbuf("l1_wld", [128, 2816], F32, es=es)
    wqb = p.sbuf("l1_wqb", [128, 2816], BF16, es=es)
    wtb = p.sbuf("l1_wtb", [128, 22 * 128], BF16, es=es)
    WQG, WQU, WQD = wq_scratch
    st_guard = list(fence)
    for wi, (wname, dst, nrow, ncol) in enumerate(
        (("L1WG", WQG, 2816, 1024), ("L1WU", WQU, 2816, 1024), ("L1WD", WQD, 1024, 2816))
    ):
        hbm = ws[wname]
        for rt in range(nrow // 128):
            slot, wsp = ctx.rx.load(
                lambda e, hbm=hbm, rt=rt, ncol=ncol: e.dma_start(
                    out=wld[:, 0:ncol], in_=hbm[rt * 128 : (rt + 1) * 128, :]
                ),
                wait=sc["q_guard"],
            )
            qdone = []
            for s0 in range(0, ncol, 1024):
                w = min(1024, ncol - s0)
                qdone = quant_tile(ctx, sc, wld[:, s0 : s0 + w], w, wqb[:, s0 : s0 + w], [wsp])
            ctx.rx.ack(slot, *qdone)
            nch = ncol // 128
            for g in range(cdiv(nch, 4)):
                nin = min(4, nch - g * 4)
                bank, bw = ctx.poolT.acquire()
                tw = bw + qdone + st_guard
                tvs = None
                for c4 in range(nin):
                    c = g * 4 + c4
                    tvs = p.step(
                        "tensor",
                        lambda t, c=c, c4=c4, bank=bank: t.transpose(
                            ctx.psT[bank][:, c4 * 128 : (c4 + 1) * 128],
                            wqb[:, c * 128 : (c + 1) * 128],
                            ctx.cba("ident"),
                        ),
                        wait=tw,
                        inc=ctx.te,
                    )
                    tw = ()
                vv = p.step(
                    "vector",
                    lambda v, g=g, nin=nin, bank=bank: v.tensor_copy(
                        wtb[:, g * 512 : g * 512 + nin * 128], ctx.psT[bank][:, 0 : nin * 128]
                    ),
                    wait=[(ctx.te, tvs)],
                    inc=ctx.ve,
                )
                ctx.poolT.release(bank, (ctx.ve, vv))
            dstap = dst[:, rt * 128 : (rt + 1) * 128].rearrange("(c p) w -> p c w", p=128)
            srcap = wtb[:, 0 : nch * 128].rearrange("p (c w) -> p c w", c=nch, w=128)
            sslot, ssp = ctx.rs.load(
                lambda e, dstap=dstap, srcap=srcap: e.dma_start(out=dstap, in_=srcap),
                wait=[(ctx.ve, ctx.ve.n)],
            )
            ctx.rs.ack(sslot, ssp)
            st_guard = [ssp]
    es.close()
    return [ssp, (ctx.ve, ctx.ve.n)]


def l1_ffn_batch(ctx, sc, wq_scratch, wq_ready):
    p = ctx.p
    NT, T = ctx.NT, ctx.T
    WQG, WQU, WQD = wq_scratch
    fence = sc["ffn_fence"]
    xq_fm = sc["xq_fm"]
    xstage = sc["xstage"]
    xg = list(fence)
    for ti in range(NT):
        col0 = (ti % 16) * 8
        c_mean = ctx.row_scr[:, col0 + 2 : col0 + 3]
        c_rstd = ctx.row_scr[:, col0 + 6 : col0 + 7]
        xr = ctx.x_res[:, ti * 1024 : (ti + 1) * 1024]
        vv = p.step("vector",
                    lambda v, xr=xr, c_mean=c_mean, c_rstd=c_rstd: v.tensor_scalar(
                        xstage[:], xr, c_mean, c_rstd, ALU.subtract, ALU.mult),
                    wait=sc["ln_stats_ready"] + xg, inc=ctx.ve)
        qd = quant_tile(ctx, sc, xstage[:], 1024, sc["xqstage"][:], [(ctx.ve, vv)])
        fmv = tm_to_fm_src(ctx, sc["xqstage"], ti, xq_fm, qd[0])
        xg = [(ctx.ve, fmv)]
    xq_ready = [(ctx.ve, ctx.ve.n)]
    if "dbg_xq" in sc:
        p.step("sync", lambda e: e.dma_start(out=sc["dbg_xq"], in_=xq_fm[:]),
               wait=xq_ready, inc=ctx.st, inc_k=16)
    wg_sb, wu_sb, wd_sb = sc["wg_sb"], sc["wu_sb"], sc["wd_sb"]
    hqT = sc["hqT"]
    hscr = sc["hscr"]
    hqstage = sc["q_u2"]
    NSUP = cdiv(NT, 4)
    wgu_guard = [list(fence), list(fence)]
    wd_guard = list(fence)
    hqT_guard = list(fence)
    for sup in range(NSUP):
        ntile4 = min(4, NT - sup * 4)
        hq_done = []
        for fs in range(6):
            fw = min(512, 2816 - fs * 512)
            gslot, gw = ctx.rw.load(
                lambda e, fs=fs, fw=fw: e.dma_start(
                    out=wg_sb[:, 0 : 8 * fw].rearrange("p (c w) -> p c w", c=8, w=fw),
                    in_=WQG[:, fs * 512 : fs * 512 + fw].rearrange("(c p) w -> p c w", p=128),
                ),
                wait=wq_ready + wgu_guard[0],
            )
            uslot, uw = ctx.rw.load(
                lambda e, fs=fs, fw=fw: e.dma_start(
                    out=wu_sb[:, 0 : 8 * fw].rearrange("p (c w) -> p c w", c=8, w=fw),
                    in_=WQU[:, fs * 512 : fs * 512 + fw].rearrange("(c p) w -> p c w", p=128),
                ),
                wait=wq_ready + wgu_guard[1],
            )
            glast = ulast = None
            for ti4 in range(ntile4):
                ti = sup * 4 + ti4
                pig, pwg = ctx.pool.acquire()
                psg = ctx.psF[pig]
                tv = None
                for c in range(8):
                    tv = p.step(
                        "tensor",
                        lambda t, psg=psg, c=c, ti=ti, fw=fw: t.matmul(
                            psg[:, 0:fw],
                            xq_fm[:, c * T + ti * 128 : c * T + ti * 128 + 128],
                            wg_sb[:, c * fw : (c + 1) * fw],
                            start=(c == 0), stop=(c == 7),
                        ),
                        wait=(pwg + [gw] + xq_ready) if c == 0 else (),
                        inc=ctx.te if c == 7 else None,
                    )
                sv = p.step("scalar",
                            lambda e, psg=psg, fw=fw: e.activation(hscr[:, 0:fw], psg[:, 0:fw], AF.Silu),
                            wait=[(ctx.te, tv)] + sc["hscr_guard"] + sc["sscr_guard"], inc=ctx.se)
                glast = sv
                ctx.pool.release(pig, (ctx.se, sv))
                piu, pwu = ctx.pool.acquire()
                psu = ctx.psF[piu]
                tv = None
                for c in range(8):
                    tv = p.step(
                        "tensor",
                        lambda t, psu=psu, c=c, ti=ti, fw=fw: t.matmul(
                            psu[:, 0:fw],
                            xq_fm[:, c * T + ti * 128 : c * T + ti * 128 + 128],
                            wu_sb[:, c * fw : (c + 1) * fw],
                            start=(c == 0), stop=(c == 7),
                        ),
                        wait=(pwu + [uw] + xq_ready) if c == 0 else (),
                        inc=ctx.te if c == 7 else None,
                    )
                vv = p.step("vector",
                            lambda v, psu=psu, fw=fw: v.tensor_mul(hscr[:, 0:fw], hscr[:, 0:fw], psu[:, 0:fw]),
                            wait=[(ctx.te, tv), (ctx.se, sv)], inc=ctx.ve)
                ulast = vv
                ctx.pool.release(piu, (ctx.ve, vv))
                sc["sscr_guard"] = [(ctx.ve, vv)]
                qd = quant_tile(ctx, sc, hscr[:, 0:fw], fw, hqstage[:, 0:fw], [(ctx.ve, vv)])
                sc["hscr_guard"] = qd
                prs = []
                for j in range(fw // 128):
                    fc = fs * 4 + j
                    prs.append((hqstage[:, j * 128 : (j + 1) * 128],
                                hqT[:, fc * 512 + ti4 * 128 : fc * 512 + ti4 * 128 + 128]))
                tr = pe_transpose_pairs(ctx, prs, wait=qd + hqT_guard)
                hq_done.append((ctx.ve, tr))
                sc["hscr_guard"] = [(ctx.ve, tr)]
            wgu_guard[0] = [(ctx.te, ctx.te.n)]
            wgu_guard[1] = [(ctx.te, ctx.te.n)]
            ctx.rw.ack(gslot, (ctx.te, ctx.te.n))
            ctx.rw.ack(uslot, (ctx.te, ctx.te.n))
        if "dbg_hqT" in sc and sup == 0:
            p.step("sync", lambda e: e.dma_start(out=sc["dbg_hqT"], in_=hqT[:]),
                   wait=hq_done + [(ctx.ve, ctx.ve.n)], inc=ctx.st, inc_k=16)
        last_add = None
        for sl in range(2):
            pis = []
            for ti4 in range(ntile4):
                pi, pw = ctx.pool.acquire()
                pis.append((pi, pw))
            tvs = [None] * ntile4
            for half in range(2):
                dslot, dw = ctx.rw.load(
                    lambda e, sl=sl, half=half: e.dma_start(
                        out=wd_sb[:, 0 : 11 * 512].rearrange("p (c w) -> p c w", c=11, w=512),
                        in_=WQD[half * 1408 : half * 1408 + 1408, sl * 512 : (sl + 1) * 512]
                        .rearrange("(c p) w -> p c w", p=128),
                    ),
                    wait=wq_ready + wd_guard,
                )
                for ti4 in range(ntile4):
                    ti = sup * 4 + ti4
                    pi, pw = pis[ti4]
                    ps = ctx.psF[pi]
                    for fc2 in range(11):
                        fc = half * 11 + fc2
                        tvs[ti4] = p.step(
                            "tensor",
                            lambda t, ps=ps, fc=fc, fc2=fc2, ti4=ti4: t.matmul(
                                ps[:],
                                hqT[:, fc * 512 + ti4 * 128 : fc * 512 + ti4 * 128 + 128],
                                wd_sb[:, fc2 * 512 : (fc2 + 1) * 512],
                                start=(fc == 0), stop=(fc == 21),
                            ),
                            wait=(pw + [dw] + hq_done) if fc2 == 0 else (),
                            inc=ctx.te if fc2 == 10 else None,
                        )
                ctx.rw.ack(dslot, (ctx.te, ctx.te.n))
                wd_guard = [(ctx.te, ctx.te.n)]
            for ti4 in range(ntile4):
                ti = sup * 4 + ti4
                pi, pw = pis[ti4]
                ps = ctx.psF[pi]
                xr = ctx.x_res[:, ti * 1024 + sl * 512 : ti * 1024 + sl * 512 + 512]
                vv = p.step("vector", lambda v, xr=xr, ps=ps: v.tensor_add(xr, ps[:], xr),
                            wait=[(ctx.te, tvs[ti4])], inc=ctx.ve)
                ctx.pool.release(pi, (ctx.ve, vv))
                last_add = vv
        hqT_guard = [(ctx.te, ctx.te.n)]
    return [(ctx.ve, ctx.ve.n)]


def layer1(ctx, ws, x_src, x_dst, wq_scratch, wq_ready, prev_store=None, parts="all"):
    p = ctx.p
    NT, T = ctx.NT, ctx.T
    ctx.xn_fm = None
    les = ExitStack()
    sc = {}
    pes = ExitStack()
    sc["xn_tm"] = p.sbuf("l1_xntm", [128, NT * 1024], BF16, es=pes)
    sc["logp_fm"] = p.sbuf("l1_logpfm", [128, 8 * T], BF16, es=pes)
    sc["ptile"] = [p.sbuf(f"l1_pt{i}", [128, 1024], BF16, es=pes) for i in range(2)]
    sc["escr"] = p.sbuf("l1_escr", [128, 1024], BF16, es=pes)
    sc["lrow"] = p.sbuf("l1_lrow", [128, 64], F32, es=pes)
    sc["probs"] = [p.sbuf(f"l1_probs{i}", [128, 512], BF16, es=pes) for i in range(2)]
    sc["probsT"] = p.sbuf("l1_probsT", [128, T], BF16, es=pes)
    sc["arow"] = p.sbuf("l1_arow", [128, 48], F32, es=pes)
    fence0 = ctx.fence()
    sc["fence0"] = fence0
    sc["probs_rel"] = [list(fence0), list(fence0)]
    sc["pvT_prev"] = list(fence0)
    sc["arow_guard"] = [list(fence0), list(fence0)]
    sc["lrow_g"] = [[] for _ in range(8)]
    ln_phase(ctx, x_src, prev_store,
             dst_fn=lambda ti: sc["xn_tm"][:, ti * 1024 : (ti + 1) * 1024])
    sc["xn_ready"] = [(ctx.ve, ctx.ve.n), (ctx.se, ctx.se.n)]
    sc["ln_stats_ready"] = [(ctx.ve, ctx.ve.n)]
    if parts in ("all", "attn"):
        logp_ready = lse_logp_tiles(ctx, sc, sc["xn_tm"], sc["logp_fm"])
        l1_attn_batch(ctx, sc, logp_ready)
    pes.close()
    fence1 = ctx.fence()
    pes2 = ExitStack()
    sc["ffn_fence"] = fence1
    sc["xq_fm"] = p.sbuf("l1_xqfm", [128, 8 * T], BF16, es=pes2)
    alloc_quant_scratch(ctx, sc, pes2, W=1024)
    sc["q_guard"] = list(fence1)
    sc["xstage"] = p.sbuf("l1_xstage", [128, 1024], F32, es=pes2)
    sc["xqstage"] = sc["q_u2"]
    sc["wg_sb"] = p.sbuf("l1_wg", [128, 8 * 512], BF16, es=pes2)
    sc["wu_sb"] = p.sbuf("l1_wu", [128, 8 * 512], BF16, es=pes2)
    sc["wd_sb"] = p.sbuf("l1_wd", [128, 11 * 512], BF16, es=pes2)
    sc["hqT"] = p.sbuf("l1_hqT", [128, 22 * 512], BF16, es=pes2)
    sc["hscr"] = p.sbuf("l1_hscr", [128, 512], F32, es=pes2)
    sc["sscr_guard"] = list(fence1)
    sc["hscr_guard"] = list(fence1)
    if parts in ("all", "ffn"):
        l1_ffn_batch(ctx, sc, wq_scratch, wq_ready)
    pes2.close()
    les.close()
    if x_dst is None:
        return None
    return store_xres(ctx, x_dst)




def l2_attn_batch(ctx, ws, sc):
    p = ctx.p
    T, NT = ctx.T, ctx.NT
    xn_fm = ctx.xn_fm
    fence = sc["fence0"]
    wq_sb, wk_sb, wv_sb = sc["wq_sb"], sc["wk_sb"], sc["wv_sb"]
    wld = []
    for wi, (wname, dst) in enumerate((("L2WQ", wq_sb), ("L2WK", wk_sb), ("L2WV", wv_sb))):
        slot, wsp = load_w_cols(ctx, ws[wname], 0, 1024, dst, wait=sc["wbig_guard"][wi])
        wld.append((slot, wsp))
    fmw = sc["fmw"]
    slot_fq, w_fq = ctx.rw.load(lambda e: e.dma_start(out=fmw[:, 0:128], in_=ws["L2FMQ"]),
                                wait=sc["fmw_guard"])
    slot_fk, w_fk = ctx.rw.load(lambda e: e.dma_start(out=fmw[:, 128:256], in_=ws["L2FMK"]),
                                wait=sc["fmw_guard"])
    frow = sc["l2rows"]
    slot_b, w_b = ctx.rx.load(lambda e: e.dma_start(out=frow[:, 0:2],
                                                    in_=ws["L2FMB"]), wait=sc["fmw_guard"])
    LN16 = float(np.log(16.0))
    vv = p.step("vector", lambda v: v.tensor_scalar(frow[:, 2:3], frow[:, 0:1], -LN16, None, ALU.add),
                wait=[w_b], inc=ctx.ve)
    vv = p.step("vector", lambda v: v.tensor_scalar(frow[:, 3:4], frow[:, 0:1], -1.0, -LN16, ALU.mult, ALU.add),
                wait=[(ctx.ve, vv)], inc=ctx.ve)
    vv = p.step("vector", lambda v: v.tensor_copy(frow[:, 4:5], frow[:, 1:2]),
                wait=[(ctx.ve, vv)], inc=ctx.ve)
    vv = p.step("vector", lambda v: v.tensor_scalar(frow[:, 5:6], frow[:, 1:2], -1.0, None, ALU.mult),
                wait=[(ctx.ve, vv)], inc=ctx.ve)
    ctx.rx.ack(slot_b, (ctx.ve, vv))
    bias_ready = (ctx.ve, vv)
    rmsrow = sc["rmsrow"]
    slot_r, w_r = ctx.rw.load(lambda e: e.dma_start(out=rmsrow[0:1, :], in_=ws["L2RMSW"]),
                              wait=sc["fmw_guard"])
    rv = bcast_row(ctx, rmsrow[0:1, :], 128, sc["rmsw_bc"][:], wait=[w_r])
    ctx.rw.ack(slot_r, (ctx.ve, rv))
    S32 = sc["S32"]
    Sbf = sc["Sbf"]
    zv = p.step("vector", lambda v: v.memset(S32[:], 0.0), wait=list(fence) + sc["S_guard"], inc=ctx.ve)
    zv = p.step("vector", lambda v: v.memset(Sbf[:], 0.0), wait=[(ctx.ve, zv)], inc=ctx.ve)
    s_ready = [[(ctx.ve, zv)] for _ in range(8)]
    qkv_sb = sc["qkv_sb"]
    vtm = sc["vtm"]
    phq = sc["phq"]
    phk = sc["phk"]
    phkT = sc["phkT"]
    am = sc["am"]
    mask = ctx.cba("mask_t2_le_t1")
    wo_slot, wo_w = load_w_cols(ctx, ws["L2WO"], 0, 1024, sc["wo_sb"], wait=sc["wbig_guard"][3])
    sc["wo_w"] = wo_w
    orow = sc["l2orow"]
    og = [[], []]
    GRP = 2
    for g in range(cdiv(NT, GRP)):
        gn = min(GRP, NT - g * GRP)
        gw512 = gn * 128
        pgd = {}
        for wi in range(3):
            for h in range(8):
                pi, pw = ctx.pool.acquire()
                ps = ctx.psF[pi]
                tv = None
                for c in range(8):
                    tv = p.step(
                        "tensor",
                        lambda t, ps=ps, c=c, h=h, wi=wi, g=g, gw512=gw512: t.matmul(
                            ps[:, 0:gw512],
                            (wq_sb, wk_sb, wv_sb)[wi][:, c * 1024 + h * 128 : c * 1024 + h * 128 + 128],
                            xn_fm[:, c * T + g * GRP * 128 : c * T + g * GRP * 128 + gw512],
                            start=(c == 0), stop=(c == 7),
                        ),
                        wait=(pw + [wld[wi][1]] + sc["xn_ready"]) if c == 0 else (),
                        inc=ctx.te if c == 7 else None,
                    )
                dst = qkv_sb[:, (wi * 8 + h) * (GRP * 128) : (wi * 8 + h) * (GRP * 128) + gw512]
                vv2 = p.step("vector", lambda v, dst=dst, ps=ps, gw512=gw512: v.tensor_copy(dst, ps[:, 0:gw512]),
                             wait=[(ctx.te, tv)] + (sc["qkv_guard"] if (wi == 0 and h == 0) else []),
                             inc=ctx.ve)
                ctx.pool.release(pi, (ctx.ve, vv2))
                pgd[(wi, h)] = vv2
        prs = []
        for h in range(8):
            for c4 in range(gn):
                src = qkv_sb[:, (2 * 8 + h) * (GRP * 128) + c4 * 128 : (2 * 8 + h) * (GRP * 128) + c4 * 128 + 128]
                dst = vtm[:, c4 * 1024 + h * 128 : c4 * 1024 + h * 128 + 128]
                prs.append((src, dst))
        trv = pe_transpose_pairs(ctx, prs, wait=[(ctx.ve, ctx.ve.n)] + sc["vtm_guard"])
        sc["vtm_guard"] = []
        vready = (ctx.ve, trv)
        for c4 in range(gn):
            cg = g * GRP + c4
            for h in range(8):
                set_ = (c4 * 8 + h) % 2
                pi, pw = ctx.pool.acquire()
                ps = ctx.psF[pi]
                tv = p.step("tensor",
                            lambda t, ps=ps, h=h, c4=c4: t.matmul(
                                ps[:, 0:128], fmw[:, 0:128],
                                qkv_sb[:, (0 * 8 + h) * (GRP * 128) + c4 * 128 : (0 * 8 + h) * (GRP * 128) + c4 * 128 + 128],
                                start=True, stop=True),
                            wait=pw + [w_fq, (ctx.ve, pgd[(0, h)])], inc=ctx.te)
                sv = p.step("scalar",
                            lambda e, ps=ps, set_=set_: e.activation(phq[set_][:, 0:128], ps[:, 0:128], AF.Exp,
                                                                     bias=frow[:, 2:3]),
                            wait=[(ctx.te, tv), bias_ready] + sc["phq_guard"][set_], inc=ctx.se)
                sv = p.step("scalar",
                            lambda e, ps=ps, set_=set_: e.activation(phq[set_][:, 128:256], ps[:, 0:128], AF.Exp,
                                                                     bias=frow[:, 3:4], scale=-1.0),
                            wait=[(ctx.se, sv)], inc=ctx.se)
                ctx.pool.release(pi, (ctx.se, sv))
                phq_r = (ctx.se, sv)
                pi2, pw2 = ctx.pool.acquire()
                ps2 = ctx.psF[pi2]
                tv = p.step("tensor",
                            lambda t, ps2=ps2, h=h, c4=c4: t.matmul(
                                ps2[:, 0:128], fmw[:, 128:256],
                                qkv_sb[:, (1 * 8 + h) * (GRP * 128) + c4 * 128 : (1 * 8 + h) * (GRP * 128) + c4 * 128 + 128],
                                start=True, stop=True),
                            wait=pw2 + [w_fk, (ctx.ve, pgd[(1, h)])], inc=ctx.te)
                sv = p.step("scalar",
                            lambda e, ps2=ps2, set_=set_: e.activation(phk[set_][:, 0:128], ps2[:, 0:128], AF.Exp,
                                                                      bias=frow[:, 4:5]),
                            wait=[(ctx.te, tv)] + sc["phk_guard"][set_], inc=ctx.se)
                sv = p.step("scalar",
                            lambda e, ps2=ps2, set_=set_: e.activation(phk[set_][:, 128:256], ps2[:, 0:128], AF.Exp,
                                                                      bias=frow[:, 5:6], scale=-1.0),
                            wait=[(ctx.se, sv)], inc=ctx.se)
                ctx.pool.release(pi2, (ctx.se, sv))
                phk_r = (ctx.se, sv)
                prs = [(phk[set_][:, 0:128], phkT[set_][:, 0:128]),
                       (phk[set_][:, 128:256], phkT[set_][:, 128:256])]
                trk = pe_transpose_pairs(ctx, prs, wait=[phk_r] + sc["phkT_guard"][set_])
                pi3, pw3 = ctx.pool.acquire()
                ps3 = ctx.psF[pi3]
                tv = None
                for fc in range(2):
                    tv = p.step("tensor",
                                lambda t, ps3=ps3, fc=fc, set_=set_: t.matmul(
                                    ps3[:, 0:128], phk[set_][:, fc * 128 : (fc + 1) * 128],
                                    phq[set_][:, fc * 128 : (fc + 1) * 128],
                                    start=(fc == 0), stop=(fc == 1)),
                                wait=(pw3 + [phk_r, phq_r]) if fc == 0 else (), inc=ctx.te if fc else None)
                amv = p.step("vector",
                             lambda v, ps3=ps3, set_=set_: v.tensor_mul(am[set_][:], ps3[:, 0:128], mask),
                             wait=[(ctx.te, tv)] + sc["am_guard"][set_], inc=ctx.ve)
                ctx.pool.release(pi3, (ctx.ve, amv))
                pi4, pw4 = ctx.pool.acquire()
                ps4 = ctx.psF[pi4]
                tv = p.step("tensor",
                            lambda t, ps4=ps4, set_=set_, c4=c4, h=h: t.matmul(
                                ps4[:, 0:128], am[set_][:],
                                vtm[:, c4 * 1024 + h * 128 : c4 * 1024 + h * 128 + 128],
                                start=True, stop=False),
                            wait=pw4 + [(ctx.ve, amv), vready], inc=None)
                for fc in range(2):
                    tv = p.step("tensor",
                                lambda t, ps4=ps4, fc=fc, set_=set_, h=h: t.matmul(
                                    ps4[:, 0:128], phq[set_][:, fc * 128 : (fc + 1) * 128],
                                    Sbf[:, h * 264 + fc * 132 : h * 264 + fc * 132 + 128],
                                    start=False, stop=(fc == 1)),
                                wait=s_ready[h] if fc == 0 else (), inc=ctx.te if fc else None)
                p1_done = (ctx.te, tv)
                pi5, pw5 = ctx.pool.acquire()
                ps5 = ctx.psF[pi5]
                tv2 = None
                for fc in range(2):
                    tv2 = p.step("tensor",
                                 lambda t, ps5=ps5, fc=fc, set_=set_, c4=c4, h=h: t.matmul(
                                     ps5[:, fc * 132 : fc * 132 + 128],
                                     phkT[set_][:, fc * 128 : (fc + 1) * 128],
                                     vtm[:, c4 * 1024 + h * 128 : c4 * 1024 + h * 128 + 128],
                                     start=(fc == 0), stop=(fc == 1)),
                                 wait=(pw5 + [(ctx.ve, trk), vready] + s_ready[h]) if fc == 0 else (),
                                 inc=ctx.te if fc else None)
                sc["phkT_guard"][set_] = [(ctx.te, tv2)]
                s32h3 = S32[:, h * 264 : h * 264 + 264].rearrange("p (c w) -> p c w", c=2, w=132)[:, :, 0:128]
                ps53 = ps5[:, 0:264].rearrange("p (c w) -> p c w", c=2, w=132)[:, :, 0:128]
                sbf3 = Sbf[:, h * 264 : h * 264 + 264].rearrange("p (c w) -> p c w", c=2, w=132)[:, :, 0:128]
                sv1 = p.step("vector",
                             lambda v, ps53=ps53, s32h3=s32h3: v.tensor_add(s32h3, ps53, s32h3),
                             wait=[(ctx.te, tv2)], inc=ctx.ve)
                sv2 = p.step("vector",
                             lambda v, sbf3=sbf3, s32h3=s32h3: v.tensor_copy(sbf3, s32h3),
                             wait=[(ctx.ve, sv1)], inc=ctx.ve)
                ctx.pool.release(pi5, (ctx.ve, sv1))
                s_ready[h] = [(ctx.ve, sv2)]
                oc0 = (set_) * 8
                o_f = sc["o_f"]
                osq = sc["osq"]
                ev = p.step("vector",
                            lambda v, ps4=ps4, o_f=o_f, set_=set_: v.tensor_copy(o_f[set_][:], ps4[:, 0:128]),
                            wait=[p1_done] + sc["of_guard"][set_], inc=ctx.ve)
                ctx.pool.release(pi4, (ctx.ve, ev))
                sq = p.step("scalar",
                            lambda e, o_f=o_f, osq=osq, set_=set_, oc0=oc0: e.activation(
                                osq[set_][:], o_f[set_][:], AF.Square, accum_out=orow[:, oc0 : oc0 + 1]),
                            wait=[(ctx.ve, ev)] + sc["osq_guard"][set_], inc=ctx.se)
                sc["osq_guard"][set_] = [(ctx.se, sq)]
                vv = p.step("vector",
                            lambda v, oc0=oc0: v.tensor_scalar(orow[:, oc0 + 1 : oc0 + 2],
                                                               orow[:, oc0 : oc0 + 1],
                                                               1.0 / 128.0, 1e-5, ALU.mult, ALU.add),
                            wait=[(ctx.se, sq)], inc=ctx.ve)
                sv3 = p.step("scalar",
                             lambda e, oc0=oc0: e.activation(orow[:, oc0 + 2 : oc0 + 3],
                                                             orow[:, oc0 + 1 : oc0 + 2], AF.Sqrt),
                             wait=[(ctx.ve, vv)], inc=ctx.se)
                vv = p.step("vector",
                            lambda v, oc0=oc0: v.reciprocal(orow[:, oc0 + 3 : oc0 + 4],
                                                            orow[:, oc0 + 2 : oc0 + 3]),
                            wait=[(ctx.se, sv3)], inc=ctx.ve)
                ost = sc["ost"]
                vv = p.step("vector",
                            lambda v, o_f=o_f, set_=set_, oc0=oc0, ost=ost: v.scalar_tensor_tensor(
                                ost[set_][:], o_f[set_][:], orow[:, oc0 + 3 : oc0 + 4],
                                sc["rmsw_bc"][:], ALU.mult, ALU.mult),
                            wait=[(ctx.ve, vv), (ctx.ve, rv)] + sc["ost_guard"][set_], inc=ctx.ve)
                sc["of_guard"][set_] = [(ctx.ve, vv)]
                oset = 0
                ofm = sc["ofm"][0]
                tro = pe_transpose_pairs(ctx, [(ost[set_][:], ofm[:, h * 128 : (h + 1) * 128])],
                                         wait=[(ctx.ve, vv)] + (sc["ofm_guard"][oset] if h == 0 else []))
                sc["ost_guard"][set_] = [(ctx.te, ctx.te.n)]
                sc["am_guard"][set_] = [(ctx.te, ctx.te.n)]
                sc["phq_guard"][set_] = [(ctx.te, ctx.te.n)]
                sc["phk_guard"][set_] = [(ctx.te, ctx.te.n)]
            oset = 0
            ofm = sc["ofm"][0]
            ofm_ready = (ctx.ve, ctx.ve.n)
            for half in range(2):
                pi, pw = ctx.pool.acquire()
                ps = ctx.psF[pi]
                tv = None
                for h2 in range(8):
                    tv = p.step(
                        "tensor",
                        lambda t, ps=ps, h2=h2, half=half, ofm=ofm: t.matmul(
                            ps[:],
                            ofm[:, h2 * 128 : (h2 + 1) * 128],
                            sc["wo_sb"][:, h2 * 1024 + half * 512 : h2 * 1024 + half * 512 + 512],
                            start=(h2 == 0), stop=(h2 == 7)),
                        wait=(pw + [sc["wo_w"], ofm_ready]) if h2 == 0 else (),
                        inc=ctx.te if h2 == 7 else None)
                xr = ctx.x_res[:, cg * 1024 + half * 512 : cg * 1024 + half * 512 + 512]
                vv3 = p.step("vector", lambda v, xr=xr, ps=ps: v.tensor_add(xr, ps[:], xr),
                             wait=[(ctx.te, tv)], inc=ctx.ve)
                ctx.pool.release(pi, (ctx.ve, vv3))
            sc["ofm_guard"][oset] = [(ctx.te, ctx.te.n)]
        sc["qkv_guard"] = [(ctx.te, ctx.te.n)]
        sc["vtm_guard"] = [(ctx.te, ctx.te.n)]
    for wi in range(3):
        ctx.rw.ack(wld[wi][0], (ctx.te, ctx.te.n))
        sc["wbig_guard"][wi] = [(ctx.te, ctx.te.n)]
    ctx.rw.ack(wo_slot, (ctx.te, ctx.te.n))
    sc["wbig_guard"][3] = [(ctx.te, ctx.te.n)]
    ctx.rw.ack(slot_fq, (ctx.te, ctx.te.n))
    ctx.rw.ack(slot_fk, (ctx.te, ctx.te.n))
    sc["fmw_guard"] = [(ctx.te, ctx.te.n)]
    return [(ctx.ve, ctx.ve.n)]


def moe_expert_pass(ctx, sc, e_idx, w1_ap, w3_ap, w2_ap, inter, gate_col_fn, silu_on_w1=True):
    p = ctx.p
    T, NT = ctx.T, ctx.NT
    xn_fm = ctx.xn_fm
    w1b, w3b, w2b = sc["w1b"], sc["w3b"], sc["w2b"]
    s1, w1w = load_w_cols(ctx, w1_ap, 0, inter, w1b, wait=sc["wmoe_guard"][0])
    s3, w3w = load_w_cols(ctx, w3_ap, 0, inter, w3b, wait=sc["wmoe_guard"][1])
    s2, w2w = load_w_cols(ctx, w2_ap, 0, 1024, w2b, wait=sc["wmoe_guard"][2], d_in=inter)
    nfc = inter // 128
    hfm = sc["hfm"]
    hscr2 = sc["hscr2"]
    for sl in range(cdiv(NT, 4)):
        sn = min(4, NT - sl * 4)
        sw512 = sn * 128
        for fc in range(nfc):
            pi, pw = ctx.pool.acquire()
            ps = ctx.psF[pi]
            tv = None
            for c in range(8):
                tv = p.step(
                    "tensor",
                    lambda t, ps=ps, c=c, fc=fc, sl=sl, sw512=sw512: t.matmul(
                        ps[:, 0:sw512],
                        w1b[:, c * inter + fc * 128 : c * inter + fc * 128 + 128],
                        xn_fm[:, c * T + sl * 512 : c * T + sl * 512 + sw512],
                        start=(c == 0), stop=(c == 7)),
                    wait=(pw + [w1w] + sc["xn_ready"]) if c == 0 else (),
                    inc=ctx.te if c == 7 else None)
            sv = p.step("scalar",
                        lambda e, ps=ps, sw512=sw512: e.activation(hscr2[:, 0:sw512], ps[:, 0:sw512], AF.Silu),
                        wait=[(ctx.te, tv)] + sc["h2_guard"], inc=ctx.se)
            ctx.pool.release(pi, (ctx.se, sv))
            pi2, pw2 = ctx.pool.acquire()
            ps2 = ctx.psF[pi2]
            tv = None
            for c in range(8):
                tv = p.step(
                    "tensor",
                    lambda t, ps2=ps2, c=c, fc=fc, sl=sl, sw512=sw512: t.matmul(
                        ps2[:, 0:sw512],
                        w3b[:, c * inter + fc * 128 : c * inter + fc * 128 + 128],
                        xn_fm[:, c * T + sl * 512 : c * T + sl * 512 + sw512],
                        start=(c == 0), stop=(c == 7)),
                    wait=(pw2 + [w3w]) if c == 0 else (),
                    inc=ctx.te if c == 7 else None)
            hv = p.step("vector",
                        lambda v, ps2=ps2, fc=fc, sw512=sw512: v.tensor_mul(
                            hfm[:, fc * 512 : fc * 512 + sw512], hscr2[:, 0:sw512], ps2[:, 0:sw512]),
                        wait=[(ctx.te, tv), (ctx.se, sv)] + sc["hfm_guard"], inc=ctx.ve)
            ctx.pool.release(pi2, (ctx.ve, hv))
            sc["h2_guard"] = [(ctx.ve, hv)]
        h_ready = (ctx.ve, ctx.ve.n)
        for t4 in range(sn):
            ti = sl * 4 + t4
            pis = []
            for half in range(2):
                pi, pw = ctx.pool.acquire()
                ps = ctx.psF[pi]
                tv = None
                for fc in range(nfc):
                    tv = p.step(
                        "tensor",
                        lambda t, ps=ps, fc=fc, t4=t4, half=half: t.matmul(
                            ps[:],
                            hfm[:, fc * 512 + t4 * 128 : fc * 512 + t4 * 128 + 128],
                            w2b[:, fc * 1024 + half * 512 : fc * 1024 + half * 512 + 512],
                            start=(fc == 0), stop=(fc == nfc - 1)),
                        wait=(pw + [w2w, h_ready]) if fc == 0 else (),
                        inc=ctx.te if fc == nfc - 1 else None)
                pis.append((pi, tv))
            for half in range(2):
                pi, tv = pis[half]
                ps = ctx.psF[pi]
                xr = ctx.x_res[:, ti * 1024 + half * 512 : ti * 1024 + half * 512 + 512]
                gc = gate_col_fn(ti)
                if gc is None:
                    vv = p.step("vector", lambda v, xr=xr, ps=ps: v.tensor_add(xr, ps[:], xr),
                                wait=[(ctx.te, tv)], inc=ctx.ve)
                else:
                    vv = p.step("vector",
                                lambda v, xr=xr, ps=ps, gc=gc: v.scalar_tensor_tensor(
                                    xr, ps[:], gc, xr, ALU.mult, ALU.add),
                                wait=[(ctx.te, tv)], inc=ctx.ve)
                ctx.pool.release(pi, (ctx.ve, vv))
        sc["hfm_guard"] = [(ctx.te, ctx.te.n)]
    te_g = [(ctx.te, ctx.te.n)]
    ctx.rw.ack(s1, *te_g)
    ctx.rw.ack(s3, *te_g)
    ctx.rw.ack(s2, *te_g)
    sc["wmoe_guard"] = [te_g + [w1w], te_g + [w3w], te_g + [w2w]]


def moe_gate8(ctx, sc, gw_name, ws, bias_mode, wts_buf):
    p = ctx.p
    T, NT = ctx.T, ctx.NT
    xn_fm = ctx.xn_fm
    gwb = sc["gwb"]
    slot, gww = load_w_cols(ctx, ws[gw_name], 0, 8, gwb, wait=sc["gw_guard"])
    grow8 = sc["grow8"]
    bname = "L2GB" if bias_mode == "l2" else "L3GB"
    slot_b, gbw = ctx.rx.load(lambda e: e.dma_start(out=grow8[0:1, 0:8], in_=ws[bname]),
                              wait=sc["gw_guard"])
    vvb = p.step("vector", lambda v: v.tensor_copy(sc["grow8b"][0:1, 0:8], grow8[0:1, 0:8]),
                 wait=[gbw], inc=ctx.ve)
    bv = bcast_row(ctx, sc["grow8b"][0:1, 0:8], 8, sc["gb_bc"][:], wait=[(ctx.ve, vvb)])
    ctx.rx.ack(slot_b, (ctx.ve, bv))
    gsc = sc["gsc8"]
    g_guard = sc["g8_guard"]
    last = None
    for ti in range(NT):
        st0 = (ti % 2) * 32
        lg = gsc[:, st0 : st0 + 8]
        m8 = gsc[:, st0 + 8 : st0 + 16]
        e8 = gsc[:, st0 + 16 : st0 + 24]
        r0 = gsc[:, st0 + 24 : st0 + 25]
        r1 = gsc[:, st0 + 25 : st0 + 26]
        r2 = gsc[:, st0 + 26 : st0 + 27]
        pi, pw = ctx.pool.acquire()
        ps = ctx.psF[pi]
        tv = None
        for c in range(8):
            tv = p.step("tensor",
                        lambda t, ps=ps, c=c, ti=ti: t.matmul(
                            ps[:, 0:8], xn_fm[:, c * T + ti * 128 : c * T + ti * 128 + 128],
                            gwb[:, c * 8 : (c + 1) * 8], start=(c == 0), stop=(c == 7)),
                        wait=(pw + [gww] + sc["xn_ready"]) if c == 0 else (),
                        inc=ctx.te if c == 7 else None)
        if bias_mode == "l3":
            vv = p.step("vector", lambda v, ps=ps, lg=lg: v.tensor_add(lg, ps[:, 0:8], sc["gb_bc"][:]),
                        wait=[(ctx.te, tv), (ctx.ve, bv)] + g_guard[ti % 2], inc=ctx.ve)
        else:
            vv = p.step("vector", lambda v, ps=ps, lg=lg: v.tensor_copy(lg, ps[:, 0:8]),
                        wait=[(ctx.te, tv)] + g_guard[ti % 2], inc=ctx.ve)
        ctx.pool.release(pi, (ctx.ve, vv))
        vv = p.step("vector", lambda v, lg=lg, r0=r0: v.reduce_max(r0, lg, axis=AX.X, negate=True),
                    wait=[(ctx.ve, vv)], inc=ctx.ve)
        sv = p.step("scalar", lambda e, lg=lg, e8=e8, r0=r0, r1=r1: e.activation(
            e8, lg, AF.Exp, bias=r0, accum_out=r1), wait=[(ctx.ve, vv)] + sc["g8s_guard"][ti % 2],
            inc=ctx.se)
        sc["g8s_guard"][ti % 2] = [(ctx.se, sv)]
        vv = p.step("vector", lambda v, r1=r1, r2=r2: v.reciprocal(r2, r1),
                    wait=[(ctx.se, sv)], inc=ctx.ve)
        vv = p.step("vector", lambda v, e8=e8, r2=r2: v.tensor_scalar(e8, e8, r2, None, ALU.mult),
                    wait=[(ctx.ve, vv)], inc=ctx.ve)
        if bias_mode == "l2":
            vv = p.step("vector", lambda v, lg=lg, e8=e8: v.tensor_add(lg, e8, sc["gb_bc"][:]),
                        wait=[(ctx.ve, vv), (ctx.ve, bv)], inc=ctx.ve)
            vv = p.step("vector", lambda v, m8=m8, lg=lg: v.max(out=m8, in_=lg),
                        wait=[(ctx.ve, vv)], inc=ctx.ve)
            vv = p.step("vector",
                        lambda v, ti=ti, lg=lg, e8=e8, m8=m8: v.scalar_tensor_tensor(
                            wts_buf[:, ti * 8 : (ti + 1) * 8], lg, m8[:, 1:2], e8, ALU.is_ge, ALU.mult),
                        wait=[(ctx.ve, vv)] + sc["wts_guard"], inc=ctx.ve)
        else:
            vv = p.step("vector", lambda v, m8=m8, e8=e8: v.max(out=m8, in_=e8),
                        wait=[(ctx.ve, vv)], inc=ctx.ve)
            vv = p.step("vector",
                        lambda v, lg=lg, e8=e8, m8=m8: v.scalar_tensor_tensor(
                            lg, e8, m8[:, 1:2], e8, ALU.is_ge, ALU.mult),
                        wait=[(ctx.ve, vv)], inc=ctx.ve)
            vv = p.step("vector", lambda v, r0=r0, m8=m8: v.tensor_add(r0, m8[:, 0:1], m8[:, 1:2]),
                        wait=[(ctx.ve, vv)], inc=ctx.ve)
        ctx.pool.release if False else None
        if bias_mode == "l3":
            vv = p.step("vector", lambda v, r1=r1, r0=r0: v.reciprocal(r1, r0),
                        wait=[(ctx.ve, vv)], inc=ctx.ve)
            vv = p.step("vector",
                        lambda v, ti=ti, lg=lg, r1=r1: v.tensor_scalar(
                            wts_buf[:, ti * 8 : (ti + 1) * 8], lg, r1, None, ALU.mult),
                        wait=[(ctx.ve, vv)] + sc["wts_guard"], inc=ctx.ve)
        g_guard[ti % 2] = [(ctx.ve, vv)]
        last = vv
    ctx.rw.ack(slot, (ctx.te, ctx.te.n))
    sc["gw_guard"] = [(ctx.te, ctx.te.n)]
    return [(ctx.ve, last)]


def layer2(ctx, ws, x_src, x_dst, prev_store=None, parts="all"):
    p = ctx.p
    NT, T = ctx.NT, ctx.T
    les = ExitStack()
    sc = {}
    ctx.xn_fm = p.sbuf("l2_xnfm", [128, 8 * T], BF16, es=les)
    ln_phase(ctx, x_src, prev_store)
    sc["xn_ready"] = [(ctx.ve, ctx.ve.n), (ctx.se, ctx.se.n)]
    fence0 = ctx.fence()
    sc["fence0"] = fence0
    if parts in ("all", "attn"):
        pes = ExitStack()
        sc["wq_sb"] = p.sbuf("l2_wq", [128, 8192], BF16, es=pes)
        sc["wk_sb"] = p.sbuf("l2_wk", [128, 8192], BF16, es=pes)
        sc["wv_sb"] = p.sbuf("l2_wv", [128, 8192], BF16, es=pes)
        sc["wo_sb"] = p.sbuf("l2_wo", [128, 8192], BF16, es=pes)
        sc["fmw"] = p.sbuf("l2_fmw", [128, 256], BF16, es=pes)
        sc["l2rows"] = p.sbuf("l2_rows", [128, 8], F32, es=pes)
        sc["rmsrow"] = p.sbuf("l2_rmsr", [1, 128], BF16, es=pes)
        sc["rmsw_bc"] = p.sbuf("l2_rmsbc", [128, 128], F32, es=pes)
        sc["S32"] = p.sbuf("l2_S32", [128, 8 * 264], F32, es=pes)
        sc["Sbf"] = p.sbuf("l2_Sbf", [128, 8 * 264], BF16, es=pes)
        sc["qkv_sb"] = p.sbuf("l2_qkv", [128, 3 * 8 * 256], BF16, es=pes)
        sc["vtm"] = p.sbuf("l2_vtm", [128, 2 * 1024], BF16, es=pes)
        sc["phq"] = [p.sbuf(f"l2_phq{i}", [128, 256], BF16, es=pes) for i in range(2)]
        sc["phk"] = [p.sbuf(f"l2_phk{i}", [128, 256], BF16, es=pes) for i in range(2)]
        sc["phkT"] = [p.sbuf(f"l2_phkT{i}", [128, 256], BF16, es=pes) for i in range(2)]
        sc["am"] = [p.sbuf(f"l2_am{i}", [128, 128], BF16, es=pes) for i in range(2)]
        sc["o_f"] = [p.sbuf(f"l2_of{i}", [128, 128], F32, es=pes) for i in range(2)]
        sc["osq"] = [p.sbuf(f"l2_osq{i}", [128, 128], BF16, es=pes) for i in range(2)]
        sc["ost"] = [p.sbuf(f"l2_ost{i}", [128, 128], BF16, es=pes) for i in range(2)]
        sc["ofm"] = [p.sbuf("l2_ofm0", [128, 1024], BF16, es=pes)]
        sc["l2orow"] = p.sbuf("l2_orow", [128, 16], F32, es=pes)
        for k in ("wbig_guard",):
            sc[k] = [list(fence0) for _ in range(4)]
        for k in ("fmw_guard", "qkv_guard", "vtm_guard", "S_guard"):
            sc[k] = list(fence0)
        for k in ("phq_guard", "phk_guard", "phkT_guard", "am_guard", "of_guard",
                  "osq_guard", "ost_guard", "ofm_guard"):
            sc[k] = [list(fence0), list(fence0)]
        sc["ofm_guard"] = [list(fence0)]
        l2_attn_batch(ctx, ws, sc)
        pes.close()
    fence1 = ctx.fence()
    if parts in ("all", "ffn"):
        pes2 = ExitStack()
        sc["w1b"] = p.sbuf("l2_w1b", [128, 8 * 1024], BF16, es=pes2)
        sc["w3b"] = p.sbuf("l2_w3b", [128, 8 * 1024], BF16, es=pes2)
        sc["w2b"] = p.sbuf("l2_w2b", [128, 8 * 1024], BF16, es=pes2)
        sc["hfm"] = p.sbuf("l2_hfm", [128, 8 * 512], BF16, es=pes2)
        sc["hscr2"] = p.sbuf("l2_hscr", [128, 512], F32, es=pes2)
        sc["gwb"] = p.sbuf("l2_gwb", [128, 64], BF16, es=pes2)
        sc["grow8"] = p.sbuf("l2_gr8", [1, 16], F32, es=pes2)
        sc["grow8b"] = p.sbuf("l2_gr8b", [1, 16], BF16, es=pes2)
        sc["gb_bc"] = p.sbuf("l2_gbbc", [128, 8], F32, es=pes2)
        sc["gsc8"] = p.sbuf("l2_gsc8", [128, 64], F32, es=pes2)
        sc["wts"] = p.sbuf("l2_wts", [128, NT * 8], F32, es=pes2)
        sc["gw_guard"] = list(fence1)
        sc["g8_guard"] = [list(fence1), list(fence1)]
        sc["g8s_guard"] = [list(fence1), list(fence1)]
        sc["wts_guard"] = list(fence1)
        sc["wmoe_guard"] = [list(fence1) for _ in range(3)]
        sc["h2_guard"] = list(fence1)
        sc["hfm_guard"] = list(fence1)
        wts_ready = moe_gate8(ctx, sc, "L2GW", ws, "l2", sc["wts"][:])
        for e in range(8):
            gcf = lambda ti, e=e: sc["wts"][:, ti * 8 + e : ti * 8 + e + 1]
            moe_expert_pass(ctx, sc, e, ws["L2W1"][e], ws["L2W3"][e], ws["L2W2"][e], 1024, gcf)
        for hf in range(2):
            moe_expert_pass(ctx, sc, 8 + hf, ws["L2SW1"][:, hf * 1024 : (hf + 1) * 1024],
                            ws["L2SW3"][:, hf * 1024 : (hf + 1) * 1024],
                            ws["L2SW2"][hf * 1024 : (hf + 1) * 1024, :], 1024, lambda ti: None)
        pes2.close()
    les.close()
    ctx.xn_fm = None
    if x_dst is None:
        return None
    return store_xres(ctx, x_dst)




def l3_attn_batch(ctx, ws, sc):
    p = ctx.p
    T, NT = ctx.T, ctx.NT
    xn_fm = ctx.xn_fm
    fence = sc["fence0"]
    RRD2 = 0.5
    R2S = float(2.0 ** -0.5)
    sub = ExitStack()
    wqkv = p.sbuf("l3_wqkv", [128, 12288], BF16, es=sub)
    sq, qw = load_w_cols(ctx, ws["L3WQ"], 0, 256, wqkv[:, 0:2048], wait=sc["l3w_guard"][0])
    sk, kw = load_w_cols(ctx, ws["L3WK"], 0, 256, wqkv[:, 2048:4096], wait=sc["l3w_guard"][1])
    sv_, vw = load_w_cols(ctx, ws["L3WV"], 0, 1024, wqkv[:, 4096:12288], wait=sc["l3w_guard"][2])
    q_tm, k_tm, v_tm = sc["q_tm"], sc["k_tm"], sc["v_tm"]
    prj_done = None
    for ti in range(NT):
        for wi, (dst, wid, nc_, scale) in enumerate(
            ((q_tm, 0, 256, RRD2), (k_tm, 1, 256, RRD2), (v_tm, 2, 1024, 1.0))
        ):
            for half in range(cdiv(nc_, 512)):
                hw_ = min(512, nc_ - half * 512)
                pi, pw = ctx.pool.acquire()
                ps = ctx.psF[pi]
                tv = None
                for c in range(8):
                    tv = p.step(
                        "tensor",
                        lambda t, ps=ps, c=c, ti=ti, wid=wid, half=half, hw_=hw_, nc_=nc_: t.matmul(
                            ps[:, 0:hw_],
                            xn_fm[:, c * T + ti * 128 : c * T + ti * 128 + 128],
                            wqkv[:, (0, 2048, 4096)[wid] + c * nc_ + half * 512 :
                                 (0, 2048, 4096)[wid] + c * nc_ + half * 512 + hw_],
                            start=(c == 0), stop=(c == 7)),
                        wait=(pw + [(qw, kw, vw)[wi]] + sc["xn_ready"]) if c == 0 else (),
                        inc=ctx.te if c == 7 else None)
                if scale != 1.0:
                    ev = p.step("scalar",
                                lambda e, ps=ps, dst=dst, ti=ti, nc_=nc_, half=half, hw_=hw_, scale=scale: e.activation(
                                    dst[:, ti * nc_ + half * 512 : ti * nc_ + half * 512 + hw_],
                                    ps[:, 0:hw_], AF.Copy, scale=scale),
                                wait=[(ctx.te, tv)], inc=ctx.se)
                    ctx.pool.release(pi, (ctx.se, ev))
                else:
                    ev = p.step("vector",
                                lambda v, ps=ps, dst=dst, ti=ti, nc_=nc_, half=half, hw_=hw_: v.tensor_copy(
                                    dst[:, ti * nc_ + half * 512 : ti * nc_ + half * 512 + hw_], ps[:, 0:hw_]),
                                wait=[(ctx.te, tv)], inc=ctx.ve)
                    ctx.pool.release(pi, (ctx.ve, ev))
    prj_done = [(ctx.se, ctx.se.n), (ctx.ve, ctx.ve.n)]
    te_prj = [(ctx.te, ctx.te.n)]
    for s_ in (sq, sk, sv_):
        ctx.rw.ack(s_, *te_prj)
    sub.close()
    wo_sb = p.sbuf("l3_wo", [128, 8192], BF16, es=sc["pes"])
    sc["l3wo"] = wo_sb
    wo_slot, wo_w = load_w_cols(ctx, ws["L3WO"], 0, 1024, wo_sb,
                                wait=te_prj + [qw, kw, vw])
    qf = sc["qf"]
    kf = sc["kf"]
    qffm = sc["qffm"]
    kfm0 = sc["kfm0"]
    vaug = sc["vaug"]
    a2 = sc["a2"]
    am3 = sc["am3"]
    ost3 = sc["ost3"]
    ofm3 = sc["ofm3"]
    S32 = sc["S32_3"]
    Sbf = sc["Sbf_3"]
    zrow = sc["zrow"]
    mask = ctx.cba("mask_t2_le_t1")
    zv = p.step("vector", lambda v: v.memset(S32[:], 0.0), wait=list(fence) + sc["S3_guard"], inc=ctx.ve)
    zv = p.step("vector", lambda v: v.memset(Sbf[:], 0.0), wait=[(ctx.ve, zv)], inc=ctx.ve)
    for b_ in qf + kf:
        zv = p.step("vector", lambda v, b_=b_: v.memset(b_[:, 256:384], 0.0), wait=[(ctx.ve, zv)], inc=ctx.ve)
        zv = p.step("vector", lambda v, b_=b_: v.memset(b_[:, 16:17], 1.0), wait=[(ctx.ve, zv)], inc=ctx.ve)
    for b_ in vaug:
        zv = p.step("vector", lambda v, b_=b_: v.memset(b_[:, 64:65], 1.0), wait=[(ctx.ve, zv)], inc=ctx.ve)
    init_done = (ctx.ve, zv)
    s_ready = [[init_done] for _ in range(16)]
    qf_guard = [[init_done], [init_done]]
    kf_guard = [[init_done], [init_done]]
    va_guard = [[init_done], [init_done]]
    ofm_ready = []
    for cg in range(NT):
        for h in range(16):
            set_ = (cg * 16 + h) % 2
            qsl = q_tm[:, cg * 256 + h * 16 : cg * 256 + h * 16 + 16]
            ksl = k_tm[:, cg * 256 + h * 16 : cg * 256 + h * 16 + 16]
            vv = p.step("vector", lambda v, qsl=qsl, set_=set_: v.tensor_copy(qf[set_][:, 0:16], qsl),
                        wait=prj_done + qf_guard[set_], inc=ctx.ve)
            q3a = qsl.rearrange("p (d one) -> p d one", d=16, one=1).broadcast_to((128, 16, 16))
            q3b = qsl.rearrange("p (one d) -> p one d", one=1, d=16).broadcast_to((128, 16, 16))
            o3 = qf[set_][:, 17:273].rearrange("p (a b) -> p a b", a=16, b=16)
            vv = p.step("vector", lambda v, o3=o3, q3a=q3a, q3b=q3b: v.scalar_tensor_tensor(
                o3, q3a, R2S, q3b, ALU.mult, ALU.mult), wait=[(ctx.ve, vv)], inc=ctx.ve)
            qfv = vv
            vv = p.step("vector", lambda v, ksl=ksl, set_=set_: v.tensor_copy(kf[set_][:, 0:16], ksl),
                        wait=prj_done + kf_guard[set_], inc=ctx.ve)
            k3a = ksl.rearrange("p (d one) -> p d one", d=16, one=1).broadcast_to((128, 16, 16))
            k3b = ksl.rearrange("p (one d) -> p one d", one=1, d=16).broadcast_to((128, 16, 16))
            ko3 = kf[set_][:, 17:273].rearrange("p (a b) -> p a b", a=16, b=16)
            vv = p.step("vector", lambda v, ko3=ko3, k3a=k3a, k3b=k3b: v.scalar_tensor_tensor(
                ko3, k3a, R2S, k3b, ALU.mult, ALU.mult), wait=[(ctx.ve, vv)], inc=ctx.ve)
            kfv = vv
            vva = p.step("vector",
                         lambda v, set_=set_, cg=cg, h=h: v.tensor_copy(
                             vaug[set_][:, 0:64], v_tm[:, cg * 1024 + h * 64 : cg * 1024 + h * 64 + 64]),
                         wait=prj_done + va_guard[set_], inc=ctx.ve)
            trq = pe_transpose_pairs(ctx, [(qf[set_][:, i * 128 : (i + 1) * 128],
                                            qffm[set_][:, i * 128 : (i + 1) * 128]) for i in range(3)],
                                     wait=[(ctx.ve, qfv)] + sc["qffm_guard"][set_])
            trk = pe_transpose_pairs(ctx, [(kf[set_][:, 0:128], kfm0[set_][:])],
                                     wait=[(ctx.ve, kfv)] + sc["kfm0_guard"][set_])
            pi, pw = ctx.pool.acquire()
            ps = ctx.psF[pi]
            tv = p.step("tensor",
                        lambda t, ps=ps, set_=set_: t.matmul(
                            ps[:, 0:128], kfm0[set_][0:16, 0:128], qffm[set_][0:16, 0:128],
                            start=True, stop=True),
                        wait=pw + [(ctx.ve, trk), (ctx.ve, trq)], inc=ctx.te)
            sv = p.step("scalar",
                        lambda e, ps=ps: e.activation(a2[:], ps[:, 0:128], AF.Square, bias=1.0),
                        wait=[(ctx.te, tv)] + sc["a2_guard"], inc=ctx.se)
            ctx.pool.release(pi, (ctx.se, sv))
            vv = p.step("vector",
                        lambda v: v.tensor_scalar(a2[:], a2[:], 0.5, 0.5, ALU.mult, ALU.add),
                        wait=[(ctx.se, sv)], inc=ctx.ve)
            amv = p.step("vector",
                         lambda v, set_=set_: v.tensor_mul(am3[set_][:], a2[:], mask),
                         wait=[(ctx.ve, vv)] + sc["am3_guard"][set_], inc=ctx.ve)
            sc["a2_guard"] = [(ctx.ve, amv)]
            pi4, pw4 = ctx.pool.acquire()
            ps4 = ctx.psF[pi4]
            tv = p.step("tensor",
                        lambda t, ps4=ps4, set_=set_: t.matmul(
                            ps4[:, 0:65], am3[set_][:], vaug[set_][:, 0:65], start=True, stop=False),
                        wait=pw4 + [(ctx.ve, amv), (ctx.ve, vva)], inc=None)
            for fc in range(3):
                tv = p.step("tensor",
                            lambda t, ps4=ps4, fc=fc, set_=set_, h=h: t.matmul(
                                ps4[:, 0:65], qffm[set_][:, fc * 128 : (fc + 1) * 128],
                                Sbf[:, h * 204 + fc * 68 : h * 204 + fc * 68 + 65],
                                start=False, stop=(fc == 2)),
                            wait=s_ready[h] if fc == 0 else (), inc=ctx.te if fc == 2 else None)
            p1_done = (ctx.te, tv)
            pi5, pw5 = ctx.pool.acquire()
            ps5 = ctx.psF[pi5]
            tv2 = None
            for fc in range(3):
                tv2 = p.step("tensor",
                             lambda t, ps5=ps5, fc=fc, set_=set_: t.matmul(
                                 ps5[:, fc * 68 : fc * 68 + 65],
                                 kf[set_][:, fc * 128 : (fc + 1) * 128],
                                 vaug[set_][:, 0:65],
                                 start=(fc == 0), stop=(fc == 2)),
                             wait=(pw5 + [(ctx.ve, kfv), (ctx.ve, vva)] + s_ready[h]) if fc == 0 else (),
                             inc=ctx.te if fc == 2 else None)
            kf_guard[set_] = [(ctx.te, tv2)]
            qf_guard[set_] = [(ctx.te, ctx.te.n)]
            va_guard[set_] = [(ctx.te, tv2)]
            s33 = S32[:, h * 204 : h * 204 + 204].rearrange("p (c w) -> p c w", c=3, w=68)[:, :, 0:65]
            ps53 = ps5[:, 0:204].rearrange("p (c w) -> p c w", c=3, w=68)[:, :, 0:65]
            sb3 = Sbf[:, h * 204 : h * 204 + 204].rearrange("p (c w) -> p c w", c=3, w=68)[:, :, 0:65]
            sv1 = p.step("vector", lambda v, ps53=ps53, s33=s33: v.tensor_add(s33, ps53, s33),
                         wait=[(ctx.te, tv2)], inc=ctx.ve)
            sv2 = p.step("vector", lambda v, sb3=sb3, s33=s33: v.tensor_copy(sb3, s33),
                         wait=[(ctx.ve, sv1)], inc=ctx.ve)
            ctx.pool.release(pi5, (ctx.ve, sv1))
            s_ready[h] = [(ctx.ve, sv2)]
            zc0 = set_ * 4
            vv = p.step("vector",
                        lambda v, ps4=ps4, zc0=zc0: v.tensor_scalar(
                            zrow[:, zc0 : zc0 + 1], ps4[:, 64:65], 1e-12, None, ALU.add),
                        wait=[p1_done] + sc["zrow_guard"][set_], inc=ctx.ve)
        
            vv = p.step("vector",
                        lambda v, zc0=zc0: v.reciprocal(zrow[:, zc0 + 1 : zc0 + 2], zrow[:, zc0 : zc0 + 1]),
                        wait=[(ctx.ve, vv)], inc=ctx.ve)
            ev = p.step("vector",
                        lambda v, ps4=ps4, set_=set_, zc0=zc0: v.tensor_scalar(
                            ost3[set_][:], ps4[:, 0:64], zrow[:, zc0 + 1 : zc0 + 2], None, ALU.mult),
                        wait=[(ctx.ve, vv)] + sc["ost3_guard"][set_], inc=ctx.ve)
            ctx.pool.release(pi4, (ctx.ve, ev))
            sc["zrow_guard"][set_] = [(ctx.ve, ev)]
            tro = pe_transpose_pairs(
                ctx, [(ost3[set_][:], ofm3[(h % 2) * 64 : (h % 2) * 64 + 64,
                                           (h // 2) * 128 : (h // 2) * 128 + 128])],
                wait=[(ctx.ve, ev)] + (sc["ofm3_guard"] if h == 0 else []), in_w=64)
            sc["ost3_guard"][set_] = [(ctx.te, ctx.te.n)]
            sc["am3_guard"][set_] = [(ctx.te, ctx.te.n)]
            sc["qffm_guard"][set_] = [(ctx.te, ctx.te.n)]
            sc["kfm0_guard"][set_] = [(ctx.te, ctx.te.n)]
        ofm_ready = (ctx.ve, ctx.ve.n)
        for half in range(2):
            pi, pw = ctx.pool.acquire()
            ps = ctx.psF[pi]
            tv = None
            for j in range(8):
                tv = p.step(
                    "tensor",
                    lambda t, ps=ps, j=j, half=half: t.matmul(
                        ps[:],
                        ofm3[:, j * 128 : (j + 1) * 128],
                        sc["l3wo"][:, j * 1024 + half * 512 : j * 1024 + half * 512 + 512],
                        start=(j == 0), stop=(j == 7)),
                    wait=(pw + [wo_w, ofm_ready]) if j == 0 else (),
                    inc=ctx.te if j == 7 else None)
            xr = ctx.x_res[:, cg * 1024 + half * 512 : cg * 1024 + half * 512 + 512]
            vv3 = p.step("vector", lambda v, xr=xr, ps=ps: v.tensor_add(xr, ps[:], xr),
                         wait=[(ctx.te, tv)], inc=ctx.ve)
            ctx.pool.release(pi, (ctx.ve, vv3))
        sc["ofm3_guard"] = [(ctx.te, ctx.te.n)]
    te_g = [(ctx.te, ctx.te.n)]
    ctx.rw.ack(wo_slot, *te_g)
    sc["l3w_guard"] = [list(te_g) for _ in range(4)]
    return [(ctx.ve, ctx.ve.n)]


def l3_ffn_batch(ctx, ws, sc):
    p = ctx.p
    T, NT = ctx.NT * 0 + ctx.T, ctx.NT
    NT = ctx.NT
    xn_fm = ctx.xn_fm
    fence = sc["ffn_fence"]
    wts = sc["wts3"]
    wts_ready = moe_gate8(ctx, sc, "L3GW", ws, "l3", wts[:])
    b1sb = sc["b1sb"]
    sb1, b1w = ctx.rx.load(lambda e: e.dma_start(out=b1sb[:], in_=ws["L3B1"]), wait=list(fence))
    w1buf = sc["w1buf"]
    w2buf = sc["w2buf"]
    hbuf = sc["hbuf"]
    w1g = sc["w1g"]
    w2g = sc["w2g"]
    hg = sc["hg"]
    for e in range(8):
        for fh in range(4):
            s1, w1w = ctx.rw.load(
                lambda eng, e=e, fh=fh: eng.dma_start(
                    out=w1buf[:].rearrange("p (c w) -> p c w", c=8, w=1024),
                    in_=ws["L3W1"][e][:, fh * 1024 : (fh + 1) * 1024].rearrange("(c p) w -> p c w", p=128)),
                wait=w1g)
            s2, w2w = ctx.rw.load(
                lambda eng, e=e, fh=fh: eng.dma_start(
                    out=w2buf[:].rearrange("p (c w) -> p c w", c=8, w=1024),
                    in_=ws["L3W2"][e][fh * 1024 : (fh + 1) * 1024, :].rearrange("(c p) w -> p c w", p=128)),
                wait=w2g)
            for fc in range(8):
                fcg = fh * 8 + fc
                for ts4 in range(cdiv(NT, 4)):
                    w512 = min(512, NT * 128 - ts4 * 512)
                    pi, pw = ctx.pool.acquire()
                    ps = ctx.psF[pi]
                    tv = None
                    for c in range(8):
                        tv = p.step(
                            "tensor",
                            lambda t, ps=ps, c=c, fc=fc, ts4=ts4, w512=w512: t.matmul(
                                ps[:, 0:w512],
                                w1buf[:, c * 1024 + fc * 128 : c * 1024 + fc * 128 + 128],
                                xn_fm[:, c * T + ts4 * 512 : c * T + ts4 * 512 + w512],
                                start=(c == 0), stop=(c == 7)),
                            wait=(pw + [w1w] + sc["xn_ready"]) if c == 0 else (),
                            inc=ctx.te if c == 7 else None)
                    ev = p.step(
                        "scalar",
                        lambda sce, ps=ps, fc=fc, fcg=fcg, ts4=ts4, w512=w512, e=e: sce.activation(
                            hbuf[:, fc * (NT * 128) + ts4 * 512 : fc * (NT * 128) + ts4 * 512 + w512],
                            ps[:, 0:w512], AF.Gelu, bias=b1sb[:, e * 32 + fcg : e * 32 + fcg + 1]),
                        wait=[(ctx.te, tv), b1w] + (hg if (fc == 0 and ts4 == 0) else []),
                        inc=ctx.se)
                    ctx.pool.release(pi, (ctx.se, ev))
            h_ready = (ctx.se, ctx.se.n)
            for ti in range(NT):
                pis = []
                for half in range(2):
                    pi, pw = ctx.pool.acquire()
                    ps = ctx.psF[pi]
                    tv = None
                    for fc in range(8):
                        tv = p.step(
                            "tensor",
                            lambda t, ps=ps, fc=fc, ti=ti, half=half: t.matmul(
                                ps[:],
                                hbuf[:, fc * (NT * 128) + ti * 128 : fc * (NT * 128) + ti * 128 + 128],
                                w2buf[:, fc * 1024 + half * 512 : fc * 1024 + half * 512 + 512],
                                start=(fc == 0), stop=(fc == 7)),
                            wait=(pw + [w2w, h_ready]) if fc == 0 else (),
                            inc=ctx.te if fc == 7 else None)
                    pis.append((pi, tv))
                for half in range(2):
                    pi, tv = pis[half]
                    ps = ctx.psF[pi]
                    xr = ctx.x_res[:, ti * 1024 + half * 512 : ti * 1024 + half * 512 + 512]
                    gc = wts[:, ti * 8 + e : ti * 8 + e + 1]
                    vv = p.step("vector",
                                lambda v, xr=xr, ps=ps, gc=gc: v.scalar_tensor_tensor(
                                    xr, ps[:], gc, xr, ALU.mult, ALU.add),
                                wait=[(ctx.te, tv)] + wts_ready, inc=ctx.ve)
                    ctx.pool.release(pi, (ctx.ve, vv))
            w1g = [(ctx.te, ctx.te.n), w1w]
            w2g = [(ctx.te, ctx.te.n), w2w]
            hg = [(ctx.te, ctx.te.n)]
            ctx.rw.ack(s1, (ctx.te, ctx.te.n))
            ctx.rw.ack(s2, (ctx.te, ctx.te.n))
    ctx.rx.ack(sb1, (ctx.se, ctx.se.n))
    b2sb = sc["b2sb"]
    sb2, b2w = ctx.rw.load(lambda e: e.dma_start(out=b2sb[:], in_=ws["L3B2"]), wait=list(fence))
    wtsf = sc["wtsf"]
    last = None
    for ti in range(NT):
        vv = p.step("vector", lambda v, ti=ti: v.tensor_copy(wtsf[:, 0:8], wts[:, ti * 8 : (ti + 1) * 8]),
                    wait=wts_ready + sc["wtsf_guard"], inc=ctx.ve)
        trw = pe_transpose_pairs(ctx, [(wtsf[:, 0:8], sc["wtsfT"][0:8, 0:128])],
                                 wait=[(ctx.ve, vv)] + sc["wtsfT_guard"], in_w=8)
        for half in range(2):
            pi, pw = ctx.pool.acquire()
            ps = ctx.psF[pi]
            tv = p.step("tensor",
                        lambda t, ps=ps, half=half: t.matmul(
                            ps[:], sc["wtsfT"][0:8, 0:128], b2sb[0:8, half * 512 : half * 512 + 512],
                            start=True, stop=True),
                        wait=pw + [b2w, (ctx.ve, trw)], inc=ctx.te)
            xr = ctx.x_res[:, ti * 1024 + half * 512 : ti * 1024 + half * 512 + 512]
            vv2 = p.step("vector", lambda v, xr=xr, ps=ps: v.tensor_add(xr, ps[:], xr),
                         wait=[(ctx.te, tv)], inc=ctx.ve)
            ctx.pool.release(pi, (ctx.ve, vv2))
            last = vv2
        sc["wtsf_guard"] = [(ctx.te, ctx.te.n)]
        sc["wtsfT_guard"] = [(ctx.te, ctx.te.n)]
    ctx.rw.ack(sb2, (ctx.te, ctx.te.n))
    return [(ctx.ve, ctx.ve.n)]


def layer3(ctx, ws, x_src, x_dst, prev_store=None, parts="all"):
    p = ctx.p
    NT, T = ctx.NT, ctx.T
    les = ExitStack()
    sc = {}
    ctx.xn_fm = p.sbuf("l3_xnfm", [128, 8 * T], BF16, es=les)
    ln_phase(ctx, x_src, prev_store)
    sc["xn_ready"] = [(ctx.ve, ctx.ve.n), (ctx.se, ctx.se.n)]
    fence0 = ctx.fence()
    sc["fence0"] = fence0
    if parts in ("all", "attn"):
        pes = ExitStack()
        sc["q_tm"] = p.sbuf("l3_qtm", [128, NT * 256], BF16, es=pes)
        sc["k_tm"] = p.sbuf("l3_ktm", [128, NT * 256], BF16, es=pes)
        sc["v_tm"] = p.sbuf("l3_vtm", [128, NT * 1024], BF16, es=pes)
        sc["qf"] = [p.sbuf(f"l3_qf{i}", [128, 384], BF16, es=pes) for i in range(2)]
        sc["kf"] = [p.sbuf(f"l3_kf{i}", [128, 384], BF16, es=pes) for i in range(2)]
        sc["qffm"] = [p.sbuf(f"l3_qffm{i}", [128, 384], BF16, es=pes) for i in range(2)]
        sc["kfm0"] = [p.sbuf(f"l3_kfm0{i}", [128, 128], BF16, es=pes) for i in range(2)]
        sc["vaug"] = [p.sbuf(f"l3_vaug{i}", [128, 68], BF16, es=pes) for i in range(2)]
        sc["a2"] = p.sbuf("l3_a2", [128, 128], F32, es=pes)
        sc["am3"] = [p.sbuf(f"l3_am{i}", [128, 128], BF16, es=pes) for i in range(2)]
        sc["ost3"] = [p.sbuf(f"l3_ost{i}", [128, 64], BF16, es=pes) for i in range(2)]
        sc["ofm3"] = p.sbuf("l3_ofm", [128, 1024], BF16, es=pes)
        sc["S32_3"] = p.sbuf("l3_S32", [128, 16 * 204], F32, es=pes)
        sc["Sbf_3"] = p.sbuf("l3_Sbf", [128, 16 * 204], BF16, es=pes)
        sc["zrow"] = p.sbuf("l3_zrow", [128, 16], F32, es=pes)
        sc["pes"] = pes
        sc["l3w_guard"] = [list(fence0) for _ in range(4)]
        sc["S3_guard"] = list(fence0)
        sc["a2_guard"] = list(fence0)
        sc["ofm3_guard"] = list(fence0)
        for k in ("qffm_guard", "kfm0_guard", "am3_guard", "ost3_guard", "zrow_guard"):
            sc[k] = [list(fence0), list(fence0)]
        l3_attn_batch(ctx, ws, sc)
        pes.close()
    fence1 = ctx.fence()
    sc["ffn_fence"] = fence1
    if parts in ("all", "ffn"):
        pes2 = ExitStack()
        sc["wts3"] = p.sbuf("l3_wts", [128, NT * 8], F32, es=pes2)
        sc["gwb"] = p.sbuf("l3_gwb", [128, 64], BF16, es=pes2)
        sc["grow8"] = p.sbuf("l3_gr8", [1, 16], F32, es=pes2)
        sc["grow8b"] = p.sbuf("l3_gr8b", [1, 16], BF16, es=pes2)
        sc["gb_bc"] = p.sbuf("l3_gbbc", [128, 8], F32, es=pes2)
        sc["gsc8"] = p.sbuf("l3_gsc8", [128, 64], F32, es=pes2)
        sc["b1sb"] = p.sbuf("l3_b1", [128, 256], F32, es=pes2)
        sc["b2sb"] = p.sbuf("l3_b2", [8, 1024], BF16, es=pes2)
        sc["w1buf"] = p.sbuf("l3_w1b", [128, 8 * 1024], BF16, es=pes2)
        sc["w2buf"] = p.sbuf("l3_w2b", [128, 8 * 1024], BF16, es=pes2)
        sc["hbuf"] = p.sbuf("l3_hb", [128, 8 * NT * 128], BF16, es=pes2)
        sc["wtsf"] = p.sbuf("l3_wtsf", [128, 8], BF16, es=pes2)
        sc["wtsfT"] = p.sbuf("l3_wtsfT", [8, 128], BF16, es=pes2)
        sc["gw_guard"] = list(fence1)
        sc["g8_guard"] = [list(fence1), list(fence1)]
        sc["g8s_guard"] = [list(fence1), list(fence1)]
        sc["wts_guard"] = list(fence1)
        sc["wtsf_guard"] = list(fence1)
        sc["wtsfT_guard"] = list(fence1)
        sc["w1g"] = list(fence1)
        sc["w2g"] = list(fence1)
        sc["hg"] = list(fence1)
        l3_ffn_batch(ctx, ws, sc)
        pes2.close()
    les.close()
    ctx.xn_fm = None
    if x_dst is None:
        return None
    return store_xres(ctx, x_dst)
from concourse.bass_utils import run_bass_kernel_spmd

D4_T = 2048
D4_NB = 2
N_CORES = 8


def _win(nc, ws, host, name, arr):
    arr = np.ascontiguousarray(arr, dtype=np.float32)
    ws[name] = nc.dram_tensor(name, list(arr.shape), F32, kind="ExternalInput").ap()
    host[name] = arr


def _prep_host_weights(inputs):
    h = {}
    Wqkv = inputs["L0_attn_Wqkv"]
    h["WQ"] = Wqkv[0:1024].T
    h["WK"] = Wqkv[1024:2048].T
    h["WV"] = Wqkv[2048:3072].T
    h["WOUT"] = inputs["L0_attn_Wout"].T
    h["SIM"] = inputs["L0_ffn_sim_matrix"]
    h["SIMT"] = inputs["L0_ffn_sim_matrix"].T
    h["GATES"] = np.asarray(inputs["L0_ffn_gates"]).reshape(1, 16)
    h["FHN"] = np.array([[inputs["L0_attn_a"], inputs["L0_attn_b"], inputs["L0_attn_dt"]]],
                        dtype=np.float32)
    h["QP"] = np.asarray(inputs["L0_ffn_q_proj"]).transpose(1, 2, 0).reshape(1024, 1024)
    h["KP"] = np.asarray(inputs["L0_ffn_k_proj"]).transpose(1, 2, 0).reshape(1024, 1024)
    h["VP"] = np.asarray(inputs["L0_ffn_v_proj"]).transpose(1, 2, 0).reshape(1024, 1024)
    h["OP"] = np.asarray(inputs["L0_ffn_o_proj"]).transpose(1, 0, 2).reshape(1024, 1024)
    h["L1WG"] = inputs["L1_ffn_w_gate"]
    h["L1WU"] = inputs["L1_ffn_w_up"]
    h["L1WD"] = inputs["L1_ffn_w_down"]
    h["L2WQ"] = inputs["L2_attn_Wq"].T
    h["L2WK"] = inputs["L2_attn_Wk"].T
    h["L2WV"] = inputs["L2_attn_Wv"].T
    h["L2WO"] = inputs["L2_attn_Wo"].T
    h["L2FMQ"] = inputs["L2_attn_fmq_w"].T
    h["L2FMK"] = inputs["L2_attn_fmk_w"].T
    h["L2FMB"] = np.stack([np.asarray(inputs["L2_attn_fmq_b"]),
                           np.asarray(inputs["L2_attn_fmk_b"])], axis=1)
    h["L2RMSW"] = np.asarray(inputs["L2_attn_rms_w"]).reshape(1, 128)
    h["L2GW"] = inputs["L2_ffn_gate_w"].T
    h["L2GB"] = np.asarray(inputs["L2_ffn_gate_bias"]).reshape(1, 8)
    h["L2W1"] = np.asarray(inputs["L2_ffn_w1"]).transpose(0, 2, 1)
    h["L2W3"] = np.asarray(inputs["L2_ffn_w3"]).transpose(0, 2, 1)
    h["L2W2"] = np.asarray(inputs["L2_ffn_w2"]).transpose(0, 2, 1)
    h["L2SW1"] = inputs["L2_ffn_sw1"].T
    h["L2SW3"] = inputs["L2_ffn_sw3"].T
    h["L2SW2"] = inputs["L2_ffn_sw2"].T
    h["L3WQ"] = inputs["L3_attn_Wq"]
    h["L3WK"] = inputs["L3_attn_Wk"]
    h["L3WV"] = inputs["L3_attn_Wv"]
    h["L3WO"] = inputs["L3_attn_Wo"]
    h["L3GW"] = inputs["L3_ffn_Wg"]
    h["L3GB"] = np.asarray(inputs["L3_ffn_bg"]).reshape(1, 8)
    h["L3W1"] = inputs["L3_ffn_W1"]
    b1 = np.asarray(inputs["L3_ffn_b1"])
    h["L3B1"] = np.concatenate([b1[e].reshape(32, 128).T for e in range(8)], axis=1)
    h["L3W2"] = inputs["L3_ffn_W2"]
    h["L3B2"] = inputs["L3_ffn_b2"]
    return {k: np.ascontiguousarray(v, dtype=np.float32) for k, v in h.items()}


_L0_KEYS = ["WQ", "WK", "WV", "WOUT", "SIM", "SIMT", "GATES", "FHN", "QP", "KP", "VP", "OP"]
_L1_KEYS = ["L1WG", "L1WU", "L1WD"]
_L2_KEYS = ["L2WQ", "L2WK", "L2WV", "L2WO", "L2FMQ", "L2FMK", "L2FMB", "L2RMSW",
            "L2GW", "L2GB", "L2W1", "L2W3", "L2W2", "L2SW1", "L2SW3", "L2SW2"]
_L3_KEYS = ["L3WQ", "L3WK", "L3WV", "L3WO", "L3GW", "L3GB", "L3W1", "L3B1", "L3W2", "L3B2"]


def _build_layer_prog(layer_idx, host):
    nc = bass.Bass("TRN2", target_bir_lowering=False)
    p = Prog(nc)
    ctx = Ctx(p, D4_T)
    ws = {}
    hw = {}
    keys = (_L0_KEYS, _L1_KEYS, _L2_KEYS, _L3_KEYS)[layer_idx]
    for k in keys:
        _win(nc, ws, hw, k, host[k])
    X = nc.dram_tensor("X", [D4_NB, D4_T, 1024], F32, kind="ExternalInput").ap()
    XO = nc.dram_tensor("XO", [D4_NB, D4_T, 1024], F32, kind="ExternalOutput").ap()
    ctx.load_consts()
    wqs = None
    wq_ready = None
    if layer_idx == 1:
        WQG = nc.dram_tensor("WQG", [1024, 2816], BF16, kind="Internal").ap()
        WQU = nc.dram_tensor("WQU", [1024, 2816], BF16, kind="Internal").ap()
        WQD = nc.dram_tensor("WQD", [2816, 1024], BF16, kind="Internal").ap()
        wqs = (WQG, WQU, WQD)
        wq_ready = l1_wquant(ctx, ws, wqs)
    prev = None
    for b in range(D4_NB):
        if layer_idx == 0:
            prev = layer0(ctx, ws, X[b], XO[b], prev_store=prev)
        elif layer_idx == 1:
            prev = layer1(ctx, ws, X[b], XO[b], wqs, wq_ready, prev_store=prev)
        elif layer_idx == 2:
            prev = layer2(ctx, ws, X[b], XO[b], prev_store=prev)
        else:
            prev = layer3(ctx, ws, X[b], XO[b], prev_store=prev)
    p.step("sync", lambda e: None, wait=[prev])
    ctx.drain_rings()
    p.finalize_block()
    p.es.close()
    consts = {"CONSTF": ctx.const_f32_host, "CONSTB": ctx.const_bf_host}
    return nc, hw, consts


def kernel(**inputs):
    x = np.ascontiguousarray(np.asarray(inputs["L0_attn_x"], dtype=np.float32))
    host = _prep_host_weights(inputs)
    cur = x
    for li in range(4):
        nc, hw, consts = _build_layer_prog(li, host)
        in_maps = []
        for c in range(N_CORES):
            m = {"X": np.ascontiguousarray(cur[c * D4_NB : (c + 1) * D4_NB])}
            m.update(hw)
            m.update(consts)
            in_maps.append(m)
        res = run_bass_kernel_spmd(nc, in_maps, core_ids=list(range(N_CORES)))
        cur = np.concatenate([np.asarray(r["XO"]) for r in res.results], axis=0)
    return cur.astype(np.float32)


_ALL_KEYS = _L0_KEYS + _L1_KEYS + _L2_KEYS + _L3_KEYS


def _build_fused_prog(host, layers=(0, 1, 2, 3)):
    nc = bass.Bass("TRN2", target_bir_lowering=False)
    p = Prog(nc)
    ctx = Ctx(p, D4_T)
    ws = {}
    hw = {}
    for k in _ALL_KEYS:
        _win(nc, ws, hw, k, host[k])
    X = nc.dram_tensor("X", [D4_NB, D4_T, 1024], F32, kind="ExternalInput").ap()
    XO = nc.dram_tensor("XO", [D4_NB, D4_T, 1024], F32, kind="ExternalOutput").ap()
    ctx.load_consts()
    WQG = nc.dram_tensor("WQG", [1024, 2816], BF16, kind="Internal").ap()
    WQU = nc.dram_tensor("WQU", [1024, 2816], BF16, kind="Internal").ap()
    WQD = nc.dram_tensor("WQD", [2816, 1024], BF16, kind="Internal").ap()
    wqs = (WQG, WQU, WQD)
    import os as _os
    dbg = _os.environ.get("K_DBG") == "1"
    DBGT = {}
    if dbg:
        for li in range(4):
            DBGT[li] = nc.dram_tensor(f"DBG{li}", [D4_NB, D4_T, 1024], F32, kind="ExternalOutput").ap()
    wq_ready = l1_wquant(ctx, ws, wqs) if 1 in layers else None
    prev = None
    for b in range(D4_NB):
        first = True
        def src_of():
            nonlocal first
            if first:
                first = False
                return X[b]
            return None
        last = max(layers)
        for li in layers:
            xs = X[b] if li == min(layers) else None
            xd = XO[b] if li == last else None
            if li == 0:
                r = layer0(ctx, ws, xs, xd, prev_store=prev)
            elif li == 1:
                r = layer1(ctx, ws, xs, xd, wqs, wq_ready, prev_store=prev)
            elif li == 2:
                r = layer2(ctx, ws, xs, xd, prev_store=prev)
            else:
                r = layer3(ctx, ws, xs, xd, prev_store=prev)
            if dbg:
                r2 = store_xres(ctx, DBGT[li][b])
                if r is None:
                    r = r2
        prev = r
    p.step("sync", lambda e: None, wait=[prev])
    ctx.drain_rings()
    p.finalize_block()
    p.es.close()
    consts = {"CONSTF": ctx.const_f32_host, "CONSTB": ctx.const_bf_host}
    return nc, hw, consts


def kernel_fused(**inputs, ):
    x = np.ascontiguousarray(np.asarray(inputs["L0_attn_x"], dtype=np.float32))
    host = _prep_host_weights(inputs)
    import os
    lys = tuple(int(c) for c in os.environ.get("K_LAYERS", "0123"))
    nc, hw, consts = _build_fused_prog(host, layers=lys)
    in_maps = []
    for c in range(N_CORES):
        m = {"X": np.ascontiguousarray(x[c * D4_NB : (c + 1) * D4_NB])}
        m.update(hw)
        m.update(consts)
        in_maps.append(m)
    res = run_bass_kernel_spmd(nc, in_maps, core_ids=list(range(N_CORES)))
    return np.concatenate([np.asarray(r["XO"]) for r in res.results], axis=0).astype(np.float32)


kernel = kernel_fused


# version 07
# speedup vs baseline: 1.1802x; 1.0345x over previous
import math
import numpy as np
import ml_dtypes
import concourse.bass as bass
import concourse.mybir as mybir
from contextlib import ExitStack

F32 = mybir.dt.float32
BF16 = mybir.dt.bfloat16
U8 = mybir.dt.uint8
AF = mybir.ActivationFunctionType
ALU = mybir.AluOpType
AX = mybir.AxisListType

ENGINES = ("sync", "vector", "scalar", "tensor", "gpsimd")
USE_GP_QUANT = False

LN_EPS = 1e-5
CONST_SCALARS = {
    "eps_ln": 1e-5,
    "neg_ln16": -float(np.log(16.0)),
    "eps8": 1e-8,
    "eps12": 1e-12,
    "eps6": 1e-6,
    "neg30": -1e30,
    "zero": 0.0,
}


class Sem:
    def __init__(self, handle, name):
        self.h = handle
        self.name = name
        self.n = 0


class Prog:

    def __init__(self, nc):
        self.nc = nc
        self.es = ExitStack()
        self.q = {e: [] for e in ENGINES}
        self.sems = {}
        self.waited = {e: {} for e in ENGINES}
        self.carry = {e: {} for e in ENGINES}

    def sem(self, name):
        if name not in self.sems:
            h = self.es.enter_context(self.nc.semaphore(name))
            self.sems[name] = Sem(h, name)
        return self.sems[name]

    _uid = 0

    def sbuf(self, name, shape, dt=F32, es=None):
        Prog._uid += 1
        return (es or self.es).enter_context(self.nc.sbuf_tensor(f"{name}_{Prog._uid}", shape, dt))

    def psum(self, name, shape, dt=F32):
        return self.es.enter_context(self.nc.psum_tensor(name, shape, dt))

    def step(self, eng, fn, wait=(), inc=None, inc_k=1):
        w = []
        hw = self.waited[eng]
        carry = self.carry[eng]
        for s, v in wait:
            if s is None:
                continue
            assert v <= s.n, f"[{eng}] wait on {s.name} for {v} but only {s.n} planned"
            if v <= 0:
                continue
            if inc is None and carry.get(s, 0) < v and not s.name.startswith(("rx", "rw", "rs", "ld", "st")):
                carry[s] = v
            if not s.name.startswith(("rx", "rw", "rs", "ld", "st")):
                if hw.get(s.name, 0) >= v:
                    continue
                hw[s.name] = v
            w.append((s, v))
        ret = None
        if inc is not None:
            have = {s: v for s, v in w}
            for s, v in carry.items():
                if have.get(s, 0) < v:
                    w.append((s, v))
            carry.clear()
            inc.n += inc_k
            ret = inc.n
        self.q[eng].append((tuple(w), fn, inc, inc_k))
        return ret

    def n_queued(self):
        return sum(len(v) for v in self.q.values())

    def finalize_block(self):
        nc = self.nc
        with nc.Block() as blk:
            for e in ENGINES:
                items = self.q[e]
                if not items:
                    continue

                def make(items):
                    def run(proxy):
                        for w, fn, inc, inc_k in items:
                            for s, v in w:
                                proxy.wait_ge(s.h, v)
                            ins = fn(proxy) if fn is not None else None
                            if inc is not None:
                                assert ins is not None, "inc with no instruction"
                                ins.then_inc(inc.h, inc_k)
                    return run

                getattr(blk, e)(make(items))
        self.q = {e: [] for e in ENGINES}


def cdiv(a, b):
    return (a + b - 1) // b


class PsPool:

    def __init__(self, tensors):
        self.t = tensors
        self.busy = [[] for _ in tensors]
        self.rr = 0

    def acquire(self):
        i = self.rr % len(self.t)
        self.rr += 1
        w = self.busy[i]
        self.busy[i] = []
        return i, list(w)

    def release(self, i, *wspecs):
        self.busy[i] = [w for w in wspecs if w is not None]


class DmaRing:

    def __init__(self, p, name, K, queue="sync"):
        self.p = p
        self.K = K
        self.queue = queue
        self.sems = [p.sem(f"{name}{j}") for j in range(K)]
        self.acks = [[] for _ in range(K)]
        self.i = 0

    def load(self, fn, wait=()):
        j = self.i % self.K
        self.i += 1
        w = list(wait)
        if self.acks[j]:
            w.extend(self.acks[j])
        v = self.p.step(self.queue, fn, wait=w, inc=self.sems[j], inc_k=16)
        self.acks[j] = []
        return j, (self.sems[j], v)

    def ack(self, j, *wspecs):
        self.acks[j] = [w for w in wspecs if w is not None]




def build_const_host():
    f32_cols = {}
    bf_cols = {}
    cf_parts = []
    cb_parts = []
    colf = [0]
    colb = [0]

    def add_f(name, arr):
        arr = np.asarray(arr, dtype=np.float32)
        a = np.zeros((128, arr.shape[1]), dtype=np.float32)
        a[: arr.shape[0]] = arr
        f32_cols[name] = (colf[0], arr.shape[1])
        colf[0] += arr.shape[1]
        cf_parts.append(a)

    def add_b(name, arr):
        arr = np.asarray(arr, dtype=np.float32)
        a = np.zeros((128, arr.shape[1]), dtype=np.float32)
        a[: arr.shape[0]] = arr
        bf_cols[name] = (colb[0], arr.shape[1])
        colb[0] += arr.shape[1]
        cb_parts.append(a.astype(ml_dtypes.bfloat16))

    for nm, val in CONST_SCALARS.items():
        add_f("s_" + nm, np.full((128, 1), val, dtype=np.float32))
    add_b("ident", np.eye(128, dtype=np.float32))
    ar = np.arange(128)
    add_b("neg_low", np.where(ar[:, None] > ar[None, :], -1e30, 0.0).astype(np.float32))
    se = np.zeros((128, 1), np.float32); se[:64] = 1.0
    so = np.zeros((128, 1), np.float32); so[64:] = 1.0
    add_b("sel_even", se)
    add_b("sel_odd", so)
    add_b("ones_col128", np.ones((128, 1), np.float32))
    add_b("mask_t2_le_t1", (np.arange(128)[:, None] <= np.arange(128)[None, :]).astype(np.float32))
    add_b("ones_row_b", np.ones((1, 128), dtype=np.float32))
    add_f("ones_row", np.ones((1, 128), dtype=np.float32))
    cf = np.concatenate(cf_parts, axis=1)
    cb = np.concatenate(cb_parts, axis=1)
    return cf, cb, f32_cols, bf_cols


class Ctx:

    def __init__(self, p: Prog, T):
        self.p = p
        self.T = T
        self.NT = T // 128
        nc = p.nc
        cf, cb, f32_cols, bf_cols = build_const_host()
        self.const_f32_host = cf
        self.const_bf_host = cb
        self.cF = nc.dram_tensor("CONSTF", list(cf.shape), F32, kind="ExternalInput").ap()
        self.cB = nc.dram_tensor("CONSTB", list(cb.shape), BF16, kind="ExternalInput").ap()
        self.cf_sb = p.sbuf("cf_sb", list(cf.shape), F32)
        self.cb_sb = p.sbuf("cb_sb", list(cb.shape), BF16)
        self.f32_cols = f32_cols
        self.bf_cols = bf_cols
        self.x_res = p.sbuf("x_res", [128, self.NT * 1024], F32)
        self.xn_fm = None
        self.xn_stage = [p.sbuf(f"xn_stage{i}", [128, 1024], BF16) for i in range(2)]
        self.sq_scr = [p.sbuf("sq_scr0", [128, 1024], F32)]
        self.row_scr = p.sbuf("row_scr", [128, 256], F32)
        self.psF = [p.psum(f"psF{i}", [128, 512], F32) for i in range(6)]
        self.psT = [p.psum(f"psT{i}", [128, 512], BF16) for i in range(2)]
        self.pool = PsPool(self.psF)
        self.poolT = PsPool(self.psT)
        self.ld = p.sem("ld")
        self.stage_guard = [[], []]
        self.scr_last = [0]
        self._gb11 = p.sbuf("g_b11", [1, 1], BF16)
        self.rx = DmaRing(p, "rx", 6, "sync")
        self.rw = DmaRing(p, "rw", 10, "gpsimd")
        self.rs = DmaRing(p, "rs", 3, "sync")
        self.st = p.sem("st")
        self.te = p.sem("te")
        self.ve = p.sem("ve")
        self.se = p.sem("se")
        self.ge = p.sem("ge")

    def drain_rings(self):
        for ring, eng in ((self.rx, "sync"), (self.rs, "sync"), (self.rw, "gpsimd")):
            for s in ring.sems:
                if s.n > 0:
                    self.p.step(eng, None, wait=[(s, s.n)])
        if self.ld.n > 0:
            self.p.step("sync", None, wait=[(self.ld, self.ld.n)])
        if self.st.n > 0:
            self.p.step("sync", None, wait=[(self.st, self.st.n)])

    def fence(self):
        w = [(self.te, self.te.n), (self.ve, self.ve.n), (self.se, self.se.n),
             (self.ge, self.ge.n)]
        for ring in (self.rx, self.rw, self.rs):
            for s in ring.sems:
                if s.n > 0:
                    w.append((s, s.n))
        if self.ld.n > 0:
            w.append((self.ld, self.ld.n))
        if self.st.n > 0:
            w.append((self.st, self.st.n))
        return w

    def cfa(self, name):
        off, w = self.f32_cols[name]
        return self.cf_sb[:, off : off + w]

    def cba(self, name):
        off, w = self.bf_cols[name]
        return self.cb_sb[:, off : off + w]

    def load_consts(self):
        p = self.p
        p.step("sync", lambda e: e.dma_start(out=self.cf_sb[:], in_=self.cF), inc=self.ld, inc_k=16)
        p.step("sync", lambda e: e.dma_start(out=self.cb_sb[:], in_=self.cB), inc=self.ld, inc_k=16)
        self.const_ready = (self.ld, self.ld.n)
        for e in ENGINES:
            p.step(e, None, wait=[self.const_ready])






def ln_from_xres(ctx, ti, xwait, scr_last, dst_tile=None):
    p = ctx.p
    rs = ctx.row_scr
    col0 = (ti % 16) * 8
    c_sum = rs[:, col0 + 0 : col0 + 1]
    c_sumsq = rs[:, col0 + 1 : col0 + 2]
    c_mean = rs[:, col0 + 2 : col0 + 3]
    c_m2 = rs[:, col0 + 3 : col0 + 4]
    c_var = rs[:, col0 + 4 : col0 + 5]
    c_sd = rs[:, col0 + 5 : col0 + 6]
    c_rstd = rs[:, col0 + 6 : col0 + 7]
    xr = ctx.x_res[:, ti * 1024 : (ti + 1) * 1024]
    scr = ctx.sq_scr[0]
    sev = p.step(
        "scalar",
        lambda e, xr=xr, scr=scr, c_sumsq=c_sumsq: e.activation(
            scr[:], xr, AF.Square, accum_out=c_sumsq
        ),
        wait=[xwait, (ctx.se, scr_last[0])],
        inc=ctx.se,
    )
    scr_last[0] = sev
    vev = p.step(
        "vector",
        lambda e, xr=xr, c_sum=c_sum: e.reduce_sum(c_sum, xr, axis=AX.X),
        wait=[xwait],
        inc=ctx.ve,
    )
    vev = p.step(
        "vector",
        lambda e, c_mean=c_mean, c_sum=c_sum: e.tensor_scalar(
            c_mean, c_sum, 1.0 / 1024.0, None, ALU.mult
        ),
        wait=[(ctx.ve, vev)],
        inc=ctx.ve,
    )
    vev = p.step(
        "vector",
        lambda e, c_m2=c_m2, c_mean=c_mean: e.tensor_mul(c_m2, c_mean, c_mean),
        wait=[(ctx.ve, vev)],
        inc=ctx.ve,
    )
    vev = p.step(
        "vector",
        lambda e, c_var=c_var, c_sumsq=c_sumsq, c_m2=c_m2: e.scalar_tensor_tensor(
            c_var, c_sumsq, 1.0 / 1024.0, c_m2, ALU.mult, ALU.subtract
        ),
        wait=[(ctx.ve, vev), (ctx.se, sev)],
        inc=ctx.ve,
    )
    sev2 = p.step(
        "scalar",
        lambda e, c_sd=c_sd, c_var=c_var: e.activation(c_sd, c_var, AF.Sqrt, bias=ctx.cfa("s_eps_ln")[:, 0:1]),
        wait=[(ctx.ve, vev)],
        inc=ctx.se,
    )
    vev = p.step(
        "vector",
        lambda e, c_rstd=c_rstd, c_sd=c_sd: e.reciprocal(c_rstd, c_sd),
        wait=[(ctx.se, sev2)],
        inc=ctx.ve,
    )
    if dst_tile is None:
        stage = ctx.xn_stage[ti % 2]
        xt = stage[:, 0:1024]
    else:
        xt = dst_tile
    vev = p.step(
        "vector",
        lambda e, xr=xr, xt=xt, c_mean=c_mean, c_rstd=c_rstd: e.tensor_scalar(
            xt, xr, c_mean, c_rstd, ALU.subtract, ALU.mult
        ),
        wait=[(ctx.ve, vev)] + (ctx.stage_guard[ti % 2] if dst_tile is None else []),
        inc=ctx.ve,
    )
    if ctx.xn_fm is not None:
        ve_fm = tm_to_fm_src(ctx, xt, ti, ctx.xn_fm, (ctx.ve, vev))
    else:
        ve_fm = vev
    if dst_tile is None:
        ctx.stage_guard[ti % 2] = [(ctx.ve, ve_fm)]
    return [(ctx.se, sev), (ctx.ve, vev)]


def tm_to_fm_src(ctx, src_tile, ti, fm_buf, wait, nchunk=8):
    p = ctx.p
    T = ctx.T
    ident = ctx.cba("ident")
    fm3 = fm_buf.rearrange("p (c t) -> p c t", c=nchunk, t=T)
    vv = None
    for g in range(cdiv(nchunk, 4)):
        nin = min(4, nchunk - g * 4)
        bank, bw = ctx.poolT.acquire()
        ps = ctx.psT[bank]
        tw = bw + list([wait])
        tvs = None
        for c4 in range(nin):
            c = g * 4 + c4
            tvs = p.step(
                "tensor",
                lambda t, c=c, c4=c4, ps=ps: t.transpose(
                    ps[:, c4 * 128 : (c4 + 1) * 128],
                    src_tile[:, c * 128 : (c + 1) * 128],
                    ident,
                ),
                wait=tw,
                inc=ctx.te,
            )
            tw = ()
        dst = fm3[:, g * 4 : g * 4 + nin, ti * 128 : (ti + 1) * 128]
        src = ps.rearrange("p (c t) -> p c t", c=4, t=128)[:, 0:nin, :]
        vv = p.step(
            "vector",
            lambda v, dst=dst, src=src: v.tensor_copy(dst, src),
            wait=[(ctx.te, tvs)],
            inc=ctx.ve,
        )
        ctx.poolT.release(bank, (ctx.ve, vv))
    return vv


def ln_phase(ctx, x_src, prev_store, dst_fn=None):
    scr_last = ctx.scr_last
    for ti in range(ctx.NT):
        dst = dst_fn(ti) if dst_fn is not None else None
        if x_src is not None:
            xr = ctx.x_res[:, ti * 1024 : (ti + 1) * 1024]
            lw = [ctx.const_ready]
            if prev_store is not None:
                lw.append(prev_store)
            slot, wsp = ctx.rx.load(
                lambda e, ti=ti, xr=xr: e.dma_start(out=xr, in_=x_src[ti * 128 : (ti + 1) * 128, :]),
                wait=lw)
            consumed = ln_from_xres(ctx, ti, wsp, scr_last, dst_tile=dst)
            ctx.rx.ack(slot, *consumed)
        else:
            ln_from_xres(ctx, ti, (ctx.ve, ctx.ve.n), scr_last, dst_tile=dst)


def store_xres(ctx, x_dst):
    p = ctx.p
    NT = ctx.NT
    v = None
    for g in range(cdiv(NT, 4)):
        n = min(4, NT - g * 4)
        src = ctx.x_res.rearrange("p (ti d) -> p ti d", ti=NT, d=1024)[:, g * 4 : g * 4 + n, :]
        dst = x_dst[g * 512 : g * 512 + n * 128, :].rearrange("(ti p) d -> p ti d", p=128)
        v = p.step(
            "sync",
            lambda e, dst=dst, src=src: e.dma_start(out=dst, in_=src),
            wait=[(ctx.ve, ctx.ve.n), (ctx.se, ctx.se.n)],
            inc=ctx.st,
            inc_k=16,
        )
    return (ctx.st, v)




def load_w_cols(ctx, hbm, c0, w, dst_sb, wait=(), d_in=1024):
    nch = d_in // 128
    src = hbm[:, c0 : c0 + w].rearrange("(c p) w -> p c w", p=128)
    return ctx.rw.load(
        lambda e, src=src, dst=dst_sb, n=nch * w, nch=nch, w=w: e.dma_start(
            out=dst[:, 0:n].rearrange("p (c w) -> p c w", c=nch, w=w), in_=src
        ),
        wait=wait,
    )


def bcast_scalar(ctx, src11_bf, dst_f32, wait=()):
    p = ctx.p
    pi, pw = ctx.pool.acquire()
    ps = ctx.psF[pi]
    tv = p.step(
        "tensor",
        lambda t: t.matmul(ps[:, 0:1], ctx.cba("ones_row_b")[0:1, :], src11_bf, start=True, stop=True),
        wait=pw + list(wait),
        inc=ctx.te,
    )
    vv = p.step("vector", lambda v: v.tensor_copy(dst_f32, ps[:, 0:1]), wait=[(ctx.te, tv)], inc=ctx.ve)
    ctx.pool.release(pi, (ctx.ve, vv))
    return vv


def bcast_row(ctx, src_row_bf, n, dst, wait=(), dst_rows=128):
    p = ctx.p
    pi, pw = ctx.pool.acquire()
    ps = ctx.psF[pi]
    tv = p.step(
        "tensor",
        lambda t: t.matmul(ps[0:dst_rows, 0:n], ctx.cba("ones_row_b")[0:1, 0:dst_rows], src_row_bf, start=True, stop=True),
        wait=pw + list(wait),
        inc=ctx.te,
    )
    vv = p.step("vector", lambda v: v.tensor_copy(dst, ps[0:dst_rows, 0:n]), wait=[(ctx.te, tv)], inc=ctx.ve)
    ctx.pool.release(pi, (ctx.ve, vv))
    return vv




def causal_flash(ctx, sc, q_fm, k_fm, v_col, out_dst, hbase, hd, negc, rd_extra, qk_ready):
    p = ctx.p
    T, NT = ctx.T, ctx.NT
    probs = sc["probs"]
    probsT = sc["probsT"]
    arow = sc["arow"]
    probs_rel = sc.setdefault("probs_rel", [[] for _ in range(len(probs))])
    while len(probs_rel) < len(probs):
        probs_rel.append([])
    pvT_prev = sc.setdefault("pvT_prev", [])
    arow_guard = sc.setdefault("arow_guard", [[], []])
    ident = ctx.cba("ident")
    neg_low = ctx.cba("neg_low")

    def stage_a(tq):
        nblk = tq + 1
        nsl = cdiv(nblk, 4)
        aset = tq % 2
        ac0 = aset * 16
        dc = arow[:, ac0 : ac0 + nsl]
        sv_last = None
        exps = []
        for s in range(nsl):
            w = min(512, nblk * 128 - s * 512)
            slot = aset * 4 + s
            pi, pw = ctx.pool.acquire()
            ps = ctx.psF[pi]
            is_last = s == nsl - 1
            mm1 = p.step(
                "tensor",
                lambda t, ps=ps, w=w, s=s, tq=tq, il=is_last: t.matmul(
                    ps[:, 0:w],
                    q_fm[hbase : hbase + hd, tq * 128 : (tq + 1) * 128],
                    k_fm[hbase : hbase + hd, s * 512 : s * 512 + w],
                    start=True,
                    stop=not il,
                ),
                wait=pw + list(qk_ready),
                inc=None if is_last else ctx.te,
            )
            if is_last:
                jd = (tq % 4) * 128
                mm1 = p.step(
                    "tensor",
                    lambda t, ps=ps, jd=jd: t.matmul(
                        ps[:, jd : jd + 128], neg_low, ident, start=False, stop=True
                    ),
                    inc=ctx.te,
                )
            ew = [(ctx.te, mm1)] + probs_rel[slot]
            if s == 0:
                ew = ew + arow_guard[aset]
            sv_last = p.step(
                "scalar",
                lambda e, ps=ps, w=w, slot=slot, dcs=dc[:, s : s + 1]: e.activation(
                    probs[slot][:, 0:w], ps[:, 0:w], AF.Exp, bias=negc, accum_out=dcs
                ),
                wait=ew,
                inc=ctx.se,
            )
            ctx.pool.release(pi, (ctx.se, sv_last))
            exps.append(sv_last)
        return (tq, exps)

    def stage_b(st):
        tq, exps = st
        nblk = tq + 1
        nsl = cdiv(nblk, 4)
        aset = tq % 2
        ac0 = aset * 16
        dc = arow[:, ac0 : ac0 + nsl]
        tr_last = None
        for s in range(nsl):
            w = min(512, nblk * 128 - s * 512)
            slot = aset * 4 + s
            for j in range(w // 128):
                bank, bw = ctx.poolT.acquire()
                tw = bw + [(ctx.se, exps[s])]
                if s == 0 and j == 0:
                    tw = tw + pvT_prev
                tr_last = p.step(
                    "tensor",
                    lambda t, bank=bank, slot=slot, j=j: t.transpose(
                        ctx.psT[bank][:, 0:128], probs[slot][:, j * 128 : (j + 1) * 128], ident
                    ),
                    wait=tw,
                    inc=ctx.te,
                )
                vv = p.step(
                    "vector",
                    lambda v, bank=bank, c=s * 4 + j: v.tensor_copy(
                        probsT[:, c * 128 : (c + 1) * 128], ctx.psT[bank][:, 0:128]
                    ),
                    wait=[(ctx.te, tr_last)],
                    inc=ctx.ve,
                )
                ctx.poolT.release(bank, (ctx.ve, vv))
            probs_rel[slot] = [(ctx.te, tr_last)]
        c_d = arow[:, ac0 + 8 : ac0 + 9]
        c_rd = arow[:, ac0 + 9 : ac0 + 10]
        vv = p.step(
            "vector",
            lambda v, dc=dc, c_d=c_d: v.reduce_sum(c_d, dc, axis=AX.X),
            wait=[(ctx.se, exps[-1])],
            inc=ctx.ve,
        )
        vv = p.step("vector", lambda v, c_d=c_d, c_rd=c_rd: v.reciprocal(c_rd, c_d),
                    wait=[(ctx.ve, vv)], inc=ctx.ve)
        if rd_extra is not None:
            vv = p.step(
                "vector",
                lambda v, c_rd=c_rd: v.tensor_mul(c_rd, c_rd, rd_extra),
                wait=[(ctx.ve, vv)],
                inc=ctx.ve,
            )
        pi, pw = ctx.pool.acquire()
        ps = ctx.psF[pi]
        mmv = None
        for c in range(nblk):
            mw = []
            if c == 0:
                mw = pw + [(ctx.ve, ctx.ve.n)]
            mmv = p.step(
                "tensor",
                lambda t, ps=ps, c=c, nblk=nblk: t.matmul(
                    ps[:, 0:hd],
                    probsT[:, c * 128 : (c + 1) * 128],
                    v_col(c),
                    start=(c == 0),
                    stop=(c == nblk - 1),
                ),
                wait=mw,
                inc=ctx.te if c == nblk - 1 else None,
            )
        pvT_prev.clear()
        pvT_prev.append((ctx.te, mmv))
        vv2 = p.step(
            "vector",
            lambda v, ps=ps, tq=tq, c_rd=c_rd: v.tensor_scalar(
                out_dst(tq), ps[:, 0:hd], c_rd, None, ALU.mult
            ),
            wait=[(ctx.te, mmv), (ctx.ve, vv)],
            inc=ctx.ve,
        )
        ctx.pool.release(pi, (ctx.ve, vv2))
        arow_guard[aset] = [(ctx.ve, vv2)]

    pend = None
    for tq in range(NT):
        st = stage_a(tq)
        if pend is not None:
            stage_b(pend)
        pend = st
    stage_b(pend)


def head_norm_sqrtmax(ctx, sq_bf, hbase, hd, arow, tmp0, outcol, ready):
    p = ctx.p
    T = ctx.T
    nsl = cdiv(T, 512)
    ones = ctx.cba("ones_col128")
    vv = None
    for s in range(nsl):
        w = min(512, T - s * 512)
        pi, pw = ctx.pool.acquire()
        ps = ctx.psF[pi]
        tv = p.step(
            "tensor",
            lambda t, ps=ps, s=s, w=w: t.matmul(
                ps[0:1, 0:w],
                ones[hbase : hbase + hd, 0:1],
                sq_bf[hbase : hbase + hd, s * 512 : s * 512 + w],
                start=True,
                stop=True,
            ),
            wait=pw + list(ready),
            inc=ctx.te,
        )
        vv = p.step(
            "vector",
            lambda v, ps=ps, s=s, w=w: v.reduce_max(arow[0:1, tmp0 + s : tmp0 + s + 1], ps[0:1, 0:w], axis=AX.X),
            wait=[(ctx.te, tv)],
            inc=ctx.ve,
        )
        ctx.pool.release(pi, (ctx.ve, vv))
    vv = p.step(
        "vector",
        lambda v: v.reduce_max(arow[0:1, tmp0 + nsl : tmp0 + nsl + 1], arow[0:1, tmp0 : tmp0 + nsl], axis=AX.X),
        wait=[(ctx.ve, vv)],
        inc=ctx.ve,
    )
    sv = p.step(
        "scalar",
        lambda e: e.activation(arow[0:1, outcol : outcol + 1], arow[0:1, tmp0 + nsl : tmp0 + nsl + 1], AF.Sqrt),
        wait=[(ctx.ve, vv)],
        inc=ctx.se,
    )
    return sv




def l0_attn_batch(ctx, ws, sc, g_bc, gv):
    p = ctx.p
    T, NT = ctx.T, ctx.NT
    xn_fm = ctx.xn_fm
    attn_tm = sc["attn_tm"]
    qf = sc["q_pair"]
    kf = sc["k_pair"]
    vt = sc["v_pair"]
    sqb = sc["sq_pair"]
    arow = sc["arow"]
    negc = sc["negc"]
    b11 = sc["b11"]
    wbuf = sc["wq3"]
    xn_ready = sc["xn_ready"]
    for pair in range(sc.get("npairs", 8)):
        wslots = []
        for wi, wname in enumerate(("WQ", "WK", "WV")):
            dst = wbuf[:, wi * 1024 : (wi + 1) * 1024]
            slot, wsp = load_w_cols(ctx, ws[wname], pair * 128, 128, dst,
                                    wait=sc["wq3_guard"][wi])
            wslots.append((slot, wsp))
        wq = wbuf[:, 0:1024]
        wk = wbuf[:, 1024:2048]
        wv = wbuf[:, 2048:3072]
        q_done = k_done = None
        for which in range(2):
            wsl, wsp = wslots[which]
            wap = (wq, wk)[which]
            dstf = (qf, kf)[which]
            scale = (0.125, 1.0)[which]
            last = None
            for s in range(cdiv(T, 512)):
                w512 = min(512, T - s * 512)
                pi, pw = ctx.pool.acquire()
                ps = ctx.psF[pi]
                tv = None
                for c in range(8):
                    tv = p.step(
                        "tensor",
                        lambda t, ps=ps, c=c, s=s, wap=wap, w512=w512: t.matmul(
                            ps[:, 0:w512],
                            wap[:, c * 128 : (c + 1) * 128],
                            xn_fm[:, c * T + s * 512 : c * T + s * 512 + w512],
                            start=(c == 0),
                            stop=(c == 7),
                        ),
                        wait=(pw + [wsp] + xn_ready) if c == 0 else (),
                        inc=ctx.te if c == 7 else None,
                    )
                ev = p.step(
                    "scalar",
                    lambda e, ps=ps, s=s, dstf=dstf, scale=scale, w512=w512: e.activation(
                        dstf[:, s * 512 : s * 512 + w512], ps[:, 0:w512], AF.Copy, scale=scale
                    ),
                    wait=[(ctx.te, tv)],
                    inc=ctx.se,
                )
                ctx.pool.release(pi, (ctx.se, ev))
                last = ev
            if which == 0:
                q_done = last
            else:
                k_done = last
        ctx.rw.ack(wslots[0][0], (ctx.se, q_done))
        ctx.rw.ack(wslots[1][0], (ctx.se, k_done))
        sc["wq3_guard"][0] = [(ctx.se, q_done)]
        sc["wq3_guard"][1] = [(ctx.se, k_done)]
        v_done = None
        for ti in range(NT):
            pi, pw = ctx.pool.acquire()
            ps = ctx.psF[pi]
            tv = None
            for c in range(8):
                tv = p.step(
                    "tensor",
                    lambda t, ps=ps, c=c, ti=ti: t.matmul(
                        ps[:, 0:128],
                        xn_fm[:, c * T + ti * 128 : c * T + ti * 128 + 128],
                        wv[:, c * 128 : (c + 1) * 128],
                        start=(c == 0),
                        stop=(c == 7),
                    ),
                    wait=(pw + [wslots[2][1]] + xn_ready) if c == 0 else (),
                    inc=ctx.te if c == 7 else None,
                )
            v_done = p.step(
                "vector",
                lambda v, ps=ps, ti=ti: v.tensor_copy(vt[:, ti * 128 : (ti + 1) * 128], ps[:, 0:128]),
                wait=[(ctx.te, tv)],
                inc=ctx.ve,
            )
            ctx.pool.release(pi, (ctx.ve, v_done))
        ctx.rw.ack(wslots[2][0], (ctx.ve, v_done))
        sc["wq3_guard"][2] = [(ctx.ve, v_done)]
        if sc.get("attn_sub") == "qkv":
            sc["sq_guard"] = [(ctx.ve, ctx.ve.n)]
            continue
        nsub = sc.get("attn_sub") or ""
        sqv = p.step(
            "scalar",
            lambda e: e.activation(sqb[:], qf[:], AF.Square),
            wait=[(ctx.se, q_done)] + sc["sq_guard"],
            inc=ctx.se,
        )
        sqk = p.step(
            "scalar",
            lambda e, sq2=sc["sq_pair2"]: e.activation(sq2[:], kf[:], AF.Square),
            wait=[(ctx.se, k_done)],
            inc=ctx.se,
        )
        if nsub == "norms0":
            sc["sq_guard"] = [(ctx.se, ctx.se.n), (ctx.ve, ctx.ve.n)]
            continue
        for hh in range(2):
            hbase = hh * 64
            qm = head_norm_sqrtmax(ctx, sqb, hbase, 64, arow, 34, 40, [(ctx.se, sqv)])
            if nsub == "norms1":
                sc["sq_guard"] = [(ctx.se, ctx.se.n), (ctx.ve, ctx.ve.n)]
                continue
            km = head_norm_sqrtmax(ctx, sc["sq_pair2"], hbase, 64, arow, 34, 41, [(ctx.se, qm), (ctx.se, sqk)])
            vv = p.step(
                "vector",
                lambda v: v.tensor_mul(arow[0:1, 42:43], arow[0:1, 40:41], arow[0:1, 41:42]),
                wait=[(ctx.se, km)],
                inc=ctx.ve,
            )
            vv = p.step(
                "vector",
                lambda v: v.tensor_scalar(b11[0:1, 0:1], arow[0:1, 42:43], -1.0, None, ALU.mult),
                wait=[(ctx.ve, vv)],
                inc=ctx.ve,
            )
            bv = bcast_scalar(ctx, b11[0:1, 0:1], negc[:, hh : hh + 1],
                              wait=[(ctx.ve, vv), (ctx.se, ctx.se.n)])
            if sc.get("attn_sub") == "norms":
                continue
            habs = pair * 2 + hh
            sc_flash = sc
            vcol = lambda c, hbase=hbase: vt[:, c * 128 + hbase : c * 128 + hbase + 64]
            odst = lambda tq, habs=habs: attn_tm[:, tq * 1024 + habs * 64 : tq * 1024 + habs * 64 + 64]
            causal_flash(
                ctx, sc_flash, qf, kf, vcol, odst, hbase, 64,
                negc[:, hh : hh + 1], g_bc,
                qk_ready=[(ctx.se, q_done), (ctx.se, k_done), (ctx.ve, v_done), (ctx.ve, bv)],
            )
        sc["sq_guard"] = [(ctx.ve, ctx.ve.n)]
    return [(ctx.ve, ctx.ve.n)]


def fhn_g(ctx, frow, fhn_wait, g_bc):
    p = ctx.p
    I0 = float(0.1 + 0.9 / (1.0 + np.exp(-5.0)))
    TAU = 12.5
    c = lambda i: frow[0:1, i : i + 1]
    a_, b_, dt_ = c(0), c(1), c(2)
    v1, r1, n1, n2, d1, rd, w1 = c(3), c(4), c(5), c(6), c(7), c(8), c(9)
    t1, t2, s2, v2, g0, fac, g = c(10), c(11), c(12), c(13), c(14), c(15), c(5)
    vv = p.step("vector", lambda v: v.tensor_scalar(v1, dt_, I0, None, ALU.mult),
                wait=list(fhn_wait), inc=ctx.ve)
    seq = [
        lambda v: v.tensor_scalar(r1, dt_, 1.0 / TAU, None, ALU.mult),
        lambda v: v.tensor_add(n1, v1, a_),
        lambda v: v.tensor_mul(n2, r1, n1),
        lambda v: v.tensor_mul(d1, r1, b_),
        lambda v: v.tensor_scalar(d1, d1, 1.0, None, ALU.add),
        lambda v: v.reciprocal(rd, d1),
        lambda v: v.tensor_mul(w1, n2, rd),
        lambda v: v.tensor_mul(t1, v1, v1),
        lambda v: v.tensor_mul(t2, t1, v1),
        lambda v: v.scalar_tensor_tensor(t2, t2, -1.0 / 3.0, v1, ALU.mult, ALU.add),
        lambda v: v.tensor_sub(s2, t2, w1),
        lambda v: v.tensor_scalar(s2, s2, I0, None, ALU.add),
        lambda v: v.tensor_mul(s2, dt_, s2),
        lambda v: v.tensor_add(v2, v1, s2),
    ]
    for fn in seq:
        vv = p.step("vector", fn, wait=[(ctx.ve, vv)], inc=ctx.ve)
    sv = p.step("scalar", lambda e: e.activation(g0, v2, AF.Sigmoid),
                wait=[(ctx.ve, vv)], inc=ctx.se)
    vv = p.step("vector", lambda v: v.tensor_scalar(fac, g0, 0.5, 0.5, ALU.mult, ALU.add),
                wait=[(ctx.se, sv)], inc=ctx.ve)
    vv = p.step("vector", lambda v: v.tensor_scalar(d1, fac, 1e-8, None, ALU.add),
                wait=[(ctx.ve, vv)], inc=ctx.ve)
    vv = p.step("vector", lambda v: v.reciprocal(rd, d1), wait=[(ctx.ve, vv)], inc=ctx.ve)
    vv = p.step("vector", lambda v: v.tensor_mul(g, fac, rd), wait=[(ctx.ve, vv)], inc=ctx.ve)
    gb = ctx._gb11
    vv = p.step("vector", lambda v: v.tensor_copy(gb[0:1, 0:1], g), wait=[(ctx.ve, vv)], inc=ctx.ve)
    return bcast_scalar(ctx, gb[0:1, 0:1], g_bc, wait=[(ctx.ve, vv)])


def l0_ffn_gate_prep(ctx, ws, sc):
    p = ctx.p
    simt = sc["simt"]
    slot_t, w_t = ctx.rx.load(lambda e: e.dma_start(out=simt[:], in_=ws["SIMT"]),
                              wait=sc.get("simt_guard", []))
    sq = ctx.sq_scr[0]
    sv = p.step("scalar", lambda e: e.activation(sq[0:16, :], simt[:], AF.Square,
                                                 accum_out=sc["r16"][0:16, 0:1]),
                wait=[w_t, (ctx.se, sc["scr_last"][0])], inc=ctx.se)
    sc["scr_last"][0] = sv
    ctx.rx.ack(slot_t, (ctx.se, sv))
    sv2 = p.step("scalar", lambda e: e.activation(sc["r16"][0:16, 1:2], sc["r16"][0:16, 0:1], AF.Sqrt),
                 wait=[(ctx.se, sv)], inc=ctx.se)
    vv = p.step("vector", lambda v: v.tensor_scalar(sc["r16"][0:16, 1:2], sc["r16"][0:16, 1:2],
                                                    1e-12, None, ALU.max),
                wait=[(ctx.se, sv2)], inc=ctx.ve)
    vv = p.step("vector", lambda v: v.reciprocal(sc["r16"][0:16, 2:3], sc["r16"][0:16, 1:2]),
                wait=[(ctx.ve, vv)], inc=ctx.ve)
    vv = p.step("vector", lambda v: v.tensor_copy(sc["r16b"][0:16, 0:1], sc["r16"][0:16, 2:3]),
                wait=[(ctx.ve, vv)], inc=ctx.ve)
    bank, bw = ctx.poolT.acquire()
    tv = p.step("tensor", lambda t: t.transpose(ctx.psT[bank][0:1, 0:16], sc["r16b"][0:16, 0:1],
                                                ctx.cba("ident")[0:16, 0:16]),
                wait=bw + [(ctx.ve, vv)], inc=ctx.te)
    vv = p.step("vector", lambda v: v.tensor_copy(sc["row16b"][0:1, 0:16], ctx.psT[bank][0:1, 0:16]),
                wait=[(ctx.te, tv)], inc=ctx.ve)
    ctx.poolT.release(bank, (ctx.ve, vv))
    rcb = sc["rcb"]
    bv = bcast_row(ctx, sc["row16b"][0:1, 0:16], 16, rcb[:], wait=[(ctx.ve, vv)])
    simb = sc["simb"]
    slot_s, w_s = load_w_cols(ctx, ws["SIM"], 0, 16, simb, wait=sc.get("simt_guard", []))
    nv = None
    for cch in range(8):
        nv = p.step(
            "vector",
            lambda v, cch=cch: v.tensor_mul(
                sc["nsim"][:, cch * 16 : (cch + 1) * 16], simb[:, cch * 16 : (cch + 1) * 16], rcb[:]
            ),
            wait=[w_s, (ctx.ve, bv)],
            inc=ctx.ve,
        )
    ctx.rw.ack(slot_s, (ctx.ve, nv))
    slot_g, w_g = ctx.rx.load(lambda e: e.dma_start(out=sc["row16"][0:1, 0:16], in_=ws["GATES"]))
    sv = p.step("scalar", lambda e: e.activation(sc["row16"][0:1, 16:32], sc["row16"][0:1, 0:16], AF.Sigmoid),
                wait=[w_g], inc=ctx.se)
    ctx.rx.ack(slot_g, (ctx.se, sv))
    vv = p.step("vector", lambda v: v.tensor_copy(sc["row16b"][0:1, 16:32], sc["row16"][0:1, 16:32]),
                wait=[(ctx.se, sv)], inc=ctx.ve)
    bv = bcast_row(ctx, sc["row16b"][0:1, 16:32], 16, sc["sg_bc"][:], wait=[(ctx.ve, vv)])
    return [(ctx.ve, max(nv, bv))]


def l0_ffn_batch(ctx, ws, sc, gate_prep_ready):
    p = ctx.p
    T, NT = ctx.T, ctx.NT
    xn_fm = ctx.xn_fm
    xn_ready = sc["xn_ready"]
    rw_buf = sc["rw_buf"]
    gscr = sc["gscr"]
    grow = sc["grow"]
    gu8 = sc["gu8"]
    gate_guard = [[], []]
    for ti in range(NT):
        st0 = (ti % 2) * 80
        gr0 = (ti % 2) * 16
        lg = gscr[:, st0 : st0 + 16]
        m8 = gscr[:, st0 + 16 : st0 + 24]
        maskf = gscr[:, st0 + 24 : st0 + 40]
        e16 = gscr[:, st0 + 40 : st0 + 56]
        gated = gscr[:, st0 + 56 : st0 + 72]
        comb = gscr[:, st0 + 72 : st0 + 80]
        crn = grow[:, gr0 + 1 : gr0 + 2]
        ccnt = grow[:, gr0 + 2 : gr0 + 3]
        cia = grow[:, gr0 + 3 : gr0 + 4]
        cnegm = grow[:, gr0 + 4 : gr0 + 5]
        cssum = grow[:, gr0 + 5 : gr0 + 6]
        crws = grow[:, gr0 + 6 : gr0 + 7]
        combu = gu8[:, gr0 : gr0 + 16]
        col0 = (ti % 16) * 8
        c_var = ctx.row_scr[:, col0 + 4 : col0 + 5]
        c_rstd = ctx.row_scr[:, col0 + 6 : col0 + 7]
        vv = p.step("vector", lambda v, cr=crn, c_var=c_var, c_rstd=c_rstd: v.tensor_mul(cr, c_var, c_rstd),
                    wait=xn_ready + gate_guard[ti % 2], inc=ctx.ve)
        vv = p.step("vector", lambda v, cr=crn, c_rstd=c_rstd: v.tensor_mul(cr, cr, c_rstd),
                    wait=[(ctx.ve, vv)], inc=ctx.ve)
        vv = p.step("vector", lambda v, cr=crn: v.tensor_scalar(cr, cr, 1024.0, None, ALU.mult),
                    wait=[(ctx.ve, vv)], inc=ctx.ve)
        sv2 = p.step("scalar", lambda e, cr=crn: e.activation(cr, cr, AF.Sqrt),
                     wait=[(ctx.ve, vv)], inc=ctx.se)
        vv = p.step("vector", lambda v, cr=crn: v.tensor_scalar(cr, cr, 1e-12, None, ALU.max),
                    wait=[(ctx.se, sv2)], inc=ctx.ve)
        vv = p.step("vector", lambda v, cr=crn: v.reciprocal(cr, cr), wait=[(ctx.ve, vv)], inc=ctx.ve)
        pi, pw = ctx.pool.acquire()
        ps = ctx.psF[pi]
        tv = None
        for cch in range(8):
            tv = p.step(
                "tensor",
                lambda t, ps=ps, cch=cch, ti=ti: t.matmul(
                    ps[:, 0:16],
                    xn_fm[:, cch * T + ti * 128 : cch * T + ti * 128 + 128],
                    sc["nsim"][:, cch * 16 : (cch + 1) * 16],
                    start=(cch == 0),
                    stop=(cch == 7),
                ),
                wait=(pw + xn_ready + gate_prep_ready) if cch == 0 else (),
                inc=ctx.te if cch == 7 else None,
            )
        vv = p.step("vector",
                    lambda v, ps=ps, lg=lg, cr=crn: v.scalar_tensor_tensor(lg, ps[:, 0:16], cr, sc["sg_bc"][:], ALU.mult, ALU.subtract),
                    wait=[(ctx.te, tv), (ctx.ve, vv)], inc=ctx.ve)
        ctx.pool.release(pi, (ctx.ve, vv))
        vv = p.step("vector", lambda v, m8=m8, lg=lg: v.max(out=m8, in_=lg), wait=[(ctx.ve, vv)], inc=ctx.ve)
        vv = p.step("vector", lambda v, maskf=maskf, lg=lg: v.tensor_scalar(maskf, lg, 0.0, None, ALU.is_gt),
                    wait=[(ctx.ve, vv)], inc=ctx.ve)
        vv = p.step("vector", lambda v, ccnt=ccnt, maskf=maskf: v.reduce_sum(ccnt, maskf, axis=AX.X),
                    wait=[(ctx.ve, vv)], inc=ctx.ve)
        vv = p.step("vector", lambda v, cia=cia, ccnt=ccnt: v.tensor_scalar(cia, ccnt, 0.5, None, ALU.is_lt),
                    wait=[(ctx.ve, vv)], inc=ctx.ve)
        vv = p.step("vector", lambda v, e16=e16, lg=lg, m8=m8: v.tensor_scalar(e16, lg, m8[:, 1:2], None, ALU.is_ge),
                    wait=[(ctx.ve, vv)], inc=ctx.ve)
        vv = p.step("vector", lambda v, e16=e16, cia=cia, maskf=maskf: v.scalar_tensor_tensor(maskf, e16, cia, maskf, ALU.mult, ALU.add),
                    wait=[(ctx.ve, vv)], inc=ctx.ve)
        vv = p.step("vector", lambda v, combu=combu, maskf=maskf: v.tensor_copy(combu, maskf),
                    wait=[(ctx.ve, vv)], inc=ctx.ve)
        vv = p.step("vector", lambda v, gated=gated, lg=lg: v.tensor_scalar(gated, lg, 0.0, None, ALU.max),
                    wait=[(ctx.ve, vv)], inc=ctx.ve)
        nb = ctx.cfa("s_neg30")[:, 0:1].broadcast_to((128, 16))
        vv = p.step("vector", lambda v, lg=lg, combu=combu, gated=gated, nb=nb: v.select(lg, combu, gated, nb, add_drain=True),
                    wait=[(ctx.ve, vv)], inc=ctx.ve)
        vv = p.step("vector", lambda v, cnegm=cnegm, lg=lg: v.reduce_max(cnegm, lg, axis=AX.X, negate=True),
                    wait=[(ctx.ve, vv)], inc=ctx.ve)
        sv3 = p.step("scalar", lambda e, e16=e16, lg=lg, cnegm=cnegm, cssum=cssum: e.activation(e16, lg, AF.Exp, bias=cnegm, accum_out=cssum),
                     wait=[(ctx.ve, vv)], inc=ctx.se)
        vv = p.step("vector", lambda v, crws=crws, cssum=cssum: v.reciprocal(crws, cssum),
                    wait=[(ctx.se, sv3)], inc=ctx.ve)
        vv = p.step("vector", lambda v, ti=ti, e16=e16, crws=crws: v.tensor_scalar(rw_buf[:, ti * 16 : (ti + 1) * 16], e16, crws, None, ALU.mult),
                    wait=[(ctx.ve, vv)], inc=ctx.ve)
        gate_guard[ti % 2] = [(ctx.ve, vv), (ctx.se, sv3)]
    rw_ready = [(ctx.ve, ctx.ve.n)]
    q1, k1, v1 = sc["q1"], sc["k1"], sc["v1"]
    wp = sc["wp"]
    mixtmp = sc["mixtmp"]
    mt_guard = []
    wp_last = [None, None, None]
    wp_guard = sc.get("wp_guard", [])
    for wi in range(3):
        wname = ("QP", "KP", "VP")[wi]
        wslot, wsp = load_w_cols(ctx, ws[wname], 0, 1024, wp, wait=wp_guard)
        for ti in range(NT):
            pis = []
            for half in range(2):
                pi, pw = ctx.pool.acquire()
                ps = ctx.psF[pi]
                tv = None
                for cch in range(8):
                    tv = p.step(
                        "tensor",
                        lambda t, ps=ps, cch=cch, ti=ti, half=half: t.matmul(
                            ps[:],
                            xn_fm[:, cch * T + ti * 128 : cch * T + ti * 128 + 128],
                            wp[:, cch * 1024 + half * 512 : cch * 1024 + half * 512 + 512],
                            start=(cch == 0),
                            stop=(cch == 7),
                        ),
                        wait=(pw + [wsp] + xn_ready) if cch == 0 else (),
                        inc=ctx.te if cch == 7 else None,
                    )
                pis.append((pi, tv))
            vv = None
            for half in range(2):
                pi, tv = pis[half]
                ps = ctx.psF[pi]
                rw3 = rw_buf[:, ti * 16 : (ti + 1) * 16].rearrange("p (one e) -> p one e", one=1, e=16).broadcast_to((128, 32, 16))
                t3 = mixtmp[:, half * 512 : (half + 1) * 512].rearrange("p (d e) -> p d e", d=32, e=16)
                p3 = ps.rearrange("p (d e) -> p d e", d=32, e=16)
                vv = p.step("vector",
                            lambda v, t3=t3, p3=p3, rw3=rw3: v.tensor_tensor(t3, p3, rw3, ALU.mult),
                            wait=[(ctx.te, tv)] + mt_guard + rw_ready, inc=ctx.ve)
                ctx.pool.release(pi, (ctx.ve, vv))
            dst = (q1, k1, v1)[wi]
            scale = 0.125 if wi == 0 else 1.0
            m3 = mixtmp.rearrange("p (d e) -> p d e", d=64, e=16)
            mr = sc["mixred"]
            o3 = mr.rearrange("p (d one) -> p d one", d=64, one=1)
            vv = p.step("vector",
                        lambda v, o3=o3, m3=m3: v.tensor_reduce(o3, m3, AX.X, ALU.add),
                        wait=[(ctx.ve, vv)], inc=ctx.ve)
            vv = p.step("vector",
                        lambda v, dst=dst, ti=ti, mr=mr, scale=scale: v.tensor_scalar(
                            dst[:, ti * 64 : (ti + 1) * 64], mr[:], scale, None, ALU.mult),
                        wait=[(ctx.ve, vv)], inc=ctx.ve)
            mt_guard = [(ctx.ve, vv)]
            wp_last[wi] = vv
        ctx.rw.ack(wslot, (ctx.te, ctx.te.n))
        wp_guard = [(ctx.te, ctx.te.n)]
    sc["wp_guard"] = [(ctx.te, ctx.te.n)]
    q1f, k1f = sc["q1f"], sc["k1f"]
    prs = []
    for ti in range(NT):
        prs.append((q1[:, ti * 64 : (ti + 1) * 64], q1f[0:64, ti * 128 : (ti + 1) * 128]))
        prs.append((k1[:, ti * 64 : (ti + 1) * 64], k1f[0:64, ti * 128 : (ti + 1) * 128]))
    tr_done = pe_transpose_pairs(ctx, prs, in_w=64, wait=[(ctx.ve, ctx.ve.n)])
    arow = sc["arow"]
    sqv = p.step("scalar", lambda e: e.activation(sc["sq_pair"][0:64, :], q1f[0:64, :], AF.Square),
                 wait=[(ctx.ve, tr_done)] + sc["sq_guard"], inc=ctx.se)
    sqk = p.step("scalar", lambda e: e.activation(sc["sq_pair2"][0:64, :], k1f[0:64, :], AF.Square),
                 wait=[(ctx.ve, tr_done)], inc=ctx.se)
    qm = head_norm_sqrtmax(ctx, sc["sq_pair"], 0, 64, arow, 34, 40, [(ctx.se, sqv)])
    km = head_norm_sqrtmax(ctx, sc["sq_pair2"], 0, 64, arow, 34, 41, [(ctx.se, qm), (ctx.se, sqk)])
    vv = p.step("vector", lambda v: v.tensor_mul(arow[0:1, 42:43], arow[0:1, 40:41], arow[0:1, 41:42]),
                wait=[(ctx.se, km)], inc=ctx.ve)
    vv = p.step("vector", lambda v: v.tensor_scalar(sc["b11"][0:1, 0:1], arow[0:1, 42:43], -1.0, None, ALU.mult),
                wait=[(ctx.ve, vv)], inc=ctx.ve)
    bv = bcast_scalar(ctx, sc["b11"][0:1, 0:1], sc["negc"][:, 0:1], wait=[(ctx.ve, vv), (ctx.se, ctx.se.n)])
    sc["sq_guard"] = [(ctx.ve, ctx.ve.n)]
    attn1 = sc["attn1"]
    vcol = lambda cblk: v1[:, cblk * 64 : (cblk + 1) * 64]
    odst = lambda tq: attn1[:, tq * 64 : (tq + 1) * 64]
    causal_flash(ctx, sc, q1f, k1f, vcol, odst, 0, 64, sc["negc"][:, 0:1], None,
                 qk_ready=[(ctx.ve, tr_done), (ctx.ve, bv)])
    attn1_ready = [(ctx.ve, ctx.ve.n)]
    wo = sc["wo_ffn"]
    slot_o, w_o = load_w_cols(ctx, ws["OP"], 0, 1024, wo, wait=sc.get("woffn_guard", []))
    mixb = sc["mixb"]
    mixf = sc["mixf"]
    wo_last = None
    mb_guard = []
    for ti in range(NT):
        a3 = attn1[:, ti * 64 : (ti + 1) * 64].rearrange("p (d one) -> p d one", d=64, one=1).broadcast_to((128, 64, 16))
        rw3 = rw_buf[:, ti * 16 : (ti + 1) * 16].rearrange("p (one e) -> p one e", one=1, e=16).broadcast_to((128, 64, 16))
        o3 = mixb.rearrange("p (d e) -> p d e", d=64, e=16)
        vv = p.step("vector", lambda v, o3=o3, a3=a3, rw3=rw3: v.tensor_tensor(o3, a3, rw3, ALU.mult),
                    wait=attn1_ready + mb_guard, inc=ctx.ve)
        tr = pe_transpose_pairs(
            ctx,
            [(mixb[:, cc * 128 : (cc + 1) * 128], mixf[:, cc * 128 : (cc + 1) * 128]) for cc in range(8)],
            wait=[(ctx.ve, vv)],
        )
        adds = []
        for half in range(2):
            pi, pw = ctx.pool.acquire()
            ps = ctx.psF[pi]
            tv = None
            for cch in range(8):
                tv = p.step(
                    "tensor",
                    lambda t, ps=ps, cch=cch, half=half: t.matmul(
                        ps[:],
                        mixf[:, cch * 128 : (cch + 1) * 128],
                        wo[:, cch * 1024 + half * 512 : cch * 1024 + half * 512 + 512],
                        start=(cch == 0),
                        stop=(cch == 7),
                    ),
                    wait=(pw + [w_o, (ctx.ve, tr)]) if cch == 0 else (),
                    inc=ctx.te if cch == 7 else None,
                )
            xr = ctx.x_res[:, ti * 1024 + half * 512 : ti * 1024 + half * 512 + 512]
            vv2 = p.step("vector", lambda v, xr=xr, ps=ps: v.tensor_add(xr, ps[:], xr),
                         wait=[(ctx.te, tv), (ctx.ve, ctx.ve.n)], inc=ctx.ve)
            ctx.pool.release(pi, (ctx.ve, vv2))
            adds.append(vv2)
            wo_last = vv2
        mb_guard = [(ctx.ve, adds[-1])]
    ctx.rw.ack(slot_o, (ctx.ve, wo_last))
    sc["woffn_guard"] = [(ctx.te, ctx.te.n)]
    return [(ctx.ve, ctx.ve.n)]


def pe_transpose_pairs(ctx, pairs, in_w=128, wait=()):
    p = ctx.p
    ident = ctx.cba("ident")
    vv = None
    first = True
    for src, dst in pairs:
        bank, bw = ctx.poolT.acquire()
        tw = bw + (list(wait) if first else [])
        first = False
        tv = p.step("tensor",
                    lambda t, bank=bank, src=src: t.transpose(ctx.psT[bank][0:in_w, 0:128], src, ident),
                    wait=tw, inc=ctx.te)
        vv = p.step("vector",
                    lambda v, bank=bank, dst=dst: v.tensor_copy(dst, ctx.psT[bank][0:in_w, 0:128]),
                    wait=[(ctx.te, tv)], inc=ctx.ve)
        ctx.poolT.release(bank, (ctx.ve, vv))
    return vv


def l0_wout(ctx, ws, sc, attn_ready):
    p = ctx.p
    T, NT = ctx.T, ctx.NT
    wo = sc["wout"]
    slot, w_o = load_w_cols(ctx, ws["WOUT"], 0, 1024, wo, wait=sc.get("wout_guard", []))
    afm = sc["mixf"]
    last = None
    guard = []
    for ti in range(NT):
        tr = pe_transpose_pairs(
            ctx,
            [(sc["attn_tm"][:, ti * 1024 + cc * 128 : ti * 1024 + (cc + 1) * 128],
              afm[:, cc * 128 : (cc + 1) * 128]) for cc in range(8)],
            wait=list(attn_ready) + guard,
        )
        adds = []
        for half in range(2):
            pi, pw = ctx.pool.acquire()
            ps = ctx.psF[pi]
            tv = None
            for cch in range(8):
                tv = p.step(
                    "tensor",
                    lambda t, ps=ps, cch=cch, half=half: t.matmul(
                        ps[:],
                        afm[:, cch * 128 : (cch + 1) * 128],
                        wo[:, cch * 1024 + half * 512 : cch * 1024 + half * 512 + 512],
                        start=(cch == 0),
                        stop=(cch == 7),
                    ),
                    wait=(pw + [w_o, (ctx.ve, tr)]) if cch == 0 else (),
                    inc=ctx.te if cch == 7 else None,
                )
            xr = ctx.x_res[:, ti * 1024 + half * 512 : ti * 1024 + half * 512 + 512]
            vv2 = p.step("vector", lambda v, xr=xr, ps=ps: v.tensor_add(xr, ps[:], xr),
                         wait=[(ctx.te, tv)], inc=ctx.ve)
            ctx.pool.release(pi, (ctx.ve, vv2))
            adds.append(vv2)
            last = vv2
        guard = [(ctx.te, ctx.te.n)]
    ctx.rw.ack(slot, (ctx.ve, last))
    sc["wout_guard"] = [(ctx.te, ctx.te.n)]
    return [(ctx.ve, ctx.ve.n)]


def layer0(ctx, ws, x_src, x_dst, prev_store=None, parts="all"):
    p = ctx.p
    NT = ctx.NT
    les = ExitStack()
    sc = {}
    ctx.xn_fm = p.sbuf("l0_xnfm", [128, 8 * ctx.T], BF16, es=les)
    sc["arow"] = p.sbuf("l0_arow", [128, 48], F32, es=les)
    sc["negc"] = p.sbuf("l0_negc", [128, 2], F32, es=les)
    sc["b11"] = p.sbuf("l0_b11", [1, 4], BF16, es=les)
    sc["frow"] = p.sbuf("l0_frow", [1, 16], F32, es=les)
    sc["g_bc"] = p.sbuf("l0_gbc", [128, 1], F32, es=les)
    sc["sq_guard"] = []
    sc["wq3_guard"] = [[], [], []]
    sc["scr_last"] = ctx.scr_last
    slot_f, w_f = ctx.rx.load(lambda e: e.dma_start(out=sc["frow"][0:1, 0:3], in_=ws["FHN"]))
    gv = fhn_g(ctx, sc["frow"], [w_f], sc["g_bc"][:, 0:1])
    ctx.rx.ack(slot_f, (ctx.ve, ctx.ve.n))
    ln_phase(ctx, x_src, prev_store)
    sc["xn_ready"] = [(ctx.ve, ctx.ve.n), (ctx.se, ctx.se.n)]
    fence0 = ctx.fence()
    pes = ExitStack()
    sc["attn_tm"] = p.sbuf("l0_attn_tm", [128, ctx.NT * 1024], BF16, es=pes)
    sc["q_pair"] = p.sbuf("l0_qp", [128, ctx.T], BF16, es=pes)
    sc["k_pair"] = p.sbuf("l0_kp", [128, ctx.T], BF16, es=pes)
    sc["v_pair"] = p.sbuf("l0_vp", [128, ctx.NT * 128], BF16, es=pes)
    sc["sq_pair"] = p.sbuf("l0_sq", [128, ctx.T], BF16, es=pes)
    sc["sq_pair2"] = p.sbuf("l0_sq2", [128, ctx.T], BF16, es=pes)
    sc["probs"] = [p.sbuf(f"l0_probs{i}", [128, 512], BF16, es=pes) for i in range(8)]
    sc["probsT"] = p.sbuf("l0_probsT", [128, ctx.T], BF16, es=pes)
    sc["wq3"] = p.sbuf("l0_wq3", [128, 3 * 1024], BF16, es=pes)
    sc["wout"] = p.sbuf("l0_wout", [128, 8192], BF16, es=pes)
    sc["mixf"] = p.sbuf("l0_afm", [128, 1024], BF16, es=pes)
    sc["probs_rel"] = [list(fence0) for _ in range(8)]
    sc["pvT_prev"] = list(fence0)
    sc["arow_guard"] = [list(fence0), list(fence0)]
    sc["wq3_guard"] = [list(fence0), list(fence0), list(fence0)]
    sc["wout_guard"] = list(fence0)
    sc["fence0"] = fence0
    if parts in ("all", "attn", "attn1p"):
        if "1p" in parts:
            sc["npairs"] = 1
        if "qkv" in parts:
            sc["attn_sub"] = "qkv"
        if "norms" in parts:
            sc["attn_sub"] = parts.split("_")[-1]
        attn_ready = l0_attn_batch(ctx, ws, sc, sc["g_bc"][:, 0:1], gv)
        if sc.get("attn_sub") is None:
            l0_wout(ctx, ws, sc, attn_ready)
    pes.close()
    fence1 = ctx.fence()
    pes2 = ExitStack()
    sc["simt"] = p.sbuf("l0_simt", [16, 1024], F32, es=pes2)
    sc["r16"] = p.sbuf("l0_r16", [16, 4], F32, es=pes2)
    sc["r16b"] = p.sbuf("l0_r16b", [16, 2], BF16, es=pes2)
    sc["row16"] = p.sbuf("l0_row16", [1, 32], F32, es=pes2)
    sc["row16b"] = p.sbuf("l0_row16b", [1, 32], BF16, es=pes2)
    sc["rcb"] = p.sbuf("l0_rcb", [128, 16], BF16, es=pes2)
    sc["simb"] = p.sbuf("l0_simb", [128, 128], BF16, es=pes2)
    sc["nsim"] = p.sbuf("l0_nsim", [128, 128], BF16, es=pes2)
    sc["sg_bc"] = p.sbuf("l0_sgbc", [128, 16], F32, es=pes2)
    sc["rw_buf"] = p.sbuf("l0_rw", [128, ctx.NT * 16], F32, es=pes2)
    sc["gscr"] = p.sbuf("l0_gscr", [128, 160], F32, es=pes2)
    sc["grow"] = p.sbuf("l0_grow", [128, 32], F32, es=pes2)
    sc["gu8"] = p.sbuf("l0_gu8", [128, 32], U8, es=pes2)
    sc["q1"] = p.sbuf("l0_q1", [128, ctx.NT * 64], BF16, es=pes2)
    sc["k1"] = p.sbuf("l0_k1", [128, ctx.NT * 64], BF16, es=pes2)
    sc["v1"] = p.sbuf("l0_v1", [128, ctx.NT * 64], BF16, es=pes2)
    sc["q1f"] = p.sbuf("l0_q1f", [128, ctx.T], BF16, es=pes2)
    sc["k1f"] = p.sbuf("l0_k1f", [128, ctx.T], BF16, es=pes2)
    sc["sq_pair"] = p.sbuf("l0_fsq", [128, ctx.T], BF16, es=pes2)
    sc["sq_pair2"] = p.sbuf("l0_fsq2", [128, ctx.T], BF16, es=pes2)
    sc["attn1"] = p.sbuf("l0_attn1", [128, ctx.NT * 64], BF16, es=pes2)
    sc["wp"] = p.sbuf("l0_wp", [128, 8192], BF16, es=pes2)
    sc["wo_ffn"] = p.sbuf("l0_woffn", [128, 8192], BF16, es=pes2)
    sc["mixtmp"] = p.sbuf("l0_mixtmp", [128, 1024], F32, es=pes2)
    sc["mixred"] = p.sbuf("l0_mixred", [128, 64], F32, es=pes2)
    sc["mixb"] = p.sbuf("l0_mixb", [128, 1024], BF16, es=pes2)
    sc["mixf"] = p.sbuf("l0_mixf", [128, 1024], BF16, es=pes2)
    sc["probs"] = [p.sbuf(f"l0_fprobs{i}", [128, 512], BF16, es=pes2) for i in range(8)]
    sc["probsT"] = p.sbuf("l0_fprobsT", [128, ctx.T], BF16, es=pes2)
    sc["probs_rel"] = [list(fence1) for _ in range(8)]
    sc["pvT_prev"] = list(fence1)
    sc["arow_guard"] = [list(fence1), list(fence1)]
    sc["sq_guard"] = list(fence1)
    sc["wp_guard"] = list(fence1)
    sc["woffn_guard"] = list(fence1)
    sc["simt_guard"] = list(fence1)
    if parts in ("all", "ffn"):
        gate_prep_ready = l0_ffn_gate_prep(ctx, ws, sc)
        l0_ffn_batch(ctx, ws, sc, gate_prep_ready)
    pes2.close()
    les.close()
    ctx.xn_fm = None
    if x_dst is None:
        return None
    return store_xres(ctx, x_dst)




def lse_logp_tiles(ctx, sc, xn_tm, logp_fm):
    p = ctx.p
    NT, T = ctx.NT, ctx.T
    lrow = sc["lrow"]
    escr = sc["escr"]
    eguard = []
    last = None
    for ti in range(NT):
        c0 = (ti % 8) * 4
        negm = lrow[:, c0 : c0 + 1]
        ssum = lrow[:, c0 + 1 : c0 + 2]
        lse = lrow[:, c0 + 2 : c0 + 3]
        xt = xn_tm[:, ti * 1024 : (ti + 1) * 1024]
        g = [(ctx.ve, sc["lrow_g"][ti % 8])] if sc["lrow_g"][ti % 8] else []
        vv = p.step("vector", lambda v, xt=xt, negm=negm: v.reduce_max(negm, xt, axis=AX.X, negate=True),
                    wait=sc["xn_ready"] + g, inc=ctx.ve)
        sv = p.step("scalar", lambda e, xt=xt, negm=negm, ssum=ssum: e.activation(
            escr[:], xt, AF.Exp, bias=negm, accum_out=ssum), wait=[(ctx.ve, vv)] + eguard, inc=ctx.se)
        eguard = [(ctx.se, sv)]
        sv2 = p.step("scalar", lambda e, ssum=ssum, lse=lse: e.activation(lse, ssum, AF.Ln),
                     wait=[(ctx.se, sv)], inc=ctx.se)
        vv = p.step("vector", lambda v, lse=lse, negm=negm: v.tensor_sub(lse, lse, negm),
                    wait=[(ctx.se, sv2)], inc=ctx.ve)
        lt = ctx.xn_stage[ti % 2][:, 0:1024]
        vv = p.step("vector", lambda v, lt=lt, xt=xt, lse=lse: v.tensor_scalar(lt, xt, lse, None, ALU.subtract),
                    wait=[(ctx.ve, vv)] + ctx.stage_guard[ti % 2], inc=ctx.ve)
        last = tm_to_fm_src(ctx, lt, ti, logp_fm, (ctx.ve, vv))
        ctx.stage_guard[ti % 2] = [(ctx.ve, last)]
        sc["lrow_g"][ti % 8] = last
    return [(ctx.ve, ctx.ve.n), (ctx.se, ctx.se.n)]


def l1_attn_batch(ctx, sc, logp_ready):
    p = ctx.p
    NT, T = ctx.NT, ctx.T
    logp_fm = sc["logp_fm"]
    xn_tm = sc["xn_tm"]
    probs = sc["probs"]
    probsT = sc["probsT"]
    arow = sc["arow"]
    probs_rel = sc["probs_rel"]
    while len(probs_rel) < len(probs):
        probs_rel.append([])
    pvT_prev = sc["pvT_prev"]
    arow_guard = sc["arow_guard"]
    ident = ctx.cba("ident")
    ptile = sc["ptile"]
    pt_guard = [list(sc["fence0"]), list(sc["fence0"])]
    nsl = cdiv(NT * 128, 512)

    def stage_a(tq):
        aset = tq % 2
        ac0 = aset * 16
        dc = arow[:, ac0 : ac0 + nsl]
        pt = ptile[aset]
        pev = None
        for c in range(8):
            pev = p.step("scalar",
                         lambda e, c=c, tq=tq, pt=pt: e.activation(
                             pt[:, c * 128 : (c + 1) * 128],
                             logp_fm[:, c * T + tq * 128 : c * T + tq * 128 + 128], AF.Exp),
                         wait=(logp_ready + pt_guard[aset]) if c == 0 else (), inc=ctx.se)
        exps = []
        for s in range(nsl):
            w = min(512, NT * 128 - s * 512)
            slot = aset * 4 + s
            pi, pw = ctx.pool.acquire()
            ps = ctx.psF[pi]
            tv = None
            for c in range(8):
                tv = p.step(
                    "tensor",
                    lambda t, ps=ps, c=c, s=s, w=w, pt=pt: t.matmul(
                        ps[:, 0:w],
                        pt[:, c * 128 : (c + 1) * 128],
                        logp_fm[:, c * T + s * 512 : c * T + s * 512 + w],
                        start=(c == 0),
                        stop=(c == 7),
                    ),
                    wait=(pw + [(ctx.se, pev)]) if c == 0 else (),
                    inc=ctx.te if c == 7 else None,
                )
            ew = [(ctx.te, tv)] + probs_rel[slot]
            if s == 0:
                ew = ew + arow_guard[aset]
            sv = p.step(
                "scalar",
                lambda e, ps=ps, w=w, slot=slot, dcs=dc[:, s : s + 1]: e.activation(
                    probs[slot][:, 0:w], ps[:, 0:w], AF.Exp, bias=0.0, accum_out=dcs
                ),
                wait=ew,
                inc=ctx.se,
            )
            ctx.pool.release(pi, (ctx.se, sv))
            exps.append(sv)
        pt_guard[aset] = [(ctx.te, ctx.te.n)]
        return (tq, exps)

    def stage_b(st):
        tq, exps = st
        aset = tq % 2
        ac0 = aset * 16
        dc = arow[:, ac0 : ac0 + nsl]
        tr_last = None
        for s in range(nsl):
            w = min(512, NT * 128 - s * 512)
            slot = aset * 4 + s
            for j in range(w // 128):
                bank, bw = ctx.poolT.acquire()
                tw = bw + [(ctx.se, exps[s])]
                if s == 0 and j == 0:
                    tw = tw + pvT_prev
                tr_last = p.step(
                    "tensor",
                    lambda t, bank=bank, slot=slot, j=j: t.transpose(
                        ctx.psT[bank][:, 0:128], probs[slot][:, j * 128 : (j + 1) * 128], ident
                    ),
                    wait=tw,
                    inc=ctx.te,
                )
                vv = p.step(
                    "vector",
                    lambda v, bank=bank, c=s * 4 + j: v.tensor_copy(
                        probsT[:, c * 128 : (c + 1) * 128], ctx.psT[bank][:, 0:128]
                    ),
                    wait=[(ctx.te, tr_last)],
                    inc=ctx.ve,
                )
                ctx.poolT.release(bank, (ctx.ve, vv))
            probs_rel[slot] = [(ctx.te, tr_last)]
        c_d = arow[:, ac0 + 8 : ac0 + 9]
        c_rd = arow[:, ac0 + 9 : ac0 + 10]
        vv = p.step("vector", lambda v, dc=dc, c_d=c_d: v.reduce_sum(c_d, dc, axis=AX.X),
                    wait=[(ctx.se, exps[-1])], inc=ctx.ve)
        vv = p.step("vector", lambda v, c_d=c_d, c_rd=c_rd: v.reciprocal(c_rd, c_d),
                    wait=[(ctx.ve, vv)], inc=ctx.ve)
        evs = []
        for sl in range(2):
            pi, pw = ctx.pool.acquire()
            ps = ctx.psF[pi]
            mmv = None
            for c in range(NT):
                mw = []
                if c == 0:
                    mw = pw + [(ctx.ve, ctx.ve.n)]
                mmv = p.step(
                    "tensor",
                    lambda t, ps=ps, c=c, sl=sl: t.matmul(
                        ps[:],
                        probsT[:, c * 128 : (c + 1) * 128],
                        xn_tm[:, c * 1024 + sl * 512 : c * 1024 + sl * 512 + 512],
                        start=(c == 0),
                        stop=(c == NT - 1),
                    ),
                    wait=mw,
                    inc=ctx.te if c == NT - 1 else None,
                )
            xr = ctx.x_res[:, tq * 1024 + sl * 512 : tq * 1024 + sl * 512 + 512]
            ev = p.step(
                "vector",
                lambda v, xr=xr, ps=ps, c_rd=c_rd: v.scalar_tensor_tensor(
                    xr, ps[:], c_rd, xr, ALU.mult, ALU.add
                ),
                wait=[(ctx.te, mmv), (ctx.ve, vv)],
                inc=ctx.ve,
            )
            ctx.pool.release(pi, (ctx.ve, ev))
            evs.append(ev)
        pvT_prev.clear()
        pvT_prev.append((ctx.te, ctx.te.n))
        arow_guard[aset] = [(ctx.ve, evs[-1])]

    pend = None
    for tq in range(NT):
        st = stage_a(tq)
        if pend is not None:
            stage_b(pend)
        pend = st
    stage_b(pend)
    return [(ctx.ve, ctx.ve.n)]


def quant_tile(ctx, sc, src_ap, W, dst_ap, wait, use_gp=USE_GP_QUANT):
    p = ctx.p
    qa, sgn, acc, t1, t2 = sc["q_qa"], sc["q_sgn"], sc["q_acc"], sc["q_t1"], sc["q_t2"]
    amax, scal, rscal = sc["q_amax"], sc["q_scal"], sc["q_rscal"]
    nb = W // 16
    g = sc["q_guard"]
    sv1 = p.step("scalar", lambda e: e.activation(qa[:, 0:W], src_ap, AF.Abs),
                 wait=list(wait) + g, inc=ctx.se)
    sv2 = p.step("scalar", lambda e: e.activation(sgn[:, 0:W], src_ap, AF.Sign),
                 wait=[(ctx.se, sv1)], inc=ctx.se)
    a3 = qa[:, 0:W].rearrange("p (b k) -> p b k", b=nb, k=16)
    am3 = amax[:, 0:nb].rearrange("p (b one) -> p b one", b=nb, one=1)
    vv = p.step("vector", lambda v: v.tensor_reduce(am3, a3, AX.X, ALU.max),
                wait=[(ctx.se, sv1)] + g, inc=ctx.ve)
    vv = p.step("vector", lambda v: v.tensor_scalar(scal[:, 0:nb], amax[:, 0:nb], 1.0 / 6.0, 1e-12, ALU.mult, ALU.max),
                wait=[(ctx.ve, vv)], inc=ctx.ve)
    vv = p.step("vector", lambda v: v.reciprocal(rscal[:, 0:nb], scal[:, 0:nb]),
                wait=[(ctx.ve, vv)], inc=ctx.ve)
    r3 = rscal[:, 0:nb].rearrange("p (b one) -> p b one", b=nb, one=1).broadcast_to((128, nb, 16))
    vv = p.step("vector", lambda v: v.tensor_tensor(a3, a3, r3, ALU.mult),
                wait=[(ctx.ve, vv)], inc=ctx.ve)
    eng = "gpsimd" if use_gp else "vector"
    esem = ctx.ge if use_gp else ctx.ve
    gv = p.step(eng, lambda v: v.tensor_scalar(acc[:, 0:W], qa[:, 0:W], 0.25, None, ALU.is_gt),
                wait=[(ctx.ve, vv)], inc=esem)
    for m in (0.75, 1.25, 1.75, 2.5, 3.5, 5.0):
        gv = p.step(eng, lambda v, m=m: v.scalar_tensor_tensor(acc[:, 0:W], qa[:, 0:W], m, acc[:, 0:W], ALU.is_gt, ALU.add),
                    wait=[(esem, gv)], inc=esem)
    gv = p.step(eng, lambda v: v.tensor_scalar(t1[:, 0:W], acc[:, 0:W], 4.0, 0.0, ALU.subtract, ALU.max),
                wait=[(esem, gv)], inc=esem)
    gv = p.step(eng, lambda v: v.tensor_scalar(t2[:, 0:W], acc[:, 0:W], 6.0, 0.0, ALU.subtract, ALU.max),
                wait=[(esem, gv)], inc=esem)
    gv = p.step(eng, lambda v: v.scalar_tensor_tensor(t1[:, 0:W], t1[:, 0:W], 0.5, t2[:, 0:W], ALU.mult, ALU.add),
                wait=[(esem, gv)], inc=esem)
    gv = p.step(eng, lambda v: v.scalar_tensor_tensor(acc[:, 0:W], acc[:, 0:W], 0.5, t1[:, 0:W], ALU.mult, ALU.add),
                wait=[(esem, gv)], inc=esem)
    vv = p.step("vector", lambda v: v.tensor_mul(acc[:, 0:W], acc[:, 0:W], sgn[:, 0:W]),
                wait=[(esem, gv), (ctx.se, sv2)], inc=ctx.ve)
    s3 = scal[:, 0:nb].rearrange("p (b one) -> p b one", b=nb, one=1).broadcast_to((128, nb, 16))
    ac3 = acc[:, 0:W].rearrange("p (b k) -> p b k", b=nb, k=16)
    d3 = dst_ap.rearrange("p (b k) -> p b k", b=nb, k=16)
    vv = p.step("vector", lambda v: v.tensor_tensor(d3, ac3, s3, ALU.mult),
                wait=[(ctx.ve, vv)], inc=ctx.ve)
    sc["q_guard"] = [(ctx.ve, vv)]
    return [(ctx.ve, vv)]


def alloc_quant_scratch(ctx, sc, es, W=1024):
    p = ctx.p
    sc["q_qa"] = p.sbuf("q_qa", [128, W], F32, es=es)
    sc["q_sgn"] = p.sbuf("q_sgn", [128, W], BF16, es=es)
    sc["q_acc"] = p.sbuf("q_acc", [128, W], F32, es=es)
    sc["q_t1"] = p.sbuf("q_t1", [128, W], F32, es=es)
    sc["q_t2"] = p.sbuf("q_t2", [128, W], BF16, es=es)
    sc["q_u2"] = p.sbuf("q_u2", [128, W], BF16, es=es)
    sc["q_amax"] = p.sbuf("q_amax", [128, W // 16], F32, es=es)
    sc["q_scal"] = p.sbuf("q_scal", [128, W // 16], F32, es=es)
    sc["q_rscal"] = p.sbuf("q_rscal", [128, W // 16], F32, es=es)
    sc["q_guard"] = []


def l1_wquant(ctx, ws, wq_scratch):
    p = ctx.p
    es = ExitStack()
    sc = {}
    alloc_quant_scratch(ctx, sc, es, W=1024)
    fence = ctx.fence()
    sc["q_guard"] = list(fence)
    wld = p.sbuf("l1_wld", [128, 2816], F32, es=es)
    wqb = p.sbuf("l1_wqb", [128, 2816], BF16, es=es)
    wtb = p.sbuf("l1_wtb", [128, 22 * 128], BF16, es=es)
    WQG, WQU, WQD = wq_scratch
    st_guard = list(fence)
    for wi, (wname, dst, nrow, ncol) in enumerate(
        (("L1WG", WQG, 2816, 1024), ("L1WU", WQU, 2816, 1024), ("L1WD", WQD, 1024, 2816))
    ):
        hbm = ws[wname]
        for rt in range(nrow // 128):
            slot, wsp = ctx.rx.load(
                lambda e, hbm=hbm, rt=rt, ncol=ncol: e.dma_start(
                    out=wld[:, 0:ncol], in_=hbm[rt * 128 : (rt + 1) * 128, :]
                ),
                wait=sc["q_guard"],
            )
            qdone = []
            for s0 in range(0, ncol, 1024):
                w = min(1024, ncol - s0)
                qdone = quant_tile(ctx, sc, wld[:, s0 : s0 + w], w, wqb[:, s0 : s0 + w], [wsp])
            ctx.rx.ack(slot, *qdone)
            nch = ncol // 128
            for g in range(cdiv(nch, 4)):
                nin = min(4, nch - g * 4)
                bank, bw = ctx.poolT.acquire()
                tw = bw + qdone + st_guard
                tvs = None
                for c4 in range(nin):
                    c = g * 4 + c4
                    tvs = p.step(
                        "tensor",
                        lambda t, c=c, c4=c4, bank=bank: t.transpose(
                            ctx.psT[bank][:, c4 * 128 : (c4 + 1) * 128],
                            wqb[:, c * 128 : (c + 1) * 128],
                            ctx.cba("ident"),
                        ),
                        wait=tw,
                        inc=ctx.te,
                    )
                    tw = ()
                vv = p.step(
                    "vector",
                    lambda v, g=g, nin=nin, bank=bank: v.tensor_copy(
                        wtb[:, g * 512 : g * 512 + nin * 128], ctx.psT[bank][:, 0 : nin * 128]
                    ),
                    wait=[(ctx.te, tvs)],
                    inc=ctx.ve,
                )
                ctx.poolT.release(bank, (ctx.ve, vv))
            dstap = dst[:, rt * 128 : (rt + 1) * 128].rearrange("(c p) w -> p c w", p=128)
            srcap = wtb[:, 0 : nch * 128].rearrange("p (c w) -> p c w", c=nch, w=128)
            sslot, ssp = ctx.rs.load(
                lambda e, dstap=dstap, srcap=srcap: e.dma_start(out=dstap, in_=srcap),
                wait=[(ctx.ve, ctx.ve.n)],
            )
            ctx.rs.ack(sslot, ssp)
            st_guard = [ssp]
    es.close()
    return [ssp, (ctx.ve, ctx.ve.n)]


def l1_ffn_batch(ctx, sc, wq_scratch, wq_ready):
    p = ctx.p
    NT, T = ctx.NT, ctx.T
    WQG, WQU, WQD = wq_scratch
    fence = sc["ffn_fence"]
    xq_fm = sc["xq_fm"]
    xstage = sc["xstage"]
    xg = list(fence)
    for ti in range(NT):
        col0 = (ti % 16) * 8
        c_mean = ctx.row_scr[:, col0 + 2 : col0 + 3]
        c_rstd = ctx.row_scr[:, col0 + 6 : col0 + 7]
        xr = ctx.x_res[:, ti * 1024 : (ti + 1) * 1024]
        vv = p.step("vector",
                    lambda v, xr=xr, c_mean=c_mean, c_rstd=c_rstd: v.tensor_scalar(
                        xstage[:], xr, c_mean, c_rstd, ALU.subtract, ALU.mult),
                    wait=sc["ln_stats_ready"] + xg, inc=ctx.ve)
        qd = quant_tile(ctx, sc, xstage[:], 1024, sc["xqstage"][:], [(ctx.ve, vv)])
        fmv = tm_to_fm_src(ctx, sc["xqstage"], ti, xq_fm, qd[0])
        xg = [(ctx.ve, fmv)]
    xq_ready = [(ctx.ve, ctx.ve.n)]
    if "dbg_xq" in sc:
        p.step("sync", lambda e: e.dma_start(out=sc["dbg_xq"], in_=xq_fm[:]),
               wait=xq_ready, inc=ctx.st, inc_k=16)
    wg_sb, wu_sb, wd_sb = sc["wg_sb"], sc["wu_sb"], sc["wd_sb"]
    hqT = sc["hqT"]
    hscr = sc["hscr"]
    hqstage = sc["q_u2"]
    NSUP = cdiv(NT, 4)
    wgu_guard = [list(fence), list(fence)]
    wd_guard = list(fence)
    hqT_guard = list(fence)
    for sup in range(NSUP):
        ntile4 = min(4, NT - sup * 4)
        hq_done = []
        for fs in range(6):
            fw = min(512, 2816 - fs * 512)
            gslot, gw = ctx.rw.load(
                lambda e, fs=fs, fw=fw: e.dma_start(
                    out=wg_sb[:, 0 : 8 * fw].rearrange("p (c w) -> p c w", c=8, w=fw),
                    in_=WQG[:, fs * 512 : fs * 512 + fw].rearrange("(c p) w -> p c w", p=128),
                ),
                wait=wq_ready + wgu_guard[0],
            )
            uslot, uw = ctx.rw.load(
                lambda e, fs=fs, fw=fw: e.dma_start(
                    out=wu_sb[:, 0 : 8 * fw].rearrange("p (c w) -> p c w", c=8, w=fw),
                    in_=WQU[:, fs * 512 : fs * 512 + fw].rearrange("(c p) w -> p c w", p=128),
                ),
                wait=wq_ready + wgu_guard[1],
            )
            glast = ulast = None
            for ti4 in range(ntile4):
                ti = sup * 4 + ti4
                pig, pwg = ctx.pool.acquire()
                psg = ctx.psF[pig]
                tv = None
                for c in range(8):
                    tv = p.step(
                        "tensor",
                        lambda t, psg=psg, c=c, ti=ti, fw=fw: t.matmul(
                            psg[:, 0:fw],
                            xq_fm[:, c * T + ti * 128 : c * T + ti * 128 + 128],
                            wg_sb[:, c * fw : (c + 1) * fw],
                            start=(c == 0), stop=(c == 7),
                        ),
                        wait=(pwg + [gw] + xq_ready) if c == 0 else (),
                        inc=ctx.te if c == 7 else None,
                    )
                sv = p.step("scalar",
                            lambda e, psg=psg, fw=fw: e.activation(hscr[:, 0:fw], psg[:, 0:fw], AF.Silu),
                            wait=[(ctx.te, tv)] + sc["hscr_guard"] + sc["sscr_guard"], inc=ctx.se)
                glast = sv
                ctx.pool.release(pig, (ctx.se, sv))
                piu, pwu = ctx.pool.acquire()
                psu = ctx.psF[piu]
                tv = None
                for c in range(8):
                    tv = p.step(
                        "tensor",
                        lambda t, psu=psu, c=c, ti=ti, fw=fw: t.matmul(
                            psu[:, 0:fw],
                            xq_fm[:, c * T + ti * 128 : c * T + ti * 128 + 128],
                            wu_sb[:, c * fw : (c + 1) * fw],
                            start=(c == 0), stop=(c == 7),
                        ),
                        wait=(pwu + [uw] + xq_ready) if c == 0 else (),
                        inc=ctx.te if c == 7 else None,
                    )
                vv = p.step("vector",
                            lambda v, psu=psu, fw=fw: v.tensor_mul(hscr[:, 0:fw], hscr[:, 0:fw], psu[:, 0:fw]),
                            wait=[(ctx.te, tv), (ctx.se, sv)], inc=ctx.ve)
                ulast = vv
                ctx.pool.release(piu, (ctx.ve, vv))
                sc["sscr_guard"] = [(ctx.ve, vv)]
                qd = quant_tile(ctx, sc, hscr[:, 0:fw], fw, hqstage[:, 0:fw], [(ctx.ve, vv)])
                sc["hscr_guard"] = qd
                prs = []
                for j in range(fw // 128):
                    fc = fs * 4 + j
                    prs.append((hqstage[:, j * 128 : (j + 1) * 128],
                                hqT[:, fc * 512 + ti4 * 128 : fc * 512 + ti4 * 128 + 128]))
                tr = pe_transpose_pairs(ctx, prs, wait=qd + hqT_guard)
                hq_done.append((ctx.ve, tr))
                sc["hscr_guard"] = [(ctx.ve, tr)]
            wgu_guard[0] = [(ctx.te, ctx.te.n)]
            wgu_guard[1] = [(ctx.te, ctx.te.n)]
            ctx.rw.ack(gslot, (ctx.te, ctx.te.n))
            ctx.rw.ack(uslot, (ctx.te, ctx.te.n))
        if "dbg_hqT" in sc and sup == 0:
            p.step("sync", lambda e: e.dma_start(out=sc["dbg_hqT"], in_=hqT[:]),
                   wait=hq_done + [(ctx.ve, ctx.ve.n)], inc=ctx.st, inc_k=16)
        last_add = None
        for sl in range(2):
            pis = []
            for ti4 in range(ntile4):
                pi, pw = ctx.pool.acquire()
                pis.append((pi, pw))
            tvs = [None] * ntile4
            for half in range(2):
                dslot, dw = ctx.rw.load(
                    lambda e, sl=sl, half=half: e.dma_start(
                        out=wd_sb[:, 0 : 11 * 512].rearrange("p (c w) -> p c w", c=11, w=512),
                        in_=WQD[half * 1408 : half * 1408 + 1408, sl * 512 : (sl + 1) * 512]
                        .rearrange("(c p) w -> p c w", p=128),
                    ),
                    wait=wq_ready + wd_guard,
                )
                for ti4 in range(ntile4):
                    ti = sup * 4 + ti4
                    pi, pw = pis[ti4]
                    ps = ctx.psF[pi]
                    for fc2 in range(11):
                        fc = half * 11 + fc2
                        tvs[ti4] = p.step(
                            "tensor",
                            lambda t, ps=ps, fc=fc, fc2=fc2, ti4=ti4: t.matmul(
                                ps[:],
                                hqT[:, fc * 512 + ti4 * 128 : fc * 512 + ti4 * 128 + 128],
                                wd_sb[:, fc2 * 512 : (fc2 + 1) * 512],
                                start=(fc == 0), stop=(fc == 21),
                            ),
                            wait=(pw + [dw] + hq_done) if fc2 == 0 else (),
                            inc=ctx.te if fc2 == 10 else None,
                        )
                ctx.rw.ack(dslot, (ctx.te, ctx.te.n))
                wd_guard = [(ctx.te, ctx.te.n)]
            for ti4 in range(ntile4):
                ti = sup * 4 + ti4
                pi, pw = pis[ti4]
                ps = ctx.psF[pi]
                xr = ctx.x_res[:, ti * 1024 + sl * 512 : ti * 1024 + sl * 512 + 512]
                vv = p.step("vector", lambda v, xr=xr, ps=ps: v.tensor_add(xr, ps[:], xr),
                            wait=[(ctx.te, tvs[ti4])], inc=ctx.ve)
                ctx.pool.release(pi, (ctx.ve, vv))
                last_add = vv
        hqT_guard = [(ctx.te, ctx.te.n)]
    return [(ctx.ve, ctx.ve.n)]


def layer1(ctx, ws, x_src, x_dst, wq_scratch, wq_ready, prev_store=None, parts="all"):
    p = ctx.p
    NT, T = ctx.NT, ctx.T
    ctx.xn_fm = None
    les = ExitStack()
    sc = {}
    pes = ExitStack()
    sc["xn_tm"] = p.sbuf("l1_xntm", [128, NT * 1024], BF16, es=pes)
    sc["logp_fm"] = p.sbuf("l1_logpfm", [128, 8 * T], BF16, es=pes)
    sc["ptile"] = [p.sbuf(f"l1_pt{i}", [128, 1024], BF16, es=pes) for i in range(2)]
    sc["escr"] = p.sbuf("l1_escr", [128, 1024], BF16, es=pes)
    sc["lrow"] = p.sbuf("l1_lrow", [128, 64], F32, es=pes)
    sc["probs"] = [p.sbuf(f"l1_probs{i}", [128, 512], BF16, es=pes) for i in range(8)]
    sc["probsT"] = p.sbuf("l1_probsT", [128, T], BF16, es=pes)
    sc["arow"] = p.sbuf("l1_arow", [128, 48], F32, es=pes)
    fence0 = ctx.fence()
    sc["fence0"] = fence0
    sc["probs_rel"] = [list(fence0) for _ in range(8)]
    sc["pvT_prev"] = list(fence0)
    sc["arow_guard"] = [list(fence0), list(fence0)]
    sc["lrow_g"] = [[] for _ in range(8)]
    ln_phase(ctx, x_src, prev_store,
             dst_fn=lambda ti: sc["xn_tm"][:, ti * 1024 : (ti + 1) * 1024])
    sc["xn_ready"] = [(ctx.ve, ctx.ve.n), (ctx.se, ctx.se.n)]
    sc["ln_stats_ready"] = [(ctx.ve, ctx.ve.n)]
    if parts in ("all", "attn"):
        logp_ready = lse_logp_tiles(ctx, sc, sc["xn_tm"], sc["logp_fm"])
        l1_attn_batch(ctx, sc, logp_ready)
    pes.close()
    fence1 = ctx.fence()
    pes2 = ExitStack()
    sc["ffn_fence"] = fence1
    sc["xq_fm"] = p.sbuf("l1_xqfm", [128, 8 * T], BF16, es=pes2)
    alloc_quant_scratch(ctx, sc, pes2, W=1024)
    sc["q_guard"] = list(fence1)
    sc["xstage"] = p.sbuf("l1_xstage", [128, 1024], F32, es=pes2)
    sc["xqstage"] = sc["q_u2"]
    sc["wg_sb"] = p.sbuf("l1_wg", [128, 8 * 512], BF16, es=pes2)
    sc["wu_sb"] = p.sbuf("l1_wu", [128, 8 * 512], BF16, es=pes2)
    sc["wd_sb"] = p.sbuf("l1_wd", [128, 11 * 512], BF16, es=pes2)
    sc["hqT"] = p.sbuf("l1_hqT", [128, 22 * 512], BF16, es=pes2)
    sc["hscr"] = p.sbuf("l1_hscr", [128, 512], F32, es=pes2)
    sc["sscr_guard"] = list(fence1)
    sc["hscr_guard"] = list(fence1)
    if parts in ("all", "ffn"):
        l1_ffn_batch(ctx, sc, wq_scratch, wq_ready)
    pes2.close()
    les.close()
    if x_dst is None:
        return None
    return store_xres(ctx, x_dst)




def l2_attn_batch(ctx, ws, sc):
    p = ctx.p
    T, NT = ctx.T, ctx.NT
    xn_fm = ctx.xn_fm
    fence = sc["fence0"]
    wq_sb, wk_sb, wv_sb = sc["wq_sb"], sc["wk_sb"], sc["wv_sb"]
    wld = []
    for wi, (wname, dst) in enumerate((("L2WQ", wq_sb), ("L2WK", wk_sb), ("L2WV", wv_sb))):
        slot, wsp = load_w_cols(ctx, ws[wname], 0, 1024, dst, wait=sc["wbig_guard"][wi])
        wld.append((slot, wsp))
    fmw = sc["fmw"]
    slot_fq, w_fq = ctx.rw.load(lambda e: e.dma_start(out=fmw[:, 0:128], in_=ws["L2FMQ"]),
                                wait=sc["fmw_guard"])
    slot_fk, w_fk = ctx.rw.load(lambda e: e.dma_start(out=fmw[:, 128:256], in_=ws["L2FMK"]),
                                wait=sc["fmw_guard"])
    frow = sc["l2rows"]
    slot_b, w_b = ctx.rx.load(lambda e: e.dma_start(out=frow[:, 0:2],
                                                    in_=ws["L2FMB"]), wait=sc["fmw_guard"])
    LN16 = float(np.log(16.0))
    vv = p.step("vector", lambda v: v.tensor_scalar(frow[:, 2:3], frow[:, 0:1], -LN16, None, ALU.add),
                wait=[w_b], inc=ctx.ve)
    vv = p.step("vector", lambda v: v.tensor_scalar(frow[:, 3:4], frow[:, 0:1], -1.0, -LN16, ALU.mult, ALU.add),
                wait=[(ctx.ve, vv)], inc=ctx.ve)
    vv = p.step("vector", lambda v: v.tensor_copy(frow[:, 4:5], frow[:, 1:2]),
                wait=[(ctx.ve, vv)], inc=ctx.ve)
    vv = p.step("vector", lambda v: v.tensor_scalar(frow[:, 5:6], frow[:, 1:2], -1.0, None, ALU.mult),
                wait=[(ctx.ve, vv)], inc=ctx.ve)
    ctx.rx.ack(slot_b, (ctx.ve, vv))
    bias_ready = (ctx.ve, vv)
    rmsrow = sc["rmsrow"]
    slot_r, w_r = ctx.rw.load(lambda e: e.dma_start(out=rmsrow[0:1, :], in_=ws["L2RMSW"]),
                              wait=sc["fmw_guard"])
    rv = bcast_row(ctx, rmsrow[0:1, :], 128, sc["rmsw_bc"][:], wait=[w_r])
    ctx.rw.ack(slot_r, (ctx.ve, rv))
    S32 = sc["S32"]
    Sbf = sc["Sbf"]
    zv = p.step("vector", lambda v: v.memset(S32[:], 0.0), wait=list(fence) + sc["S_guard"], inc=ctx.ve)
    zv = p.step("vector", lambda v: v.memset(Sbf[:], 0.0), wait=[(ctx.ve, zv)], inc=ctx.ve)
    s_ready = [[(ctx.ve, zv)] for _ in range(8)]
    qkv_sb = sc["qkv_sb"]
    vtm = sc["vtm"]
    phq = sc["phq"]
    phk = sc["phk"]
    phkT = sc["phkT"]
    am = sc["am"]
    mask = ctx.cba("mask_t2_le_t1")
    wo_slot, wo_w = load_w_cols(ctx, ws["L2WO"], 0, 1024, sc["wo_sb"], wait=sc["wbig_guard"][3])
    sc["wo_w"] = wo_w
    orow = sc["l2orow"]
    og = [[], []]
    GRP = 2
    for g in range(cdiv(NT, GRP)):
        gn = min(GRP, NT - g * GRP)
        gw512 = gn * 128
        pgd = {}
        for wi in range(3):
            for h in range(8):
                pi, pw = ctx.pool.acquire()
                ps = ctx.psF[pi]
                tv = None
                for c in range(8):
                    tv = p.step(
                        "tensor",
                        lambda t, ps=ps, c=c, h=h, wi=wi, g=g, gw512=gw512: t.matmul(
                            ps[:, 0:gw512],
                            (wq_sb, wk_sb, wv_sb)[wi][:, c * 1024 + h * 128 : c * 1024 + h * 128 + 128],
                            xn_fm[:, c * T + g * GRP * 128 : c * T + g * GRP * 128 + gw512],
                            start=(c == 0), stop=(c == 7),
                        ),
                        wait=(pw + [wld[wi][1]] + sc["xn_ready"]) if c == 0 else (),
                        inc=ctx.te if c == 7 else None,
                    )
                dst = qkv_sb[:, (wi * 8 + h) * (GRP * 128) : (wi * 8 + h) * (GRP * 128) + gw512]
                vv2 = p.step("vector", lambda v, dst=dst, ps=ps, gw512=gw512: v.tensor_copy(dst, ps[:, 0:gw512]),
                             wait=[(ctx.te, tv)] + (sc["qkv_guard"] if (wi == 0 and h == 0) else []),
                             inc=ctx.ve)
                ctx.pool.release(pi, (ctx.ve, vv2))
                pgd[(wi, h)] = vv2
        prs = []
        for h in range(8):
            for c4 in range(gn):
                src = qkv_sb[:, (2 * 8 + h) * (GRP * 128) + c4 * 128 : (2 * 8 + h) * (GRP * 128) + c4 * 128 + 128]
                dst = vtm[:, c4 * 1024 + h * 128 : c4 * 1024 + h * 128 + 128]
                prs.append((src, dst))
        trv = pe_transpose_pairs(ctx, prs, wait=[(ctx.ve, ctx.ve.n)] + sc["vtm_guard"])
        sc["vtm_guard"] = []
        vready = (ctx.ve, trv)
        for c4 in range(gn):
            cg = g * GRP + c4
            for h in range(8):
                set_ = (c4 * 8 + h) % 2
                pi, pw = ctx.pool.acquire()
                ps = ctx.psF[pi]
                tv = p.step("tensor",
                            lambda t, ps=ps, h=h, c4=c4: t.matmul(
                                ps[:, 0:128], fmw[:, 0:128],
                                qkv_sb[:, (0 * 8 + h) * (GRP * 128) + c4 * 128 : (0 * 8 + h) * (GRP * 128) + c4 * 128 + 128],
                                start=True, stop=True),
                            wait=pw + [w_fq, (ctx.ve, pgd[(0, h)])], inc=ctx.te)
                sv = p.step("scalar",
                            lambda e, ps=ps, set_=set_: e.activation(phq[set_][:, 0:128], ps[:, 0:128], AF.Exp,
                                                                     bias=frow[:, 2:3]),
                            wait=[(ctx.te, tv), bias_ready] + sc["phq_guard"][set_], inc=ctx.se)
                sv = p.step("scalar",
                            lambda e, ps=ps, set_=set_: e.activation(phq[set_][:, 128:256], ps[:, 0:128], AF.Exp,
                                                                     bias=frow[:, 3:4], scale=-1.0),
                            wait=[(ctx.se, sv)], inc=ctx.se)
                ctx.pool.release(pi, (ctx.se, sv))
                phq_r = (ctx.se, sv)
                pi2, pw2 = ctx.pool.acquire()
                ps2 = ctx.psF[pi2]
                tv = p.step("tensor",
                            lambda t, ps2=ps2, h=h, c4=c4: t.matmul(
                                ps2[:, 0:128], fmw[:, 128:256],
                                qkv_sb[:, (1 * 8 + h) * (GRP * 128) + c4 * 128 : (1 * 8 + h) * (GRP * 128) + c4 * 128 + 128],
                                start=True, stop=True),
                            wait=pw2 + [w_fk, (ctx.ve, pgd[(1, h)])], inc=ctx.te)
                sv = p.step("scalar",
                            lambda e, ps2=ps2, set_=set_: e.activation(phk[set_][:, 0:128], ps2[:, 0:128], AF.Exp,
                                                                      bias=frow[:, 4:5]),
                            wait=[(ctx.te, tv)] + sc["phk_guard"][set_], inc=ctx.se)
                sv = p.step("scalar",
                            lambda e, ps2=ps2, set_=set_: e.activation(phk[set_][:, 128:256], ps2[:, 0:128], AF.Exp,
                                                                      bias=frow[:, 5:6], scale=-1.0),
                            wait=[(ctx.se, sv)], inc=ctx.se)
                ctx.pool.release(pi2, (ctx.se, sv))
                phk_r = (ctx.se, sv)
                prs = [(phk[set_][:, 0:128], phkT[set_][:, 0:128]),
                       (phk[set_][:, 128:256], phkT[set_][:, 128:256])]
                trk = pe_transpose_pairs(ctx, prs, wait=[phk_r] + sc["phkT_guard"][set_])
                pi3, pw3 = ctx.pool.acquire()
                ps3 = ctx.psF[pi3]
                tv = None
                for fc in range(2):
                    tv = p.step("tensor",
                                lambda t, ps3=ps3, fc=fc, set_=set_: t.matmul(
                                    ps3[:, 0:128], phk[set_][:, fc * 128 : (fc + 1) * 128],
                                    phq[set_][:, fc * 128 : (fc + 1) * 128],
                                    start=(fc == 0), stop=(fc == 1)),
                                wait=(pw3 + [phk_r, phq_r]) if fc == 0 else (), inc=ctx.te if fc else None)
                amv = p.step("vector",
                             lambda v, ps3=ps3, set_=set_: v.tensor_mul(am[set_][:], ps3[:, 0:128], mask),
                             wait=[(ctx.te, tv)] + sc["am_guard"][set_], inc=ctx.ve)
                ctx.pool.release(pi3, (ctx.ve, amv))
                pi4, pw4 = ctx.pool.acquire()
                ps4 = ctx.psF[pi4]
                tv = p.step("tensor",
                            lambda t, ps4=ps4, set_=set_, c4=c4, h=h: t.matmul(
                                ps4[:, 0:128], am[set_][:],
                                vtm[:, c4 * 1024 + h * 128 : c4 * 1024 + h * 128 + 128],
                                start=True, stop=False),
                            wait=pw4 + [(ctx.ve, amv), vready], inc=None)
                for fc in range(2):
                    tv = p.step("tensor",
                                lambda t, ps4=ps4, fc=fc, set_=set_, h=h: t.matmul(
                                    ps4[:, 0:128], phq[set_][:, fc * 128 : (fc + 1) * 128],
                                    Sbf[:, h * 264 + fc * 132 : h * 264 + fc * 132 + 128],
                                    start=False, stop=(fc == 1)),
                                wait=s_ready[h] if fc == 0 else (), inc=ctx.te if fc else None)
                p1_done = (ctx.te, tv)
                pi5, pw5 = ctx.pool.acquire()
                ps5 = ctx.psF[pi5]
                tv2 = None
                for fc in range(2):
                    tv2 = p.step("tensor",
                                 lambda t, ps5=ps5, fc=fc, set_=set_, c4=c4, h=h: t.matmul(
                                     ps5[:, fc * 132 : fc * 132 + 128],
                                     phkT[set_][:, fc * 128 : (fc + 1) * 128],
                                     vtm[:, c4 * 1024 + h * 128 : c4 * 1024 + h * 128 + 128],
                                     start=(fc == 0), stop=(fc == 1)),
                                 wait=(pw5 + [(ctx.ve, trk), vready] + s_ready[h]) if fc == 0 else (),
                                 inc=ctx.te if fc else None)
                sc["phkT_guard"][set_] = [(ctx.te, tv2)]
                s32h3 = S32[:, h * 264 : h * 264 + 264].rearrange("p (c w) -> p c w", c=2, w=132)[:, :, 0:128]
                ps53 = ps5[:, 0:264].rearrange("p (c w) -> p c w", c=2, w=132)[:, :, 0:128]
                sbf3 = Sbf[:, h * 264 : h * 264 + 264].rearrange("p (c w) -> p c w", c=2, w=132)[:, :, 0:128]
                sv1 = p.step("vector",
                             lambda v, ps53=ps53, s32h3=s32h3: v.tensor_add(s32h3, ps53, s32h3),
                             wait=[(ctx.te, tv2)], inc=ctx.ve)
                sv2 = p.step("vector",
                             lambda v, sbf3=sbf3, s32h3=s32h3: v.tensor_copy(sbf3, s32h3),
                             wait=[(ctx.ve, sv1)], inc=ctx.ve)
                ctx.pool.release(pi5, (ctx.ve, sv1))
                s_ready[h] = [(ctx.ve, sv2)]
                oc0 = (set_) * 8
                o_f = sc["o_f"]
                osq = sc["osq"]
                ev = p.step("vector",
                            lambda v, ps4=ps4, o_f=o_f, set_=set_: v.tensor_copy(o_f[set_][:], ps4[:, 0:128]),
                            wait=[p1_done] + sc["of_guard"][set_], inc=ctx.ve)
                ctx.pool.release(pi4, (ctx.ve, ev))
                sq = p.step("scalar",
                            lambda e, o_f=o_f, osq=osq, set_=set_, oc0=oc0: e.activation(
                                osq[set_][:], o_f[set_][:], AF.Square, accum_out=orow[:, oc0 : oc0 + 1]),
                            wait=[(ctx.ve, ev)] + sc["osq_guard"][set_], inc=ctx.se)
                sc["osq_guard"][set_] = [(ctx.se, sq)]
                vv = p.step("vector",
                            lambda v, oc0=oc0: v.tensor_scalar(orow[:, oc0 + 1 : oc0 + 2],
                                                               orow[:, oc0 : oc0 + 1],
                                                               1.0 / 128.0, 1e-5, ALU.mult, ALU.add),
                            wait=[(ctx.se, sq)], inc=ctx.ve)
                sv3 = p.step("scalar",
                             lambda e, oc0=oc0: e.activation(orow[:, oc0 + 2 : oc0 + 3],
                                                             orow[:, oc0 + 1 : oc0 + 2], AF.Sqrt),
                             wait=[(ctx.ve, vv)], inc=ctx.se)
                vv = p.step("vector",
                            lambda v, oc0=oc0: v.reciprocal(orow[:, oc0 + 3 : oc0 + 4],
                                                            orow[:, oc0 + 2 : oc0 + 3]),
                            wait=[(ctx.se, sv3)], inc=ctx.ve)
                ost = sc["ost"]
                vv = p.step("vector",
                            lambda v, o_f=o_f, set_=set_, oc0=oc0, ost=ost: v.scalar_tensor_tensor(
                                ost[set_][:], o_f[set_][:], orow[:, oc0 + 3 : oc0 + 4],
                                sc["rmsw_bc"][:], ALU.mult, ALU.mult),
                            wait=[(ctx.ve, vv), (ctx.ve, rv)] + sc["ost_guard"][set_], inc=ctx.ve)
                sc["of_guard"][set_] = [(ctx.ve, vv)]
                oset = 0
                ofm = sc["ofm"][0]
                tro = pe_transpose_pairs(ctx, [(ost[set_][:], ofm[:, h * 128 : (h + 1) * 128])],
                                         wait=[(ctx.ve, vv)] + (sc["ofm_guard"][oset] if h == 0 else []))
                sc["ost_guard"][set_] = [(ctx.te, ctx.te.n)]
                sc["am_guard"][set_] = [(ctx.te, ctx.te.n)]
                sc["phq_guard"][set_] = [(ctx.te, ctx.te.n)]
                sc["phk_guard"][set_] = [(ctx.te, ctx.te.n)]
            oset = 0
            ofm = sc["ofm"][0]
            ofm_ready = (ctx.ve, ctx.ve.n)
            for half in range(2):
                pi, pw = ctx.pool.acquire()
                ps = ctx.psF[pi]
                tv = None
                for h2 in range(8):
                    tv = p.step(
                        "tensor",
                        lambda t, ps=ps, h2=h2, half=half, ofm=ofm: t.matmul(
                            ps[:],
                            ofm[:, h2 * 128 : (h2 + 1) * 128],
                            sc["wo_sb"][:, h2 * 1024 + half * 512 : h2 * 1024 + half * 512 + 512],
                            start=(h2 == 0), stop=(h2 == 7)),
                        wait=(pw + [sc["wo_w"], ofm_ready]) if h2 == 0 else (),
                        inc=ctx.te if h2 == 7 else None)
                xr = ctx.x_res[:, cg * 1024 + half * 512 : cg * 1024 + half * 512 + 512]
                vv3 = p.step("vector", lambda v, xr=xr, ps=ps: v.tensor_add(xr, ps[:], xr),
                             wait=[(ctx.te, tv)], inc=ctx.ve)
                ctx.pool.release(pi, (ctx.ve, vv3))
            sc["ofm_guard"][oset] = [(ctx.te, ctx.te.n)]
        sc["qkv_guard"] = [(ctx.te, ctx.te.n)]
        sc["vtm_guard"] = [(ctx.te, ctx.te.n)]
    for wi in range(3):
        ctx.rw.ack(wld[wi][0], (ctx.te, ctx.te.n))
        sc["wbig_guard"][wi] = [(ctx.te, ctx.te.n)]
    ctx.rw.ack(wo_slot, (ctx.te, ctx.te.n))
    sc["wbig_guard"][3] = [(ctx.te, ctx.te.n)]
    ctx.rw.ack(slot_fq, (ctx.te, ctx.te.n))
    ctx.rw.ack(slot_fk, (ctx.te, ctx.te.n))
    sc["fmw_guard"] = [(ctx.te, ctx.te.n)]
    return [(ctx.ve, ctx.ve.n)]


def moe_expert_pass(ctx, sc, e_idx, w1_ap, w3_ap, w2_ap, inter, gate_col_fn, silu_on_w1=True):
    p = ctx.p
    T, NT = ctx.T, ctx.NT
    xn_fm = ctx.xn_fm
    w1b, w3b, w2b = sc["w1b"], sc["w3b"], sc["w2b"]
    s1, w1w = load_w_cols(ctx, w1_ap, 0, inter, w1b, wait=sc["wmoe_guard"][0])
    s3, w3w = load_w_cols(ctx, w3_ap, 0, inter, w3b, wait=sc["wmoe_guard"][1])
    s2, w2w = load_w_cols(ctx, w2_ap, 0, 1024, w2b, wait=sc["wmoe_guard"][2], d_in=inter)
    nfc = inter // 128
    hfm = sc["hfm"]
    hscr2 = sc["hscr2"]
    for sl in range(cdiv(NT, 4)):
        sn = min(4, NT - sl * 4)
        sw512 = sn * 128
        for fc in range(nfc):
            pi, pw = ctx.pool.acquire()
            ps = ctx.psF[pi]
            tv = None
            for c in range(8):
                tv = p.step(
                    "tensor",
                    lambda t, ps=ps, c=c, fc=fc, sl=sl, sw512=sw512: t.matmul(
                        ps[:, 0:sw512],
                        w1b[:, c * inter + fc * 128 : c * inter + fc * 128 + 128],
                        xn_fm[:, c * T + sl * 512 : c * T + sl * 512 + sw512],
                        start=(c == 0), stop=(c == 7)),
                    wait=(pw + [w1w] + sc["xn_ready"]) if c == 0 else (),
                    inc=ctx.te if c == 7 else None)
            sv = p.step("scalar",
                        lambda e, ps=ps, sw512=sw512: e.activation(hscr2[:, 0:sw512], ps[:, 0:sw512], AF.Silu),
                        wait=[(ctx.te, tv)] + sc["h2_guard"], inc=ctx.se)
            ctx.pool.release(pi, (ctx.se, sv))
            pi2, pw2 = ctx.pool.acquire()
            ps2 = ctx.psF[pi2]
            tv = None
            for c in range(8):
                tv = p.step(
                    "tensor",
                    lambda t, ps2=ps2, c=c, fc=fc, sl=sl, sw512=sw512: t.matmul(
                        ps2[:, 0:sw512],
                        w3b[:, c * inter + fc * 128 : c * inter + fc * 128 + 128],
                        xn_fm[:, c * T + sl * 512 : c * T + sl * 512 + sw512],
                        start=(c == 0), stop=(c == 7)),
                    wait=(pw2 + [w3w]) if c == 0 else (),
                    inc=ctx.te if c == 7 else None)
            hv = p.step("vector",
                        lambda v, ps2=ps2, fc=fc, sw512=sw512: v.tensor_mul(
                            hfm[:, fc * 512 : fc * 512 + sw512], hscr2[:, 0:sw512], ps2[:, 0:sw512]),
                        wait=[(ctx.te, tv), (ctx.se, sv)] + sc["hfm_guard"], inc=ctx.ve)
            ctx.pool.release(pi2, (ctx.ve, hv))
            sc["h2_guard"] = [(ctx.ve, hv)]
        h_ready = (ctx.ve, ctx.ve.n)
        for t4 in range(sn):
            ti = sl * 4 + t4
            pis = []
            for half in range(2):
                pi, pw = ctx.pool.acquire()
                ps = ctx.psF[pi]
                tv = None
                for fc in range(nfc):
                    tv = p.step(
                        "tensor",
                        lambda t, ps=ps, fc=fc, t4=t4, half=half: t.matmul(
                            ps[:],
                            hfm[:, fc * 512 + t4 * 128 : fc * 512 + t4 * 128 + 128],
                            w2b[:, fc * 1024 + half * 512 : fc * 1024 + half * 512 + 512],
                            start=(fc == 0), stop=(fc == nfc - 1)),
                        wait=(pw + [w2w, h_ready]) if fc == 0 else (),
                        inc=ctx.te if fc == nfc - 1 else None)
                pis.append((pi, tv))
            for half in range(2):
                pi, tv = pis[half]
                ps = ctx.psF[pi]
                xr = ctx.x_res[:, ti * 1024 + half * 512 : ti * 1024 + half * 512 + 512]
                gc = gate_col_fn(ti)
                if gc is None:
                    vv = p.step("vector", lambda v, xr=xr, ps=ps: v.tensor_add(xr, ps[:], xr),
                                wait=[(ctx.te, tv)], inc=ctx.ve)
                else:
                    vv = p.step("vector",
                                lambda v, xr=xr, ps=ps, gc=gc: v.scalar_tensor_tensor(
                                    xr, ps[:], gc, xr, ALU.mult, ALU.add),
                                wait=[(ctx.te, tv)], inc=ctx.ve)
                ctx.pool.release(pi, (ctx.ve, vv))
        sc["hfm_guard"] = [(ctx.te, ctx.te.n)]
    te_g = [(ctx.te, ctx.te.n)]
    ctx.rw.ack(s1, *te_g)
    ctx.rw.ack(s3, *te_g)
    ctx.rw.ack(s2, *te_g)
    sc["wmoe_guard"] = [te_g + [w1w], te_g + [w3w], te_g + [w2w]]


def moe_gate8(ctx, sc, gw_name, ws, bias_mode, wts_buf):
    p = ctx.p
    T, NT = ctx.T, ctx.NT
    xn_fm = ctx.xn_fm
    gwb = sc["gwb"]
    slot, gww = load_w_cols(ctx, ws[gw_name], 0, 8, gwb, wait=sc["gw_guard"])
    grow8 = sc["grow8"]
    bname = "L2GB" if bias_mode == "l2" else "L3GB"
    slot_b, gbw = ctx.rx.load(lambda e: e.dma_start(out=grow8[0:1, 0:8], in_=ws[bname]),
                              wait=sc["gw_guard"])
    vvb = p.step("vector", lambda v: v.tensor_copy(sc["grow8b"][0:1, 0:8], grow8[0:1, 0:8]),
                 wait=[gbw], inc=ctx.ve)
    bv = bcast_row(ctx, sc["grow8b"][0:1, 0:8], 8, sc["gb_bc"][:], wait=[(ctx.ve, vvb)])
    ctx.rx.ack(slot_b, (ctx.ve, bv))
    gsc = sc["gsc8"]
    g_guard = sc["g8_guard"]
    last = None
    for ti in range(NT):
        st0 = (ti % 2) * 32
        lg = gsc[:, st0 : st0 + 8]
        m8 = gsc[:, st0 + 8 : st0 + 16]
        e8 = gsc[:, st0 + 16 : st0 + 24]
        r0 = gsc[:, st0 + 24 : st0 + 25]
        r1 = gsc[:, st0 + 25 : st0 + 26]
        r2 = gsc[:, st0 + 26 : st0 + 27]
        pi, pw = ctx.pool.acquire()
        ps = ctx.psF[pi]
        tv = None
        for c in range(8):
            tv = p.step("tensor",
                        lambda t, ps=ps, c=c, ti=ti: t.matmul(
                            ps[:, 0:8], xn_fm[:, c * T + ti * 128 : c * T + ti * 128 + 128],
                            gwb[:, c * 8 : (c + 1) * 8], start=(c == 0), stop=(c == 7)),
                        wait=(pw + [gww] + sc["xn_ready"]) if c == 0 else (),
                        inc=ctx.te if c == 7 else None)
        if bias_mode == "l3":
            vv = p.step("vector", lambda v, ps=ps, lg=lg: v.tensor_add(lg, ps[:, 0:8], sc["gb_bc"][:]),
                        wait=[(ctx.te, tv), (ctx.ve, bv)] + g_guard[ti % 2], inc=ctx.ve)
        else:
            vv = p.step("vector", lambda v, ps=ps, lg=lg: v.tensor_copy(lg, ps[:, 0:8]),
                        wait=[(ctx.te, tv)] + g_guard[ti % 2], inc=ctx.ve)
        ctx.pool.release(pi, (ctx.ve, vv))
        vv = p.step("vector", lambda v, lg=lg, r0=r0: v.reduce_max(r0, lg, axis=AX.X, negate=True),
                    wait=[(ctx.ve, vv)], inc=ctx.ve)
        sv = p.step("scalar", lambda e, lg=lg, e8=e8, r0=r0, r1=r1: e.activation(
            e8, lg, AF.Exp, bias=r0, accum_out=r1), wait=[(ctx.ve, vv)] + sc["g8s_guard"][ti % 2],
            inc=ctx.se)
        sc["g8s_guard"][ti % 2] = [(ctx.se, sv)]
        vv = p.step("vector", lambda v, r1=r1, r2=r2: v.reciprocal(r2, r1),
                    wait=[(ctx.se, sv)], inc=ctx.ve)
        vv = p.step("vector", lambda v, e8=e8, r2=r2: v.tensor_scalar(e8, e8, r2, None, ALU.mult),
                    wait=[(ctx.ve, vv)], inc=ctx.ve)
        if bias_mode == "l2":
            vv = p.step("vector", lambda v, lg=lg, e8=e8: v.tensor_add(lg, e8, sc["gb_bc"][:]),
                        wait=[(ctx.ve, vv), (ctx.ve, bv)], inc=ctx.ve)
            vv = p.step("vector", lambda v, m8=m8, lg=lg: v.max(out=m8, in_=lg),
                        wait=[(ctx.ve, vv)], inc=ctx.ve)
            vv = p.step("vector",
                        lambda v, ti=ti, lg=lg, e8=e8, m8=m8: v.scalar_tensor_tensor(
                            wts_buf[:, ti * 8 : (ti + 1) * 8], lg, m8[:, 1:2], e8, ALU.is_ge, ALU.mult),
                        wait=[(ctx.ve, vv)] + sc["wts_guard"], inc=ctx.ve)
        else:
            vv = p.step("vector", lambda v, m8=m8, e8=e8: v.max(out=m8, in_=e8),
                        wait=[(ctx.ve, vv)], inc=ctx.ve)
            vv = p.step("vector",
                        lambda v, lg=lg, e8=e8, m8=m8: v.scalar_tensor_tensor(
                            lg, e8, m8[:, 1:2], e8, ALU.is_ge, ALU.mult),
                        wait=[(ctx.ve, vv)], inc=ctx.ve)
            vv = p.step("vector", lambda v, r0=r0, m8=m8: v.tensor_add(r0, m8[:, 0:1], m8[:, 1:2]),
                        wait=[(ctx.ve, vv)], inc=ctx.ve)
        ctx.pool.release if False else None
        if bias_mode == "l3":
            vv = p.step("vector", lambda v, r1=r1, r0=r0: v.reciprocal(r1, r0),
                        wait=[(ctx.ve, vv)], inc=ctx.ve)
            vv = p.step("vector",
                        lambda v, ti=ti, lg=lg, r1=r1: v.tensor_scalar(
                            wts_buf[:, ti * 8 : (ti + 1) * 8], lg, r1, None, ALU.mult),
                        wait=[(ctx.ve, vv)] + sc["wts_guard"], inc=ctx.ve)
        g_guard[ti % 2] = [(ctx.ve, vv)]
        last = vv
    ctx.rw.ack(slot, (ctx.te, ctx.te.n))
    sc["gw_guard"] = [(ctx.te, ctx.te.n)]
    return [(ctx.ve, last)]


def layer2(ctx, ws, x_src, x_dst, prev_store=None, parts="all"):
    p = ctx.p
    NT, T = ctx.NT, ctx.T
    les = ExitStack()
    sc = {}
    ctx.xn_fm = p.sbuf("l2_xnfm", [128, 8 * T], BF16, es=les)
    ln_phase(ctx, x_src, prev_store)
    sc["xn_ready"] = [(ctx.ve, ctx.ve.n), (ctx.se, ctx.se.n)]
    fence0 = ctx.fence()
    sc["fence0"] = fence0
    if parts in ("all", "attn"):
        pes = ExitStack()
        sc["wq_sb"] = p.sbuf("l2_wq", [128, 8192], BF16, es=pes)
        sc["wk_sb"] = p.sbuf("l2_wk", [128, 8192], BF16, es=pes)
        sc["wv_sb"] = p.sbuf("l2_wv", [128, 8192], BF16, es=pes)
        sc["wo_sb"] = p.sbuf("l2_wo", [128, 8192], BF16, es=pes)
        sc["fmw"] = p.sbuf("l2_fmw", [128, 256], BF16, es=pes)
        sc["l2rows"] = p.sbuf("l2_rows", [128, 8], F32, es=pes)
        sc["rmsrow"] = p.sbuf("l2_rmsr", [1, 128], BF16, es=pes)
        sc["rmsw_bc"] = p.sbuf("l2_rmsbc", [128, 128], F32, es=pes)
        sc["S32"] = p.sbuf("l2_S32", [128, 8 * 264], F32, es=pes)
        sc["Sbf"] = p.sbuf("l2_Sbf", [128, 8 * 264], BF16, es=pes)
        sc["qkv_sb"] = p.sbuf("l2_qkv", [128, 3 * 8 * 256], BF16, es=pes)
        sc["vtm"] = p.sbuf("l2_vtm", [128, 2 * 1024], BF16, es=pes)
        sc["phq"] = [p.sbuf(f"l2_phq{i}", [128, 256], BF16, es=pes) for i in range(2)]
        sc["phk"] = [p.sbuf(f"l2_phk{i}", [128, 256], BF16, es=pes) for i in range(2)]
        sc["phkT"] = [p.sbuf(f"l2_phkT{i}", [128, 256], BF16, es=pes) for i in range(2)]
        sc["am"] = [p.sbuf(f"l2_am{i}", [128, 128], BF16, es=pes) for i in range(2)]
        sc["o_f"] = [p.sbuf(f"l2_of{i}", [128, 128], F32, es=pes) for i in range(2)]
        sc["osq"] = [p.sbuf(f"l2_osq{i}", [128, 128], BF16, es=pes) for i in range(2)]
        sc["ost"] = [p.sbuf(f"l2_ost{i}", [128, 128], BF16, es=pes) for i in range(2)]
        sc["ofm"] = [p.sbuf("l2_ofm0", [128, 1024], BF16, es=pes)]
        sc["l2orow"] = p.sbuf("l2_orow", [128, 16], F32, es=pes)
        for k in ("wbig_guard",):
            sc[k] = [list(fence0) for _ in range(4)]
        for k in ("fmw_guard", "qkv_guard", "vtm_guard", "S_guard"):
            sc[k] = list(fence0)
        for k in ("phq_guard", "phk_guard", "phkT_guard", "am_guard", "of_guard",
                  "osq_guard", "ost_guard", "ofm_guard"):
            sc[k] = [list(fence0), list(fence0)]
        sc["ofm_guard"] = [list(fence0)]
        l2_attn_batch(ctx, ws, sc)
        pes.close()
    fence1 = ctx.fence()
    if parts in ("all", "ffn"):
        pes2 = ExitStack()
        sc["w1b"] = p.sbuf("l2_w1b", [128, 8 * 1024], BF16, es=pes2)
        sc["w3b"] = p.sbuf("l2_w3b", [128, 8 * 1024], BF16, es=pes2)
        sc["w2b"] = p.sbuf("l2_w2b", [128, 8 * 1024], BF16, es=pes2)
        sc["hfm"] = p.sbuf("l2_hfm", [128, 8 * 512], BF16, es=pes2)
        sc["hscr2"] = p.sbuf("l2_hscr", [128, 512], F32, es=pes2)
        sc["gwb"] = p.sbuf("l2_gwb", [128, 64], BF16, es=pes2)
        sc["grow8"] = p.sbuf("l2_gr8", [1, 16], F32, es=pes2)
        sc["grow8b"] = p.sbuf("l2_gr8b", [1, 16], BF16, es=pes2)
        sc["gb_bc"] = p.sbuf("l2_gbbc", [128, 8], F32, es=pes2)
        sc["gsc8"] = p.sbuf("l2_gsc8", [128, 64], F32, es=pes2)
        sc["wts"] = p.sbuf("l2_wts", [128, NT * 8], F32, es=pes2)
        sc["gw_guard"] = list(fence1)
        sc["g8_guard"] = [list(fence1), list(fence1)]
        sc["g8s_guard"] = [list(fence1), list(fence1)]
        sc["wts_guard"] = list(fence1)
        sc["wmoe_guard"] = [list(fence1) for _ in range(3)]
        sc["h2_guard"] = list(fence1)
        sc["hfm_guard"] = list(fence1)
        wts_ready = moe_gate8(ctx, sc, "L2GW", ws, "l2", sc["wts"][:])
        for e in range(8):
            gcf = lambda ti, e=e: sc["wts"][:, ti * 8 + e : ti * 8 + e + 1]
            moe_expert_pass(ctx, sc, e, ws["L2W1"][e], ws["L2W3"][e], ws["L2W2"][e], 1024, gcf)
        for hf in range(2):
            moe_expert_pass(ctx, sc, 8 + hf, ws["L2SW1"][:, hf * 1024 : (hf + 1) * 1024],
                            ws["L2SW3"][:, hf * 1024 : (hf + 1) * 1024],
                            ws["L2SW2"][hf * 1024 : (hf + 1) * 1024, :], 1024, lambda ti: None)
        pes2.close()
    les.close()
    ctx.xn_fm = None
    if x_dst is None:
        return None
    return store_xres(ctx, x_dst)




def l3_attn_batch(ctx, ws, sc):
    p = ctx.p
    T, NT = ctx.T, ctx.NT
    xn_fm = ctx.xn_fm
    fence = sc["fence0"]
    RRD2 = 0.5
    R2S = float(2.0 ** -0.5)
    sub = ExitStack()
    wqkv = p.sbuf("l3_wqkv", [128, 12288], BF16, es=sub)
    sq, qw = load_w_cols(ctx, ws["L3WQ"], 0, 256, wqkv[:, 0:2048], wait=sc["l3w_guard"][0])
    sk, kw = load_w_cols(ctx, ws["L3WK"], 0, 256, wqkv[:, 2048:4096], wait=sc["l3w_guard"][1])
    sv_, vw = load_w_cols(ctx, ws["L3WV"], 0, 1024, wqkv[:, 4096:12288], wait=sc["l3w_guard"][2])
    q_tm, k_tm, v_tm = sc["q_tm"], sc["k_tm"], sc["v_tm"]
    prj_done = None
    for ti in range(NT):
        for wi, (dst, wid, nc_, scale) in enumerate(
            ((q_tm, 0, 256, RRD2), (k_tm, 1, 256, RRD2), (v_tm, 2, 1024, 1.0))
        ):
            for half in range(cdiv(nc_, 512)):
                hw_ = min(512, nc_ - half * 512)
                pi, pw = ctx.pool.acquire()
                ps = ctx.psF[pi]
                tv = None
                for c in range(8):
                    tv = p.step(
                        "tensor",
                        lambda t, ps=ps, c=c, ti=ti, wid=wid, half=half, hw_=hw_, nc_=nc_: t.matmul(
                            ps[:, 0:hw_],
                            xn_fm[:, c * T + ti * 128 : c * T + ti * 128 + 128],
                            wqkv[:, (0, 2048, 4096)[wid] + c * nc_ + half * 512 :
                                 (0, 2048, 4096)[wid] + c * nc_ + half * 512 + hw_],
                            start=(c == 0), stop=(c == 7)),
                        wait=(pw + [(qw, kw, vw)[wi]] + sc["xn_ready"]) if c == 0 else (),
                        inc=ctx.te if c == 7 else None)
                if scale != 1.0:
                    ev = p.step("scalar",
                                lambda e, ps=ps, dst=dst, ti=ti, nc_=nc_, half=half, hw_=hw_, scale=scale: e.activation(
                                    dst[:, ti * nc_ + half * 512 : ti * nc_ + half * 512 + hw_],
                                    ps[:, 0:hw_], AF.Copy, scale=scale),
                                wait=[(ctx.te, tv)], inc=ctx.se)
                    ctx.pool.release(pi, (ctx.se, ev))
                else:
                    ev = p.step("vector",
                                lambda v, ps=ps, dst=dst, ti=ti, nc_=nc_, half=half, hw_=hw_: v.tensor_copy(
                                    dst[:, ti * nc_ + half * 512 : ti * nc_ + half * 512 + hw_], ps[:, 0:hw_]),
                                wait=[(ctx.te, tv)], inc=ctx.ve)
                    ctx.pool.release(pi, (ctx.ve, ev))
    prj_done = [(ctx.se, ctx.se.n), (ctx.ve, ctx.ve.n)]
    te_prj = [(ctx.te, ctx.te.n)]
    for s_ in (sq, sk, sv_):
        ctx.rw.ack(s_, *te_prj)
    sub.close()
    wo_sb = p.sbuf("l3_wo", [128, 8192], BF16, es=sc["pes"])
    sc["l3wo"] = wo_sb
    wo_slot, wo_w = load_w_cols(ctx, ws["L3WO"], 0, 1024, wo_sb,
                                wait=te_prj + [qw, kw, vw])
    qf = sc["qf"]
    kf = sc["kf"]
    qffm = sc["qffm"]
    kfm0 = sc["kfm0"]
    vaug = sc["vaug"]
    a2 = sc["a2"]
    am3 = sc["am3"]
    ost3 = sc["ost3"]
    ofm3 = sc["ofm3"]
    S32 = sc["S32_3"]
    Sbf = sc["Sbf_3"]
    zrow = sc["zrow"]
    mask = ctx.cba("mask_t2_le_t1")
    zv = p.step("vector", lambda v: v.memset(S32[:], 0.0), wait=list(fence) + sc["S3_guard"], inc=ctx.ve)
    zv = p.step("vector", lambda v: v.memset(Sbf[:], 0.0), wait=[(ctx.ve, zv)], inc=ctx.ve)
    for b_ in qf + kf:
        zv = p.step("vector", lambda v, b_=b_: v.memset(b_[:, 256:384], 0.0), wait=[(ctx.ve, zv)], inc=ctx.ve)
        zv = p.step("vector", lambda v, b_=b_: v.memset(b_[:, 16:17], 1.0), wait=[(ctx.ve, zv)], inc=ctx.ve)
    for b_ in vaug:
        zv = p.step("vector", lambda v, b_=b_: v.memset(b_[:, 64:65], 1.0), wait=[(ctx.ve, zv)], inc=ctx.ve)
    init_done = (ctx.ve, zv)
    s_ready = [[init_done] for _ in range(16)]
    qf_guard = [[init_done], [init_done]]
    kf_guard = [[init_done], [init_done]]
    va_guard = [[init_done], [init_done]]
    ofm_ready = []
    for cg in range(NT):
        for h in range(16):
            set_ = (cg * 16 + h) % 2
            qsl = q_tm[:, cg * 256 + h * 16 : cg * 256 + h * 16 + 16]
            ksl = k_tm[:, cg * 256 + h * 16 : cg * 256 + h * 16 + 16]
            vv = p.step("vector", lambda v, qsl=qsl, set_=set_: v.tensor_copy(qf[set_][:, 0:16], qsl),
                        wait=prj_done + qf_guard[set_], inc=ctx.ve)
            q3a = qsl.rearrange("p (d one) -> p d one", d=16, one=1).broadcast_to((128, 16, 16))
            q3b = qsl.rearrange("p (one d) -> p one d", one=1, d=16).broadcast_to((128, 16, 16))
            o3 = qf[set_][:, 17:273].rearrange("p (a b) -> p a b", a=16, b=16)
            vv = p.step("vector", lambda v, o3=o3, q3a=q3a, q3b=q3b: v.scalar_tensor_tensor(
                o3, q3a, R2S, q3b, ALU.mult, ALU.mult), wait=[(ctx.ve, vv)], inc=ctx.ve)
            qfv = vv
            vv = p.step("vector", lambda v, ksl=ksl, set_=set_: v.tensor_copy(kf[set_][:, 0:16], ksl),
                        wait=prj_done + kf_guard[set_], inc=ctx.ve)
            k3a = ksl.rearrange("p (d one) -> p d one", d=16, one=1).broadcast_to((128, 16, 16))
            k3b = ksl.rearrange("p (one d) -> p one d", one=1, d=16).broadcast_to((128, 16, 16))
            ko3 = kf[set_][:, 17:273].rearrange("p (a b) -> p a b", a=16, b=16)
            vv = p.step("vector", lambda v, ko3=ko3, k3a=k3a, k3b=k3b: v.scalar_tensor_tensor(
                ko3, k3a, R2S, k3b, ALU.mult, ALU.mult), wait=[(ctx.ve, vv)], inc=ctx.ve)
            kfv = vv
            vva = p.step("vector",
                         lambda v, set_=set_, cg=cg, h=h: v.tensor_copy(
                             vaug[set_][:, 0:64], v_tm[:, cg * 1024 + h * 64 : cg * 1024 + h * 64 + 64]),
                         wait=prj_done + va_guard[set_], inc=ctx.ve)
            trq = pe_transpose_pairs(ctx, [(qf[set_][:, i * 128 : (i + 1) * 128],
                                            qffm[set_][:, i * 128 : (i + 1) * 128]) for i in range(3)],
                                     wait=[(ctx.ve, qfv)] + sc["qffm_guard"][set_])
            trk = pe_transpose_pairs(ctx, [(kf[set_][:, 0:128], kfm0[set_][:])],
                                     wait=[(ctx.ve, kfv)] + sc["kfm0_guard"][set_])
            pi, pw = ctx.pool.acquire()
            ps = ctx.psF[pi]
            tv = p.step("tensor",
                        lambda t, ps=ps, set_=set_: t.matmul(
                            ps[:, 0:128], kfm0[set_][0:16, 0:128], qffm[set_][0:16, 0:128],
                            start=True, stop=True),
                        wait=pw + [(ctx.ve, trk), (ctx.ve, trq)], inc=ctx.te)
            sv = p.step("scalar",
                        lambda e, ps=ps: e.activation(a2[:], ps[:, 0:128], AF.Square, bias=1.0),
                        wait=[(ctx.te, tv)] + sc["a2_guard"], inc=ctx.se)
            ctx.pool.release(pi, (ctx.se, sv))
            vv = p.step("vector",
                        lambda v: v.tensor_scalar(a2[:], a2[:], 0.5, 0.5, ALU.mult, ALU.add),
                        wait=[(ctx.se, sv)], inc=ctx.ve)
            amv = p.step("vector",
                         lambda v, set_=set_: v.tensor_mul(am3[set_][:], a2[:], mask),
                         wait=[(ctx.ve, vv)] + sc["am3_guard"][set_], inc=ctx.ve)
            sc["a2_guard"] = [(ctx.ve, amv)]
            pi4, pw4 = ctx.pool.acquire()
            ps4 = ctx.psF[pi4]
            tv = p.step("tensor",
                        lambda t, ps4=ps4, set_=set_: t.matmul(
                            ps4[:, 0:65], am3[set_][:], vaug[set_][:, 0:65], start=True, stop=False),
                        wait=pw4 + [(ctx.ve, amv), (ctx.ve, vva)], inc=None)
            for fc in range(3):
                tv = p.step("tensor",
                            lambda t, ps4=ps4, fc=fc, set_=set_, h=h: t.matmul(
                                ps4[:, 0:65], qffm[set_][:, fc * 128 : (fc + 1) * 128],
                                Sbf[:, h * 204 + fc * 68 : h * 204 + fc * 68 + 65],
                                start=False, stop=(fc == 2)),
                            wait=s_ready[h] if fc == 0 else (), inc=ctx.te if fc == 2 else None)
            p1_done = (ctx.te, tv)
            pi5, pw5 = ctx.pool.acquire()
            ps5 = ctx.psF[pi5]
            tv2 = None
            for fc in range(3):
                tv2 = p.step("tensor",
                             lambda t, ps5=ps5, fc=fc, set_=set_: t.matmul(
                                 ps5[:, fc * 68 : fc * 68 + 65],
                                 kf[set_][:, fc * 128 : (fc + 1) * 128],
                                 vaug[set_][:, 0:65],
                                 start=(fc == 0), stop=(fc == 2)),
                             wait=(pw5 + [(ctx.ve, kfv), (ctx.ve, vva)] + s_ready[h]) if fc == 0 else (),
                             inc=ctx.te if fc == 2 else None)
            kf_guard[set_] = [(ctx.te, tv2)]
            qf_guard[set_] = [(ctx.te, ctx.te.n)]
            va_guard[set_] = [(ctx.te, tv2)]
            s33 = S32[:, h * 204 : h * 204 + 204].rearrange("p (c w) -> p c w", c=3, w=68)[:, :, 0:65]
            ps53 = ps5[:, 0:204].rearrange("p (c w) -> p c w", c=3, w=68)[:, :, 0:65]
            sb3 = Sbf[:, h * 204 : h * 204 + 204].rearrange("p (c w) -> p c w", c=3, w=68)[:, :, 0:65]
            sv1 = p.step("vector", lambda v, ps53=ps53, s33=s33: v.tensor_add(s33, ps53, s33),
                         wait=[(ctx.te, tv2)], inc=ctx.ve)
            sv2 = p.step("vector", lambda v, sb3=sb3, s33=s33: v.tensor_copy(sb3, s33),
                         wait=[(ctx.ve, sv1)], inc=ctx.ve)
            ctx.pool.release(pi5, (ctx.ve, sv1))
            s_ready[h] = [(ctx.ve, sv2)]
            zc0 = set_ * 4
            vv = p.step("vector",
                        lambda v, ps4=ps4, zc0=zc0: v.tensor_scalar(
                            zrow[:, zc0 : zc0 + 1], ps4[:, 64:65], 1e-12, None, ALU.add),
                        wait=[p1_done] + sc["zrow_guard"][set_], inc=ctx.ve)
        
            vv = p.step("vector",
                        lambda v, zc0=zc0: v.reciprocal(zrow[:, zc0 + 1 : zc0 + 2], zrow[:, zc0 : zc0 + 1]),
                        wait=[(ctx.ve, vv)], inc=ctx.ve)
            ev = p.step("vector",
                        lambda v, ps4=ps4, set_=set_, zc0=zc0: v.tensor_scalar(
                            ost3[set_][:], ps4[:, 0:64], zrow[:, zc0 + 1 : zc0 + 2], None, ALU.mult),
                        wait=[(ctx.ve, vv)] + sc["ost3_guard"][set_], inc=ctx.ve)
            ctx.pool.release(pi4, (ctx.ve, ev))
            sc["zrow_guard"][set_] = [(ctx.ve, ev)]
            tro = pe_transpose_pairs(
                ctx, [(ost3[set_][:], ofm3[(h % 2) * 64 : (h % 2) * 64 + 64,
                                           (h // 2) * 128 : (h // 2) * 128 + 128])],
                wait=[(ctx.ve, ev)] + (sc["ofm3_guard"] if h == 0 else []), in_w=64)
            sc["ost3_guard"][set_] = [(ctx.te, ctx.te.n)]
            sc["am3_guard"][set_] = [(ctx.te, ctx.te.n)]
            sc["qffm_guard"][set_] = [(ctx.te, ctx.te.n)]
            sc["kfm0_guard"][set_] = [(ctx.te, ctx.te.n)]
        ofm_ready = (ctx.ve, ctx.ve.n)
        for half in range(2):
            pi, pw = ctx.pool.acquire()
            ps = ctx.psF[pi]
            tv = None
            for j in range(8):
                tv = p.step(
                    "tensor",
                    lambda t, ps=ps, j=j, half=half: t.matmul(
                        ps[:],
                        ofm3[:, j * 128 : (j + 1) * 128],
                        sc["l3wo"][:, j * 1024 + half * 512 : j * 1024 + half * 512 + 512],
                        start=(j == 0), stop=(j == 7)),
                    wait=(pw + [wo_w, ofm_ready]) if j == 0 else (),
                    inc=ctx.te if j == 7 else None)
            xr = ctx.x_res[:, cg * 1024 + half * 512 : cg * 1024 + half * 512 + 512]
            vv3 = p.step("vector", lambda v, xr=xr, ps=ps: v.tensor_add(xr, ps[:], xr),
                         wait=[(ctx.te, tv)], inc=ctx.ve)
            ctx.pool.release(pi, (ctx.ve, vv3))
        sc["ofm3_guard"] = [(ctx.te, ctx.te.n)]
    te_g = [(ctx.te, ctx.te.n)]
    ctx.rw.ack(wo_slot, *te_g)
    sc["l3w_guard"] = [list(te_g) for _ in range(4)]
    return [(ctx.ve, ctx.ve.n)]


def l3_ffn_batch(ctx, ws, sc):
    p = ctx.p
    T, NT = ctx.NT * 0 + ctx.T, ctx.NT
    NT = ctx.NT
    xn_fm = ctx.xn_fm
    fence = sc["ffn_fence"]
    wts = sc["wts3"]
    wts_ready = moe_gate8(ctx, sc, "L3GW", ws, "l3", wts[:])
    b1sb = sc["b1sb"]
    sb1, b1w = ctx.rx.load(lambda e: e.dma_start(out=b1sb[:], in_=ws["L3B1"]), wait=list(fence))
    w1buf = sc["w1buf"]
    w2buf = sc["w2buf"]
    hbuf = sc["hbuf"]
    w1g = sc["w1g"]
    w2g = sc["w2g"]
    hg = sc["hg"]
    for e in range(8):
        for fh in range(4):
            s1, w1w = ctx.rw.load(
                lambda eng, e=e, fh=fh: eng.dma_start(
                    out=w1buf[:].rearrange("p (c w) -> p c w", c=8, w=1024),
                    in_=ws["L3W1"][e][:, fh * 1024 : (fh + 1) * 1024].rearrange("(c p) w -> p c w", p=128)),
                wait=w1g)
            s2, w2w = ctx.rw.load(
                lambda eng, e=e, fh=fh: eng.dma_start(
                    out=w2buf[:].rearrange("p (c w) -> p c w", c=8, w=1024),
                    in_=ws["L3W2"][e][fh * 1024 : (fh + 1) * 1024, :].rearrange("(c p) w -> p c w", p=128)),
                wait=w2g)
            for fc in range(8):
                fcg = fh * 8 + fc
                for ts4 in range(cdiv(NT, 4)):
                    w512 = min(512, NT * 128 - ts4 * 512)
                    pi, pw = ctx.pool.acquire()
                    ps = ctx.psF[pi]
                    tv = None
                    for c in range(8):
                        tv = p.step(
                            "tensor",
                            lambda t, ps=ps, c=c, fc=fc, ts4=ts4, w512=w512: t.matmul(
                                ps[:, 0:w512],
                                w1buf[:, c * 1024 + fc * 128 : c * 1024 + fc * 128 + 128],
                                xn_fm[:, c * T + ts4 * 512 : c * T + ts4 * 512 + w512],
                                start=(c == 0), stop=(c == 7)),
                            wait=(pw + [w1w] + sc["xn_ready"]) if c == 0 else (),
                            inc=ctx.te if c == 7 else None)
                    ev = p.step(
                        "scalar",
                        lambda sce, ps=ps, fc=fc, fcg=fcg, ts4=ts4, w512=w512, e=e: sce.activation(
                            hbuf[:, fc * (NT * 128) + ts4 * 512 : fc * (NT * 128) + ts4 * 512 + w512],
                            ps[:, 0:w512], AF.Gelu, bias=b1sb[:, e * 32 + fcg : e * 32 + fcg + 1]),
                        wait=[(ctx.te, tv), b1w] + (hg if (fc == 0 and ts4 == 0) else []),
                        inc=ctx.se)
                    ctx.pool.release(pi, (ctx.se, ev))
            h_ready = (ctx.se, ctx.se.n)
            for ti in range(NT):
                pis = []
                for half in range(2):
                    pi, pw = ctx.pool.acquire()
                    ps = ctx.psF[pi]
                    tv = None
                    for fc in range(8):
                        tv = p.step(
                            "tensor",
                            lambda t, ps=ps, fc=fc, ti=ti, half=half: t.matmul(
                                ps[:],
                                hbuf[:, fc * (NT * 128) + ti * 128 : fc * (NT * 128) + ti * 128 + 128],
                                w2buf[:, fc * 1024 + half * 512 : fc * 1024 + half * 512 + 512],
                                start=(fc == 0), stop=(fc == 7)),
                            wait=(pw + [w2w, h_ready]) if fc == 0 else (),
                            inc=ctx.te if fc == 7 else None)
                    pis.append((pi, tv))
                for half in range(2):
                    pi, tv = pis[half]
                    ps = ctx.psF[pi]
                    xr = ctx.x_res[:, ti * 1024 + half * 512 : ti * 1024 + half * 512 + 512]
                    gc = wts[:, ti * 8 + e : ti * 8 + e + 1]
                    vv = p.step("vector",
                                lambda v, xr=xr, ps=ps, gc=gc: v.scalar_tensor_tensor(
                                    xr, ps[:], gc, xr, ALU.mult, ALU.add),
                                wait=[(ctx.te, tv)] + wts_ready, inc=ctx.ve)
                    ctx.pool.release(pi, (ctx.ve, vv))
            w1g = [(ctx.te, ctx.te.n), w1w]
            w2g = [(ctx.te, ctx.te.n), w2w]
            hg = [(ctx.te, ctx.te.n)]
            ctx.rw.ack(s1, (ctx.te, ctx.te.n))
            ctx.rw.ack(s2, (ctx.te, ctx.te.n))
    ctx.rx.ack(sb1, (ctx.se, ctx.se.n))
    b2sb = sc["b2sb"]
    sb2, b2w = ctx.rw.load(lambda e: e.dma_start(out=b2sb[:], in_=ws["L3B2"]), wait=list(fence))
    wtsf = sc["wtsf"]
    last = None
    for ti in range(NT):
        vv = p.step("vector", lambda v, ti=ti: v.tensor_copy(wtsf[:, 0:8], wts[:, ti * 8 : (ti + 1) * 8]),
                    wait=wts_ready + sc["wtsf_guard"], inc=ctx.ve)
        trw = pe_transpose_pairs(ctx, [(wtsf[:, 0:8], sc["wtsfT"][0:8, 0:128])],
                                 wait=[(ctx.ve, vv)] + sc["wtsfT_guard"], in_w=8)
        for half in range(2):
            pi, pw = ctx.pool.acquire()
            ps = ctx.psF[pi]
            tv = p.step("tensor",
                        lambda t, ps=ps, half=half: t.matmul(
                            ps[:], sc["wtsfT"][0:8, 0:128], b2sb[0:8, half * 512 : half * 512 + 512],
                            start=True, stop=True),
                        wait=pw + [b2w, (ctx.ve, trw)], inc=ctx.te)
            xr = ctx.x_res[:, ti * 1024 + half * 512 : ti * 1024 + half * 512 + 512]
            vv2 = p.step("vector", lambda v, xr=xr, ps=ps: v.tensor_add(xr, ps[:], xr),
                         wait=[(ctx.te, tv)], inc=ctx.ve)
            ctx.pool.release(pi, (ctx.ve, vv2))
            last = vv2
        sc["wtsf_guard"] = [(ctx.te, ctx.te.n)]
        sc["wtsfT_guard"] = [(ctx.te, ctx.te.n)]
    ctx.rw.ack(sb2, (ctx.te, ctx.te.n))
    return [(ctx.ve, ctx.ve.n)]


def layer3(ctx, ws, x_src, x_dst, prev_store=None, parts="all"):
    p = ctx.p
    NT, T = ctx.NT, ctx.T
    les = ExitStack()
    sc = {}
    ctx.xn_fm = p.sbuf("l3_xnfm", [128, 8 * T], BF16, es=les)
    ln_phase(ctx, x_src, prev_store)
    sc["xn_ready"] = [(ctx.ve, ctx.ve.n), (ctx.se, ctx.se.n)]
    fence0 = ctx.fence()
    sc["fence0"] = fence0
    if parts in ("all", "attn"):
        pes = ExitStack()
        sc["q_tm"] = p.sbuf("l3_qtm", [128, NT * 256], BF16, es=pes)
        sc["k_tm"] = p.sbuf("l3_ktm", [128, NT * 256], BF16, es=pes)
        sc["v_tm"] = p.sbuf("l3_vtm", [128, NT * 1024], BF16, es=pes)
        sc["qf"] = [p.sbuf(f"l3_qf{i}", [128, 384], BF16, es=pes) for i in range(2)]
        sc["kf"] = [p.sbuf(f"l3_kf{i}", [128, 384], BF16, es=pes) for i in range(2)]
        sc["qffm"] = [p.sbuf(f"l3_qffm{i}", [128, 384], BF16, es=pes) for i in range(2)]
        sc["kfm0"] = [p.sbuf(f"l3_kfm0{i}", [128, 128], BF16, es=pes) for i in range(2)]
        sc["vaug"] = [p.sbuf(f"l3_vaug{i}", [128, 68], BF16, es=pes) for i in range(2)]
        sc["a2"] = p.sbuf("l3_a2", [128, 128], F32, es=pes)
        sc["am3"] = [p.sbuf(f"l3_am{i}", [128, 128], BF16, es=pes) for i in range(2)]
        sc["ost3"] = [p.sbuf(f"l3_ost{i}", [128, 64], BF16, es=pes) for i in range(2)]
        sc["ofm3"] = p.sbuf("l3_ofm", [128, 1024], BF16, es=pes)
        sc["S32_3"] = p.sbuf("l3_S32", [128, 16 * 204], F32, es=pes)
        sc["Sbf_3"] = p.sbuf("l3_Sbf", [128, 16 * 204], BF16, es=pes)
        sc["zrow"] = p.sbuf("l3_zrow", [128, 16], F32, es=pes)
        sc["pes"] = pes
        sc["l3w_guard"] = [list(fence0) for _ in range(4)]
        sc["S3_guard"] = list(fence0)
        sc["a2_guard"] = list(fence0)
        sc["ofm3_guard"] = list(fence0)
        for k in ("qffm_guard", "kfm0_guard", "am3_guard", "ost3_guard", "zrow_guard"):
            sc[k] = [list(fence0), list(fence0)]
        l3_attn_batch(ctx, ws, sc)
        pes.close()
    fence1 = ctx.fence()
    sc["ffn_fence"] = fence1
    if parts in ("all", "ffn"):
        pes2 = ExitStack()
        sc["wts3"] = p.sbuf("l3_wts", [128, NT * 8], F32, es=pes2)
        sc["gwb"] = p.sbuf("l3_gwb", [128, 64], BF16, es=pes2)
        sc["grow8"] = p.sbuf("l3_gr8", [1, 16], F32, es=pes2)
        sc["grow8b"] = p.sbuf("l3_gr8b", [1, 16], BF16, es=pes2)
        sc["gb_bc"] = p.sbuf("l3_gbbc", [128, 8], F32, es=pes2)
        sc["gsc8"] = p.sbuf("l3_gsc8", [128, 64], F32, es=pes2)
        sc["b1sb"] = p.sbuf("l3_b1", [128, 256], F32, es=pes2)
        sc["b2sb"] = p.sbuf("l3_b2", [8, 1024], BF16, es=pes2)
        sc["w1buf"] = p.sbuf("l3_w1b", [128, 8 * 1024], BF16, es=pes2)
        sc["w2buf"] = p.sbuf("l3_w2b", [128, 8 * 1024], BF16, es=pes2)
        sc["hbuf"] = p.sbuf("l3_hb", [128, 8 * NT * 128], BF16, es=pes2)
        sc["wtsf"] = p.sbuf("l3_wtsf", [128, 8], BF16, es=pes2)
        sc["wtsfT"] = p.sbuf("l3_wtsfT", [8, 128], BF16, es=pes2)
        sc["gw_guard"] = list(fence1)
        sc["g8_guard"] = [list(fence1), list(fence1)]
        sc["g8s_guard"] = [list(fence1), list(fence1)]
        sc["wts_guard"] = list(fence1)
        sc["wtsf_guard"] = list(fence1)
        sc["wtsfT_guard"] = list(fence1)
        sc["w1g"] = list(fence1)
        sc["w2g"] = list(fence1)
        sc["hg"] = list(fence1)
        l3_ffn_batch(ctx, ws, sc)
        pes2.close()
    les.close()
    ctx.xn_fm = None
    if x_dst is None:
        return None
    return store_xres(ctx, x_dst)
from concourse.bass_utils import run_bass_kernel_spmd

D4_T = 2048
D4_NB = 2
N_CORES = 8


def _win(nc, ws, host, name, arr):
    arr = np.ascontiguousarray(arr, dtype=np.float32)
    ws[name] = nc.dram_tensor(name, list(arr.shape), F32, kind="ExternalInput").ap()
    host[name] = arr


def _prep_host_weights(inputs):
    h = {}
    Wqkv = inputs["L0_attn_Wqkv"]
    h["WQ"] = Wqkv[0:1024].T
    h["WK"] = Wqkv[1024:2048].T
    h["WV"] = Wqkv[2048:3072].T
    h["WOUT"] = inputs["L0_attn_Wout"].T
    h["SIM"] = inputs["L0_ffn_sim_matrix"]
    h["SIMT"] = inputs["L0_ffn_sim_matrix"].T
    h["GATES"] = np.asarray(inputs["L0_ffn_gates"]).reshape(1, 16)
    h["FHN"] = np.array([[inputs["L0_attn_a"], inputs["L0_attn_b"], inputs["L0_attn_dt"]]],
                        dtype=np.float32)
    h["QP"] = np.asarray(inputs["L0_ffn_q_proj"]).transpose(1, 2, 0).reshape(1024, 1024)
    h["KP"] = np.asarray(inputs["L0_ffn_k_proj"]).transpose(1, 2, 0).reshape(1024, 1024)
    h["VP"] = np.asarray(inputs["L0_ffn_v_proj"]).transpose(1, 2, 0).reshape(1024, 1024)
    h["OP"] = np.asarray(inputs["L0_ffn_o_proj"]).transpose(1, 0, 2).reshape(1024, 1024)
    h["L1WG"] = inputs["L1_ffn_w_gate"]
    h["L1WU"] = inputs["L1_ffn_w_up"]
    h["L1WD"] = inputs["L1_ffn_w_down"]
    h["L2WQ"] = inputs["L2_attn_Wq"].T
    h["L2WK"] = inputs["L2_attn_Wk"].T
    h["L2WV"] = inputs["L2_attn_Wv"].T
    h["L2WO"] = inputs["L2_attn_Wo"].T
    h["L2FMQ"] = inputs["L2_attn_fmq_w"].T
    h["L2FMK"] = inputs["L2_attn_fmk_w"].T
    h["L2FMB"] = np.stack([np.asarray(inputs["L2_attn_fmq_b"]),
                           np.asarray(inputs["L2_attn_fmk_b"])], axis=1)
    h["L2RMSW"] = np.asarray(inputs["L2_attn_rms_w"]).reshape(1, 128)
    h["L2GW"] = inputs["L2_ffn_gate_w"].T
    h["L2GB"] = np.asarray(inputs["L2_ffn_gate_bias"]).reshape(1, 8)
    h["L2W1"] = np.asarray(inputs["L2_ffn_w1"]).transpose(0, 2, 1)
    h["L2W3"] = np.asarray(inputs["L2_ffn_w3"]).transpose(0, 2, 1)
    h["L2W2"] = np.asarray(inputs["L2_ffn_w2"]).transpose(0, 2, 1)
    h["L2SW1"] = inputs["L2_ffn_sw1"].T
    h["L2SW3"] = inputs["L2_ffn_sw3"].T
    h["L2SW2"] = inputs["L2_ffn_sw2"].T
    h["L3WQ"] = inputs["L3_attn_Wq"]
    h["L3WK"] = inputs["L3_attn_Wk"]
    h["L3WV"] = inputs["L3_attn_Wv"]
    h["L3WO"] = inputs["L3_attn_Wo"]
    h["L3GW"] = inputs["L3_ffn_Wg"]
    h["L3GB"] = np.asarray(inputs["L3_ffn_bg"]).reshape(1, 8)
    h["L3W1"] = inputs["L3_ffn_W1"]
    b1 = np.asarray(inputs["L3_ffn_b1"])
    h["L3B1"] = np.concatenate([b1[e].reshape(32, 128).T for e in range(8)], axis=1)
    h["L3W2"] = inputs["L3_ffn_W2"]
    h["L3B2"] = inputs["L3_ffn_b2"]
    return {k: np.ascontiguousarray(v, dtype=np.float32) for k, v in h.items()}


_L0_KEYS = ["WQ", "WK", "WV", "WOUT", "SIM", "SIMT", "GATES", "FHN", "QP", "KP", "VP", "OP"]
_L1_KEYS = ["L1WG", "L1WU", "L1WD"]
_L2_KEYS = ["L2WQ", "L2WK", "L2WV", "L2WO", "L2FMQ", "L2FMK", "L2FMB", "L2RMSW",
            "L2GW", "L2GB", "L2W1", "L2W3", "L2W2", "L2SW1", "L2SW3", "L2SW2"]
_L3_KEYS = ["L3WQ", "L3WK", "L3WV", "L3WO", "L3GW", "L3GB", "L3W1", "L3B1", "L3W2", "L3B2"]


def _build_layer_prog(layer_idx, host):
    nc = bass.Bass("TRN2", target_bir_lowering=False)
    p = Prog(nc)
    ctx = Ctx(p, D4_T)
    ws = {}
    hw = {}
    keys = (_L0_KEYS, _L1_KEYS, _L2_KEYS, _L3_KEYS)[layer_idx]
    for k in keys:
        _win(nc, ws, hw, k, host[k])
    X = nc.dram_tensor("X", [D4_NB, D4_T, 1024], F32, kind="ExternalInput").ap()
    XO = nc.dram_tensor("XO", [D4_NB, D4_T, 1024], F32, kind="ExternalOutput").ap()
    ctx.load_consts()
    wqs = None
    wq_ready = None
    if layer_idx == 1:
        WQG = nc.dram_tensor("WQG", [1024, 2816], BF16, kind="Internal").ap()
        WQU = nc.dram_tensor("WQU", [1024, 2816], BF16, kind="Internal").ap()
        WQD = nc.dram_tensor("WQD", [2816, 1024], BF16, kind="Internal").ap()
        wqs = (WQG, WQU, WQD)
        wq_ready = l1_wquant(ctx, ws, wqs)
    prev = None
    for b in range(D4_NB):
        if layer_idx == 0:
            prev = layer0(ctx, ws, X[b], XO[b], prev_store=prev)
        elif layer_idx == 1:
            prev = layer1(ctx, ws, X[b], XO[b], wqs, wq_ready, prev_store=prev)
        elif layer_idx == 2:
            prev = layer2(ctx, ws, X[b], XO[b], prev_store=prev)
        else:
            prev = layer3(ctx, ws, X[b], XO[b], prev_store=prev)
    p.step("sync", lambda e: None, wait=[prev])
    ctx.drain_rings()
    p.finalize_block()
    p.es.close()
    consts = {"CONSTF": ctx.const_f32_host, "CONSTB": ctx.const_bf_host}
    return nc, hw, consts


def kernel(**inputs):
    x = np.ascontiguousarray(np.asarray(inputs["L0_attn_x"], dtype=np.float32))
    host = _prep_host_weights(inputs)
    cur = x
    for li in range(4):
        nc, hw, consts = _build_layer_prog(li, host)
        in_maps = []
        for c in range(N_CORES):
            m = {"X": np.ascontiguousarray(cur[c * D4_NB : (c + 1) * D4_NB])}
            m.update(hw)
            m.update(consts)
            in_maps.append(m)
        res = run_bass_kernel_spmd(nc, in_maps, core_ids=list(range(N_CORES)))
        cur = np.concatenate([np.asarray(r["XO"]) for r in res.results], axis=0)
    return cur.astype(np.float32)


_ALL_KEYS = _L0_KEYS + _L1_KEYS + _L2_KEYS + _L3_KEYS


def _build_fused_prog(host, layers=(0, 1, 2, 3)):
    nc = bass.Bass("TRN2", target_bir_lowering=False)
    p = Prog(nc)
    ctx = Ctx(p, D4_T)
    ws = {}
    hw = {}
    for k in _ALL_KEYS:
        _win(nc, ws, hw, k, host[k])
    X = nc.dram_tensor("X", [D4_NB, D4_T, 1024], F32, kind="ExternalInput").ap()
    XO = nc.dram_tensor("XO", [D4_NB, D4_T, 1024], F32, kind="ExternalOutput").ap()
    ctx.load_consts()
    WQG = nc.dram_tensor("WQG", [1024, 2816], BF16, kind="Internal").ap()
    WQU = nc.dram_tensor("WQU", [1024, 2816], BF16, kind="Internal").ap()
    WQD = nc.dram_tensor("WQD", [2816, 1024], BF16, kind="Internal").ap()
    wqs = (WQG, WQU, WQD)
    import os as _os
    dbg = _os.environ.get("K_DBG") == "1"
    DBGT = {}
    if dbg:
        for li in range(4):
            DBGT[li] = nc.dram_tensor(f"DBG{li}", [D4_NB, D4_T, 1024], F32, kind="ExternalOutput").ap()
    wq_ready = None
    prev = None
    for b in range(D4_NB):
        first = True
        def src_of():
            nonlocal first
            if first:
                first = False
                return X[b]
            return None
        last = max(layers)
        for li in layers:
            if li == 1 and wq_ready is None and 1 in layers:
                wq_ready = l1_wquant(ctx, ws, wqs)
            xs = X[b] if li == min(layers) else None
            xd = XO[b] if li == last else None
            if li == 0:
                r = layer0(ctx, ws, xs, xd, prev_store=prev)
            elif li == 1:
                r = layer1(ctx, ws, xs, xd, wqs, wq_ready, prev_store=prev)
            elif li == 2:
                r = layer2(ctx, ws, xs, xd, prev_store=prev)
            else:
                r = layer3(ctx, ws, xs, xd, prev_store=prev)
            if dbg:
                r2 = store_xres(ctx, DBGT[li][b])
                if r is None:
                    r = r2
        prev = r
    p.step("sync", lambda e: None, wait=[prev])
    ctx.drain_rings()
    p.finalize_block()
    p.es.close()
    consts = {"CONSTF": ctx.const_f32_host, "CONSTB": ctx.const_bf_host}
    return nc, hw, consts


def kernel_fused(**inputs, ):
    x = np.ascontiguousarray(np.asarray(inputs["L0_attn_x"], dtype=np.float32))
    host = _prep_host_weights(inputs)
    import os
    lys = tuple(int(c) for c in os.environ.get("K_LAYERS", "0123"))
    nc, hw, consts = _build_fused_prog(host, layers=lys)
    in_maps = []
    for c in range(N_CORES):
        m = {"X": np.ascontiguousarray(x[c * D4_NB : (c + 1) * D4_NB])}
        m.update(hw)
        m.update(consts)
        in_maps.append(m)
    res = run_bass_kernel_spmd(nc, in_maps, core_ids=list(range(N_CORES)))
    return np.concatenate([np.asarray(r["XO"]) for r in res.results], axis=0).astype(np.float32)


kernel = kernel_fused
